# version 25
# baseline (speedup 1.0000x reference)
import math
import numpy as np
import contextlib
from collections import defaultdict

import concourse.bass as bass
import concourse.mybir as mybir
from concourse.alu_op_type import AluOpType as OP
from concourse.bass_utils import run_bass_kernel_spmd

F32 = mybir.dt.float32
BF16 = mybir.dt.bfloat16
I32 = mybir.dt.int32
AF = mybir.ActivationFunctionType
AX = mybir.AxisListType.X

S = 2048
D = 1024

QUEUES = ("sync", "gpsimd", "vector", "scalar", "tensor")
ENG_SEM = ("vector", "scalar", "tensor", "gpsimd")


class Sched:

    def __init__(self, nc, es):
        self.nc = nc
        self.es = es
        self.q = defaultdict(list)
        self.cnt = {}
        self.esem = {}
        for eng in ENG_SEM:
            self.esem[eng] = es.enter_context(nc.semaphore(f"prog_{eng}"))
        self.eng_pos = defaultdict(int)
        self.pe_inc_positions = []
        self.last_w = {}
        self.readers = defaultdict(list)
        self.max_waited = defaultdict(dict)
        self.nwaits = 0
        self.nops = 0
        self.dma_sems = []

    def new_sem(self, name):
        s = self.es.enter_context(self.nc.semaphore(name))
        return s

    def new_dma_sem(self, name):
        s = self.new_sem(name)
        self.dma_sems.append(s)
        return s

    def retire_dmas(self, queue="vector"):
        waits = [(s, self.count(s)) for s in self.dma_sems]
        self._emit(queue, lambda e: e.nop(), waits, None, 1)

    def count(self, sem):
        return self.cnt.get(id(sem), 0)

    def _bump(self, sem, k=1):
        self.cnt[id(sem)] = self.cnt.get(id(sem), 0) + k
        return self.cnt[id(sem)]

    def _resolve(self, h, dst_eng):
        if h is None:
            return None
        sem, val, src_eng = h
        if src_eng == "tensor" and dst_eng == "tensor":
            return None
        if src_eng == "dma":
            val = self.cnt.get(id(sem), val)
        return (sem, val)

    def _emit(self, queue, fn, waits, inc_sem, inc_k):
        bysem = {}
        for w in waits:
            if w is None:
                continue
            s, v = w
            if v <= 0:
                continue
            k = id(s)
            if k not in bysem or v > bysem[k][1]:
                bysem[k] = (s, v)
        wl = []
        for s, v in bysem.values():
            prev = self.max_waited[queue].get(id(s), -1)
            if v > prev:
                self.max_waited[queue][id(s)] = v
                wl.append((s, v))
        self.nwaits += len(wl)
        self.nops += 1

        def emit(e, fn=fn, wl=tuple(wl), inc_sem=inc_sem, inc_k=inc_k):
            for (s, v) in wl:
                e.wait_ge(s, v)
            inst = fn(e)
            if inc_sem is not None:
                inst.then_inc(inc_sem, inc_k)

        self.q[queue].append(emit)

    def _dep_waits(self, dst_eng, reads, writes):
        waits = []
        for tok in reads:
            try:
                waits.append(self._resolve(self.last_w.get(tok), dst_eng))
            except RuntimeError as e:
                raise RuntimeError(f"{e} (RAW on token {tok})")
        for tok in writes:
            for h in self.readers.get(tok, ()):
                try:
                    waits.append(self._resolve(h, dst_eng))
                except RuntimeError as e:
                    raise RuntimeError(f"{e} (WAR on token {tok})")
            try:
                waits.append(self._resolve(self.last_w.get(tok), dst_eng))
            except RuntimeError as e:
                raise RuntimeError(f"{e} (WAW on token {tok})")
        return waits

    def op(self, eng, fn, reads=(), writes=(), waits=(), inc=True):
        dep = self._dep_waits(eng, reads, writes)
        dep.extend(waits)
        val = self._bump(self.esem[eng])
        self._emit(eng, fn, dep, self.esem[eng], 1)
        handle = (self.esem[eng], val, eng)
        self.eng_pos[eng] += 1
        for tok in writes:
            self.last_w[tok] = handle
            self.readers[tok] = []
        for tok in reads:
            self.readers[tok].append(handle)
        return handle

    def dma(self, queue, fn, sem, reads=(), writes=(), waits=()):
        dep = self._dep_waits("dma", reads, writes)
        dep.extend(waits)
        val = self._bump(sem, 16)
        self._emit(queue, fn, dep, sem, 16)
        handle = (sem, val, "dma")
        for tok in writes:
            self.last_w[tok] = handle
            self.readers[tok] = []
        for tok in reads:
            self.readers[tok].append(handle)
        return handle

    def fence_tokens(self, queue, toks):
        waits = self._dep_waits("dma", toks, toks)
        self._emit(queue, lambda e: e.nop(), waits, None, 1)

    def barrier(self):
        with self.nc.Block(no_gpsimd_drain=True) as block:
            for eng in QUEUES:
                items = self.q.get(eng)
                if not items:
                    continue

                def runner(e, items=list(items)):
                    for it in items:
                        it(e)

                getattr(block, eng)(runner)
        self.q.clear()


def mm_F(sch, psum_ap, lhsT_ap, rhs_ap, start, stop, reads=(), writes=(),
         waits=(), fmax=512, tile_position=None, abs_off=0):
    NF = rhs_ap.shape[-1]
    cuts = [0]
    pos = 0
    while pos < NF:
        nxt = min(NF, pos + fmax - ((abs_off + pos) % fmax))
        cuts.append(nxt)
        pos = nxt
    h = None
    for i in range(len(cuts) - 1):
        lo, hi = cuts[i], cuts[i + 1]
        h = sch.op("tensor",
                   lambda t, lo=lo, hi=hi: t.matmul(
                       psum_ap[:, lo:hi], lhsT_ap, rhs_ap[:, lo:hi],
                       start=start, stop=stop, tile_position=tile_position),
                   reads=reads, writes=writes,
                   waits=waits if i == 0 else (),
                   inc=(stop and i == len(cuts) - 2))
    return h


class DBuf:

    def __init__(self, nc, es, sch, name, shape, dtype, nbuf=2, queue="gpsimd",
                 tiles=None, shared_sem=False):
        self.sch = sch
        self.queue = queue
        self.name = name
        if tiles is None:
            self.tiles = [es.enter_context(nc.sbuf_tensor(f"{name}_{i}", shape, dtype))
                          for i in range(nbuf)]
        else:
            self.tiles = tiles
        if shared_sem:
            s = sch.new_dma_sem(f"{name}_s")
            self.sems = [s] * nbuf
        else:
            self.sems = [sch.new_dma_sem(f"{name}_s{i}") for i in range(nbuf)]
        self.nbuf = nbuf

    def tok(self, idx):
        return (self.name, idx % self.nbuf)

    def load(self, idx, src_ap, reads=()):
        b = idx % self.nbuf
        tile = self.tiles[b]
        dst = tile[:] if hasattr(tile, "ap") else tile
        self.sch.dma(self.queue,
                     lambda e, dst=dst, src_ap=src_ap: e.dma_start(out=dst, in_=src_ap),
                     sem=self.sems[b], reads=list(reads), writes=[self.tok(idx)])
        return tile


def make_env(nc, es, sch):
    env = {"nc": nc}
    t = lambda name, shape, dt: es.enter_context(nc.sbuf_tensor(name, shape, dt))
    env["sbuf"] = t
    env["ones128"] = t("c_ones128", [128, 1], BF16)
    env["ones1"] = t("c_ones1", [1, 128], BF16)
    env["ones1f"] = t("c_ones1f", [1, 128], F32)
    NCONST = 16
    cst = t("c_fconsts", [128, NCONST], F32)
    env["_cst_tile"] = cst
    env["_cst_vals"] = []

    def const(value):
        vals = env["_cst_vals"]
        value = float(value)
        if value not in vals:
            assert len(vals) < NCONST
            vals.append(value)
            i = len(vals) - 1
            nc.const_aps.register(F32, value, cst[:, i:i + 1])
            sch.op("vector", lambda v, i=i, value=value: v.memset(
                cst[:, i:i + 1], value), writes=[("cst",)])
        return value

    env["const"] = const
    env["ln_ybf"] = [t(f"ln_ybf{i}", [128, S // 2], BF16) for i in range(2)]
    env["ln_ysq"] = [t(f"ln_ysq{i}", [128, S // 2], BF16) for i in range(2)]
    env["ln_rows"] = t("ln_rows", [1, 3 * (S // 2)], F32)
    env["ln_tnorm"] = t("ln_tnorm", [128, S // 2], F32)
    sch.op("vector", lambda v: v.memset(env["ones128"][:], 1.0), writes=["ones128"])
    sch.op("vector", lambda v: v.memset(env["ones1"][:], 1.0), writes=["ones1"])
    sch.op("vector", lambda v: v.memset(env["ones1f"][:], 1.0), writes=["ones1"])
    return env



def emit_layernorm(sch, env, y_chunks, y_toks, ps_rows, pb, uid,
                   out_f32=None, out_f32_toks=None, out_bf16=None,
                   out_bf16_toks=None, eps=1e-5, out_dma=None):
    N = y_chunks[0].shape[-1]
    ybf, ysq = env["ln_ybf"], env["ln_ysq"]
    rows, tnorm = env["ln_rows"], env["ln_tnorm"]
    ones128, ones1 = env["ones128"], env["ones1"]
    r_mean, r_var, r_m2 = rows[:, 0:N], rows[:, N:2 * N], rows[:, 2 * N:3 * N]
    ps_sum, ps_sq = ps_rows[:, 0:N], ps_rows[:, N:2 * N]
    pb_m, pb_r = pb[:, 0:N], pb[:, N:2 * N]
    T_ps, T_pb = (uid, "ps_rows"), (uid, "pb")
    b_eps = env["const"](eps)
    s_half = env["const"](-0.5)

    for c in range(8):
        tb, tq = ("ln_ybf", c % 2), ("ln_ysq", c % 2)
        sch.op("vector", lambda v, c=c: v.tensor_copy(ybf[c % 2][:, 0:N], y_chunks[c]),
               reads=[y_toks[c]], writes=[tb])
        sch.op("vector", lambda v, c=c: v.tensor_tensor(
            ysq[c % 2][:, 0:N], ybf[c % 2][:, 0:N], ybf[c % 2][:, 0:N], op=OP.mult),
            reads=[tb], writes=[tq])
        mm_F(sch, ps_sum, ones128[:], ybf[c % 2][:, 0:N], start=(c == 0), stop=(c == 7),
             reads=[tb, "ones128"], writes=[T_ps])
        mm_F(sch, ps_sq, ones128[:], ysq[c % 2][:, 0:N], start=(c == 0), stop=(c == 7),
             reads=[tq, "ones128"], writes=[T_ps])

    sch.op("vector", lambda v: v.tensor_scalar(r_mean, ps_sum, 1.0 / D, None, op0=OP.mult),
           reads=[T_ps], writes=[("ln", "mean")])
    sch.op("vector", lambda v: v.tensor_tensor(r_m2, r_mean, r_mean, op=OP.mult),
           reads=[("ln", "mean")], writes=[("ln", "m2")])
    sch.op("vector", lambda v: v.scalar_tensor_tensor(r_var, ps_sq, 1.0 / D, r_m2,
                                                      op0=OP.mult, op1=OP.subtract),
           reads=[T_ps, ("ln", "m2")], writes=[("ln", "var")])
    sch.op("scalar", lambda s: s.activation(out=r_var, in_=r_var, func=AF.Ln, bias=b_eps),
           reads=[("ln", "var"), ("cst",)], writes=[("ln", "var")])
    sch.op("scalar", lambda s: s.activation(out=r_var, in_=r_var, func=AF.Exp, scale=s_half),
           reads=[("ln", "var")], writes=[("ln", "var")])

    mm_F(sch, pb_m, env["ones1f"][:], r_mean, start=True, stop=True,
         reads=[("ln", "mean"), "ones1"], writes=[T_pb])
    mm_F(sch, pb_r, env["ones1f"][:], r_var, start=True, stop=True,
         reads=[("ln", "var"), "ones1"], writes=[T_pb])

    fins = {}
    for c in range(8):
        tt = ("ln_tnorm",)
        sch.op("vector", lambda v, c=c: v.tensor_tensor(tnorm[:, 0:N], y_chunks[c], pb_m, op=OP.subtract),
               reads=[y_toks[c], T_pb], writes=[tt])
        if out_f32 is not None:
            dst, dtok = out_f32[c], out_f32_toks[c]
            sch.op("vector", lambda v, dst=dst: v.tensor_tensor(dst, tnorm[:, 0:N], pb_r, op=OP.mult),
                   reads=[tt, T_pb], writes=[dtok])
            src, stok = dst, dtok
        else:
            sch.op("vector", lambda v: v.tensor_tensor(tnorm[:, 0:N], tnorm[:, 0:N], pb_r, op=OP.mult),
                   reads=[tt, T_pb], writes=[tt])
            src, stok = tnorm[:, 0:N], tt
        fins[c] = stok
        if out_bf16 is not None:
            sch.op("vector", lambda v, c=c, src=src: v.tensor_copy(out_bf16[c], src),
                   reads=[stok], writes=[out_bf16_toks[c]])
        if out_dma is not None:
            dram_ap, dsem = out_dma(c)
            sch.dma("sync", lambda sy, src=src, dram_ap=dram_ap: sy.dma_start(out=dram_ap, in_=src),
                    sem=dsem, reads=[stok], writes=[(uid, "xout")])
    return fins



def emit_l0_ffn(nc, sch, es, env, xT, wg_blk, wu_blk, wd_blk, outT, st_sem,
                uid="f0", out_bf16=None, out_bf16_toks=None,
                x_bf=None, x_bf_toks=None, x_dep=(), pump=None):
    HID = 3072
    NHS = HID // 128
    NH = S // 2
    t = lambda name, shape, dt: es.enter_context(nc.sbuf_tensor(name, shape, dt))
    if x_bf is None:
        x_st = DBuf(nc, es, sch, f"{uid}_xbf", [128, S], BF16, nbuf=8, shared_sem=True)
        xbf = x_st.tiles
        xbf_toks = [x_st.tok(c) for c in range(8)]
        for c in range(8):
            x_st.load(c, xT[128 * c:128 * (c + 1), :], reads=x_dep)
    else:
        xbf, xbf_toks = x_bf, x_bf_toks
    H = [t(f"{uid}_H{i}", [128, NH], BF16) for i in range(NHS)]
    sg = [t(f"{uid}_sg{i}", [128, NH // 2], F32) for i in range(2)]
    y = [t(f"{uid}_y{c}", [128, NH], F32) for c in range(8)]
    wg_st = DBuf(nc, es, sch, f"{uid}_wg", [128, 8 * 128], BF16)
    wu_st = DBuf(nc, es, sch, f"{uid}_wu", [128, 8 * 128], BF16)
    wd_st = DBuf(nc, es, sch, f"{uid}_wd", [128, NHS * 64], BF16)
    xf_st = DBuf(nc, es, sch, f"{uid}_xf", [128, NH], F32, queue="sync")

    for half in range(2):
        hlo = half * NH
        with contextlib.ExitStack() as pes:
            p = lambda name, shape: pes.enter_context(nc.psum_tensor(name, shape, F32))
            ps_g = [p(f"{uid}_ps_g{half}_{i}", [128, NH // 2]) for i in range(2)]
            ps_u = [p(f"{uid}_ps_u{half}_{i}", [128, NH // 2]) for i in range(2)]

            def load_gu(hs):
                wg_st.load(hs, wg_blk[hs].rearrange("p c h -> p (c h)"))
                wu_st.load(hs, wu_blk[hs].rearrange("p c h -> p (c h)"))

            load_gu(0)
            load_gu(1)
            for hs in range(NHS):
                if pump:
                    pump()
                b = hs % 2
                wg_b, wu_b = wg_st.tiles[b], wu_st.tiles[b]
                for nh in range(2):
                    nlo = hlo + nh * (NH // 2)
                    for c in range(8):
                        mm_F(sch, ps_g[nh][:], wg_b[:, 128 * c:128 * (c + 1)],
                             xbf[c][:, nlo:nlo + NH // 2], start=(c == 0), stop=(c == 7),
                             reads=[wg_st.tok(hs), xbf_toks[c]],
                             writes=[(uid, "ps_g", half, nh)])
                    for c in range(8):
                        mm_F(sch, ps_u[nh][:], wu_b[:, 128 * c:128 * (c + 1)],
                             xbf[c][:, nlo:nlo + NH // 2], start=(c == 0), stop=(c == 7),
                             reads=[wu_st.tok(hs), xbf_toks[c]],
                             writes=[(uid, "ps_u", half, nh)])
                    s127 = env["const"](1.0 / 127.0)
                    sch.op("scalar", lambda s_, nh=nh, s127=s127: s_.activation(
                        out=sg[nh][:], in_=ps_g[nh][:], func=AF.Silu, scale=s127),
                        reads=[(uid, "ps_g", half, nh), ("cst",)],
                        writes=[(uid, "sg", nh)])
                    sch.op("vector", lambda v_, hs=hs, nh=nh: v_.scalar_tensor_tensor(
                        H[hs][:, nh * (NH // 2):(nh + 1) * (NH // 2)], ps_u[nh][:],
                        1.0 / 127.0, sg[nh][:], op0=OP.mult, op1=OP.mult),
                        reads=[(uid, "ps_u", half, nh), (uid, "sg", nh)],
                        writes=[(uid, "H", hs)])
                if hs + 2 < NHS:
                    load_gu(hs + 2)
            sch.barrier()

        with contextlib.ExitStack() as pes:
            p = lambda name, shape: pes.enter_context(nc.psum_tensor(name, shape, F32))
            ps_d = [p(f"{uid}_ps_d{half}_{i}", [128, NH]) for i in range(2)]

            HH = NHS // 2

            def load_dh(idx):
                os, hh = idx // 2, idx % 2
                wd_st.load(idx, wd_blk[os][:, HH * hh:HH * (hh + 1), :]
                           .rearrange("p c h -> p (c h)"))

            def load_x(os):
                xf_st.load(os, xT[128 * os:128 * (os + 1), hlo:hlo + NH], reads=x_dep)

            load_dh(0)
            load_dh(1)
            load_x(0)
            load_x(1)
            for os in range(8):
                b = os % 2
                xf = xf_st.tiles[b]
                for hc in range(NHS):
                    widx = 2 * os + hc // HH
                    wd_b = wd_st.tiles[widx % 2]
                    mm_F(sch, ps_d[b][:], wd_b[:, 128 * (hc % HH):128 * (hc % HH + 1)],
                         H[hc][:], start=(hc == 0), stop=(hc == NHS - 1),
                         reads=[wd_st.tok(widx), (uid, "H", hc)],
                         writes=[(uid, "ps_d", half, b)])
                    if hc == HH and 2 * os + 2 < 16:
                        load_dh(2 * os + 2)
                sch.op("vector", lambda v_, os=os, b=b, xf=xf: v_.scalar_tensor_tensor(
                    y[os][:], ps_d[b][:], 1.0 / 127.0, xf[:],
                    op0=OP.mult, op1=OP.add),
                    reads=[(uid, "ps_d", half, b), xf_st.tok(os)],
                    writes=[(uid, "y", os)])
                if pump:
                    pump()
                if 2 * os + 3 < 16:
                    load_dh(2 * os + 3)
                if os + 2 < 8:
                    load_x(os + 2)
            sch.barrier()

        with contextlib.ExitStack() as pes:
            ps_rows = pes.enter_context(nc.psum_tensor(f"{uid}_lnr{half}", [1, 2 * NH], F32)).ap()
            pb = pes.enter_context(nc.psum_tensor(f"{uid}_lnb{half}", [128, 2 * NH], F32)).ap()
            ob = [out_bf16[c][:, hlo:hlo + NH] for c in range(8)] if out_bf16 else None
            obt = ([(tok, half) for tok in out_bf16_toks] if out_bf16 else None)
            emit_layernorm(sch, env, [y[c][:] for c in range(8)],
                           [(uid, "y", c) for c in range(8)], ps_rows, pb, uid,
                           out_bf16=ob, out_bf16_toks=obt,
                           out_dma=lambda c: (outT[128 * c:128 * (c + 1), hlo:hlo + NH], st_sem))
            sch.barrier()


def build_l0_ffn_standalone():
    nc = bass.Bass("TRN2", target_bir_lowering=False)
    NHS = 24
    xT = nc.dram_tensor("xT", [D, S], F32, kind="ExternalInput").ap()
    wg_blk = nc.dram_tensor("wg_blk", [NHS, 128, 8, 128], I32, kind="ExternalInput").ap()
    wu_blk = nc.dram_tensor("wu_blk", [NHS, 128, 8, 128], I32, kind="ExternalInput").ap()
    wd_blk = nc.dram_tensor("wd_blk", [8, 128, NHS, 128], I32, kind="ExternalInput").ap()
    outT = nc.dram_tensor("outT", [D, S], F32, kind="ExternalOutput").ap()
    es = contextlib.ExitStack()
    with es:
        sch = Sched(nc, es)
        env = make_env(nc, es, sch)
        st_sem = es.enter_context(nc.semaphore("sem_st"))
        emit_l0_ffn(nc, sch, es, env, xT, wg_blk, wu_blk, wd_blk, outT, st_sem)
        sch._emit("sync", lambda e: e.nop(), [(st_sem, sch.count(st_sem))], None, 1)
        sch.barrier()
    return nc


def prep_l0_ffn_inputs(x_bt, w_gate, w_up, w_down):
    xT = np.ascontiguousarray(x_bt.T)
    wg_blk = np.ascontiguousarray(
        w_gate.T.reshape(8, 128, 24, 128).transpose(2, 1, 0, 3))
    wu_blk = np.ascontiguousarray(
        w_up.T.reshape(8, 128, 24, 128).transpose(2, 1, 0, 3))
    wd_blk = np.ascontiguousarray(
        w_down.T.reshape(24, 128, 8, 128).transpose(2, 1, 0, 3))
    return {"xT": xT, "wg_blk": wg_blk, "wu_blk": wu_blk, "wd_blk": wd_blk}



def emit_l1_ffn(nc, sch, es, env, xT, w1_blk, w2_blk, h1_blk, h2_blk, w3_blk,
                scl, outT, st_sem, uid="f1", out_bf16=None, out_bf16_toks=None,
                x_dep=(), pump=None):
    DH = 4096
    NHS = DH // 128
    NH = S // 2
    NQ = NH // 2
    t = lambda name, shape, dt: es.enter_context(nc.sbuf_tensor(name, shape, dt))
    A = [t(f"{uid}_A{i}", [128, NH], BF16) for i in range(NHS)]
    B = [t(f"{uid}_B{i}", [128, NH], BF16) for i in range(NHS)]
    sg = [t(f"{uid}_sg{i}", [128, NH // 2], F32) for i in range(2)]
    scl_sb = t(f"{uid}_scl", [128, 4 * NHS + 8], F32)
    s1_sb = scl_sb[:, 0:NHS]
    s2_sb = scl_sb[:, NHS:2 * NHS]
    h1s_sb = scl_sb[:, 2 * NHS:3 * NHS]
    h2s_sb = scl_sb[:, 3 * NHS:4 * NHS]
    w3s_sb = scl_sb[:, 4 * NHS:4 * NHS + 8]
    x_st = DBuf(nc, es, sch, f"{uid}_xbf", [128, NH], BF16, nbuf=8, shared_sem=True)
    y = [x_st.tiles[c].ap().bitcast(F32) for c in range(8)]
    w1_st = DBuf(nc, es, sch, f"{uid}_w1", [128, 8 * 128], BF16)
    w2_st = DBuf(nc, es, sch, f"{uid}_w2", [128, 8 * 128], BF16)
    hw_st = DBuf(nc, es, sch, f"{uid}_hw", [128, NHS * 128], BF16)
    xf_st = DBuf(nc, es, sch, f"{uid}_xf", [128, NQ], F32, queue="sync")
    ssem = sch.new_dma_sem(f"{uid}_ssem")
    sch.dma("sync", lambda sy: sy.dma_start(out=scl_sb[:], in_=scl),
            sem=ssem, writes=[(uid, "scl")])

    hw_idx = [0]

    def hw_load(blk_row):
        i = hw_idx[0]
        hw_idx[0] += 1
        hw_st.load(i, blk_row.rearrange("p c h -> p (c h)"))
        return hw_st.tiles[i % 2], hw_st.tok(i)

    for half in range(2):
        hlo = half * NH
        for c in range(8):
            x_st.load(c + 8 * half, xT[128 * c:128 * (c + 1), hlo:hlo + NH], reads=x_dep)
        xbf = x_st.tiles
        xbf_toks = [x_st.tok(c) for c in range(8)]

        with contextlib.ExitStack() as pes:
            p = lambda name, shape: pes.enter_context(nc.psum_tensor(name, shape, F32))
            ps_g = [p(f"{uid}_psg{half}_{i}", [128, NH // 2]) for i in range(2)]
            ps_u = [p(f"{uid}_psu{half}_{i}", [128, NH // 2]) for i in range(2)]

            def load_gu(hs):
                w1_st.load(hs, w1_blk[hs].rearrange("p c h -> p (c h)"))
                w2_st.load(hs, w2_blk[hs].rearrange("p c h -> p (c h)"))

            load_gu(0)
            load_gu(1)
            for hs in range(NHS):
                if pump:
                    pump()
                b = hs % 2
                for nh in range(2):
                    nlo = nh * (NH // 2)
                    for c in range(8):
                        mm_F(sch, ps_g[nh][:], w1_st.tiles[b][:, 128 * c:128 * (c + 1)],
                             xbf[c][:, nlo:nlo + NH // 2], start=(c == 0), stop=(c == 7),
                             reads=[w1_st.tok(hs), xbf_toks[c]],
                             writes=[(uid, "psg", half, nh)])
                    for c in range(8):
                        mm_F(sch, ps_u[nh][:], w2_st.tiles[b][:, 128 * c:128 * (c + 1)],
                             xbf[c][:, nlo:nlo + NH // 2], start=(c == 0), stop=(c == 7),
                             reads=[w2_st.tok(hs), xbf_toks[c]],
                             writes=[(uid, "psu", half, nh)])
                    sch.op("scalar", lambda s_, nh=nh, hs=hs: s_.activation(
                        out=sg[nh][:], in_=ps_g[nh][:], func=AF.Silu,
                        scale=s1_sb[:, hs:hs + 1]),
                        reads=[(uid, "psg", half, nh), (uid, "scl")],
                        writes=[(uid, "sg", nh)])
                    sch.op("vector", lambda v_, hs=hs, nh=nh, nlo=nlo: v_.scalar_tensor_tensor(
                        A[hs][:, nlo:nlo + NH // 2], ps_u[nh][:],
                        s2_sb[:, hs:hs + 1], sg[nh][:], op0=OP.mult, op1=OP.mult),
                        reads=[(uid, "psu", half, nh), (uid, "sg", nh), (uid, "scl")],
                        writes=[(uid, "A", hs)])
                if hs + 2 < NHS:
                    load_gu(hs + 2)
            sch.barrier()

        for phase, (w_blk, s_sb, s_name, src, src_tokname, dst, dst_tokname) in enumerate(
                ((h1_blk, h1s_sb, "h1s", A, "A", B, "B"),
                 (h2_blk, h2s_sb, "h2s", B, "B", A, "A"))):
            with contextlib.ExitStack() as pes:
                p = lambda name, shape: pes.enter_context(nc.psum_tensor(name, shape, F32))
                ps_b = [p(f"{uid}_psb{half}_{phase}_{i}", [128, NH]) for i in range(2)]
                tiles = {}
                tiles[0] = hw_load(w_blk[0])
                tiles[1] = hw_load(w_blk[1])
                for hs in range(NHS):
                    wtile, wtok = tiles[hs]
                    b = hs % 2
                    for c in range(NHS):
                        mm_F(sch, ps_b[b][:], wtile[:, 128 * c:128 * (c + 1)],
                             src[c][:], start=(c == 0), stop=(c == NHS - 1),
                             reads=[wtok, (uid, src_tokname, c)],
                             writes=[(uid, "psb", phase, b)])
                    sch.op("scalar", lambda s_, hs=hs, b=b, dst=dst, s_sb=s_sb: s_.activation(
                        out=dst[hs][:], in_=ps_b[b][:], func=AF.Gelu,
                        scale=s_sb[:, hs:hs + 1]),
                        reads=[(uid, "psb", phase, b), (uid, "scl", s_name)],
                        writes=[(uid, dst_tokname, hs)])
                    if hs + 2 < NHS:
                        tiles[hs + 2] = hw_load(w_blk[hs + 2])
                sch.barrier()

        for q in range(2):
            qlo = hlo + q * NQ
            with contextlib.ExitStack() as pes:
                p = lambda name, shape: pes.enter_context(nc.psum_tensor(name, shape, F32))
                ps_d = [p(f"{uid}_psd{half}_{q}_{i}", [128, NQ]) for i in range(2)]

                def load_d(os):
                    wtile, wtok = hw_load(w3_blk[os])
                    xf_st.load(os, xT[128 * os:128 * (os + 1), qlo:qlo + NQ], reads=x_dep)
                    return wtile, wtok

                dtiles = {0: load_d(0), 1: load_d(1)}
                for os in range(8):
                    wtile, wtok = dtiles[os]
                    b = os % 2
                    for hc in range(NHS):
                        mm_F(sch, ps_d[b][:], wtile[:, 128 * hc:128 * (hc + 1)],
                             A[hc][:, q * NQ:(q + 1) * NQ], start=(hc == 0),
                             stop=(hc == NHS - 1),
                             reads=[wtok, (uid, "A", hc)],
                             writes=[(uid, "psd", b)])
                    sch.op("vector", lambda v_, os=os, b=b: v_.scalar_tensor_tensor(
                        y[os], ps_d[b][:], w3s_sb[:, os:os + 1], xf_st.tiles[b][:],
                        op0=OP.mult, op1=OP.add),
                        reads=[(uid, "psd", b), xf_st.tok(os), (uid, "scl")],
                        writes=[(uid, "y", os)])
                    if os + 2 < 8:
                        dtiles[os + 2] = load_d(os + 2)
                sch.barrier()
            with contextlib.ExitStack() as pes:
                ps_rows = pes.enter_context(nc.psum_tensor(f"{uid}_lnr{half}_{q}", [1, 2 * NQ], F32)).ap()
                pb = pes.enter_context(nc.psum_tensor(f"{uid}_lnb{half}_{q}", [128, 2 * NQ], F32)).ap()
                ob = [out_bf16[c][:, qlo:qlo + NQ] for c in range(8)] if out_bf16 else None
                obt = ([(tok, half, q) for tok in out_bf16_toks] if out_bf16 else None)
                emit_layernorm(sch, env, [y[c] for c in range(8)],
                               [(uid, "y", c) for c in range(8)], ps_rows, pb, uid,
                               out_bf16=ob, out_bf16_toks=obt,
                               out_dma=lambda c: (outT[128 * c:128 * (c + 1), qlo:qlo + NQ], st_sem))
                sch.barrier()


def build_l1_ffn_standalone():
    nc = bass.Bass("TRN2", target_bir_lowering=False)
    NHS = 32
    xT = nc.dram_tensor("xT", [D, S], F32, kind="ExternalInput").ap()
    w1_blk = nc.dram_tensor("w1_blk", [NHS, 128, 8, 128], F32, kind="ExternalInput").ap()
    w2_blk = nc.dram_tensor("w2_blk", [NHS, 128, 8, 128], F32, kind="ExternalInput").ap()
    h1_blk = nc.dram_tensor("h1_blk", [NHS, 128, NHS, 128], F32, kind="ExternalInput").ap()
    h2_blk = nc.dram_tensor("h2_blk", [NHS, 128, NHS, 128], F32, kind="ExternalInput").ap()
    w3_blk = nc.dram_tensor("w3_blk", [8, 128, NHS, 128], F32, kind="ExternalInput").ap()
    scl = nc.dram_tensor("scl", [128, 4 * NHS + 8], F32, kind="ExternalInput").ap()[:, :]
    outT = nc.dram_tensor("outT", [D, S], F32, kind="ExternalOutput").ap()
    es = contextlib.ExitStack()
    with es:
        sch = Sched(nc, es)
        env = make_env(nc, es, sch)
        st_sem = es.enter_context(nc.semaphore("sem_st"))
        emit_l1_ffn(nc, sch, es, env, xT, w1_blk, w2_blk, h1_blk, h2_blk, w3_blk,
                    scl, outT, st_sem)
        sch._emit("sync", lambda e: e.nop(), [(st_sem, sch.count(st_sem))], None, 1)
        sch.barrier()
    return nc


def prep_l1_ffn_inputs(x_bt, w1w, w1s, w2w, w2s, h1w, h1s, h2w, h2s, w3w, w3s):
    xT = np.ascontiguousarray(x_bt.T)
    blk = lambda w, nin, nout: np.ascontiguousarray(
        w.T.reshape(nin // 128, 128, nout // 128, 128).transpose(2, 1, 0, 3))
    fm = lambda s: np.ascontiguousarray(s.reshape(-1, 128).T)
    scl = np.concatenate([fm(w1s), fm(w2s), fm(h1s), fm(h2s), fm(w3s)], axis=1)
    return {"xT": xT,
            "w1_blk": blk(w1w, 1024, 4096), "w2_blk": blk(w2w, 1024, 4096),
            "h1_blk": blk(h1w, 4096, 4096), "h2_blk": blk(h2w, 4096, 4096),
            "w3_blk": blk(w3w, 4096, 1024),
            "scl": np.ascontiguousarray(scl)}



def emit_l0_attn(nc, sch, es, env, xT, wq_blk, wk_blk, wv_blk, wo_blk,
                 biases, lqk, hnw, cs_tab_h, ss_tab_h, perm_h, ut_h, iden_h,
                 v_scr, outT, st_sem, uid="a0", out_bf16=None, out_bf16_toks=None,
                 debug_afm=False, pump=None):
    t = lambda name, shape, dt: es.enter_context(nc.sbuf_tensor(name, shape, dt))
    NHEAD = 16
    cs_tab = t(f"{uid}_cs", [128, S], F32)
    ss_tab = t(f"{uid}_ss", [128, S], F32)
    perm = t(f"{uid}_perm", [128, 128], F32)
    ut = t(f"{uid}_ut", [128, 128], BF16)
    iden = t(f"{uid}_iden", [128, 128], BF16)
    bias_sb = t(f"{uid}_bias", [128, 48], F32)
    hnw_sb = t(f"{uid}_hnw", [64, 16], F32)
    hnw08 = t(f"{uid}_hnw08", [64, 16], F32)
    lqk_sb = t(f"{uid}_lqk", [64, 64], F32)
    lqk_prod = t(f"{uid}_lqkp", [64, 32], BF16)
    lamt = t(f"{uid}_lam", [1, 64], F32)
    attn_fm = [t(f"{uid}_afm{c}", [128, S], BF16) for c in range(8)]
    aug = [t(f"{uid}_aug{i}", [128, S], BF16) for i in range(8)]
    y = [aug[c].ap().bitcast(F32) for c in range(8)]
    qpre = [t(f"{uid}_qpre{i}", [128, 1024], F32) for i in range(2)]
    tcs = t(f"{uid}_tcs", [128, 1024], F32)
    uss = t(f"{uid}_uss", [128, 1024], F32)
    vstage = t(f"{uid}_vstage", [128, 1024], BF16)
    sqt = t(f"{uid}_sqt", [64, 1024], BF16)
    E = [t(f"{uid}_E{i}", [128, 1024], BF16) for i in range(2)]
    bsq = E[0][0:64, 0:1024]
    o12 = [t(f"{uid}_o{i}", [64, 1024], BF16) for i in range(2)]
    comb = t(f"{uid}_comb", [64, 1024], F32)
    combt = t(f"{uid}_combt", [64, 1024], F32)

    rows = t(f"{uid}_rows", [1, 1032], F32)

    rowsb = t(f"{uid}_rowsb", [1, 2048], F32)
    rbh = t(f"{uid}_rbh", [1, 2048], BF16)
    sc16 = t(f"{uid}_sc16", [1, 16], F32)
    fint = comb.ap().bitcast(BF16)[0:64, 0:1024]
    v_aug = [t(f"{uid}_vaug{i}", [128, 16 * 80], BF16) for i in range(2)]
    wq_st = DBuf(nc, es, sch, f"{uid}_wq", [128, 8 * 128], BF16)
    wk_st = DBuf(nc, es, sch, f"{uid}_wk", [128, 8 * 128], BF16)
    wv_st = DBuf(nc, es, sch, f"{uid}_wv", [128, 8 * 128], BF16)
    x_st = DBuf(nc, es, sch, f"{uid}_xbf", [128, S], BF16, nbuf=8, shared_sem=True)

    vssem = sch.new_dma_sem(f"{uid}_vs")
    vtsem = [sch.new_dma_sem(f"{uid}_vt{i}") for i in range(2)]

    SQ8 = env["const"](0.125)

    for c in range(8):
        x_st.load(c, xT[128 * c:128 * (c + 1), :])
    xbf, xbf_toks = x_st.tiles, [x_st.tok(c) for c in range(8)]
    for i, (dst, srca, tok, q) in enumerate((
            (cs_tab, cs_tab_h, "cs", "sync"), (ss_tab, ss_tab_h, "ss", "sync"),
            (perm, perm_h, "perm", "sync"), (ut, ut_h, "ut", "gpsimd"),
            (iden, iden_h, "iden", "gpsimd"), (bias_sb, biases, "bias", "sync"),
            (hnw_sb, hnw, "hnw", "sync"), (lqk_sb, lqk, "lqk", "sync"))):
        csem_i = sch.new_dma_sem(f"{uid}_cs{i}")
        sch.dma(q, lambda e, dst=dst, srca=srca: e.dma_start(out=dst[:], in_=srca),
                sem=csem_i, writes=[(uid, tok)])
    sch.op("vector", lambda v: v.tensor_scalar(hnw08[:], hnw_sb[:], 0.8, None, op0=OP.mult),
           reads=[(uid, "hnw")], writes=[(uid, "hnw08")])
    sch.op("vector", lambda v: v.tensor_tensor(lqk_prod[:, 0:16], lqk_sb[:, 0:16],
                                               lqk_sb[:, 16:32], op=OP.mult),
           reads=[(uid, "lqk")], writes=[(uid, "p12")])
    sch.op("vector", lambda v: v.tensor_tensor(lqk_prod[:, 16:32], lqk_sb[:, 32:48],
                                               lqk_sb[:, 48:64], op=OP.mult),
           reads=[(uid, "lqk")], writes=[(uid, "p12")])
    with contextlib.ExitStack() as pes0:
        ps0 = pes0.enter_context(nc.psum_tensor(f"{uid}_ps0", [1, 32], F32))
        mm_F(sch, ps0[:], env["ones128"][0:64, :], lqk_prod[:],
             start=True, stop=True, reads=[(uid, "p12"), "ones128"],
             writes=[(uid, "ps0")])
        sch.op("scalar", lambda s: s.activation(out=lamt[:, 32:64], in_=ps0[:], func=AF.Exp),
               reads=[(uid, "ps0")], writes=[(uid, "lame")])
    sch.op("vector", lambda v: v.tensor_tensor(lamt[:, 0:16], lamt[:, 32:48],
                                               lamt[:, 48:64], op=OP.subtract),
           reads=[(uid, "lame")], writes=[(uid, "lamf")])
    sch.op("vector", lambda v: v.tensor_scalar(lamt[:, 0:16], lamt[:, 0:16], 0.2, None, op0=OP.add),
           reads=[(uid, "lamf")], writes=[(uid, "lamf")])
    lam_row = lamt[:, 0:16]

    for j in range(8):
        sch.op("vector", lambda v, j=j: v.memset(aug[j][64:128, :], 0.0),
               writes=[(uid, "augpad"), (uid, "aug_ones", j)])
        if j % 4 >= 2:
            sch.op("vector", lambda v, j=j: v.memset(aug[j][64:65, :], 1.0),
                   writes=[(uid, "aug_ones", j)], reads=[(uid, "augpad")])
    sch.barrier()

    g_pes = contextlib.ExitStack()
    psA = g_pes.enter_context(nc.psum_tensor(f"{uid}_psA", [128, 1024], F32))
    psB = g_pes.enter_context(nc.psum_tensor(f"{uid}_psB", [128, 1024], F32))
    psC = g_pes.enter_context(nc.psum_tensor(f"{uid}_psC", [128, S], F32))
    TA, TB, TC = (uid, "psA"), (uid, "psB"), (uid, "psC")
    TOKP = [TA, TB]
    DEFER_EPI = 1
    pending = [None]
    pend_shift = [None]

    def epi(h, qh, defer_shift=False):
        sch.op("vector", lambda v: v.tensor_copy(o12[0][:], p1[0:64, :]),
               reads=[TC], writes=[(uid, "o1")])
        sch.op("vector", lambda v: v.tensor_copy(o12[1][:], p2[0:64, :]),
               reads=[TC], writes=[(uid, "o2")])
        sch.op("scalar", lambda s_: s_.activation(out=rowsb[:, 0:2048], in_=psC[64:65, 0:2048], func=AF.Ln),
               reads=[TC], writes=[(uid, "rowB")])
        sch.op("scalar", lambda s_: s_.activation(out=rbh[:, 0:2048], in_=rowsb[:, 0:2048],
                                                  func=AF.Exp, scale=-1.0),
               reads=[(uid, "rowB"), ("cst",)], writes=[(uid, "rowB")])
        sch.op("vector", lambda v, h=h: v.tensor_scalar(
            rbh[:, 1024:2048], rbh[:, 1024:2048], lam_row[:, h:h + 1], None, op0=OP.mult),
            reads=[(uid, "rowB"), (uid, "lamf")], writes=[(uid, "rowB")])
        mm_F(sch, psC[0:64, 0:1024], env["ones1"][:, 0:64], rbh[:, 0:1024],
             start=True, stop=True, reads=[(uid, "rowB"), "ones1"],
             writes=[TC])
        mm_F(sch, psC[0:64, 1024:2048], env["ones1"][:, 0:64], rbh[:, 1024:2048],
             start=True, stop=True, reads=[(uid, "rowB"), "ones1"],
             writes=[TC])
        sch.op("vector", lambda v: v.tensor_tensor(comb[:], o12[0][:], psC[0:64, 0:1024], op=OP.mult),
               reads=[(uid, "o1"), TC], writes=[(uid, "comb")])
        sch.op("vector", lambda v: v.tensor_tensor(combt[:], o12[1][:], psC[0:64, 1024:2048], op=OP.mult),
               reads=[(uid, "o2"), TC], writes=[(uid, "combt")])
        sch.op("vector", lambda v: v.tensor_tensor(comb[:], comb[:], combt[:], op=OP.subtract),
               reads=[(uid, "combt"), (uid, "comb")], writes=[(uid, "comb")])
        sch.op("scalar", lambda s_: s_.activation(out=sqt[:, :], in_=comb[:], func=AF.Square),
               reads=[(uid, "comb")], writes=[(uid, "sqt")])
        mm_F(sch, psC[0:1, 0:1024], env["ones128"][0:64, :], sqt[:, :],
             start=True, stop=True, reads=[(uid, "sqt"), "ones128"],
             writes=[TC])
        c164 = env["const"](1.0 / 64.0)
        c1em6 = env["const"](1e-6)
        sch.op("scalar", lambda s_, c164=c164, c1em6=c1em6: s_.activation(
            out=rows[:, 0:1024], in_=psC[0:1, 0:1024], func=AF.Ln, scale=c164, bias=c1em6),
            reads=[TC, ("cst",)], writes=[(uid, "rowA")])
        s_half = env["const"](-0.5)
        sch.op("scalar", lambda s_, s_half=s_half: s_.activation(
            out=rbh[:, 0:1024], in_=rows[:, 0:1024], func=AF.Exp, scale=s_half),
            reads=[(uid, "rowA")], writes=[(uid, "rowB")])
        mm_F(sch, psC[0:64, 1024:2048], env["ones1"][:, 0:64], rbh[:, 0:1024],
             start=True, stop=True, reads=[(uid, "rowB"), "ones1"],
             writes=[TC])
        sch.op("vector", lambda v: v.tensor_tensor(combt[:], comb[:], psC[0:64, 1024:2048], op=OP.mult),
               reads=[(uid, "comb"), TC], writes=[(uid, "combt")])
        cdst = attn_fm[h // 2]
        if h % 2 == 0:
            sch.op("vector", lambda v, cdst=cdst, h=h, qh=qh: v.tensor_scalar(
                cdst[0:64, 1024 * qh:1024 * (qh + 1)], combt[:], hnw08[:, h:h + 1], None, op0=OP.mult),
                reads=[(uid, "combt"), (uid, "hnw08")], writes=[(uid, "afm", h // 2)])
            return None
        sch.op("vector", lambda v, h=h: v.tensor_scalar(
            fint, combt[:], hnw08[:, h:h + 1], None, op0=OP.mult),
            reads=[(uid, "combt"), (uid, "hnw08")], writes=[(uid, "fint")])

        def shift():
            mm_F(sch, ps_s[1][64:128, :], iden[0:64, 0:64], fint,
                 start=True, stop=True, reads=[(uid, "fint"), (uid, "iden")],
                 writes=[TB], tile_position=(0, 64))
            sch.op("vector", lambda v, cdst=cdst, qh=qh: v.tensor_copy(
                cdst[64:128, 1024 * qh:1024 * (qh + 1)], ps_s[1][64:128, :]),
                reads=[TB], writes=[(uid, "afm", h // 2)])
        if defer_shift:
            return shift
        shift()
        return None

    for g in range(8):
        if pending[0] is not None:
            ep, pending[0] = pending[0], None
            epi(*ep)
        if pump:
            pump()
        heads = (2 * g, 2 * g + 1)
        if True:
            ps_p = [psA, psB]
            ps_pm = psC

            def proj_rope(st, blk, si, bias_col, scale, dst1, dst1_tok, dst2,
                          dst2_tok, widx):
                wtile = st.load(widx, blk[si].rearrange("p c h -> p (c h)"))
                for nh in range(2):
                    slot = nh
                    qp, qtok = qpre[slot], (uid, "qpre", slot)
                    for c in range(8):
                        mm_F(sch, ps_p[slot][:], wtile[:, 128 * c:128 * (c + 1)],
                             xbf[c][:, 1024 * nh:1024 * (nh + 1)],
                             start=(c == 0), stop=(c == 7),
                             reads=[st.tok(widx), xbf_toks[c]],
                             writes=[TOKP[slot]])
                    sch.op("vector", lambda v_, slot=slot, qp=qp, bias_col=bias_col, scale=scale:
                           v_.tensor_scalar(qp[:], ps_p[slot][:],
                                            bias_col, scale, op0=OP.add, op1=OP.mult),
                           reads=[TOKP[slot], (uid, "bias")],
                           writes=[qtok])
                    cs_sl = cs_tab[:, 1024 * nh:1024 * (nh + 1)]
                    ss_sl = ss_tab[:, 1024 * nh:1024 * (nh + 1)]
                    pm_sl = ps_pm[:, 1024 * nh:1024 * (nh + 1)]
                    mm_F(sch, pm_sl, perm[:], qp[:], start=True, stop=True,
                         reads=[qtok, (uid, "perm")], writes=[TC],
                         abs_off=1024 * nh)
                    sch.op("vector", lambda v, qp=qp, cs_sl=cs_sl: v.tensor_tensor(
                        tcs[:], qp[:], cs_sl, op=OP.mult),
                        reads=[qtok, (uid, "cs")], writes=[(uid, "tcs")])
                    sch.op("vector", lambda v, pm_sl=pm_sl, ss_sl=ss_sl: v.tensor_tensor(
                        uss[:], pm_sl, ss_sl, op=OP.mult),
                        reads=[TC, (uid, "ss")], writes=[(uid, "uss")])
                    sch.op("vector", lambda v, dst1=dst1, nh=nh: v.tensor_tensor(
                        dst1[0:64, 1024 * nh:1024 * (nh + 1)], tcs[0:64, :], uss[0:64, :], op=OP.add),
                        reads=[(uid, "tcs"), (uid, "uss")], writes=[dst1_tok])
                    sch.op("vector", lambda v, dst2=dst2, nh=nh: v.tensor_tensor(
                        dst2[0:64, 1024 * nh:1024 * (nh + 1)], tcs[64:128, :], uss[64:128, :], op=OP.add),
                        reads=[(uid, "tcs"), (uid, "uss")], writes=[dst2_tok])

            for j, h in enumerate(heads):
                proj_rope(wq_st, wq_blk, h, bias_sb[:, h:h + 1], SQ8,
                          aug[4 * j + 0], (uid, "q1a", j), aug[4 * j + 1], (uid, "q2a", j), h)
                proj_rope(wk_st, wk_blk, h, bias_sb[:, 16 + h:17 + h], 1.0,
                          aug[4 * j + 2], (uid, "k1a", j), aug[4 * j + 3], (uid, "k2a", j), h)
            vtile = wv_st.load(g, wv_blk[g].rearrange("p c h -> p (c h)"))
            for nh in range(2):
                slot = nh
                for c in range(8):
                    mm_F(sch, ps_p[slot][:], vtile[:, 128 * c:128 * (c + 1)],
                         xbf[c][:, 1024 * nh:1024 * (nh + 1)], start=(c == 0), stop=(c == 7),
                         reads=[wv_st.tok(g), xbf_toks[c]], writes=[TOKP[slot]])
                sch.op("vector", lambda v_, slot=slot, g=g: v_.tensor_scalar(
                    vstage[:], ps_p[slot][:],
                    bias_sb[:, 32 + g:33 + g], None, op0=OP.add),
                    reads=[TOKP[slot], (uid, "bias")], writes=[(uid, "vstage")])
                sch.dma("sync", lambda sy, g=g, nh=nh: sy.dma_start(
                    out=v_scr[128 * g:128 * (g + 1), 1024 * nh:1024 * (nh + 1)], in_=vstage[:]),
                    sem=vssem, reads=[(uid, "vstage")], writes=[(uid, "vscr", g)])

        if True:
            ps_s = [psA, psB]
            p1 = psC[0:65, 0:1024]
            p2 = psC[0:65, 1024:2048]
            for j, h in enumerate(heads):
                q1a, q2a, k1a, k2a = aug[4 * j], aug[4 * j + 1], aug[4 * j + 2], aug[4 * j + 3]
                tq1, tq2 = (uid, "q1a", j), (uid, "q2a", j)
                tk1, tk2 = (uid, "k1a", j), (uid, "k2a", j)
                va = v_aug[j]
                ones_cols = va.ap().rearrange("p (k c) -> p k c", c=80)[:, :, 64:65]
                tva = [(uid, "vaug1", j)] + [(uid, "vaugd", j, kc) for kc in range(16)]
                sch.op("vector", lambda v, ones_cols=ones_cols: v.memset(ones_cols, 1.0),
                       writes=[(uid, "vaug1", j)])
                for kc in range(16):
                    sch.dma("sync", lambda sy, va=va, kc=kc, h=h: sy.dma_start(
                        out=va[:, 80 * kc:80 * kc + 64],
                        in_=v_scr[64 * h:64 * h + 64, 128 * kc:128 * (kc + 1)],
                        transpose=True),
                        sem=vtsem[j], reads=[(uid, "vscr", g)], writes=[(uid, "vaugd", j, kc)])
                M2 = {}
                for idx, (ka, tk) in enumerate(((k1a, tk1), (k2a, tk2))):
                    for piece in range(2):
                        sch.op("scalar", lambda s_, ka=ka, piece=piece: s_.activation(
                            out=bsq, in_=ka[0:64, 1024 * piece:1024 * (piece + 1)],
                            func=AF.Square),
                            reads=[tk], writes=[(uid, "E", 0)])
                        mm_F(sch, ps_s[piece][0:1, :], env["ones128"][0:64, :],
                             bsq, start=True, stop=True,
                             reads=[(uid, "E", 0), "ones128"], writes=[TOKP[piece]])
                        sch.op("vector", lambda v, piece=piece, idx=idx: v.tensor_reduce(
                            rows[:, 1024 + 2 * idx + piece:1025 + 2 * idx + piece],
                            ps_s[piece][0:1, :], axis=AX, op=OP.max),
                            reads=[TOKP[piece]], writes=[(uid, "m2r", idx)])
                    sch.op("vector", lambda v, idx=idx: v.tensor_reduce(
                        sc16[:, idx:idx + 1], rows[:, 1024 + 2 * idx:1026 + 2 * idx],
                        axis=AX, op=OP.max),
                        reads=[(uid, "m2r", idx)], writes=[(uid, "sc")])
                    M2[idx] = sc16[:, idx:idx + 1]
                for idx, (qa, tq) in enumerate(((q1a, tq1), (q2a, tq2))):
                    sch.op("scalar", lambda s_, qa=qa: s_.activation(
                        out=bsq, in_=qa[0:64, 0:1024], func=AF.Square),
                        reads=[tq], writes=[(uid, "E", 0)])
                    mm_F(sch, ps_s[idx][0:1, :], env["ones128"][0:64, :], bsq,
                         start=True, stop=True,
                         reads=[(uid, "E", 0), "ones128"], writes=[TOKP[idx]])
                    sch.op("vector", lambda v, idx=idx: v.tensor_reduce(
                        sc16[:, 2 + idx:3 + idx], ps_s[idx][0:1, :], axis=AX, op=OP.max),
                        reads=[TOKP[idx]], writes=[(uid, "sc")])
                sch.op("scalar", lambda s_: s_.activation(
                    out=sc16[:, 4:8], in_=sc16[:, 0:4], func=AF.Ln),
                    reads=[(uid, "sc")], writes=[(uid, "sc")])
                sch.op("vector", lambda v: v.tensor_tensor(
                    sc16[:, 8:10], sc16[:, 4:6], sc16[:, 6:8], op=OP.subtract),
                    reads=[(uid, "sc")], writes=[(uid, "sc")])
                sch.op("vector", lambda v: v.tensor_tensor(
                    sc16[:, 10:12], sc16[:, 4:6], sc16[:, 6:8], op=OP.add),
                    reads=[(uid, "sc")], writes=[(uid, "sc")])
                sch.op("scalar", lambda s_: s_.activation(
                    out=sc16[:, 8:10], in_=sc16[:, 8:10], func=AF.Exp, scale=0.5),
                    reads=[(uid, "sc")], writes=[(uid, "sc")])
                sch.op("scalar", lambda s_: s_.activation(
                    out=sc16[:, 10:12], in_=sc16[:, 10:12], func=AF.Exp, scale=0.5),
                    reads=[(uid, "sc")], writes=[(uid, "sc")])
                sch.op("vector", lambda v: v.tensor_scalar(
                    sc16[:, 8:10], sc16[:, 8:10], -0.5, None, op0=OP.mult),
                    reads=[(uid, "sc")], writes=[(uid, "sc")])
                sch.op("vector", lambda v: v.tensor_scalar(
                    sc16[:, 10:12], sc16[:, 10:12], 0.5, None, op0=OP.mult),
                    reads=[(uid, "sc")], writes=[(uid, "sc")])
                for qh in range(2):
                    qlo_h = 1024 * qh
                    if DEFER_EPI == 2 and pending[0] is not None:
                        ep, pending[0] = pending[0], None
                        epi(*ep)
                    for idx, (qa, tq) in enumerate(((q1a, tq1), (q2a, tq2))):
                        sch.op("scalar", lambda s_, qa=qa, qlo_h=qlo_h: s_.activation(
                            out=bsq, in_=qa[0:64, qlo_h:qlo_h + 1024], func=AF.Square),
                            reads=[tq], writes=[(uid, "E", 0)])
                        mm_F(sch, ps_s[0][0:1, :], env["ones128"][0:64, :], bsq,
                             start=True, stop=True,
                             reads=[(uid, "E", 0), "ones128"], writes=[TA])
                        sch.op("vector", lambda v, idx=idx, qa=qa, qlo_h=qlo_h: v.tensor_scalar(
                            qa[64:65, qlo_h:qlo_h + 1024], ps_s[0][0:1, :],
                            sc16[:, 8 + idx:9 + idx], sc16[:, 10 + idx:11 + idx],
                            op0=OP.mult, op1=OP.subtract),
                            reads=[TA, (uid, "sc"), (uid, "augpad")],
                            writes=[tq])
                    kmax = 8 * (qh + 1)
                    sides = ((q1a, k1a, tq1, tk1, p1, TC),
                             (q2a, k2a, tq2, tk2, p2, TC))

                    def s_mm(kc):
                        qlo = max(qlo_h, 128 * kc)
                        F = qlo_h + 1024 - qlo
                        for idx, (qa, ka, tq, tk, pv, tpv) in enumerate(sides):
                            mm_F(sch, ps_s[idx][:, 0:F], ka[0:128, 128 * kc:128 * (kc + 1)],
                                 qa[0:128, qlo:qlo + F], start=True, stop=True,
                                 reads=[tq, tk, (uid, "augpad")], writes=[TOKP[idx]])

                    s_mm(0)
                    for kc in range(kmax):
                        qlo = max(qlo_h, 128 * kc)
                        F = qlo_h + 1024 - qlo
                        loc = qlo - qlo_h
                        for idx, (qa, ka, tq, tk, pv, tpv) in enumerate(sides):
                            sch.op("scalar", lambda s_, idx=idx, loc=loc, F=F: s_.activation(
                                out=E[idx][:, loc:loc + F], in_=ps_s[idx][:, 0:F], func=AF.Exp),
                                reads=[TOKP[idx]], writes=[(uid, "E", idx)])
                            if 128 * kc >= qlo_h:
                                d0 = 128 * kc - qlo_h
                                sch.op("vector", lambda v, idx=idx, d0=d0: v.tensor_tensor(
                                    E[idx][:, d0:d0 + 128], E[idx][:, d0:d0 + 128], ut[:], op=OP.mult),
                                    reads=[(uid, "E", idx), (uid, "ut")], writes=[(uid, "E", idx)])
                        if kc == 1 and pend_shift[0] is not None:
                            sf, pend_shift[0] = pend_shift[0], None
                            sf()
                        if kc + 1 < kmax:
                            s_mm(kc + 1)
                        if kc == 0 and pending[0] is not None:
                            ep, pending[0] = pending[0], None
                            pend_shift[0] = epi(*ep, defer_shift=True)
                        for idx, (qa, ka, tq, tk, pv, tpv) in enumerate(sides):
                            pos = loc
                            while pos < loc + F:
                                nxt = min(loc + F, pos + 512 - (pos % 512))
                                bank = pos // 512
                                last_kc_of_bank = min(kmax - 1, (512 * (bank + 1) - 1 + 1024 * qh) // 128)
                                sch.op("tensor",
                                       lambda t, pos=pos, nxt=nxt, kc=kc, idx=idx, pv=pv, va=va,
                                       st_=(kc == 0), sp_=(kc == last_kc_of_bank): t.matmul(
                                           pv[:, pos:nxt], va[:, 80 * kc:80 * kc + 65],
                                           E[idx][:, pos:nxt], start=st_, stop=sp_),
                                       reads=[(uid, "E", idx)] + tva, writes=[tpv])
                                pos = nxt
                    pending[0] = (h, qh)
                    if DEFER_EPI == 0:
                        ep, pending[0] = pending[0], None
                        epi(*ep)
    if pending[0] is not None:
        ep, pending[0] = pending[0], None
        epi(*ep)
    g_pes.close()
    sch.barrier()

    if debug_afm:
        dsem = es.enter_context(nc.semaphore(f"{uid}_dbg"))
        for c in range(8):
            sch.op("vector", lambda v, c=c: v.tensor_copy(env["ln_tnorm"][:, 0:1024], attn_fm[c][:, 0:1024]),
                   reads=[(uid, "afm", c)], writes=[("dbg",)])
            sch.dma("sync", lambda sy, c=c: sy.dma_start(out=outT[128 * c:128 * (c + 1), 0:1024], in_=env["ln_tnorm"][:, 0:1024]),
                    sem=dsem, reads=[("dbg",)])
            sch.op("vector", lambda v, c=c: v.tensor_copy(env["ln_tnorm"][:, 0:1024], attn_fm[c][:, 1024:2048]),
                   reads=[(uid, "afm", c)], writes=[("dbg",)])
            sch.dma("sync", lambda sy, c=c: sy.dma_start(out=outT[128 * c:128 * (c + 1), 1024:2048], in_=env["ln_tnorm"][:, 0:1024]),
                    sem=dsem, reads=[("dbg",)])
        sch.barrier()
        return
    qp_bf = qpre[0].ap().bitcast(BF16)
    wo_st = DBuf(nc, es, sch, f"{uid}_wo", None, None,
                 tiles=[qp_bf[:, 0:1024], qp_bf[:, 1024:2048]])
    xf_st = DBuf(nc, es, sch, f"{uid}_xf", None, None, queue="sync",
                 tiles=[tcs.ap(), uss.ap()])
    NH = 1024
    for half in range(2):
        hlo = half * NH
        with contextlib.ExitStack() as pes:
            p = lambda name, shape: pes.enter_context(nc.psum_tensor(name, shape, F32))
            ps_o = [p(f"{uid}_pso{half}_{i}", [128, NH]) for i in range(2)]
            for os in range(8):
                b = os % 2
                wtile = wo_st.load(os + 8 * half, wo_blk[os].rearrange("p c h -> p (c h)"))
                xf_st.load(os + 8 * half, xT[128 * os:128 * (os + 1), hlo:hlo + NH])
                for c in range(8):
                    mm_F(sch, ps_o[b][:], wtile[:, 128 * c:128 * (c + 1)],
                         attn_fm[c][:, hlo:hlo + NH], start=(c == 0), stop=(c == 7),
                         reads=[wo_st.tok(os), (uid, "afm", c)], writes=[(uid, "pso", b)])
                sch.op("vector", lambda v, os=os, b=b: v.scalar_tensor_tensor(
                    y[os][:, 0:NH], ps_o[b][:], bias_sb[:, 40 + os:41 + os],
                    xf_st.tiles[b], op0=OP.add, op1=OP.add),
                    reads=[(uid, "pso", b), xf_st.tok(os), (uid, "bias")],
                    writes=[(uid, "y", os)])
            sch.barrier()
        with contextlib.ExitStack() as pes:
            ps_rows = pes.enter_context(nc.psum_tensor(f"{uid}_lnr{half}", [1, 2 * NH], F32)).ap()
            pb = pes.enter_context(nc.psum_tensor(f"{uid}_lnb{half}", [128, 2 * NH], F32)).ap()
            ob = [out_bf16[c][:, hlo:hlo + NH] for c in range(8)] if out_bf16 else None
            obt = ([(tok, half) for tok in out_bf16_toks] if out_bf16 else None)
            emit_layernorm(sch, env, [y[c][:, 0:NH] for c in range(8)],
                           [(uid, "y", c) for c in range(8)], ps_rows, pb, uid,
                           out_bf16=ob, out_bf16_toks=obt,
                           out_dma=lambda c: (outT[128 * c:128 * (c + 1), hlo:hlo + NH], st_sem))
            sch.barrier()


def build_l0_attn_standalone(debug_afm=False):
    nc = bass.Bass("TRN2", target_bir_lowering=False)
    xT = nc.dram_tensor("xT", [D, S], F32, kind="ExternalInput").ap()
    wq_blk = nc.dram_tensor("wq_blk", [16, 128, 8, 128], F32, kind="ExternalInput").ap()
    wk_blk = nc.dram_tensor("wk_blk", [16, 128, 8, 128], F32, kind="ExternalInput").ap()
    wv_blk = nc.dram_tensor("wv_blk", [8, 128, 8, 128], F32, kind="ExternalInput").ap()
    wo_blk = nc.dram_tensor("wo_blk", [8, 128, 8, 128], F32, kind="ExternalInput").ap()
    biases = nc.dram_tensor("biases", [128, 48], F32, kind="ExternalInput").ap()
    lqk = nc.dram_tensor("lqk", [64, 64], F32, kind="ExternalInput").ap()
    hnw = nc.dram_tensor("hnw", [64, 16], F32, kind="ExternalInput").ap()
    cs_tab = nc.dram_tensor("cs_tab", [128, S], F32, kind="ExternalInput").ap()
    ss_tab = nc.dram_tensor("ss_tab", [128, S], F32, kind="ExternalInput").ap()
    perm = nc.dram_tensor("perm", [128, 128], F32, kind="ExternalInput").ap()
    ut = nc.dram_tensor("ut", [128, 128], F32, kind="ExternalInput").ap()
    iden = nc.dram_tensor("iden", [128, 128], F32, kind="ExternalInput").ap()
    v_scr = nc.dram_tensor("v_scr", [D, S], BF16, kind="Internal").ap()
    outT = nc.dram_tensor("outT", [D, S], F32, kind="ExternalOutput").ap()
    es = contextlib.ExitStack()
    with es:
        sch = Sched(nc, es)
        env = make_env(nc, es, sch)
        st_sem = es.enter_context(nc.semaphore("sem_st"))
        emit_l0_attn(nc, sch, es, env, xT, wq_blk, wk_blk, wv_blk, wo_blk,
                     biases[:, :], lqk[:, :], hnw[:, :], cs_tab[:, :], ss_tab[:, :],
                     perm[:, :], ut[:, :], iden[:, :], v_scr, outT, st_sem,
                     debug_afm=debug_afm)
        sch._emit("sync", lambda e: e.nop(), [(st_sem, sch.count(st_sem))], None, 1)
        sch.barrier()
    return nc


def prep_l0_attn_inputs(x_bt, Wq, bq, Wk, bk, Wv, bv, Wo, bo, hnw, lq1, lk1, lq2, lk2):
    xT = np.ascontiguousarray(x_bt.T)
    blk = lambda w, nout: np.ascontiguousarray(
        w.T.reshape(8, 128, nout // 128, 128).transpose(2, 1, 0, 3))
    fm = lambda b: np.ascontiguousarray(b.reshape(-1, 128).T)
    biases = np.concatenate([fm(bq), fm(bk), fm(bv), fm(bo)], axis=1)
    lqk = np.concatenate([lq1.T, lk1.T, lq2.T, lk2.T], axis=1)
    half = 32
    inv = (1.0 / 10000.0 ** (np.arange(half, dtype=np.float64) / half))
    ang = np.arange(S, dtype=np.float64)[:, None] * inv[None, :]
    cosv = np.cos(ang).astype(np.float32)
    sinv = np.sin(ang).astype(np.float32)
    cs = np.zeros((128, S), np.float32)
    ss = np.zeros((128, S), np.float32)
    for p in range(128):
        j = p % 32
        cs[p] = cosv[:, j]
        ss[p] = sinv[:, j] if (p % 64) >= 32 else -sinv[:, j]
    perm = np.zeros((128, 128), np.float32)
    for i in range(128):
        sig = i + 32 if (i % 64) < 32 else i - 32
        perm[sig, i] = 1.0
    ut = np.triu(np.ones((128, 128), np.float32))
    iden = np.eye(128, dtype=np.float32)
    return {"xT": xT, "wq_blk": blk(Wq, 2048), "wk_blk": blk(Wk, 2048),
            "wv_blk": blk(Wv, 1024), "wo_blk": blk(Wo, 1024),
            "biases": biases, "lqk": lqk, "hnw": np.ascontiguousarray(hnw.T),
            "cs_tab": cs, "ss_tab": ss, "perm": perm, "ut": ut, "iden": iden}



def emit_l1_sw_stage(nc, sch, es, env, swT, sclT, grid_h, sw_scr, uid="a1"):
    t = lambda name, shape, dt: es.enter_context(nc.sbuf_tensor(name, shape, dt))
    grid_sb = t(f"{uid}_sgrid", [1, 2], F32)
    c_col = t(f"{uid}_ccol", [128, 1], F32)
    sw_in = DBuf(nc, es, sch, f"{uid}_swin", [128, 3072], BF16, nbuf=3)
    scl_in = DBuf(nc, es, sch, f"{uid}_sclin", [128, 3072], BF16)
    sw_mid = DBuf(nc, es, sch, f"{uid}_swmid", [128, 3072], BF16)
    gsem = sch.new_dma_sem(f"{uid}_sgsem")
    LN6 = env["const"](float(-np.log(6.0)))

    sch.dma("gpsimd", lambda g: g.dma_start(out=grid_sb[:], in_=grid_h[0:1, 0:2]),
            sem=gsem, writes=[(uid, "sgrid")])
    with nc.psum_tensor(f"{uid}_psc0", [128, 1], F32) as ps_c:
        sch.op("vector", lambda v: v.tensor_tensor(
            c_col[0:1, :], grid_sb[0:1, 1:2], grid_sb[0:1, 0:1], op=OP.subtract),
            reads=[(uid, "sgrid")], writes=[(uid, "ccol")])
        sch.op("scalar", lambda s_: s_.activation(
            out=c_col[0:1, :], in_=c_col[0:1, :], func=AF.Ln),
            reads=[(uid, "ccol")], writes=[(uid, "ccol")])
        sch.op("scalar", lambda s_, LN6=LN6: s_.activation(
            out=c_col[0:1, :], in_=c_col[0:1, :], func=AF.Exp, scale=-3.0, bias=LN6),
            reads=[(uid, "ccol"), ("cst",)], writes=[(uid, "ccol")])
        mm_F(sch, ps_c[:], env["ones1f"][:, 0:128], c_col[0:1, :],
             start=True, stop=True, reads=[(uid, "ccol"), "ones1"],
             writes=[(uid, "psc")])
        sch.op("vector", lambda v: v.tensor_copy(c_col[:], ps_c[:]),
               reads=[(uid, "psc")], writes=[(uid, "ccol")])

    seq = [(ir, b8 * 8 + ir) for ir in range(8) for b8 in range(8)]
    st = {"l": 0, "c": 0}
    DEFER = 2

    def do_load(si):
        ir, cc = seq[si]
        if si % 8 == 0:
            scl_in.load(ir, sclT[128 * ir:128 * (ir + 1), :])
        sw_in.load(si, swT[128 * cc:128 * (cc + 1), :])

    def do_comp(si):
        ir, cc = seq[si]
        if si % 8 == 0:
            sch.op("vector", lambda v, ir=ir: v.tensor_scalar(
                scl_in.tiles[ir % 2][:], scl_in.tiles[ir % 2][:], c_col[:, 0:1],
                None, op0=OP.mult),
                reads=[scl_in.tok(ir), (uid, "ccol")], writes=[scl_in.tok(ir)])
        mb = sw_mid.tiles[si % 2]
        sch.op("vector", lambda v, si=si, mb=mb, ir=ir: v.tensor_tensor(
            mb[:], sw_in.tiles[si % 3][:], scl_in.tiles[ir % 2][:], op=OP.mult),
            reads=[sw_in.tok(si), scl_in.tok(ir)],
            writes=[sw_mid.tok(si)])
        for gg in range(3):
            sch.dma("sync", lambda sy, cc=cc, gg=gg, mb=mb: sy.dma_start(
                out=sw_scr[gg, cc], in_=mb[:, 1024 * gg:1024 * (gg + 1)]),
                sem=sw_mid.sems[si % 2], reads=[sw_mid.tok(si)],
                writes=[(uid, "swscr", gg, cc)])

    def pump():
        if st["l"] < 64:
            do_load(st["l"])
            st["l"] += 1
        if st["c"] < st["l"] and (st["l"] - st["c"] > DEFER or st["l"] == 64):
            do_comp(st["c"])
            st["c"] += 1
        return st["c"] < 64
    return pump


def emit_l1_attn(nc, sch, es, env, xT, bw_blk, swT, sclT, grid_h, RT_h, ow_blk,
                 outb, sw_scr, qk_scr, v_scr, outT, st_sem, uid="a1",
                 out_bf16=None, out_bf16_toks=None, x_dep=()):
    t = lambda name, shape, dt: es.enter_context(nc.sbuf_tensor(name, shape, dt))
    GS = 512
    NT = S // GS
    bases = t(f"{uid}_bases", [128, 64 * GS], BF16)
    bas = lambda cc: bases[:, GS * cc:GS * (cc + 1)]
    grid_sb = t(f"{uid}_grid", [128, 96], F32)
    gsem = sch.new_dma_sem(f"{uid}_gsem")
    sch.dma("sync", lambda sy: sy.dma_start(out=grid_sb[:], in_=grid_h),
            sem=gsem, writes=[(uid, "grid")])

    attn_fm = [t(f"{uid}_afm{c}", [128, S], BF16) for c in range(8)]
    Dj = [attn_fm[i // 4][:, GS * (i % 4):GS * (i % 4 + 1)] for i in range(12)]
    xf_st = DBuf(nc, es, sch, f"{uid}_xf", [128, GS], F32, nbuf=8, queue="sync", shared_sem=True)
    silu_x = t(f"{uid}_silu", [128, 8 * GS], BF16)
    L = [t(f"{uid}_L{i}", [128, GS], BF16) for i in range(11)]

    E = [t(f"{uid}_E{i}", [128, 1024], BF16) for i in range(2)]
    stage = E[0][:, 0:GS]
    tmp1f = t(f"{uid}_t1", [128, GS], F32)
    t1s = [tmp1f.ap().bitcast(BF16)[:, 0:GS], tmp1f.ap().bitcast(BF16)[:, GS:2 * GS]]
    t2s_ = [E[1][:, 0:GS], E[1][:, GS:2 * GS]]
    tmp1, tmp2 = t1s[0], t2s_[0]
    xb16 = E[0][:, GS:2 * GS]

    invd = t(f"{uid}_invd", [128, 64], F32)

    sw_st = DBuf(nc, es, sch, f"{uid}_swst", [128, 1024], BF16, nbuf=3, queue="sync")
    bw_st = DBuf(nc, es, sch, f"{uid}_bwst", [128, 8 * 128], BF16, nbuf=2)
    qsem = sch.new_dma_sem(f"{uid}_qsem")

    INVD_OFF = {1: 0, 2: 11, 3: 21}

    q_pes = contextlib.ExitStack()
    ps_o = [q_pes.enter_context(nc.psum_tensor(f"{uid}_pso{i}", [128, GS], F32))
            for i in range(8)]
    v_aug = [t(f"{uid}_vaug{i}", [128, 16 * 80], BF16) for i in range(2)]
    q_aug = t(f"{uid}_qaug", [128, S], BF16)
    k_aug = t(f"{uid}_kaug", [128, S], BF16)
    parts = []
    for s in (attn_fm[3], attn_fm[4], attn_fm[5], attn_fm[6], attn_fm[7],
              q_aug, k_aug):
        f = s.ap().bitcast(F32)
        parts += [f[:, 0:512], f[:, 512:1024]]
    parts += [v_aug[i].ap().bitcast(F32)[:, 0:512] for i in range(2)]
    part_x = t(f"{uid}_partx", [128, 8 * 512], F32)
    parts += [part_x[:, 512 * i:512 * (i + 1)] for i in range(8)]
    SETS = ([0, 1, 2, 3], [4, 5, 6, 7])

    def emit_xloads(tt):
        tlo = tt * GS
        for c in range(8):
            xf_st.load(c + 8 * tt, xT[128 * c:128 * (c + 1), tlo:tlo + GS], reads=x_dep)

    def emit_silu(tt):
        for c in range(8):
            sch.op("scalar", lambda s_, c=c: s_.activation(
                out=silu_x[:, GS * c:GS * (c + 1)], in_=xf_st.tiles[c][:], func=AF.Silu),
                reads=[xf_st.tok(c)], writes=[(uid, "silu", c)])

    def emit_rec(tt, ics):
        for ic in ics:
            xf = xf_st.tiles[ic]
            xtok = xf_st.tok(ic)
            gcol = lambda j: grid_sb[:, 12 * ic + j:12 * ic + j + 1]
            sch.op("vector", lambda v, xf=xf: v.tensor_copy(xb16, xf[:]),
                   reads=[xtok], writes=[(uid, "xb16")])
            xbtok = (uid, "xb16")
            for j in range(12):
                sch.op("vector", lambda v, j=j, gc=gcol(j): v.tensor_scalar(
                    Dj[j], xb16, gc, None, op0=OP.subtract),
                    reads=[xbtok, (uid, "grid")], writes=[(uid, "D", j)])
            ge = [tmp1, tmp2]
            getok = [(uid, "t1", 0), (uid, "t2", 0)]
            sch.op("vector", lambda v: v.tensor_scalar(
                ge[0], Dj[0], 0.0, None, op0=OP.is_ge),
                reads=[(uid, "D", 0)], writes=[getok[0]])
            for j in range(11):
                sch.op("vector", lambda v, j=j: v.tensor_scalar(
                    ge[(j + 1) % 2], Dj[j + 1], 0.0, None, op0=OP.is_ge),
                    reads=[(uid, "D", j + 1)], writes=[getok[(j + 1) % 2]])
                sch.op("vector", lambda v, j=j: v.tensor_tensor(
                    L[j][:], ge[j % 2], ge[(j + 1) % 2], op=OP.subtract),
                    reads=[getok[0], getok[1]], writes=[(uid, "L", j)])
            nb = 0
            for pp in (1, 2, 3):
                n_out = 11 - pp
                for j in range(n_out):
                    last = (pp == 3)
                    dst = bas(j * 8 + ic) if last else L[j][:]
                    dst_tok = (uid, "bas", j * 8 + ic) if last else (uid, "L", j)
                    par = nb % 2
                    nb += 1
                    u1, u2 = t1s[par], t2s_[par]
                    tk1, tk2 = (uid, "t1", par), (uid, "t2", par)
                    sch.op("vector", lambda v, j=j, u1=u1: v.tensor_tensor(
                        u1, Dj[j], L[j][:], op=OP.mult),
                        reads=[(uid, "D", j), (uid, "L", j)], writes=[tk1])
                    sch.op("vector", lambda v, j=j, pp=pp, u2=u2: v.tensor_tensor(
                        u2, Dj[j + pp + 1], L[j + 1][:], op=OP.mult),
                        reads=[(uid, "D", j + pp + 1), (uid, "L", j + 1)], writes=[tk2])
                    sch.op("vector", lambda v, dst=dst, u1=u1, u2=u2: v.tensor_tensor(
                        dst, u1, u2, op=OP.subtract),
                        reads=[tk1, tk2], writes=[dst_tok])

    def emit_mm_half(tt, hs):
        tlo = tt * GS
        ccs = [j * 8 + ic for ic in SETS[hs] for j in range(8)]
        nA = len(SETS[hs]) * 8
        base = (tt * 2 + hs) * 96
        flat = [(gg, cc) for gg in range(3) for cc in ccs]

        def sload(k):
            gg, cc = flat[k]
            sw_st.load(base + k, sw_scr[gg, cc], reads=[(uid, "swscr", gg, cc)])

        sload(0)
        sload(1)
        for k, (gg, cc) in enumerate(flat):
            if hs == 0 and k % nA == 0:
                for oti in range(8):
                    os = gg * 8 + oti
                    bw = bw_st.load(os + 24 * tt, bw_blk[os].rearrange("p c h -> p (c h)"))
                    for c8 in range(8):
                        mm_F(sch, ps_o[oti][:], bw[:, 128 * c8:128 * (c8 + 1)],
                             silu_x[:, GS * c8:GS * (c8 + 1)], start=(c8 == 0), stop=False,
                             reads=[bw_st.tok(os + 24 * tt), (uid, "silu", c8)],
                             writes=[(uid, "pso", oti)])
            if True:
                ci = k % nA
                sidx = base + k
                wt = sw_st.tiles[sidx % sw_st.nbuf]
                for oti in range(8):
                    mm_F(sch, ps_o[oti][:], wt[:, 128 * oti:128 * (oti + 1)],
                         bas(cc), start=(hs == 1 and ci == 0),
                         stop=(ci == nA - 1),
                         reads=[sw_st.tok(sidx), (uid, "bas", cc)],
                         writes=[(uid, "pso", oti)])
                if k + 2 < len(flat):
                    sload(k + 2)
            if ci != nA - 1:
                continue
            if hs == 0:
                for oti in range(8):
                    os = gg * 8 + oti
                    sch.op("scalar", lambda s_, oti=oti, os=os: s_.activation(
                        out=parts[os], in_=ps_o[oti][:], func=AF.Copy),
                        reads=[(uid, "pso", oti)], writes=[(uid, "part", os)])
            else:
                for oti in range(8):
                    os = gg * 8 + oti
                    sch.op("vector", lambda v, oti=oti, os=os: v.tensor_tensor(
                        stage[:], ps_o[oti][:], parts[os], op=OP.add),
                        reads=[(uid, "pso", oti), (uid, "part", os)],
                        writes=[(uid, "stage")])
                    o0 = 128 * os
                    for half in range(2):
                        m = (o0 + 64 * half) // 64
                        h, kind = m // 3, m % 3
                        if kind == 0:
                            dstp = qk_scr[64 * h:64 * h + 64, tlo:tlo + GS]
                        elif kind == 1:
                            dstp = qk_scr[1024 + 64 * h:1024 + 64 * h + 64, tlo:tlo + GS]
                        else:
                            dstp = v_scr[64 * h:64 * h + 64, tlo:tlo + GS]
                        sch.dma("sync", lambda sy, half=half, dstp=dstp: sy.dma_start(
                            out=dstp, in_=stage[64 * half:64 * (half + 1), :]),
                            sem=qsem, reads=[(uid, "stage")],
                            writes=[(uid, "qkv", os, half)])

    emit_xloads(0)
    emit_silu(0)
    emit_rec(0, SETS[0])
    emit_rec(0, SETS[1])
    for tt in range(NT):
        emit_mm_half(tt, 0)
        if tt + 1 < NT:
            emit_xloads(tt + 1)
            emit_rec(tt + 1, SETS[0])
        emit_mm_half(tt, 1)
        if tt + 1 < NT:
            emit_silu(tt + 1)
            emit_rec(tt + 1, SETS[1])
    sch.fence_tokens("vector", [(uid, "stage")])
    sch.barrier()
    q_pes.close()

    RT_sb = t(f"{uid}_RT", [64, 64], BF16)
    rsem = sch.new_dma_sem(f"{uid}_rsem")
    sch.dma("gpsimd", lambda g: g.dma_start(out=RT_sb[:], in_=RT_h),
            sem=rsem, writes=[(uid, "RT")])
    iden = t(f"{uid}_iden", [128, 128], BF16)
    isem = sch.new_dma_sem(f"{uid}_isem")
    sch.dma("gpsimd", lambda g: g.dma_start(out=iden[:], in_=env["iden_h"]),
            sem=isem, writes=[(uid, "iden")])
    _px_bf = part_x.ap().bitcast(BF16)
    qk_in = DBuf(nc, es, sch, f"{uid}_qkin", None, None, queue="sync",
                 tiles=[_px_bf[0:64, 0:2048], _px_bf[0:64, 2048:4096]])
    vtsem = [sch.new_dma_sem(f"{uid}_vt{i}") for i in range(2)]
    y = [bases.ap().bitcast(F32)[:, 2048 + 1024 * c:3072 + 1024 * c]
         for c in range(8)]
    sqt = tmp1f.ap().bitcast(BF16)[0:64, :]
    o1 = t(f"{uid}_o1", [64, 1024], F32)
    comb = silu_x.ap().bitcast(F32)[0:64, 0:1024]
    rows = silu_x.ap().bitcast(F32)[0:1, 1024:2048]
    rowsb = bases.ap().bitcast(F32)[0:1, 4096:5120]
    rbh1 = bases[0:1, 2048:3072]
    m2row = silu_x.ap().bitcast(F32)[32:33, 1024:1040]
    q_augs = [q_aug, _px_bf[:, 4096:6144]]
    k_augs = [k_aug, _px_bf[:, 6144:8192]]
    m2rows = [m2row, silu_x.ap().bitcast(F32)[32:33, 1040:1056]]
    for p in range(2):
        sch.op("vector", lambda v, p=p: v.memset(k_augs[p][64:128, :], 0.0),
               writes=[(uid, "augpad", p), (uid, "kones", p)])
        sch.op("vector", lambda v, p=p: v.memset(k_augs[p][64:65, :], 1.0),
               writes=[(uid, "kones", p)], reads=[(uid, "augpad", p)])
        sch.op("vector", lambda v, p=p: v.memset(q_augs[p][64:128, :], 0.0),
               writes=[(uid, "augpad", p)])

    a_pes = contextlib.ExitStack()
    ps_r = a_pes.enter_context(nc.psum_tensor(f"{uid}_psr", [64, 1024], F32))
    ps_s = [a_pes.enter_context(nc.psum_tensor(f"{uid}_pss{i}", [128, 1024], F32))
            for i in range(2)]
    p1 = a_pes.enter_context(nc.psum_tensor(f"{uid}_p1", [65, 1024], F32))
    def prep_head(h):
        j = h % 2
        qa_, ka_, m2_ = q_augs[j], k_augs[j], m2rows[j]
        TQ, TK = (uid, "qa", j), (uid, "ka", j)
        for (row0, dsta, dtok, scl) in ((64 * h, qa_, TQ, 0.125),
                                        (1024 + 64 * h, ka_, TK, 1.0)):
            qt = qk_in.load(h * 2 + (0 if row0 < 1024 else 1),
                            qk_scr[row0:row0 + 64, :], reads=[(uid, "qkv", 0, 0)])
            for hf in range(2):
                mm_F(sch, ps_r[:], RT_sb[:], qt[:, 1024 * hf:1024 * (hf + 1)],
                     start=True, stop=True,
                     reads=[qk_in.tok(h * 2 + (0 if row0 < 1024 else 1)), (uid, "RT")],
                     writes=[(uid, "psr")])
                sch.op("scalar", lambda s_, dsta=dsta, scl=scl, hf=hf: s_.activation(
                    out=dsta[0:64, 1024 * hf:1024 * (hf + 1)], in_=ps_r[:],
                    func=AF.Copy, scale=scl),
                    reads=[(uid, "psr"), ("cst",)], writes=[dtok])
        va = v_aug[j]
        ones_cols = va.ap().rearrange("p (k c) -> p k c", c=80)[:, :, 64:65]
        sch.op("vector", lambda v, ones_cols=ones_cols: v.memset(ones_cols, 1.0),
               writes=[(uid, "vaug1", j)])
        for kc in range(16):
            sch.dma("sync", lambda sy, va=va, kc=kc, h=h: sy.dma_start(
                out=va[:, 80 * kc:80 * kc + 64],
                in_=v_scr[64 * h:64 * h + 64, 128 * kc:128 * (kc + 1)],
                transpose=True),
                sem=vtsem[j], reads=[(uid, "qkv", 0, 0)],
                writes=[(uid, "vaugd", j, kc)])
        for piece in range(2):
            sch.op("scalar", lambda s_, piece=piece, ka_=ka_, TK=TK: s_.activation(
                out=sqt, in_=ka_[0:64, 1024 * piece:1024 * (piece + 1)],
                func=AF.Square), reads=[TK], writes=[(uid, "sqt")])
            mm_F(sch, ps_s[piece][0:1, :], env["ones128"][0:64, :], sqt,
                 start=True, stop=True, reads=[(uid, "sqt"), "ones128"],
                 writes=[(uid, "pss", piece)])
            sch.op("vector", lambda v, piece=piece, m2_=m2_: v.tensor_reduce(
                m2_[:, piece:piece + 1], ps_s[piece][0:1, :], axis=AX, op=OP.max),
                reads=[(uid, "pss", piece)], writes=[(uid, "m2r", j)])
        sch.op("vector", lambda v, m2_=m2_: v.tensor_reduce(
            m2_[:, 4:5], m2_[:, 0:2], axis=AX, op=OP.max),
            reads=[(uid, "m2r", j)], writes=[(uid, "M2", j)])
        sch.op("scalar", lambda s_, qa_=qa_, TQ=TQ: s_.activation(
            out=sqt, in_=qa_[0:64, 0:1024], func=AF.Square),
            reads=[TQ], writes=[(uid, "sqt")])
        mm_F(sch, ps_s[0][0:1, :], env["ones128"][0:64, :], sqt,
             start=True, stop=True, reads=[(uid, "sqt"), "ones128"],
             writes=[(uid, "pss", 0)])
        sch.op("vector", lambda v, m2_=m2_: v.tensor_reduce(
            m2_[:, 5:6], ps_s[0][0:1, :], axis=AX, op=OP.max),
            reads=[(uid, "pss", 0)], writes=[(uid, "M2", j)])
        sch.op("scalar", lambda s_, m2_=m2_: s_.activation(
            out=m2_[:, 6:8], in_=m2_[:, 4:6], func=AF.Ln),
            reads=[(uid, "M2", j)], writes=[(uid, "M2", j)])
        sch.op("vector", lambda v, m2_=m2_: v.tensor_tensor(
            m2_[:, 8:9], m2_[:, 6:7], m2_[:, 7:8], op=OP.subtract),
            reads=[(uid, "M2", j)], writes=[(uid, "M2", j)])
        sch.op("vector", lambda v, m2_=m2_: v.tensor_tensor(
            m2_[:, 9:10], m2_[:, 6:7], m2_[:, 7:8], op=OP.add),
            reads=[(uid, "M2", j)], writes=[(uid, "M2", j)])
        sch.op("scalar", lambda s_, m2_=m2_: s_.activation(
            out=m2_[:, 8:9], in_=m2_[:, 8:9], func=AF.Exp, scale=0.5),
            reads=[(uid, "M2", j)], writes=[(uid, "M2", j)])
        sch.op("scalar", lambda s_, m2_=m2_: s_.activation(
            out=m2_[:, 9:10], in_=m2_[:, 9:10], func=AF.Exp, scale=0.5),
            reads=[(uid, "M2", j)], writes=[(uid, "M2", j)])
        sch.op("vector", lambda v, m2_=m2_: v.tensor_scalar(
            m2_[:, 8:9], m2_[:, 8:9], -0.5, None, op0=OP.mult),
            reads=[(uid, "M2", j)], writes=[(uid, "M2", j)])
        sch.op("vector", lambda v, m2_=m2_: v.tensor_scalar(
            m2_[:, 9:10], m2_[:, 9:10], 0.5, None, op0=OP.mult),
            reads=[(uid, "M2", j)], writes=[(uid, "M2", j)])
        sch.op("vector", lambda v, qa_=qa_, m2_=m2_, TQ=TQ, j=j: v.tensor_scalar(
            qa_[64:65, 0:1024], ps_s[0][0:1, :],
            m2_[:, 8:9], m2_[:, 9:10], op0=OP.mult, op1=OP.subtract),
            reads=[(uid, "pss", 0), (uid, "M2", j), (uid, "augpad", j)],
            writes=[TQ])

    def emit_qh(h, qh):
        j = h % 2
        qa_, ka_, m2_ = q_augs[j], k_augs[j], m2rows[j]
        TQ, TK = (uid, "qa", j), (uid, "ka", j)
        va = v_aug[j]
        tva = [(uid, "vaug1", j)] + [(uid, "vaugd", j, kc) for kc in range(16)]
        qlo_h = 1024 * qh
        if qh == 1:
            sch.op("scalar", lambda s_, qlo_h=qlo_h, qa_=qa_, TQ=TQ: s_.activation(
                out=sqt, in_=qa_[0:64, qlo_h:qlo_h + 1024], func=AF.Square),
                reads=[TQ], writes=[(uid, "sqt")])
            mm_F(sch, ps_s[0][0:1, :], env["ones128"][0:64, :], sqt,
                 start=True, stop=True, reads=[(uid, "sqt"), "ones128"],
                 writes=[(uid, "pss", 0)])
            sch.op("vector", lambda v, qlo_h=qlo_h, qa_=qa_, m2_=m2_, TQ=TQ, j=j: v.tensor_scalar(
                qa_[64:65, qlo_h:qlo_h + 1024], ps_s[0][0:1, :],
                m2_[:, 8:9], m2_[:, 9:10], op0=OP.mult, op1=OP.subtract),
                reads=[(uid, "pss", 0), (uid, "M2", j), (uid, "augpad", j)],
                writes=[TQ])

        def s_mm(kc):
            idx = kc % 2
            mm_F(sch, ps_s[idx][:, :], ka_[0:128, 128 * kc:128 * (kc + 1)],
                 qa_[0:128, qlo_h:qlo_h + 1024], start=True, stop=True,
                 reads=[TQ, TK, (uid, "kones", j), (uid, "augpad", j)],
                 writes=[(uid, "pss", idx)])

        s_mm(0)
        for kc in range(16):
            idx = kc % 2
            sch.op("scalar", lambda s_, idx=idx: s_.activation(
                out=E[idx][:, :], in_=ps_s[idx][:, :], func=AF.Exp),
                reads=[(uid, "pss", idx)], writes=[(uid, "E", idx)])
            if kc + 1 < 16:
                s_mm(kc + 1)
            mm_F(sch, p1[:, :], va[:, 80 * kc:80 * kc + 65], E[idx][:, :],
                 start=(kc == 0), stop=(kc == 15),
                 reads=[(uid, "E", idx)] + tva, writes=[(uid, "p1")])
        sch.op("scalar", lambda s_: s_.activation(out=rows[:, :], in_=p1[64:65, :], func=AF.Ln),
               reads=[(uid, "p1")], writes=[(uid, "rowA")])
        sch.op("scalar", lambda s_: s_.activation(out=rbh1[:, :], in_=rows[:, :],
                                                  func=AF.Exp, scale=-1.0),
               reads=[(uid, "rowA"), ("cst",)], writes=[(uid, "rowB")])
        mm_F(sch, ps_r[0:64, :], env["ones1"][:, 0:64], rbh1[:, :],
             start=True, stop=True, reads=[(uid, "rowB"), "ones1"],
             writes=[(uid, "psr")])
        sch.op("vector", lambda v: v.tensor_copy(o1[:], p1[0:64, :]),
               reads=[(uid, "p1")], writes=[(uid, "o1")])
        cdst = attn_fm[h // 2]
        if h % 2 == 0:
            sch.op("vector", lambda v, cdst=cdst, qh=qh: v.tensor_tensor(
                cdst[0:64, 1024 * qh:1024 * (qh + 1)], o1[:], ps_r[0:64, :], op=OP.mult),
                reads=[(uid, "o1"), (uid, "psr")], writes=[(uid, "afm", h // 2)])
        else:
            sch.op("vector", lambda v: v.tensor_tensor(
                comb, o1[:], ps_r[0:64, :], op=OP.mult),
                reads=[(uid, "o1"), (uid, "psr")], writes=[(uid, "comb")])
            sch.op("vector", lambda v: v.tensor_copy(sqt, comb),
                   reads=[(uid, "comb")], writes=[(uid, "sqt")])
            mm_F(sch, ps_s[1][64:128, :], iden[0:64, 0:64], sqt,
                 start=True, stop=True, reads=[(uid, "sqt"), (uid, "iden")],
                 writes=[(uid, "pss", 1)], tile_position=(0, 64))
            sch.op("vector", lambda v, cdst=cdst, qh=qh: v.tensor_copy(
                cdst[64:128, 1024 * qh:1024 * (qh + 1)], ps_s[1][64:128, :]),
                reads=[(uid, "pss", 1)], writes=[(uid, "afm", h // 2)])

    prep_head(0)
    for h in range(16):
        emit_qh(h, 0)
        if h + 1 < 16:
            prep_head(h + 1)
        emit_qh(h, 1)
    a_pes.close()
    sch.barrier()

    ob_sb = t(f"{uid}_ob", [128, 8], F32)
    obsem = sch.new_dma_sem(f"{uid}_obsem")
    sch.dma("sync", lambda sy: sy.dma_start(out=ob_sb[:], in_=outb),
            sem=obsem, writes=[(uid, "ob")])
    qa_bf = q_aug.ap()
    wo_st = DBuf(nc, es, sch, f"{uid}_wo", None, None, queue="gpsimd",
                 tiles=[qa_bf[:, 0:1024], qa_bf[:, 1024:2048]])
    ka_f32 = k_aug.ap().bitcast(F32)
    xf2b = bases.ap().bitcast(F32)[:, 0:1024]
    xf2_st = DBuf(nc, es, sch, f"{uid}_xf2", None, None, queue="sync",
                  tiles=[ka_f32, xf2b])
    NH = 1024
    for half in range(2):
        hlo = half * NH
        with contextlib.ExitStack() as pes:
            p = lambda name, shape: pes.enter_context(nc.psum_tensor(name, shape, F32))
            ps_p = [p(f"{uid}_psp{half}_{i}", [128, NH]) for i in range(2)]
            for os in range(8):
                b = os % 2
                wtile = wo_st.load(os + 8 * half, ow_blk[os].rearrange("p c h -> p (c h)"))
                xf2_st.load(os + 8 * half, xT[128 * os:128 * (os + 1), hlo:hlo + NH], reads=x_dep)
                for c in range(8):
                    mm_F(sch, ps_p[b][:], wtile[:, 128 * c:128 * (c + 1)],
                         attn_fm[c][:, hlo:hlo + NH], start=(c == 0), stop=(c == 7),
                         reads=[wo_st.tok(os), (uid, "afm", c)], writes=[(uid, "psp", b)])
                sch.op("vector", lambda v, os=os, b=b: v.scalar_tensor_tensor(
                    y[os][:, 0:NH], ps_p[b][:], ob_sb[:, os:os + 1],
                    xf2_st.tiles[b], op0=OP.add, op1=OP.add),
                    reads=[(uid, "psp", b), xf2_st.tok(os), (uid, "ob")],
                    writes=[(uid, "y", os)])
            sch.barrier()
        with contextlib.ExitStack() as pes:
            ps_rows = pes.enter_context(nc.psum_tensor(f"{uid}_lnr{half}", [1, 2 * NH], F32)).ap()
            pb = pes.enter_context(nc.psum_tensor(f"{uid}_lnb{half}", [128, 2 * NH], F32)).ap()
            ob2 = [out_bf16[c][:, hlo:hlo + NH] for c in range(8)] if out_bf16 else None
            obt = ([(tok, half) for tok in out_bf16_toks] if out_bf16 else None)
            emit_layernorm(sch, env, [y[c][:, 0:NH] for c in range(8)],
                           [(uid, "y", c) for c in range(8)], ps_rows, pb, uid,
                           out_bf16=ob2, out_bf16_toks=obt,
                           out_dma=lambda c: (outT[128 * c:128 * (c + 1), hlo:hlo + NH], st_sem))
            sch.barrier()


def build_l1_attn_standalone():
    nc = bass.Bass("TRN2", target_bir_lowering=False)
    xT = nc.dram_tensor("xT", [D, S], F32, kind="ExternalInput").ap()
    bw_blk = nc.dram_tensor("bw_blk", [24, 128, 8, 128], F32, kind="ExternalInput").ap()
    swT = nc.dram_tensor("swT", [8192, 3072], F32, kind="ExternalInput").ap()
    sclT = nc.dram_tensor("sclT", [1024, 3072], F32, kind="ExternalInput").ap()
    grid_h = nc.dram_tensor("grid_h", [128, 96], F32, kind="ExternalInput").ap()
    RT_h = nc.dram_tensor("RT_h", [64, 64], F32, kind="ExternalInput").ap()
    ow_blk = nc.dram_tensor("ow_blk", [8, 128, 8, 128], F32, kind="ExternalInput").ap()
    outb = nc.dram_tensor("outb", [128, 8], F32, kind="ExternalInput").ap()
    iden_h = nc.dram_tensor("iden_h", [128, 128], F32, kind="ExternalInput").ap()
    sw_scr = nc.dram_tensor("sw_scr", [3, 64, 128, 1024], BF16, kind="Internal").ap()
    qk_scr = nc.dram_tensor("qk_scr", [2048, S], BF16, kind="Internal").ap()
    v_scr = nc.dram_tensor("v_scr", [D, S], BF16, kind="Internal").ap()
    outT = nc.dram_tensor("outT", [D, S], F32, kind="ExternalOutput").ap()
    es = contextlib.ExitStack()
    with es:
        sch = Sched(nc, es)
        env = make_env(nc, es, sch)
        env["iden_h"] = iden_h[:, :]
        st_sem = es.enter_context(nc.semaphore("sem_st"))
        with contextlib.ExitStack() as ses:
            pump = emit_l1_sw_stage(nc, sch, ses, env, swT[:, :], sclT,
                                    grid_h[:, :], sw_scr, uid="a1")
            while pump():
                pass
            sch.retire_dmas()
            sch.barrier()
        emit_l1_attn(nc, sch, es, env, xT, bw_blk, swT[:, :], sclT, grid_h[:, :],
                     RT_h[:, :], ow_blk, outb[:, :], sw_scr, qk_scr, v_scr, outT, st_sem)
        sch._emit("sync", lambda e: e.nop(), [(st_sem, sch.count(st_sem))], None, 1)
        sch.barrier()
    return nc


def prep_l1_attn_inputs(x_bt, base_weight, spline_weight, spline_scaler, grid,
                        rotation_matrix, out_w, out_b):
    xT = np.ascontiguousarray(x_bt.T)
    blk = lambda w, nin, nout: np.ascontiguousarray(
        w.T.reshape(nin // 128, 128, nout // 128, 128).transpose(2, 1, 0, 3))
    swT = np.ascontiguousarray(spline_weight.transpose(2, 1, 0).reshape(8192, 3072))
    sclT = np.ascontiguousarray(spline_scaler.T)
    grid_h = np.ascontiguousarray(grid.reshape(8, 128, 12).transpose(1, 0, 2).reshape(128, 96))
    return {"xT": xT, "bw_blk": blk(base_weight, 1024, 3072), "swT": swT,
            "sclT": sclT, "grid_h": grid_h,
            "RT_h": np.ascontiguousarray(rotation_matrix.T),
            "ow_blk": blk(out_w, 1024, 1024),
            "outb": np.ascontiguousarray(out_b.reshape(8, 128).T),
            "iden_h": np.eye(128, dtype=np.float32)}



_PROG_CACHE = {}


def _get_prog(name, builder):
    if name not in _PROG_CACHE:
        _PROG_CACHE[name] = builder()
    return _PROG_CACHE[name]


def kernel_unfused(L0_attn_x, L0_attn_Wq, L0_attn_bq, L0_attn_Wk, L0_attn_bk,
           L0_attn_Wv, L0_attn_bv, L0_attn_Wo, L0_attn_bo, L0_attn_head_norm_w,
           L0_attn_lq1, L0_attn_lk1, L0_attn_lq2, L0_attn_lk2,
           L0_ffn_w_gate, L0_ffn_w_up, L0_ffn_w_down,
           L1_attn_base_weight, L1_attn_spline_weight, L1_attn_spline_scaler,
           L1_attn_grid, L1_attn_rotation_matrix, L1_attn_out_w, L1_attn_out_b,
           L1_ffn_w1_w, L1_ffn_w1_s, L1_ffn_w2_w, L1_ffn_w2_s,
           L1_ffn_h1_w, L1_ffn_h1_s, L1_ffn_h2_w, L1_ffn_h2_s,
           L1_ffn_w3_w, L1_ffn_w3_s):
    B = L0_attn_x.shape[0]
    x = np.ascontiguousarray(L0_attn_x)

    common_a0 = prep_l0_attn_inputs(
        x[0], L0_attn_Wq, L0_attn_bq, L0_attn_Wk, L0_attn_bk, L0_attn_Wv,
        L0_attn_bv, L0_attn_Wo, L0_attn_bo, L0_attn_head_norm_w,
        L0_attn_lq1, L0_attn_lk1, L0_attn_lq2, L0_attn_lk2)
    in_maps = []
    for b in range(B):
        m = dict(common_a0)
        m["xT"] = np.ascontiguousarray(x[b].T)
        in_maps.append(m)
    nc = _get_prog("l0attn", build_l0_attn_standalone)
    res = run_bass_kernel_spmd(nc, in_maps, core_ids=list(range(B)))
    x1 = [r["outT"] for r in res.results]

    common_f0 = prep_l0_ffn_inputs(x[0], L0_ffn_w_gate, L0_ffn_w_up, L0_ffn_w_down)
    in_maps = []
    for b in range(B):
        m = dict(common_f0)
        m["xT"] = x1[b]
        in_maps.append(m)
    nc = _get_prog("l0ffn", build_l0_ffn_standalone)
    res = run_bass_kernel_spmd(nc, in_maps, core_ids=list(range(B)))
    x2 = [r["outT"] for r in res.results]

    common_a1 = prep_l1_attn_inputs(
        x[0], L1_attn_base_weight, L1_attn_spline_weight, L1_attn_spline_scaler,
        L1_attn_grid, L1_attn_rotation_matrix, L1_attn_out_w, L1_attn_out_b)
    in_maps = []
    for b in range(B):
        m = dict(common_a1)
        m["xT"] = x2[b]
        in_maps.append(m)
    nc = _get_prog("l1attn", build_l1_attn_standalone)
    res = run_bass_kernel_spmd(nc, in_maps, core_ids=list(range(B)))
    x3 = [r["outT"] for r in res.results]

    common_f1 = prep_l1_ffn_inputs(
        x[0], L1_ffn_w1_w, L1_ffn_w1_s, L1_ffn_w2_w, L1_ffn_w2_s,
        L1_ffn_h1_w, L1_ffn_h1_s, L1_ffn_h2_w, L1_ffn_h2_s,
        L1_ffn_w3_w, L1_ffn_w3_s)
    in_maps = []
    for b in range(B):
        m = dict(common_f1)
        m["xT"] = x3[b]
        in_maps.append(m)
    nc = _get_prog("l1ffn", build_l1_ffn_standalone)
    res = run_bass_kernel_spmd(nc, in_maps, core_ids=list(range(B)))
    out = np.stack([r["outT"].T for r in res.results], axis=0)
    return out



def build_full():
    nc = bass.Bass("TRN2", target_bir_lowering=False)
    ap = lambda n, s, d, k: nc.dram_tensor(n, s, d, kind=k).ap()
    x0T = ap("xT", [D, S], F32, "ExternalInput")
    wq_blk = ap("wq_blk", [16, 128, 8, 128], F32, "ExternalInput")
    wk_blk = ap("wk_blk", [16, 128, 8, 128], F32, "ExternalInput")
    wv_blk = ap("wv_blk", [8, 128, 8, 128], F32, "ExternalInput")
    wo_blk = ap("wo_blk", [8, 128, 8, 128], F32, "ExternalInput")
    biases = ap("biases", [128, 48], F32, "ExternalInput")
    lqk = ap("lqk", [64, 64], F32, "ExternalInput")
    hnw = ap("hnw", [64, 16], F32, "ExternalInput")
    cs_tab = ap("cs_tab", [128, S], F32, "ExternalInput")
    ss_tab = ap("ss_tab", [128, S], F32, "ExternalInput")
    perm = ap("perm", [128, 128], F32, "ExternalInput")
    ut = ap("ut", [128, 128], F32, "ExternalInput")
    iden = ap("iden", [128, 128], F32, "ExternalInput")
    wg_blk = ap("wg_blk", [24, 128, 8, 128], I32, "ExternalInput")
    wu_blk = ap("wu_blk", [24, 128, 8, 128], I32, "ExternalInput")
    wd_blk = ap("wd_blk", [8, 128, 24, 128], I32, "ExternalInput")
    bw_blk = ap("bw_blk", [24, 128, 8, 128], F32, "ExternalInput")
    swT = ap("swT", [8192, 3072], F32, "ExternalInput")
    sclT = ap("sclT", [1024, 3072], F32, "ExternalInput")
    grid_h = ap("grid_h", [128, 96], F32, "ExternalInput")
    RT_h = ap("RT_h", [64, 64], F32, "ExternalInput")
    ow_blk = ap("ow_blk", [8, 128, 8, 128], F32, "ExternalInput")
    outb = ap("outb", [128, 8], F32, "ExternalInput")
    w1_blk = ap("w1_blk", [32, 128, 8, 128], F32, "ExternalInput")
    w2_blk = ap("w2_blk", [32, 128, 8, 128], F32, "ExternalInput")
    h1_blk = ap("h1_blk", [32, 128, 32, 128], F32, "ExternalInput")
    h2_blk = ap("h2_blk", [32, 128, 32, 128], F32, "ExternalInput")
    w3_blk = ap("w3_blk", [8, 128, 32, 128], F32, "ExternalInput")
    scl = ap("scl", [128, 4 * 32 + 8], F32, "ExternalInput")
    v_scr0 = ap("v_scr0", [D, S], BF16, "Internal")
    sw_scr = ap("sw_scr", [3, 64, 128, 1024], BF16, "Internal")
    qk_scr = ap("qk_scr", [2048, S], BF16, "Internal")
    v_scr1 = ap("v_scr1", [D, S], BF16, "Internal")
    x1T = ap("x1T", [D, S], F32, "Internal")
    x2T = ap("x2T", [D, S], F32, "Internal")
    x3T = ap("x3T", [D, S], F32, "Internal")
    outT = ap("outT", [D, S], F32, "ExternalOutput")

    es = contextlib.ExitStack()
    with es:
        sch = Sched(nc, es)
        env = make_env(nc, es, sch)
        env["iden_h"] = iden
        sts = [sch.new_dma_sem(f"st{i}") for i in range(4)]
        with contextlib.ExitStack() as bes:
            emit_l0_attn(nc, sch, bes, env, x0T, wq_blk, wk_blk, wv_blk, wo_blk,
                         biases[:, :], lqk, hnw, cs_tab, ss_tab, perm, ut, iden,
                         v_scr0, x1T, sts[0], uid="a0")
            sch.retire_dmas()
            sch.barrier()
        with contextlib.ExitStack() as bes:
            pump = emit_l1_sw_stage(nc, sch, bes, env, swT[:, :], sclT, grid_h,
                                    sw_scr, uid="a1")
            emit_l0_ffn(nc, sch, bes, env, x1T, wg_blk, wu_blk, wd_blk, x2T,
                        sts[1], uid="f0", x_dep=[("a0", "xout")], pump=pump)
            while pump():
                pass
            sch.retire_dmas()
            sch.barrier()
        with contextlib.ExitStack() as bes:
            emit_l1_attn(nc, sch, bes, env, x2T, bw_blk, swT[:, :], sclT, grid_h,
                         RT_h, ow_blk, outb[:, :], sw_scr, qk_scr, v_scr1, x3T,
                         sts[2], uid="a1", x_dep=[("f0", "xout")])
            sch.retire_dmas()
            sch.barrier()
        with contextlib.ExitStack() as bes:
            emit_l1_ffn(nc, sch, bes, env, x3T, w1_blk, w2_blk, h1_blk, h2_blk,
                        w3_blk, scl, outT, sts[3], uid="f1", x_dep=[("a1", "xout")])
            sch.retire_dmas()
            sch.barrier()
        sch._emit("sync", lambda e: e.nop(), [(sts[3], sch.count(sts[3]))], None, 1)
        sch.retire_dmas("sync")
        sch.barrier()
    return nc


def prep_full_inputs(inputs):
    g = inputs
    m = {}
    m.update(prep_l0_attn_inputs(
        g["L0_attn_x"][0], g["L0_attn_Wq"], g["L0_attn_bq"], g["L0_attn_Wk"],
        g["L0_attn_bk"], g["L0_attn_Wv"], g["L0_attn_bv"], g["L0_attn_Wo"],
        g["L0_attn_bo"], g["L0_attn_head_norm_w"], g["L0_attn_lq1"],
        g["L0_attn_lk1"], g["L0_attn_lq2"], g["L0_attn_lk2"]))
    m.update(prep_l0_ffn_inputs(g["L0_attn_x"][0], g["L0_ffn_w_gate"],
                                g["L0_ffn_w_up"], g["L0_ffn_w_down"]))
    m.update(prep_l1_attn_inputs(
        g["L0_attn_x"][0], g["L1_attn_base_weight"], g["L1_attn_spline_weight"],
        g["L1_attn_spline_scaler"], g["L1_attn_grid"],
        g["L1_attn_rotation_matrix"], g["L1_attn_out_w"], g["L1_attn_out_b"]))
    m.update(prep_l1_ffn_inputs(
        g["L0_attn_x"][0], g["L1_ffn_w1_w"], g["L1_ffn_w1_s"], g["L1_ffn_w2_w"],
        g["L1_ffn_w2_s"], g["L1_ffn_h1_w"], g["L1_ffn_h1_s"], g["L1_ffn_h2_w"],
        g["L1_ffn_h2_s"], g["L1_ffn_w3_w"], g["L1_ffn_w3_s"]))
    m.pop("iden_h", None)
    return m


def kernel(**inputs):
    x = np.ascontiguousarray(inputs["L0_attn_x"])
    B = x.shape[0]
    common = prep_full_inputs(inputs)
    in_maps = []
    for b in range(B):
        m = dict(common)
        m["xT"] = np.ascontiguousarray(x[b].T)
        in_maps.append(m)
    nc = _get_prog("full", build_full)
    res = run_bass_kernel_spmd(nc, in_maps, core_ids=list(range(B)))
    return np.stack([r["outT"].T for r in res.results], axis=0)


# version 26
# speedup vs baseline: 1.0351x; 1.0351x over previous
import math
import numpy as np
import contextlib
from collections import defaultdict

import concourse.bass as bass
import concourse.mybir as mybir
from concourse.alu_op_type import AluOpType as OP
from concourse.bass_utils import run_bass_kernel_spmd

F32 = mybir.dt.float32
BF16 = mybir.dt.bfloat16
I32 = mybir.dt.int32
AF = mybir.ActivationFunctionType
AX = mybir.AxisListType.X

S = 2048
D = 1024

QUEUES = ("sync", "gpsimd", "vector", "scalar", "tensor")
ENG_SEM = ("vector", "scalar", "tensor", "gpsimd")


class Sched:

    def __init__(self, nc, es):
        self.nc = nc
        self.es = es
        self.q = defaultdict(list)
        self.cnt = {}
        self.esem = {}
        for eng in ENG_SEM:
            self.esem[eng] = es.enter_context(nc.semaphore(f"prog_{eng}"))
        self.eng_pos = defaultdict(int)
        self.pe_inc_positions = []
        self.last_w = {}
        self.readers = defaultdict(list)
        self.max_waited = defaultdict(dict)
        self.nwaits = 0
        self.nops = 0
        self.dma_sems = []

    def new_sem(self, name):
        s = self.es.enter_context(self.nc.semaphore(name))
        return s

    def new_dma_sem(self, name):
        s = self.new_sem(name)
        self.dma_sems.append(s)
        return s

    def retire_dmas(self, queue="vector"):
        waits = [(s, self.count(s)) for s in self.dma_sems]
        self._emit(queue, lambda e: e.nop(), waits, None, 1)

    def count(self, sem):
        return self.cnt.get(id(sem), 0)

    def _bump(self, sem, k=1):
        self.cnt[id(sem)] = self.cnt.get(id(sem), 0) + k
        return self.cnt[id(sem)]

    def _resolve(self, h, dst_eng):
        if h is None:
            return None
        sem, val, src_eng = h
        if src_eng == "tensor" and dst_eng == "tensor":
            return None
        if src_eng == "dma":
            val = self.cnt.get(id(sem), val)
        return (sem, val)

    def _emit(self, queue, fn, waits, inc_sem, inc_k):
        bysem = {}
        for w in waits:
            if w is None:
                continue
            s, v = w
            if v <= 0:
                continue
            k = id(s)
            if k not in bysem or v > bysem[k][1]:
                bysem[k] = (s, v)
        wl = []
        for s, v in bysem.values():
            prev = self.max_waited[queue].get(id(s), -1)
            if v > prev:
                self.max_waited[queue][id(s)] = v
                wl.append((s, v))
        self.nwaits += len(wl)
        self.nops += 1

        def emit(e, fn=fn, wl=tuple(wl), inc_sem=inc_sem, inc_k=inc_k):
            for (s, v) in wl:
                e.wait_ge(s, v)
            inst = fn(e)
            if inc_sem is not None:
                inst.then_inc(inc_sem, inc_k)

        self.q[queue].append(emit)

    def _dep_waits(self, dst_eng, reads, writes):
        waits = []
        for tok in reads:
            try:
                waits.append(self._resolve(self.last_w.get(tok), dst_eng))
            except RuntimeError as e:
                raise RuntimeError(f"{e} (RAW on token {tok})")
        for tok in writes:
            for h in self.readers.get(tok, ()):
                try:
                    waits.append(self._resolve(h, dst_eng))
                except RuntimeError as e:
                    raise RuntimeError(f"{e} (WAR on token {tok})")
            try:
                waits.append(self._resolve(self.last_w.get(tok), dst_eng))
            except RuntimeError as e:
                raise RuntimeError(f"{e} (WAW on token {tok})")
        return waits

    def op(self, eng, fn, reads=(), writes=(), waits=(), inc=True):
        dep = self._dep_waits(eng, reads, writes)
        dep.extend(waits)
        val = self._bump(self.esem[eng])
        self._emit(eng, fn, dep, self.esem[eng], 1)
        handle = (self.esem[eng], val, eng)
        self.eng_pos[eng] += 1
        for tok in writes:
            self.last_w[tok] = handle
            self.readers[tok] = []
        for tok in reads:
            self.readers[tok].append(handle)
        return handle

    def dma(self, queue, fn, sem, reads=(), writes=(), waits=()):
        dep = self._dep_waits("dma", reads, writes)
        dep.extend(waits)
        val = self._bump(sem, 16)
        self._emit(queue, fn, dep, sem, 16)
        handle = (sem, val, "dma")
        for tok in writes:
            self.last_w[tok] = handle
            self.readers[tok] = []
        for tok in reads:
            self.readers[tok].append(handle)
        return handle

    def fence_tokens(self, queue, toks):
        waits = self._dep_waits("dma", toks, toks)
        self._emit(queue, lambda e: e.nop(), waits, None, 1)

    def barrier(self):
        with self.nc.Block(no_gpsimd_drain=True) as block:
            for eng in QUEUES:
                items = self.q.get(eng)
                if not items:
                    continue

                def runner(e, items=list(items)):
                    for it in items:
                        it(e)

                getattr(block, eng)(runner)
        self.q.clear()


def mm_F(sch, psum_ap, lhsT_ap, rhs_ap, start, stop, reads=(), writes=(),
         waits=(), fmax=512, tile_position=None, abs_off=0):
    NF = rhs_ap.shape[-1]
    cuts = [0]
    pos = 0
    while pos < NF:
        nxt = min(NF, pos + fmax - ((abs_off + pos) % fmax))
        cuts.append(nxt)
        pos = nxt
    h = None
    for i in range(len(cuts) - 1):
        lo, hi = cuts[i], cuts[i + 1]
        h = sch.op("tensor",
                   lambda t, lo=lo, hi=hi: t.matmul(
                       psum_ap[:, lo:hi], lhsT_ap, rhs_ap[:, lo:hi],
                       start=start, stop=stop, tile_position=tile_position),
                   reads=reads, writes=writes,
                   waits=waits if i == 0 else (),
                   inc=(stop and i == len(cuts) - 2))
    return h


class DBuf:

    def __init__(self, nc, es, sch, name, shape, dtype, nbuf=2, queue="gpsimd",
                 tiles=None, shared_sem=False):
        self.sch = sch
        self.queue = queue
        self.name = name
        if tiles is None:
            self.tiles = [es.enter_context(nc.sbuf_tensor(f"{name}_{i}", shape, dtype))
                          for i in range(nbuf)]
        else:
            self.tiles = tiles
        if shared_sem:
            s = sch.new_dma_sem(f"{name}_s")
            self.sems = [s] * nbuf
        else:
            self.sems = [sch.new_dma_sem(f"{name}_s{i}") for i in range(nbuf)]
        self.nbuf = nbuf

    def tok(self, idx):
        return (self.name, idx % self.nbuf)

    def load(self, idx, src_ap, reads=()):
        b = idx % self.nbuf
        tile = self.tiles[b]
        dst = tile[:] if hasattr(tile, "ap") else tile
        self.sch.dma(self.queue,
                     lambda e, dst=dst, src_ap=src_ap: e.dma_start(out=dst, in_=src_ap),
                     sem=self.sems[b], reads=list(reads), writes=[self.tok(idx)])
        return tile


def make_env(nc, es, sch):
    env = {"nc": nc}
    t = lambda name, shape, dt: es.enter_context(nc.sbuf_tensor(name, shape, dt))
    env["sbuf"] = t
    env["ones128"] = t("c_ones128", [128, 1], BF16)
    env["ones1"] = t("c_ones1", [1, 128], BF16)
    env["ones1f"] = t("c_ones1f", [1, 128], F32)
    NCONST = 16
    cst = t("c_fconsts", [128, NCONST], F32)
    env["_cst_tile"] = cst
    env["_cst_vals"] = []

    def const(value):
        vals = env["_cst_vals"]
        value = float(value)
        if value not in vals:
            assert len(vals) < NCONST
            vals.append(value)
            i = len(vals) - 1
            nc.const_aps.register(F32, value, cst[:, i:i + 1])
            sch.op("vector", lambda v, i=i, value=value: v.memset(
                cst[:, i:i + 1], value), writes=[("cst",)])
        return value

    env["const"] = const
    env["ln_ybf"] = [t(f"ln_ybf{i}", [128, S // 2], BF16) for i in range(2)]
    env["ln_ysq"] = [t(f"ln_ysq{i}", [128, S // 2], BF16) for i in range(2)]
    env["ln_rows"] = t("ln_rows", [1, 3 * (S // 2)], F32)
    env["ln_tnorm"] = t("ln_tnorm", [128, S // 2], F32)
    sch.op("vector", lambda v: v.memset(env["ones128"][:], 1.0), writes=["ones128"])
    sch.op("vector", lambda v: v.memset(env["ones1"][:], 1.0), writes=["ones1"])
    sch.op("vector", lambda v: v.memset(env["ones1f"][:], 1.0), writes=["ones1"])
    return env



def emit_layernorm(sch, env, y_chunks, y_toks, ps_rows, pb, uid,
                   out_f32=None, out_f32_toks=None, out_bf16=None,
                   out_bf16_toks=None, eps=1e-5, out_dma=None,
                   ps_tok=None, pb_tok=None):
    N = y_chunks[0].shape[-1]
    ybf, ysq = env["ln_ybf"], env["ln_ysq"]
    rows, tnorm = env["ln_rows"], env["ln_tnorm"]
    ones128, ones1 = env["ones128"], env["ones1"]
    r_mean, r_var, r_m2 = rows[:, 0:N], rows[:, N:2 * N], rows[:, 2 * N:3 * N]
    ps_sum, ps_sq = ps_rows[:, 0:N], ps_rows[:, N:2 * N]
    pb_m, pb_r = pb[:, 0:N], pb[:, N:2 * N]
    T_ps, T_pb = ps_tok or (uid, "ps_rows"), pb_tok or (uid, "pb")
    b_eps = env["const"](eps)
    s_half = env["const"](-0.5)

    for c in range(8):
        tb, tq = ("ln_ybf", c % 2), ("ln_ysq", c % 2)
        sch.op("vector", lambda v, c=c: v.tensor_copy(ybf[c % 2][:, 0:N], y_chunks[c]),
               reads=[y_toks[c]], writes=[tb])
        sch.op("vector", lambda v, c=c: v.tensor_tensor(
            ysq[c % 2][:, 0:N], ybf[c % 2][:, 0:N], ybf[c % 2][:, 0:N], op=OP.mult),
            reads=[tb], writes=[tq])
        mm_F(sch, ps_sum, ones128[:], ybf[c % 2][:, 0:N], start=(c == 0), stop=(c == 7),
             reads=[tb, "ones128"], writes=[T_ps])
        mm_F(sch, ps_sq, ones128[:], ysq[c % 2][:, 0:N], start=(c == 0), stop=(c == 7),
             reads=[tq, "ones128"], writes=[T_ps])

    sch.op("vector", lambda v: v.tensor_scalar(r_mean, ps_sum, 1.0 / D, None, op0=OP.mult),
           reads=[T_ps], writes=[("ln", "mean")])
    sch.op("vector", lambda v: v.tensor_tensor(r_m2, r_mean, r_mean, op=OP.mult),
           reads=[("ln", "mean")], writes=[("ln", "m2")])
    sch.op("vector", lambda v: v.scalar_tensor_tensor(r_var, ps_sq, 1.0 / D, r_m2,
                                                      op0=OP.mult, op1=OP.subtract),
           reads=[T_ps, ("ln", "m2")], writes=[("ln", "var")])
    sch.op("scalar", lambda s: s.activation(out=r_var, in_=r_var, func=AF.Ln, bias=b_eps),
           reads=[("ln", "var"), ("cst",)], writes=[("ln", "var")])
    sch.op("scalar", lambda s: s.activation(out=r_var, in_=r_var, func=AF.Exp, scale=s_half),
           reads=[("ln", "var")], writes=[("ln", "var")])

    mm_F(sch, pb_m, env["ones1f"][:], r_mean, start=True, stop=True,
         reads=[("ln", "mean"), "ones1"], writes=[T_pb])
    mm_F(sch, pb_r, env["ones1f"][:], r_var, start=True, stop=True,
         reads=[("ln", "var"), "ones1"], writes=[T_pb])

    fins = {}
    for c in range(8):
        tt = ("ln_tnorm",)
        sch.op("vector", lambda v, c=c: v.tensor_tensor(tnorm[:, 0:N], y_chunks[c], pb_m, op=OP.subtract),
               reads=[y_toks[c], T_pb], writes=[tt])
        if out_f32 is not None:
            dst, dtok = out_f32[c], out_f32_toks[c]
            sch.op("vector", lambda v, dst=dst: v.tensor_tensor(dst, tnorm[:, 0:N], pb_r, op=OP.mult),
                   reads=[tt, T_pb], writes=[dtok])
            src, stok = dst, dtok
        else:
            sch.op("vector", lambda v: v.tensor_tensor(tnorm[:, 0:N], tnorm[:, 0:N], pb_r, op=OP.mult),
                   reads=[tt, T_pb], writes=[tt])
            src, stok = tnorm[:, 0:N], tt
        fins[c] = stok
        if out_bf16 is not None:
            sch.op("vector", lambda v, c=c, src=src: v.tensor_copy(out_bf16[c], src),
                   reads=[stok], writes=[out_bf16_toks[c]])
        if out_dma is not None:
            dram_ap, dsem = out_dma(c)
            sch.dma("sync", lambda sy, src=src, dram_ap=dram_ap: sy.dma_start(out=dram_ap, in_=src),
                    sem=dsem, reads=[stok], writes=[(uid, "xout")])
    return fins



def emit_l0_ffn(nc, sch, es, env, xT, wg_blk, wu_blk, wd_blk, outT, st_sem,
                uid="f0", out_bf16=None, out_bf16_toks=None,
                x_bf=None, x_bf_toks=None, x_dep=(), pump=None):
    HID = 3072
    NHS = HID // 128
    NH = S // 2
    t = lambda name, shape, dt: es.enter_context(nc.sbuf_tensor(name, shape, dt))
    if x_bf is None:
        x_st = DBuf(nc, es, sch, f"{uid}_xbf", [128, S], BF16, nbuf=8, shared_sem=True)
        xbf = x_st.tiles
        xbf_toks = [x_st.tok(c) for c in range(8)]
        for c in range(8):
            x_st.load(c, xT[128 * c:128 * (c + 1), :], reads=x_dep)
    else:
        xbf, xbf_toks = x_bf, x_bf_toks
    H = [t(f"{uid}_H{i}", [128, NH], BF16) for i in range(NHS)]
    sg = [t(f"{uid}_sg{i}", [128, NH // 2], F32) for i in range(2)]
    y = [t(f"{uid}_y{c}", [128, NH], F32) for c in range(8)]
    wg_st = DBuf(nc, es, sch, f"{uid}_wg", [128, 8 * 128], BF16)
    wu_st = DBuf(nc, es, sch, f"{uid}_wu", [128, 8 * 128], BF16)
    wd_st = DBuf(nc, es, sch, f"{uid}_wd", [128, NHS * 64], BF16)
    xf_st = DBuf(nc, es, sch, f"{uid}_xf", [128, NH], F32, queue="sync")

    for half in range(2):
        hlo = half * NH
        with contextlib.ExitStack() as pes:
            p = lambda name, shape: pes.enter_context(nc.psum_tensor(name, shape, F32))
            ps_g = [p(f"{uid}_ps_g{half}_{i}", [128, NH // 2]) for i in range(2)]
            ps_u = [p(f"{uid}_ps_u{half}_{i}", [128, NH // 2]) for i in range(2)]

            def load_gu(hs):
                wg_st.load(hs, wg_blk[hs].rearrange("p c h -> p (c h)"))
                wu_st.load(hs, wu_blk[hs].rearrange("p c h -> p (c h)"))

            load_gu(0)
            load_gu(1)
            for hs in range(NHS):
                if pump:
                    pump()
                b = hs % 2
                wg_b, wu_b = wg_st.tiles[b], wu_st.tiles[b]
                for nh in range(2):
                    nlo = hlo + nh * (NH // 2)
                    for c in range(8):
                        mm_F(sch, ps_g[nh][:], wg_b[:, 128 * c:128 * (c + 1)],
                             xbf[c][:, nlo:nlo + NH // 2], start=(c == 0), stop=(c == 7),
                             reads=[wg_st.tok(hs), xbf_toks[c]],
                             writes=[(uid, "ps_g", half, nh)])
                    for c in range(8):
                        mm_F(sch, ps_u[nh][:], wu_b[:, 128 * c:128 * (c + 1)],
                             xbf[c][:, nlo:nlo + NH // 2], start=(c == 0), stop=(c == 7),
                             reads=[wu_st.tok(hs), xbf_toks[c]],
                             writes=[(uid, "ps_u", half, nh)])
                    s127 = env["const"](1.0 / 127.0)
                    sch.op("scalar", lambda s_, nh=nh, s127=s127: s_.activation(
                        out=sg[nh][:], in_=ps_g[nh][:], func=AF.Silu, scale=s127),
                        reads=[(uid, "ps_g", half, nh), ("cst",)],
                        writes=[(uid, "sg", nh)])
                    sch.op("vector", lambda v_, hs=hs, nh=nh: v_.scalar_tensor_tensor(
                        H[hs][:, nh * (NH // 2):(nh + 1) * (NH // 2)], ps_u[nh][:],
                        1.0 / 127.0, sg[nh][:], op0=OP.mult, op1=OP.mult),
                        reads=[(uid, "ps_u", half, nh), (uid, "sg", nh)],
                        writes=[(uid, "H", hs)])
                if hs + 2 < NHS:
                    load_gu(hs + 2)
            sch.barrier()

        with contextlib.ExitStack() as pes:
            p = lambda name, shape: pes.enter_context(nc.psum_tensor(name, shape, F32))
            ps_d = [p(f"{uid}_ps_d{half}_{i}", [128, NH]) for i in range(2)]

            HH = NHS // 2

            def load_dh(idx):
                os, hh = idx // 2, idx % 2
                wd_st.load(idx, wd_blk[os][:, HH * hh:HH * (hh + 1), :]
                           .rearrange("p c h -> p (c h)"))

            def load_x(os):
                xf_st.load(os, xT[128 * os:128 * (os + 1), hlo:hlo + NH], reads=x_dep)

            load_dh(0)
            load_dh(1)
            load_x(0)
            load_x(1)
            for os in range(8):
                b = os % 2
                xf = xf_st.tiles[b]
                for hc in range(NHS):
                    widx = 2 * os + hc // HH
                    wd_b = wd_st.tiles[widx % 2]
                    mm_F(sch, ps_d[b][:], wd_b[:, 128 * (hc % HH):128 * (hc % HH + 1)],
                         H[hc][:], start=(hc == 0), stop=(hc == NHS - 1),
                         reads=[wd_st.tok(widx), (uid, "H", hc)],
                         writes=[(uid, "ps_d", half, b)])
                    if hc == HH and 2 * os + 2 < 16:
                        load_dh(2 * os + 2)
                sch.op("vector", lambda v_, os=os, b=b, xf=xf: v_.scalar_tensor_tensor(
                    y[os][:], ps_d[b][:], 1.0 / 127.0, xf[:],
                    op0=OP.mult, op1=OP.add),
                    reads=[(uid, "ps_d", half, b), xf_st.tok(os)],
                    writes=[(uid, "y", os)])
                if pump:
                    pump()
                if 2 * os + 3 < 16:
                    load_dh(2 * os + 3)
                if os + 2 < 8:
                    load_x(os + 2)
            sch.barrier()

        with contextlib.ExitStack() as pes:
            ps_rows = pes.enter_context(nc.psum_tensor(f"{uid}_lnr{half}", [1, 2 * NH], F32)).ap()
            pb = pes.enter_context(nc.psum_tensor(f"{uid}_lnb{half}", [128, 2 * NH], F32)).ap()
            ob = [out_bf16[c][:, hlo:hlo + NH] for c in range(8)] if out_bf16 else None
            obt = ([(tok, half) for tok in out_bf16_toks] if out_bf16 else None)
            emit_layernorm(sch, env, [y[c][:] for c in range(8)],
                           [(uid, "y", c) for c in range(8)], ps_rows, pb, uid,
                           out_bf16=ob, out_bf16_toks=obt,
                           out_dma=lambda c: (outT[128 * c:128 * (c + 1), hlo:hlo + NH], st_sem))
            sch.barrier()


def build_l0_ffn_standalone():
    nc = bass.Bass("TRN2", target_bir_lowering=False)
    NHS = 24
    xT = nc.dram_tensor("xT", [D, S], F32, kind="ExternalInput").ap()
    wg_blk = nc.dram_tensor("wg_blk", [NHS, 128, 8, 128], I32, kind="ExternalInput").ap()
    wu_blk = nc.dram_tensor("wu_blk", [NHS, 128, 8, 128], I32, kind="ExternalInput").ap()
    wd_blk = nc.dram_tensor("wd_blk", [8, 128, NHS, 128], I32, kind="ExternalInput").ap()
    outT = nc.dram_tensor("outT", [D, S], F32, kind="ExternalOutput").ap()
    es = contextlib.ExitStack()
    with es:
        sch = Sched(nc, es)
        env = make_env(nc, es, sch)
        st_sem = es.enter_context(nc.semaphore("sem_st"))
        emit_l0_ffn(nc, sch, es, env, xT, wg_blk, wu_blk, wd_blk, outT, st_sem)
        sch._emit("sync", lambda e: e.nop(), [(st_sem, sch.count(st_sem))], None, 1)
        sch.barrier()
    return nc


def prep_l0_ffn_inputs(x_bt, w_gate, w_up, w_down):
    xT = np.ascontiguousarray(x_bt.T)
    wg_blk = np.ascontiguousarray(
        w_gate.T.reshape(8, 128, 24, 128).transpose(2, 1, 0, 3))
    wu_blk = np.ascontiguousarray(
        w_up.T.reshape(8, 128, 24, 128).transpose(2, 1, 0, 3))
    wd_blk = np.ascontiguousarray(
        w_down.T.reshape(24, 128, 8, 128).transpose(2, 1, 0, 3))
    return {"xT": xT, "wg_blk": wg_blk, "wu_blk": wu_blk, "wd_blk": wd_blk}



def emit_l1_ffn(nc, sch, es, env, xT, w1_blk, w2_blk, h1_blk, h2_blk, w3_blk,
                scl, outT, st_sem, uid="f1", out_bf16=None, out_bf16_toks=None,
                x_dep=(), pump=None):
    DH = 4096
    NHS = DH // 128
    NH = S // 2
    NQ = NH // 2
    t = lambda name, shape, dt: es.enter_context(nc.sbuf_tensor(name, shape, dt))
    A = [t(f"{uid}_A{i}", [128, NH], BF16) for i in range(NHS)]
    B = [t(f"{uid}_B{i}", [128, NH], BF16) for i in range(NHS)]
    sg = [t(f"{uid}_sg{i}", [128, NH // 2], F32) for i in range(2)]
    scl_sb = t(f"{uid}_scl", [128, 4 * NHS + 8], F32)
    s1_sb = scl_sb[:, 0:NHS]
    s2_sb = scl_sb[:, NHS:2 * NHS]
    h1s_sb = scl_sb[:, 2 * NHS:3 * NHS]
    h2s_sb = scl_sb[:, 3 * NHS:4 * NHS]
    w3s_sb = scl_sb[:, 4 * NHS:4 * NHS + 8]
    x_st = DBuf(nc, es, sch, f"{uid}_xbf", [128, NH], BF16, nbuf=8, shared_sem=True)
    y = [B[c].ap().bitcast(F32)[:, 0:NQ] for c in range(8)]
    w1_st = DBuf(nc, es, sch, f"{uid}_w1", [128, 8 * 128], BF16)
    w2_st = DBuf(nc, es, sch, f"{uid}_w2", [128, 8 * 128], BF16)
    hw_st = DBuf(nc, es, sch, f"{uid}_hw", [128, NHS * 128], BF16)
    xf_st = DBuf(nc, es, sch, f"{uid}_xf", [128, NQ], F32, queue="sync")
    ssem = sch.new_dma_sem(f"{uid}_ssem")
    sch.dma("sync", lambda sy: sy.dma_start(out=scl_sb[:], in_=scl),
            sem=ssem, writes=[(uid, "scl")])

    f_pes = contextlib.ExitStack()
    psA = [f_pes.enter_context(nc.psum_tensor(f"{uid}_psA{i}", [128, NH // 2], F32))
           for i in range(4)]
    psB = [f_pes.enter_context(nc.psum_tensor(f"{uid}_psB{i}", [128, NH], F32))
           for i in range(2)]
    TPA = [(uid, "psA", i) for i in range(4)]
    TPB = [(uid, "psB", i) for i in range(2)]
    hw_idx = [0]

    def hw_load(blk_row):
        i = hw_idx[0]
        hw_idx[0] += 1
        hw_st.load(i, blk_row.rearrange("p c h -> p (c h)"))
        return hw_st.tiles[i % 2], hw_st.tok(i)

    for half in range(2):
        hlo = half * NH
        for c in range(8):
            x_st.load(c + 8 * half, xT[128 * c:128 * (c + 1), hlo:hlo + NH], reads=x_dep)
        xbf = x_st.tiles
        xbf_toks = [x_st.tok(c) for c in range(8)]

        if True:
            ps_g, ps_u = psA[0:2], psA[2:4]

            def load_gu(hs):
                w1_st.load(hs, w1_blk[hs].rearrange("p c h -> p (c h)"))
                w2_st.load(hs, w2_blk[hs].rearrange("p c h -> p (c h)"))

            load_gu(0)
            load_gu(1)
            for hs in range(NHS):
                if pump:
                    pump()
                b = hs % 2
                for nh in range(2):
                    nlo = nh * (NH // 2)
                    for c in range(8):
                        mm_F(sch, ps_g[nh][:], w1_st.tiles[b][:, 128 * c:128 * (c + 1)],
                             xbf[c][:, nlo:nlo + NH // 2], start=(c == 0), stop=(c == 7),
                             reads=[w1_st.tok(hs), xbf_toks[c]],
                             writes=[TPA[nh]])
                    for c in range(8):
                        mm_F(sch, ps_u[nh][:], w2_st.tiles[b][:, 128 * c:128 * (c + 1)],
                             xbf[c][:, nlo:nlo + NH // 2], start=(c == 0), stop=(c == 7),
                             reads=[w2_st.tok(hs), xbf_toks[c]],
                             writes=[TPA[2 + nh]])
                    sch.op("scalar", lambda s_, nh=nh, hs=hs: s_.activation(
                        out=sg[nh][:], in_=ps_g[nh][:], func=AF.Silu,
                        scale=s1_sb[:, hs:hs + 1]),
                        reads=[TPA[nh], (uid, "scl")],
                        writes=[(uid, "sg", nh)])
                    sch.op("vector", lambda v_, hs=hs, nh=nh, nlo=nlo: v_.scalar_tensor_tensor(
                        A[hs][:, nlo:nlo + NH // 2], ps_u[nh][:],
                        s2_sb[:, hs:hs + 1], sg[nh][:], op0=OP.mult, op1=OP.mult),
                        reads=[TPA[2 + nh], (uid, "sg", nh), (uid, "scl")],
                        writes=[(uid, "A", hs)])
                if hs + 2 < NHS:
                    load_gu(hs + 2)

        for phase, (w_blk, s_sb, s_name, src, src_tokname, dst, dst_tokname) in enumerate(
                ((h1_blk, h1s_sb, "h1s", A, "A", B, "B"),
                 (h2_blk, h2s_sb, "h2s", B, "B", A, "A"))):
            if True:
                ps_b = psB
                tiles = {}
                tiles[0] = hw_load(w_blk[0])
                tiles[1] = hw_load(w_blk[1])
                for hs in range(NHS):
                    wtile, wtok = tiles[hs]
                    b = hs % 2
                    for c in range(NHS):
                        mm_F(sch, ps_b[b][:], wtile[:, 128 * c:128 * (c + 1)],
                             src[c][:], start=(c == 0), stop=(c == NHS - 1),
                             reads=[wtok, (uid, src_tokname, c)],
                             writes=[TPB[b]])
                    wl = [(uid, dst_tokname, hs)]
                    if dst_tokname == "B" and hs < 8:
                        wl.append((uid, "y", hs))
                    sch.op("scalar", lambda s_, hs=hs, b=b, dst=dst, s_sb=s_sb: s_.activation(
                        out=dst[hs][:], in_=ps_b[b][:], func=AF.Gelu,
                        scale=s_sb[:, hs:hs + 1]),
                        reads=[TPB[b], (uid, "scl", s_name)],
                        writes=wl)
                    if hs + 2 < NHS:
                        tiles[hs + 2] = hw_load(w_blk[hs + 2])

        for q in range(2):
            qlo = hlo + q * NQ
            if True:
                ps_d = psA[0:2]

                def load_d(os):
                    wtile, wtok = hw_load(w3_blk[os])
                    xf_st.load(os, xT[128 * os:128 * (os + 1), qlo:qlo + NQ], reads=x_dep)
                    return wtile, wtok

                dtiles = {0: load_d(0), 1: load_d(1)}
                for os in range(8):
                    wtile, wtok = dtiles[os]
                    b = os % 2
                    for hc in range(NHS):
                        mm_F(sch, ps_d[b][:], wtile[:, 128 * hc:128 * (hc + 1)],
                             A[hc][:, q * NQ:(q + 1) * NQ], start=(hc == 0),
                             stop=(hc == NHS - 1),
                             reads=[wtok, (uid, "A", hc)],
                             writes=[TPA[b]])
                    sch.op("vector", lambda v_, os=os, b=b: v_.scalar_tensor_tensor(
                        y[os], ps_d[b][:], w3s_sb[:, os:os + 1], xf_st.tiles[b][:],
                        op0=OP.mult, op1=OP.add),
                        reads=[TPA[b], xf_st.tok(os), (uid, "scl")],
                        writes=[(uid, "y", os), (uid, "B", os)])
                    if os + 2 < 8:
                        dtiles[os + 2] = load_d(os + 2)
            if True:
                ps_rows = psB[1].ap()[0:1, 0:2 * NQ]
                pb = psB[0].ap()[:, 0:2 * NQ]
                ob = [out_bf16[c][:, qlo:qlo + NQ] for c in range(8)] if out_bf16 else None
                obt = ([(tok, half, q) for tok in out_bf16_toks] if out_bf16 else None)
                emit_layernorm(sch, env, [y[c] for c in range(8)],
                               [(uid, "y", c) for c in range(8)], ps_rows, pb, uid,
                               out_bf16=ob, out_bf16_toks=obt,
                               out_dma=lambda c: (outT[128 * c:128 * (c + 1), qlo:qlo + NQ], st_sem),
                               ps_tok=TPB[1], pb_tok=TPB[0])
    f_pes.close()


def build_l1_ffn_standalone():
    nc = bass.Bass("TRN2", target_bir_lowering=False)
    NHS = 32
    xT = nc.dram_tensor("xT", [D, S], F32, kind="ExternalInput").ap()
    w1_blk = nc.dram_tensor("w1_blk", [NHS, 128, 8, 128], F32, kind="ExternalInput").ap()
    w2_blk = nc.dram_tensor("w2_blk", [NHS, 128, 8, 128], F32, kind="ExternalInput").ap()
    h1_blk = nc.dram_tensor("h1_blk", [NHS, 128, NHS, 128], F32, kind="ExternalInput").ap()
    h2_blk = nc.dram_tensor("h2_blk", [NHS, 128, NHS, 128], F32, kind="ExternalInput").ap()
    w3_blk = nc.dram_tensor("w3_blk", [8, 128, NHS, 128], F32, kind="ExternalInput").ap()
    scl = nc.dram_tensor("scl", [128, 4 * NHS + 8], F32, kind="ExternalInput").ap()[:, :]
    outT = nc.dram_tensor("outT", [D, S], F32, kind="ExternalOutput").ap()
    es = contextlib.ExitStack()
    with es:
        sch = Sched(nc, es)
        env = make_env(nc, es, sch)
        st_sem = es.enter_context(nc.semaphore("sem_st"))
        emit_l1_ffn(nc, sch, es, env, xT, w1_blk, w2_blk, h1_blk, h2_blk, w3_blk,
                    scl, outT, st_sem)
        sch._emit("sync", lambda e: e.nop(), [(st_sem, sch.count(st_sem))], None, 1)
        sch.barrier()
    return nc


def prep_l1_ffn_inputs(x_bt, w1w, w1s, w2w, w2s, h1w, h1s, h2w, h2s, w3w, w3s):
    xT = np.ascontiguousarray(x_bt.T)
    blk = lambda w, nin, nout: np.ascontiguousarray(
        w.T.reshape(nin // 128, 128, nout // 128, 128).transpose(2, 1, 0, 3))
    fm = lambda s: np.ascontiguousarray(s.reshape(-1, 128).T)
    scl = np.concatenate([fm(w1s), fm(w2s), fm(h1s), fm(h2s), fm(w3s)], axis=1)
    return {"xT": xT,
            "w1_blk": blk(w1w, 1024, 4096), "w2_blk": blk(w2w, 1024, 4096),
            "h1_blk": blk(h1w, 4096, 4096), "h2_blk": blk(h2w, 4096, 4096),
            "w3_blk": blk(w3w, 4096, 1024),
            "scl": np.ascontiguousarray(scl)}



def emit_l0_attn(nc, sch, es, env, xT, wq_blk, wk_blk, wv_blk, wo_blk,
                 biases, lqk, hnw, cs_tab_h, ss_tab_h, perm_h, ut_h, iden_h,
                 v_scr, outT, st_sem, uid="a0", out_bf16=None, out_bf16_toks=None,
                 debug_afm=False, pump=None):
    t = lambda name, shape, dt: es.enter_context(nc.sbuf_tensor(name, shape, dt))
    NHEAD = 16
    cs_tab = t(f"{uid}_cs", [128, S], F32)
    ss_tab = t(f"{uid}_ss", [128, S], F32)
    perm = t(f"{uid}_perm", [128, 128], F32)
    ut = t(f"{uid}_ut", [128, 128], BF16)
    iden = t(f"{uid}_iden", [128, 128], BF16)
    bias_sb = t(f"{uid}_bias", [128, 48], F32)
    hnw_sb = t(f"{uid}_hnw", [64, 16], F32)
    hnw08 = t(f"{uid}_hnw08", [64, 16], F32)
    lqk_sb = t(f"{uid}_lqk", [64, 64], F32)
    lqk_prod = t(f"{uid}_lqkp", [64, 32], BF16)
    lamt = t(f"{uid}_lam", [1, 64], F32)
    attn_fm = [t(f"{uid}_afm{c}", [128, S], BF16) for c in range(8)]
    aug = [t(f"{uid}_aug{i}", [128, S], BF16) for i in range(8)]
    y = [aug[c].ap().bitcast(F32) for c in range(8)]
    qpre = [t(f"{uid}_qpre{i}", [128, 1024], F32) for i in range(2)]
    tcs = t(f"{uid}_tcs", [128, 1024], F32)
    uss = t(f"{uid}_uss", [128, 1024], F32)
    vstage = t(f"{uid}_vstage", [128, 1024], BF16)
    sqt = t(f"{uid}_sqt", [64, 1024], BF16)
    E = [t(f"{uid}_E{i}", [128, 1024], BF16) for i in range(2)]
    bsq = E[0][0:64, 0:1024]
    o12 = [t(f"{uid}_o{i}", [64, 1024], BF16) for i in range(2)]
    comb = t(f"{uid}_comb", [64, 1024], F32)
    combt = t(f"{uid}_combt", [64, 1024], F32)

    rows = t(f"{uid}_rows", [1, 1032], F32)

    rowsb = t(f"{uid}_rowsb", [1, 2048], F32)
    rbh = t(f"{uid}_rbh", [1, 2048], BF16)
    sc16 = t(f"{uid}_sc16", [1, 16], F32)
    fint = comb.ap().bitcast(BF16)[0:64, 0:1024]
    v_aug = [t(f"{uid}_vaug{i}", [128, 16 * 80], BF16) for i in range(2)]
    wq_st = DBuf(nc, es, sch, f"{uid}_wq", [128, 8 * 128], BF16)
    wk_st = DBuf(nc, es, sch, f"{uid}_wk", [128, 8 * 128], BF16)
    wv_st = DBuf(nc, es, sch, f"{uid}_wv", [128, 8 * 128], BF16)
    x_st = DBuf(nc, es, sch, f"{uid}_xbf", [128, S], BF16, nbuf=8, shared_sem=True)

    vssem = sch.new_dma_sem(f"{uid}_vs")
    vtsem = [sch.new_dma_sem(f"{uid}_vt{i}") for i in range(2)]

    SQ8 = env["const"](0.125)

    for c in range(8):
        x_st.load(c, xT[128 * c:128 * (c + 1), :])
    xbf, xbf_toks = x_st.tiles, [x_st.tok(c) for c in range(8)]
    for i, (dst, srca, tok, q) in enumerate((
            (cs_tab, cs_tab_h, "cs", "sync"), (ss_tab, ss_tab_h, "ss", "sync"),
            (perm, perm_h, "perm", "sync"), (ut, ut_h, "ut", "gpsimd"),
            (iden, iden_h, "iden", "gpsimd"), (bias_sb, biases, "bias", "sync"),
            (hnw_sb, hnw, "hnw", "sync"), (lqk_sb, lqk, "lqk", "sync"))):
        csem_i = sch.new_dma_sem(f"{uid}_cs{i}")
        sch.dma(q, lambda e, dst=dst, srca=srca: e.dma_start(out=dst[:], in_=srca),
                sem=csem_i, writes=[(uid, tok)])
    sch.op("vector", lambda v: v.tensor_scalar(hnw08[:], hnw_sb[:], 0.8, None, op0=OP.mult),
           reads=[(uid, "hnw")], writes=[(uid, "hnw08")])
    sch.op("vector", lambda v: v.tensor_tensor(lqk_prod[:, 0:16], lqk_sb[:, 0:16],
                                               lqk_sb[:, 16:32], op=OP.mult),
           reads=[(uid, "lqk")], writes=[(uid, "p12")])
    sch.op("vector", lambda v: v.tensor_tensor(lqk_prod[:, 16:32], lqk_sb[:, 32:48],
                                               lqk_sb[:, 48:64], op=OP.mult),
           reads=[(uid, "lqk")], writes=[(uid, "p12")])
    with contextlib.ExitStack() as pes0:
        ps0 = pes0.enter_context(nc.psum_tensor(f"{uid}_ps0", [1, 32], F32))
        mm_F(sch, ps0[:], env["ones128"][0:64, :], lqk_prod[:],
             start=True, stop=True, reads=[(uid, "p12"), "ones128"],
             writes=[(uid, "ps0")])
        sch.op("scalar", lambda s: s.activation(out=lamt[:, 32:64], in_=ps0[:], func=AF.Exp),
               reads=[(uid, "ps0")], writes=[(uid, "lame")])
    sch.op("vector", lambda v: v.tensor_tensor(lamt[:, 0:16], lamt[:, 32:48],
                                               lamt[:, 48:64], op=OP.subtract),
           reads=[(uid, "lame")], writes=[(uid, "lamf")])
    sch.op("vector", lambda v: v.tensor_scalar(lamt[:, 0:16], lamt[:, 0:16], 0.2, None, op0=OP.add),
           reads=[(uid, "lamf")], writes=[(uid, "lamf")])
    lam_row = lamt[:, 0:16]

    for j in range(8):
        sch.op("vector", lambda v, j=j: v.memset(aug[j][64:128, :], 0.0),
               writes=[(uid, "augpad"), (uid, "aug_ones", j)])
        if j % 4 >= 2:
            sch.op("vector", lambda v, j=j: v.memset(aug[j][64:65, :], 1.0),
                   writes=[(uid, "aug_ones", j)], reads=[(uid, "augpad")])
    sch.barrier()

    g_pes = contextlib.ExitStack()
    psA = g_pes.enter_context(nc.psum_tensor(f"{uid}_psA", [128, 1024], F32))
    psB = g_pes.enter_context(nc.psum_tensor(f"{uid}_psB", [128, 1024], F32))
    psC = g_pes.enter_context(nc.psum_tensor(f"{uid}_psC", [128, S], F32))
    TA, TB, TC = (uid, "psA"), (uid, "psB"), (uid, "psC")
    TOKP = [TA, TB]
    DEFER_EPI = 1
    pending = [None]
    pend_shift = [None]

    def epi(h, qh, defer_shift=False):
        sch.op("vector", lambda v: v.tensor_copy(o12[0][:], p1[0:64, :]),
               reads=[TC], writes=[(uid, "o1")])
        sch.op("vector", lambda v: v.tensor_copy(o12[1][:], p2[0:64, :]),
               reads=[TC], writes=[(uid, "o2")])
        sch.op("scalar", lambda s_: s_.activation(out=rowsb[:, 0:2048], in_=psC[64:65, 0:2048], func=AF.Ln),
               reads=[TC], writes=[(uid, "rowB")])
        sch.op("scalar", lambda s_: s_.activation(out=rbh[:, 0:2048], in_=rowsb[:, 0:2048],
                                                  func=AF.Exp, scale=-1.0),
               reads=[(uid, "rowB"), ("cst",)], writes=[(uid, "rowB")])
        sch.op("vector", lambda v, h=h: v.tensor_scalar(
            rbh[:, 1024:2048], rbh[:, 1024:2048], lam_row[:, h:h + 1], None, op0=OP.mult),
            reads=[(uid, "rowB"), (uid, "lamf")], writes=[(uid, "rowB")])
        mm_F(sch, psC[0:64, 0:1024], env["ones1"][:, 0:64], rbh[:, 0:1024],
             start=True, stop=True, reads=[(uid, "rowB"), "ones1"],
             writes=[TC])
        mm_F(sch, psC[0:64, 1024:2048], env["ones1"][:, 0:64], rbh[:, 1024:2048],
             start=True, stop=True, reads=[(uid, "rowB"), "ones1"],
             writes=[TC])
        sch.op("vector", lambda v: v.tensor_tensor(comb[:], o12[0][:], psC[0:64, 0:1024], op=OP.mult),
               reads=[(uid, "o1"), TC], writes=[(uid, "comb")])
        sch.op("vector", lambda v: v.tensor_tensor(combt[:], o12[1][:], psC[0:64, 1024:2048], op=OP.mult),
               reads=[(uid, "o2"), TC], writes=[(uid, "combt")])
        sch.op("vector", lambda v: v.tensor_tensor(comb[:], comb[:], combt[:], op=OP.subtract),
               reads=[(uid, "combt"), (uid, "comb")], writes=[(uid, "comb")])
        sch.op("scalar", lambda s_: s_.activation(out=sqt[:, :], in_=comb[:], func=AF.Square),
               reads=[(uid, "comb")], writes=[(uid, "sqt")])
        mm_F(sch, psC[0:1, 0:1024], env["ones128"][0:64, :], sqt[:, :],
             start=True, stop=True, reads=[(uid, "sqt"), "ones128"],
             writes=[TC])
        c164 = env["const"](1.0 / 64.0)
        c1em6 = env["const"](1e-6)
        sch.op("scalar", lambda s_, c164=c164, c1em6=c1em6: s_.activation(
            out=rows[:, 0:1024], in_=psC[0:1, 0:1024], func=AF.Ln, scale=c164, bias=c1em6),
            reads=[TC, ("cst",)], writes=[(uid, "rowA")])
        s_half = env["const"](-0.5)
        sch.op("scalar", lambda s_, s_half=s_half: s_.activation(
            out=rbh[:, 0:1024], in_=rows[:, 0:1024], func=AF.Exp, scale=s_half),
            reads=[(uid, "rowA")], writes=[(uid, "rowB")])
        mm_F(sch, psC[0:64, 1024:2048], env["ones1"][:, 0:64], rbh[:, 0:1024],
             start=True, stop=True, reads=[(uid, "rowB"), "ones1"],
             writes=[TC])
        sch.op("vector", lambda v: v.tensor_tensor(combt[:], comb[:], psC[0:64, 1024:2048], op=OP.mult),
               reads=[(uid, "comb"), TC], writes=[(uid, "combt")])
        cdst = attn_fm[h // 2]
        if h % 2 == 0:
            sch.op("vector", lambda v, cdst=cdst, h=h, qh=qh: v.tensor_scalar(
                cdst[0:64, 1024 * qh:1024 * (qh + 1)], combt[:], hnw08[:, h:h + 1], None, op0=OP.mult),
                reads=[(uid, "combt"), (uid, "hnw08")], writes=[(uid, "afm", h // 2)])
            return None
        sch.op("vector", lambda v, h=h: v.tensor_scalar(
            fint, combt[:], hnw08[:, h:h + 1], None, op0=OP.mult),
            reads=[(uid, "combt"), (uid, "hnw08")], writes=[(uid, "fint")])

        def shift():
            mm_F(sch, ps_s[1][64:128, :], iden[0:64, 0:64], fint,
                 start=True, stop=True, reads=[(uid, "fint"), (uid, "iden")],
                 writes=[TB], tile_position=(0, 64))
            sch.op("vector", lambda v, cdst=cdst, qh=qh: v.tensor_copy(
                cdst[64:128, 1024 * qh:1024 * (qh + 1)], ps_s[1][64:128, :]),
                reads=[TB], writes=[(uid, "afm", h // 2)])
        if defer_shift:
            return shift
        shift()
        return None

    for g in range(8):
        if pending[0] is not None:
            ep, pending[0] = pending[0], None
            epi(*ep)
        if pump:
            pump()
        heads = (2 * g, 2 * g + 1)
        if True:
            ps_p = [psA, psB]
            ps_pm = psC

            def proj_rope(st, blk, si, bias_col, scale, dst1, dst1_tok, dst2,
                          dst2_tok, widx):
                wtile = st.load(widx, blk[si].rearrange("p c h -> p (c h)"))
                for nh in range(2):
                    slot = nh
                    qp, qtok = qpre[slot], (uid, "qpre", slot)
                    for c in range(8):
                        mm_F(sch, ps_p[slot][:], wtile[:, 128 * c:128 * (c + 1)],
                             xbf[c][:, 1024 * nh:1024 * (nh + 1)],
                             start=(c == 0), stop=(c == 7),
                             reads=[st.tok(widx), xbf_toks[c]],
                             writes=[TOKP[slot]])
                    sch.op("vector", lambda v_, slot=slot, qp=qp, bias_col=bias_col, scale=scale:
                           v_.tensor_scalar(qp[:], ps_p[slot][:],
                                            bias_col, scale, op0=OP.add, op1=OP.mult),
                           reads=[TOKP[slot], (uid, "bias")],
                           writes=[qtok])
                    cs_sl = cs_tab[:, 1024 * nh:1024 * (nh + 1)]
                    ss_sl = ss_tab[:, 1024 * nh:1024 * (nh + 1)]
                    pm_sl = ps_pm[:, 1024 * nh:1024 * (nh + 1)]
                    mm_F(sch, pm_sl, perm[:], qp[:], start=True, stop=True,
                         reads=[qtok, (uid, "perm")], writes=[TC],
                         abs_off=1024 * nh)
                    sch.op("vector", lambda v, qp=qp, cs_sl=cs_sl: v.tensor_tensor(
                        tcs[:], qp[:], cs_sl, op=OP.mult),
                        reads=[qtok, (uid, "cs")], writes=[(uid, "tcs")])
                    sch.op("vector", lambda v, pm_sl=pm_sl, ss_sl=ss_sl: v.tensor_tensor(
                        uss[:], pm_sl, ss_sl, op=OP.mult),
                        reads=[TC, (uid, "ss")], writes=[(uid, "uss")])
                    sch.op("vector", lambda v, dst1=dst1, nh=nh: v.tensor_tensor(
                        dst1[0:64, 1024 * nh:1024 * (nh + 1)], tcs[0:64, :], uss[0:64, :], op=OP.add),
                        reads=[(uid, "tcs"), (uid, "uss")], writes=[dst1_tok])
                    sch.op("vector", lambda v, dst2=dst2, nh=nh: v.tensor_tensor(
                        dst2[0:64, 1024 * nh:1024 * (nh + 1)], tcs[64:128, :], uss[64:128, :], op=OP.add),
                        reads=[(uid, "tcs"), (uid, "uss")], writes=[dst2_tok])

            for j, h in enumerate(heads):
                proj_rope(wq_st, wq_blk, h, bias_sb[:, h:h + 1], SQ8,
                          aug[4 * j + 0], (uid, "q1a", j), aug[4 * j + 1], (uid, "q2a", j), h)
                proj_rope(wk_st, wk_blk, h, bias_sb[:, 16 + h:17 + h], 1.0,
                          aug[4 * j + 2], (uid, "k1a", j), aug[4 * j + 3], (uid, "k2a", j), h)
            vtile = wv_st.load(g, wv_blk[g].rearrange("p c h -> p (c h)"))
            for nh in range(2):
                slot = nh
                for c in range(8):
                    mm_F(sch, ps_p[slot][:], vtile[:, 128 * c:128 * (c + 1)],
                         xbf[c][:, 1024 * nh:1024 * (nh + 1)], start=(c == 0), stop=(c == 7),
                         reads=[wv_st.tok(g), xbf_toks[c]], writes=[TOKP[slot]])
                sch.op("vector", lambda v_, slot=slot, g=g: v_.tensor_scalar(
                    vstage[:], ps_p[slot][:],
                    bias_sb[:, 32 + g:33 + g], None, op0=OP.add),
                    reads=[TOKP[slot], (uid, "bias")], writes=[(uid, "vstage")])
                sch.dma("sync", lambda sy, g=g, nh=nh: sy.dma_start(
                    out=v_scr[128 * g:128 * (g + 1), 1024 * nh:1024 * (nh + 1)], in_=vstage[:]),
                    sem=vssem, reads=[(uid, "vstage")], writes=[(uid, "vscr", g)])

        if True:
            ps_s = [psA, psB]
            p1 = psC[0:65, 0:1024]
            p2 = psC[0:65, 1024:2048]
            for j, h in enumerate(heads):
                q1a, q2a, k1a, k2a = aug[4 * j], aug[4 * j + 1], aug[4 * j + 2], aug[4 * j + 3]
                tq1, tq2 = (uid, "q1a", j), (uid, "q2a", j)
                tk1, tk2 = (uid, "k1a", j), (uid, "k2a", j)
                va = v_aug[j]
                ones_cols = va.ap().rearrange("p (k c) -> p k c", c=80)[:, :, 64:65]
                tva = [(uid, "vaug1", j)] + [(uid, "vaugd", j, kc) for kc in range(16)]
                sch.op("vector", lambda v, ones_cols=ones_cols: v.memset(ones_cols, 1.0),
                       writes=[(uid, "vaug1", j)])
                for kc in range(16):
                    sch.dma("sync", lambda sy, va=va, kc=kc, h=h: sy.dma_start(
                        out=va[:, 80 * kc:80 * kc + 64],
                        in_=v_scr[64 * h:64 * h + 64, 128 * kc:128 * (kc + 1)],
                        transpose=True),
                        sem=vtsem[j], reads=[(uid, "vscr", g)], writes=[(uid, "vaugd", j, kc)])
                M2 = {}
                for idx, (ka, tk) in enumerate(((k1a, tk1), (k2a, tk2))):
                    for piece in range(2):
                        sch.op("scalar", lambda s_, ka=ka, piece=piece: s_.activation(
                            out=bsq, in_=ka[0:64, 1024 * piece:1024 * (piece + 1)],
                            func=AF.Square),
                            reads=[tk], writes=[(uid, "E", 0)])
                        mm_F(sch, ps_s[piece][0:1, :], env["ones128"][0:64, :],
                             bsq, start=True, stop=True,
                             reads=[(uid, "E", 0), "ones128"], writes=[TOKP[piece]])
                        sch.op("vector", lambda v, piece=piece, idx=idx: v.tensor_reduce(
                            rows[:, 1024 + 2 * idx + piece:1025 + 2 * idx + piece],
                            ps_s[piece][0:1, :], axis=AX, op=OP.max),
                            reads=[TOKP[piece]], writes=[(uid, "m2r", idx)])
                    sch.op("vector", lambda v, idx=idx: v.tensor_reduce(
                        sc16[:, idx:idx + 1], rows[:, 1024 + 2 * idx:1026 + 2 * idx],
                        axis=AX, op=OP.max),
                        reads=[(uid, "m2r", idx)], writes=[(uid, "sc")])
                    M2[idx] = sc16[:, idx:idx + 1]
                for idx, (qa, tq) in enumerate(((q1a, tq1), (q2a, tq2))):
                    sch.op("scalar", lambda s_, qa=qa: s_.activation(
                        out=bsq, in_=qa[0:64, 0:1024], func=AF.Square),
                        reads=[tq], writes=[(uid, "E", 0)])
                    mm_F(sch, ps_s[idx][0:1, :], env["ones128"][0:64, :], bsq,
                         start=True, stop=True,
                         reads=[(uid, "E", 0), "ones128"], writes=[TOKP[idx]])
                    sch.op("vector", lambda v, idx=idx: v.tensor_reduce(
                        sc16[:, 2 + idx:3 + idx], ps_s[idx][0:1, :], axis=AX, op=OP.max),
                        reads=[TOKP[idx]], writes=[(uid, "sc")])
                sch.op("scalar", lambda s_: s_.activation(
                    out=sc16[:, 4:8], in_=sc16[:, 0:4], func=AF.Ln),
                    reads=[(uid, "sc")], writes=[(uid, "sc")])
                sch.op("vector", lambda v: v.tensor_tensor(
                    sc16[:, 8:10], sc16[:, 4:6], sc16[:, 6:8], op=OP.subtract),
                    reads=[(uid, "sc")], writes=[(uid, "sc")])
                sch.op("vector", lambda v: v.tensor_tensor(
                    sc16[:, 10:12], sc16[:, 4:6], sc16[:, 6:8], op=OP.add),
                    reads=[(uid, "sc")], writes=[(uid, "sc")])
                sch.op("scalar", lambda s_: s_.activation(
                    out=sc16[:, 8:10], in_=sc16[:, 8:10], func=AF.Exp, scale=0.5),
                    reads=[(uid, "sc")], writes=[(uid, "sc")])
                sch.op("scalar", lambda s_: s_.activation(
                    out=sc16[:, 10:12], in_=sc16[:, 10:12], func=AF.Exp, scale=0.5),
                    reads=[(uid, "sc")], writes=[(uid, "sc")])
                sch.op("vector", lambda v: v.tensor_scalar(
                    sc16[:, 8:10], sc16[:, 8:10], -0.5, None, op0=OP.mult),
                    reads=[(uid, "sc")], writes=[(uid, "sc")])
                sch.op("vector", lambda v: v.tensor_scalar(
                    sc16[:, 10:12], sc16[:, 10:12], 0.5, None, op0=OP.mult),
                    reads=[(uid, "sc")], writes=[(uid, "sc")])
                for qh in range(2):
                    qlo_h = 1024 * qh
                    if DEFER_EPI == 2 and pending[0] is not None:
                        ep, pending[0] = pending[0], None
                        epi(*ep)
                    for idx, (qa, tq) in enumerate(((q1a, tq1), (q2a, tq2))):
                        sch.op("scalar", lambda s_, qa=qa, qlo_h=qlo_h: s_.activation(
                            out=bsq, in_=qa[0:64, qlo_h:qlo_h + 1024], func=AF.Square),
                            reads=[tq], writes=[(uid, "E", 0)])
                        mm_F(sch, ps_s[0][0:1, :], env["ones128"][0:64, :], bsq,
                             start=True, stop=True,
                             reads=[(uid, "E", 0), "ones128"], writes=[TA])
                        sch.op("vector", lambda v, idx=idx, qa=qa, qlo_h=qlo_h: v.tensor_scalar(
                            qa[64:65, qlo_h:qlo_h + 1024], ps_s[0][0:1, :],
                            sc16[:, 8 + idx:9 + idx], sc16[:, 10 + idx:11 + idx],
                            op0=OP.mult, op1=OP.subtract),
                            reads=[TA, (uid, "sc"), (uid, "augpad")],
                            writes=[tq])
                    kmax = 8 * (qh + 1)
                    sides = ((q1a, k1a, tq1, tk1, p1, TC),
                             (q2a, k2a, tq2, tk2, p2, TC))

                    def s_mm(kc):
                        qlo = max(qlo_h, 128 * kc)
                        F = qlo_h + 1024 - qlo
                        for idx, (qa, ka, tq, tk, pv, tpv) in enumerate(sides):
                            mm_F(sch, ps_s[idx][:, 0:F], ka[0:128, 128 * kc:128 * (kc + 1)],
                                 qa[0:128, qlo:qlo + F], start=True, stop=True,
                                 reads=[tq, tk, (uid, "augpad")], writes=[TOKP[idx]])

                    s_mm(0)
                    for kc in range(kmax):
                        qlo = max(qlo_h, 128 * kc)
                        F = qlo_h + 1024 - qlo
                        loc = qlo - qlo_h
                        for idx, (qa, ka, tq, tk, pv, tpv) in enumerate(sides):
                            sch.op("scalar", lambda s_, idx=idx, loc=loc, F=F: s_.activation(
                                out=E[idx][:, loc:loc + F], in_=ps_s[idx][:, 0:F], func=AF.Exp),
                                reads=[TOKP[idx]], writes=[(uid, "E", idx)])
                            if 128 * kc >= qlo_h:
                                d0 = 128 * kc - qlo_h
                                sch.op("vector", lambda v, idx=idx, d0=d0: v.tensor_tensor(
                                    E[idx][:, d0:d0 + 128], E[idx][:, d0:d0 + 128], ut[:], op=OP.mult),
                                    reads=[(uid, "E", idx), (uid, "ut")], writes=[(uid, "E", idx)])
                        if kc == 1 and pend_shift[0] is not None:
                            sf, pend_shift[0] = pend_shift[0], None
                            sf()
                        if kc + 1 < kmax:
                            s_mm(kc + 1)
                        if kc == 0 and pending[0] is not None:
                            ep, pending[0] = pending[0], None
                            pend_shift[0] = epi(*ep, defer_shift=True)
                        for idx, (qa, ka, tq, tk, pv, tpv) in enumerate(sides):
                            pos = loc
                            while pos < loc + F:
                                nxt = min(loc + F, pos + 512 - (pos % 512))
                                bank = pos // 512
                                last_kc_of_bank = min(kmax - 1, (512 * (bank + 1) - 1 + 1024 * qh) // 128)
                                sch.op("tensor",
                                       lambda t, pos=pos, nxt=nxt, kc=kc, idx=idx, pv=pv, va=va,
                                       st_=(kc == 0), sp_=(kc == last_kc_of_bank): t.matmul(
                                           pv[:, pos:nxt], va[:, 80 * kc:80 * kc + 65],
                                           E[idx][:, pos:nxt], start=st_, stop=sp_),
                                       reads=[(uid, "E", idx)] + tva, writes=[tpv])
                                pos = nxt
                    pending[0] = (h, qh)
                    if DEFER_EPI == 0:
                        ep, pending[0] = pending[0], None
                        epi(*ep)
    if pending[0] is not None:
        ep, pending[0] = pending[0], None
        epi(*ep)
    g_pes.close()
    sch.barrier()

    if debug_afm:
        dsem = es.enter_context(nc.semaphore(f"{uid}_dbg"))
        for c in range(8):
            sch.op("vector", lambda v, c=c: v.tensor_copy(env["ln_tnorm"][:, 0:1024], attn_fm[c][:, 0:1024]),
                   reads=[(uid, "afm", c)], writes=[("dbg",)])
            sch.dma("sync", lambda sy, c=c: sy.dma_start(out=outT[128 * c:128 * (c + 1), 0:1024], in_=env["ln_tnorm"][:, 0:1024]),
                    sem=dsem, reads=[("dbg",)])
            sch.op("vector", lambda v, c=c: v.tensor_copy(env["ln_tnorm"][:, 0:1024], attn_fm[c][:, 1024:2048]),
                   reads=[(uid, "afm", c)], writes=[("dbg",)])
            sch.dma("sync", lambda sy, c=c: sy.dma_start(out=outT[128 * c:128 * (c + 1), 1024:2048], in_=env["ln_tnorm"][:, 0:1024]),
                    sem=dsem, reads=[("dbg",)])
        sch.barrier()
        return
    qp_bf = qpre[0].ap().bitcast(BF16)
    wo_st = DBuf(nc, es, sch, f"{uid}_wo", None, None,
                 tiles=[qp_bf[:, 0:1024], qp_bf[:, 1024:2048]])
    xf_st = DBuf(nc, es, sch, f"{uid}_xf", None, None, queue="sync",
                 tiles=[tcs.ap(), uss.ap()])
    NH = 1024
    for half in range(2):
        hlo = half * NH
        with contextlib.ExitStack() as pes:
            p = lambda name, shape: pes.enter_context(nc.psum_tensor(name, shape, F32))
            ps_o = [p(f"{uid}_pso{half}_{i}", [128, NH]) for i in range(2)]
            for os in range(8):
                b = os % 2
                wtile = wo_st.load(os + 8 * half, wo_blk[os].rearrange("p c h -> p (c h)"))
                xf_st.load(os + 8 * half, xT[128 * os:128 * (os + 1), hlo:hlo + NH])
                for c in range(8):
                    mm_F(sch, ps_o[b][:], wtile[:, 128 * c:128 * (c + 1)],
                         attn_fm[c][:, hlo:hlo + NH], start=(c == 0), stop=(c == 7),
                         reads=[wo_st.tok(os), (uid, "afm", c)], writes=[(uid, "pso", b)])
                sch.op("vector", lambda v, os=os, b=b: v.scalar_tensor_tensor(
                    y[os][:, 0:NH], ps_o[b][:], bias_sb[:, 40 + os:41 + os],
                    xf_st.tiles[b], op0=OP.add, op1=OP.add),
                    reads=[(uid, "pso", b), xf_st.tok(os), (uid, "bias")],
                    writes=[(uid, "y", os)])
            sch.barrier()
        with contextlib.ExitStack() as pes:
            ps_rows = pes.enter_context(nc.psum_tensor(f"{uid}_lnr{half}", [1, 2 * NH], F32)).ap()
            pb = pes.enter_context(nc.psum_tensor(f"{uid}_lnb{half}", [128, 2 * NH], F32)).ap()
            ob = [out_bf16[c][:, hlo:hlo + NH] for c in range(8)] if out_bf16 else None
            obt = ([(tok, half) for tok in out_bf16_toks] if out_bf16 else None)
            emit_layernorm(sch, env, [y[c][:, 0:NH] for c in range(8)],
                           [(uid, "y", c) for c in range(8)], ps_rows, pb, uid,
                           out_bf16=ob, out_bf16_toks=obt,
                           out_dma=lambda c: (outT[128 * c:128 * (c + 1), hlo:hlo + NH], st_sem))
            sch.barrier()


def build_l0_attn_standalone(debug_afm=False):
    nc = bass.Bass("TRN2", target_bir_lowering=False)
    xT = nc.dram_tensor("xT", [D, S], F32, kind="ExternalInput").ap()
    wq_blk = nc.dram_tensor("wq_blk", [16, 128, 8, 128], F32, kind="ExternalInput").ap()
    wk_blk = nc.dram_tensor("wk_blk", [16, 128, 8, 128], F32, kind="ExternalInput").ap()
    wv_blk = nc.dram_tensor("wv_blk", [8, 128, 8, 128], F32, kind="ExternalInput").ap()
    wo_blk = nc.dram_tensor("wo_blk", [8, 128, 8, 128], F32, kind="ExternalInput").ap()
    biases = nc.dram_tensor("biases", [128, 48], F32, kind="ExternalInput").ap()
    lqk = nc.dram_tensor("lqk", [64, 64], F32, kind="ExternalInput").ap()
    hnw = nc.dram_tensor("hnw", [64, 16], F32, kind="ExternalInput").ap()
    cs_tab = nc.dram_tensor("cs_tab", [128, S], F32, kind="ExternalInput").ap()
    ss_tab = nc.dram_tensor("ss_tab", [128, S], F32, kind="ExternalInput").ap()
    perm = nc.dram_tensor("perm", [128, 128], F32, kind="ExternalInput").ap()
    ut = nc.dram_tensor("ut", [128, 128], F32, kind="ExternalInput").ap()
    iden = nc.dram_tensor("iden", [128, 128], F32, kind="ExternalInput").ap()
    v_scr = nc.dram_tensor("v_scr", [D, S], BF16, kind="Internal").ap()
    outT = nc.dram_tensor("outT", [D, S], F32, kind="ExternalOutput").ap()
    es = contextlib.ExitStack()
    with es:
        sch = Sched(nc, es)
        env = make_env(nc, es, sch)
        st_sem = es.enter_context(nc.semaphore("sem_st"))
        emit_l0_attn(nc, sch, es, env, xT, wq_blk, wk_blk, wv_blk, wo_blk,
                     biases[:, :], lqk[:, :], hnw[:, :], cs_tab[:, :], ss_tab[:, :],
                     perm[:, :], ut[:, :], iden[:, :], v_scr, outT, st_sem,
                     debug_afm=debug_afm)
        sch._emit("sync", lambda e: e.nop(), [(st_sem, sch.count(st_sem))], None, 1)
        sch.barrier()
    return nc


def prep_l0_attn_inputs(x_bt, Wq, bq, Wk, bk, Wv, bv, Wo, bo, hnw, lq1, lk1, lq2, lk2):
    xT = np.ascontiguousarray(x_bt.T)
    blk = lambda w, nout: np.ascontiguousarray(
        w.T.reshape(8, 128, nout // 128, 128).transpose(2, 1, 0, 3))
    fm = lambda b: np.ascontiguousarray(b.reshape(-1, 128).T)
    biases = np.concatenate([fm(bq), fm(bk), fm(bv), fm(bo)], axis=1)
    lqk = np.concatenate([lq1.T, lk1.T, lq2.T, lk2.T], axis=1)
    half = 32
    inv = (1.0 / 10000.0 ** (np.arange(half, dtype=np.float64) / half))
    ang = np.arange(S, dtype=np.float64)[:, None] * inv[None, :]
    cosv = np.cos(ang).astype(np.float32)
    sinv = np.sin(ang).astype(np.float32)
    cs = np.zeros((128, S), np.float32)
    ss = np.zeros((128, S), np.float32)
    for p in range(128):
        j = p % 32
        cs[p] = cosv[:, j]
        ss[p] = sinv[:, j] if (p % 64) >= 32 else -sinv[:, j]
    perm = np.zeros((128, 128), np.float32)
    for i in range(128):
        sig = i + 32 if (i % 64) < 32 else i - 32
        perm[sig, i] = 1.0
    ut = np.triu(np.ones((128, 128), np.float32))
    iden = np.eye(128, dtype=np.float32)
    return {"xT": xT, "wq_blk": blk(Wq, 2048), "wk_blk": blk(Wk, 2048),
            "wv_blk": blk(Wv, 1024), "wo_blk": blk(Wo, 1024),
            "biases": biases, "lqk": lqk, "hnw": np.ascontiguousarray(hnw.T),
            "cs_tab": cs, "ss_tab": ss, "perm": perm, "ut": ut, "iden": iden}



def emit_l1_sw_stage(nc, sch, es, env, swT, sclT, grid_h, sw_scr, uid="a1"):
    t = lambda name, shape, dt: es.enter_context(nc.sbuf_tensor(name, shape, dt))
    grid_sb = t(f"{uid}_sgrid", [1, 2], F32)
    c_col = t(f"{uid}_ccol", [128, 1], F32)
    sw_in = DBuf(nc, es, sch, f"{uid}_swin", [128, 3072], BF16, nbuf=3)
    scl_in = DBuf(nc, es, sch, f"{uid}_sclin", [128, 3072], BF16)
    sw_mid = DBuf(nc, es, sch, f"{uid}_swmid", [128, 3072], BF16)
    gsem = sch.new_dma_sem(f"{uid}_sgsem")
    LN6 = env["const"](float(-np.log(6.0)))

    sch.dma("gpsimd", lambda g: g.dma_start(out=grid_sb[:], in_=grid_h[0:1, 0:2]),
            sem=gsem, writes=[(uid, "sgrid")])
    with nc.psum_tensor(f"{uid}_psc0", [128, 1], F32) as ps_c:
        sch.op("vector", lambda v: v.tensor_tensor(
            c_col[0:1, :], grid_sb[0:1, 1:2], grid_sb[0:1, 0:1], op=OP.subtract),
            reads=[(uid, "sgrid")], writes=[(uid, "ccol")])
        sch.op("scalar", lambda s_: s_.activation(
            out=c_col[0:1, :], in_=c_col[0:1, :], func=AF.Ln),
            reads=[(uid, "ccol")], writes=[(uid, "ccol")])
        sch.op("scalar", lambda s_, LN6=LN6: s_.activation(
            out=c_col[0:1, :], in_=c_col[0:1, :], func=AF.Exp, scale=-3.0, bias=LN6),
            reads=[(uid, "ccol"), ("cst",)], writes=[(uid, "ccol")])
        mm_F(sch, ps_c[:], env["ones1f"][:, 0:128], c_col[0:1, :],
             start=True, stop=True, reads=[(uid, "ccol"), "ones1"],
             writes=[(uid, "psc")])
        sch.op("vector", lambda v: v.tensor_copy(c_col[:], ps_c[:]),
               reads=[(uid, "psc")], writes=[(uid, "ccol")])

    seq = [(ir, b8 * 8 + ir) for ir in range(8) for b8 in range(8)]
    st = {"l": 0, "c": 0}
    DEFER = 2

    def do_load(si):
        ir, cc = seq[si]
        if si % 8 == 0:
            scl_in.load(ir, sclT[128 * ir:128 * (ir + 1), :])
        sw_in.load(si, swT[128 * cc:128 * (cc + 1), :])

    def do_comp(si):
        ir, cc = seq[si]
        if si % 8 == 0:
            sch.op("vector", lambda v, ir=ir: v.tensor_scalar(
                scl_in.tiles[ir % 2][:], scl_in.tiles[ir % 2][:], c_col[:, 0:1],
                None, op0=OP.mult),
                reads=[scl_in.tok(ir), (uid, "ccol")], writes=[scl_in.tok(ir)])
        mb = sw_mid.tiles[si % 2]
        sch.op("vector", lambda v, si=si, mb=mb, ir=ir: v.tensor_tensor(
            mb[:], sw_in.tiles[si % 3][:], scl_in.tiles[ir % 2][:], op=OP.mult),
            reads=[sw_in.tok(si), scl_in.tok(ir)],
            writes=[sw_mid.tok(si)])
        for gg in range(3):
            sch.dma("sync", lambda sy, cc=cc, gg=gg, mb=mb: sy.dma_start(
                out=sw_scr[gg, cc], in_=mb[:, 1024 * gg:1024 * (gg + 1)]),
                sem=sw_mid.sems[si % 2], reads=[sw_mid.tok(si)],
                writes=[(uid, "swscr", gg, cc)])

    def pump():
        if st["l"] < 64:
            do_load(st["l"])
            st["l"] += 1
        if st["c"] < st["l"] and (st["l"] - st["c"] > DEFER or st["l"] == 64):
            do_comp(st["c"])
            st["c"] += 1
        return st["c"] < 64
    return pump


def emit_l1_attn(nc, sch, es, env, xT, bw_blk, swT, sclT, grid_h, RT_h, ow_blk,
                 outb, sw_scr, qk_scr, v_scr, outT, st_sem, uid="a1",
                 out_bf16=None, out_bf16_toks=None, x_dep=()):
    t = lambda name, shape, dt: es.enter_context(nc.sbuf_tensor(name, shape, dt))
    GS = 512
    NT = S // GS
    bases = t(f"{uid}_bases", [128, 64 * GS], BF16)
    bas = lambda cc: bases[:, GS * cc:GS * (cc + 1)]
    grid_sb = t(f"{uid}_grid", [128, 96], F32)
    gsem = sch.new_dma_sem(f"{uid}_gsem")
    sch.dma("sync", lambda sy: sy.dma_start(out=grid_sb[:], in_=grid_h),
            sem=gsem, writes=[(uid, "grid")])

    attn_fm = [t(f"{uid}_afm{c}", [128, S], BF16) for c in range(8)]
    Dj = [attn_fm[i // 4][:, GS * (i % 4):GS * (i % 4 + 1)] for i in range(12)]
    xf_st = DBuf(nc, es, sch, f"{uid}_xf", [128, GS], F32, nbuf=8, queue="sync", shared_sem=True)
    silu_x = t(f"{uid}_silu", [128, 8 * GS], BF16)
    L = [t(f"{uid}_L{i}", [128, GS], BF16) for i in range(11)]

    E = [t(f"{uid}_E{i}", [128, 1024], BF16) for i in range(2)]
    stage = E[0][:, 0:GS]
    tmp1f = t(f"{uid}_t1", [128, GS], F32)
    t1s = [tmp1f.ap().bitcast(BF16)[:, 0:GS], tmp1f.ap().bitcast(BF16)[:, GS:2 * GS]]
    t2s_ = [E[1][:, 0:GS], E[1][:, GS:2 * GS]]
    tmp1, tmp2 = t1s[0], t2s_[0]
    xb16 = E[0][:, GS:2 * GS]

    invd = t(f"{uid}_invd", [128, 64], F32)

    sw_st = DBuf(nc, es, sch, f"{uid}_swst", [128, 1024], BF16, nbuf=3, queue="sync")
    bw_st = DBuf(nc, es, sch, f"{uid}_bwst", [128, 8 * 128], BF16, nbuf=2)
    qsem = sch.new_dma_sem(f"{uid}_qsem")

    INVD_OFF = {1: 0, 2: 11, 3: 21}

    q_pes = contextlib.ExitStack()
    ps_o = [q_pes.enter_context(nc.psum_tensor(f"{uid}_pso{i}", [128, GS], F32))
            for i in range(8)]
    v_aug = [t(f"{uid}_vaug{i}", [128, 16 * 80], BF16) for i in range(2)]
    q_aug = t(f"{uid}_qaug", [128, S], BF16)
    k_aug = t(f"{uid}_kaug", [128, S], BF16)
    parts = []
    for s in (attn_fm[3], attn_fm[4], attn_fm[5], attn_fm[6], attn_fm[7],
              q_aug, k_aug):
        f = s.ap().bitcast(F32)
        parts += [f[:, 0:512], f[:, 512:1024]]
    parts += [v_aug[i].ap().bitcast(F32)[:, 0:512] for i in range(2)]
    part_x = t(f"{uid}_partx", [128, 8 * 512], F32)
    parts += [part_x[:, 512 * i:512 * (i + 1)] for i in range(8)]
    SETS = ([0, 1, 2, 3], [4, 5, 6, 7])

    def emit_xloads(tt):
        tlo = tt * GS
        for c in range(8):
            xf_st.load(c + 8 * tt, xT[128 * c:128 * (c + 1), tlo:tlo + GS], reads=x_dep)

    def emit_silu(tt):
        for c in range(8):
            sch.op("scalar", lambda s_, c=c: s_.activation(
                out=silu_x[:, GS * c:GS * (c + 1)], in_=xf_st.tiles[c][:], func=AF.Silu),
                reads=[xf_st.tok(c)], writes=[(uid, "silu", c)])

    def emit_rec(tt, ics):
        for ic in ics:
            xf = xf_st.tiles[ic]
            xtok = xf_st.tok(ic)
            gcol = lambda j: grid_sb[:, 12 * ic + j:12 * ic + j + 1]
            sch.op("vector", lambda v, xf=xf: v.tensor_copy(xb16, xf[:]),
                   reads=[xtok], writes=[(uid, "xb16")])
            xbtok = (uid, "xb16")
            for j in range(12):
                sch.op("vector", lambda v, j=j, gc=gcol(j): v.tensor_scalar(
                    Dj[j], xb16, gc, None, op0=OP.subtract),
                    reads=[xbtok, (uid, "grid")], writes=[(uid, "D", j)])
            ge = [tmp1, tmp2]
            getok = [(uid, "t1", 0), (uid, "t2", 0)]
            sch.op("vector", lambda v: v.tensor_scalar(
                ge[0], Dj[0], 0.0, None, op0=OP.is_ge),
                reads=[(uid, "D", 0)], writes=[getok[0]])
            for j in range(11):
                sch.op("vector", lambda v, j=j: v.tensor_scalar(
                    ge[(j + 1) % 2], Dj[j + 1], 0.0, None, op0=OP.is_ge),
                    reads=[(uid, "D", j + 1)], writes=[getok[(j + 1) % 2]])
                sch.op("vector", lambda v, j=j: v.tensor_tensor(
                    L[j][:], ge[j % 2], ge[(j + 1) % 2], op=OP.subtract),
                    reads=[getok[0], getok[1]], writes=[(uid, "L", j)])
            nb = 0
            for pp in (1, 2, 3):
                n_out = 11 - pp
                for j in range(n_out):
                    last = (pp == 3)
                    dst = bas(j * 8 + ic) if last else L[j][:]
                    dst_tok = (uid, "bas", j * 8 + ic) if last else (uid, "L", j)
                    par = nb % 2
                    nb += 1
                    u1, u2 = t1s[par], t2s_[par]
                    tk1, tk2 = (uid, "t1", par), (uid, "t2", par)
                    sch.op("vector", lambda v, j=j, u1=u1: v.tensor_tensor(
                        u1, Dj[j], L[j][:], op=OP.mult),
                        reads=[(uid, "D", j), (uid, "L", j)], writes=[tk1])
                    sch.op("vector", lambda v, j=j, pp=pp, u2=u2: v.tensor_tensor(
                        u2, Dj[j + pp + 1], L[j + 1][:], op=OP.mult),
                        reads=[(uid, "D", j + pp + 1), (uid, "L", j + 1)], writes=[tk2])
                    sch.op("vector", lambda v, dst=dst, u1=u1, u2=u2: v.tensor_tensor(
                        dst, u1, u2, op=OP.subtract),
                        reads=[tk1, tk2], writes=[dst_tok])

    def emit_mm_half(tt, hs):
        tlo = tt * GS
        ccs = [j * 8 + ic for ic in SETS[hs] for j in range(8)]
        nA = len(SETS[hs]) * 8
        base = (tt * 2 + hs) * 96
        flat = [(gg, cc) for gg in range(3) for cc in ccs]

        def sload(k):
            gg, cc = flat[k]
            sw_st.load(base + k, sw_scr[gg, cc], reads=[(uid, "swscr", gg, cc)])

        sload(0)
        sload(1)
        for k, (gg, cc) in enumerate(flat):
            if True:
                ci = k % nA
                sidx = base + k
                wt = sw_st.tiles[sidx % sw_st.nbuf]
                for oti in range(8):
                    mm_F(sch, ps_o[oti][:], wt[:, 128 * oti:128 * (oti + 1)],
                         bas(cc), start=(ci == 0),
                         stop=(hs == 0 and ci == nA - 1),
                         reads=[sw_st.tok(sidx), (uid, "bas", cc)],
                         writes=[(uid, "pso", oti)])
                if k + 2 < len(flat):
                    sload(k + 2)
            if ci != nA - 1:
                continue
            if hs == 0:
                for oti in range(8):
                    os = gg * 8 + oti
                    sch.op("scalar", lambda s_, oti=oti, os=os: s_.activation(
                        out=parts[os], in_=ps_o[oti][:], func=AF.Copy),
                        reads=[(uid, "pso", oti)], writes=[(uid, "part", os)])
            else:
                for oti in range(8):
                    os = gg * 8 + oti
                    bw = bw_st.load(os + 24 * tt, bw_blk[os].rearrange("p c h -> p (c h)"))
                    for c8 in range(8):
                        mm_F(sch, ps_o[oti][:], bw[:, 128 * c8:128 * (c8 + 1)],
                             silu_x[:, GS * c8:GS * (c8 + 1)], start=False, stop=(c8 == 7),
                             reads=[bw_st.tok(os + 24 * tt), (uid, "silu", c8)],
                             writes=[(uid, "pso", oti)])
                    sch.op("vector", lambda v, oti=oti, os=os: v.tensor_tensor(
                        stage[:], ps_o[oti][:], parts[os], op=OP.add),
                        reads=[(uid, "pso", oti), (uid, "part", os)],
                        writes=[(uid, "stage")])
                    o0 = 128 * os
                    for half in range(2):
                        m = (o0 + 64 * half) // 64
                        h, kind = m // 3, m % 3
                        if kind == 0:
                            dstp = qk_scr[64 * h:64 * h + 64, tlo:tlo + GS]
                        elif kind == 1:
                            dstp = qk_scr[1024 + 64 * h:1024 + 64 * h + 64, tlo:tlo + GS]
                        else:
                            dstp = v_scr[64 * h:64 * h + 64, tlo:tlo + GS]
                        sch.dma("sync", lambda sy, half=half, dstp=dstp: sy.dma_start(
                            out=dstp, in_=stage[64 * half:64 * (half + 1), :]),
                            sem=qsem, reads=[(uid, "stage")],
                            writes=[(uid, "qkv", os, half)])

    emit_xloads(0)
    emit_silu(0)
    emit_rec(0, SETS[0])
    emit_rec(0, SETS[1])
    for tt in range(NT):
        emit_mm_half(tt, 0)
        if tt + 1 < NT:
            emit_xloads(tt + 1)
            emit_rec(tt + 1, SETS[0])
        emit_mm_half(tt, 1)
        if tt + 1 < NT:
            emit_silu(tt + 1)
            emit_rec(tt + 1, SETS[1])
    sch.fence_tokens("vector", [(uid, "stage")])
    sch.barrier()
    q_pes.close()

    RT_sb = t(f"{uid}_RT", [64, 64], BF16)
    rsem = sch.new_dma_sem(f"{uid}_rsem")
    sch.dma("gpsimd", lambda g: g.dma_start(out=RT_sb[:], in_=RT_h),
            sem=rsem, writes=[(uid, "RT")])
    iden = t(f"{uid}_iden", [128, 128], BF16)
    isem = sch.new_dma_sem(f"{uid}_isem")
    sch.dma("gpsimd", lambda g: g.dma_start(out=iden[:], in_=env["iden_h"]),
            sem=isem, writes=[(uid, "iden")])
    _px_bf = part_x.ap().bitcast(BF16)
    qk_in = DBuf(nc, es, sch, f"{uid}_qkin", None, None, queue="sync",
                 tiles=[_px_bf[0:64, 0:2048], _px_bf[0:64, 2048:4096]])
    vtsem = [sch.new_dma_sem(f"{uid}_vt{i}") for i in range(2)]
    y = [bases.ap().bitcast(F32)[:, 2048 + 1024 * c:3072 + 1024 * c]
         for c in range(8)]
    sqt = tmp1f.ap().bitcast(BF16)[0:64, :]
    o1 = t(f"{uid}_o1", [64, 1024], F32)
    comb = silu_x.ap().bitcast(F32)[0:64, 0:1024]
    rows = silu_x.ap().bitcast(F32)[0:1, 1024:2048]
    rowsb = bases.ap().bitcast(F32)[0:1, 4096:5120]
    rbh1 = bases[0:1, 2048:3072]
    m2row = silu_x.ap().bitcast(F32)[32:33, 1024:1040]
    q_augs = [q_aug, _px_bf[:, 4096:6144]]
    k_augs = [k_aug, _px_bf[:, 6144:8192]]
    m2rows = [m2row, silu_x.ap().bitcast(F32)[32:33, 1040:1056]]
    for p in range(2):
        sch.op("vector", lambda v, p=p: v.memset(k_augs[p][64:128, :], 0.0),
               writes=[(uid, "augpad", p), (uid, "kones", p)])
        sch.op("vector", lambda v, p=p: v.memset(k_augs[p][64:65, :], 1.0),
               writes=[(uid, "kones", p)], reads=[(uid, "augpad", p)])
        sch.op("vector", lambda v, p=p: v.memset(q_augs[p][64:128, :], 0.0),
               writes=[(uid, "augpad", p)])

    a_pes = contextlib.ExitStack()
    ps_r = a_pes.enter_context(nc.psum_tensor(f"{uid}_psr", [64, 1024], F32))
    ps_s = [a_pes.enter_context(nc.psum_tensor(f"{uid}_pss{i}", [128, 1024], F32))
            for i in range(2)]
    p1 = a_pes.enter_context(nc.psum_tensor(f"{uid}_p1", [65, 1024], F32))
    def prep_head(h):
        j = h % 2
        qa_, ka_, m2_ = q_augs[j], k_augs[j], m2rows[j]
        TQ, TK = (uid, "qa", j), (uid, "ka", j)
        for (row0, dsta, dtok, scl) in ((64 * h, qa_, TQ, 0.125),
                                        (1024 + 64 * h, ka_, TK, 1.0)):
            qt = qk_in.load(h * 2 + (0 if row0 < 1024 else 1),
                            qk_scr[row0:row0 + 64, :], reads=[(uid, "qkv", 0, 0)])
            for hf in range(2):
                mm_F(sch, ps_r[:], RT_sb[:], qt[:, 1024 * hf:1024 * (hf + 1)],
                     start=True, stop=True,
                     reads=[qk_in.tok(h * 2 + (0 if row0 < 1024 else 1)), (uid, "RT")],
                     writes=[(uid, "psr")])
                sch.op("scalar", lambda s_, dsta=dsta, scl=scl, hf=hf: s_.activation(
                    out=dsta[0:64, 1024 * hf:1024 * (hf + 1)], in_=ps_r[:],
                    func=AF.Copy, scale=scl),
                    reads=[(uid, "psr"), ("cst",)], writes=[dtok])
        va = v_aug[j]
        ones_cols = va.ap().rearrange("p (k c) -> p k c", c=80)[:, :, 64:65]
        sch.op("vector", lambda v, ones_cols=ones_cols: v.memset(ones_cols, 1.0),
               writes=[(uid, "vaug1", j)])
        for kc in range(16):
            sch.dma("sync", lambda sy, va=va, kc=kc, h=h: sy.dma_start(
                out=va[:, 80 * kc:80 * kc + 64],
                in_=v_scr[64 * h:64 * h + 64, 128 * kc:128 * (kc + 1)],
                transpose=True),
                sem=vtsem[j], reads=[(uid, "qkv", 0, 0)],
                writes=[(uid, "vaugd", j, kc)])
        for piece in range(2):
            sch.op("scalar", lambda s_, piece=piece, ka_=ka_, TK=TK: s_.activation(
                out=sqt, in_=ka_[0:64, 1024 * piece:1024 * (piece + 1)],
                func=AF.Square), reads=[TK], writes=[(uid, "sqt")])
            mm_F(sch, ps_s[piece][0:1, :], env["ones128"][0:64, :], sqt,
                 start=True, stop=True, reads=[(uid, "sqt"), "ones128"],
                 writes=[(uid, "pss", piece)])
            sch.op("vector", lambda v, piece=piece, m2_=m2_: v.tensor_reduce(
                m2_[:, piece:piece + 1], ps_s[piece][0:1, :], axis=AX, op=OP.max),
                reads=[(uid, "pss", piece)], writes=[(uid, "m2r", j)])
        sch.op("vector", lambda v, m2_=m2_: v.tensor_reduce(
            m2_[:, 4:5], m2_[:, 0:2], axis=AX, op=OP.max),
            reads=[(uid, "m2r", j)], writes=[(uid, "M2", j)])
        sch.op("scalar", lambda s_, qa_=qa_, TQ=TQ: s_.activation(
            out=sqt, in_=qa_[0:64, 0:1024], func=AF.Square),
            reads=[TQ], writes=[(uid, "sqt")])
        mm_F(sch, ps_s[0][0:1, :], env["ones128"][0:64, :], sqt,
             start=True, stop=True, reads=[(uid, "sqt"), "ones128"],
             writes=[(uid, "pss", 0)])
        sch.op("vector", lambda v, m2_=m2_: v.tensor_reduce(
            m2_[:, 5:6], ps_s[0][0:1, :], axis=AX, op=OP.max),
            reads=[(uid, "pss", 0)], writes=[(uid, "M2", j)])
        sch.op("scalar", lambda s_, m2_=m2_: s_.activation(
            out=m2_[:, 6:8], in_=m2_[:, 4:6], func=AF.Ln),
            reads=[(uid, "M2", j)], writes=[(uid, "M2", j)])
        sch.op("vector", lambda v, m2_=m2_: v.tensor_tensor(
            m2_[:, 8:9], m2_[:, 6:7], m2_[:, 7:8], op=OP.subtract),
            reads=[(uid, "M2", j)], writes=[(uid, "M2", j)])
        sch.op("vector", lambda v, m2_=m2_: v.tensor_tensor(
            m2_[:, 9:10], m2_[:, 6:7], m2_[:, 7:8], op=OP.add),
            reads=[(uid, "M2", j)], writes=[(uid, "M2", j)])
        sch.op("scalar", lambda s_, m2_=m2_: s_.activation(
            out=m2_[:, 8:9], in_=m2_[:, 8:9], func=AF.Exp, scale=0.5),
            reads=[(uid, "M2", j)], writes=[(uid, "M2", j)])
        sch.op("scalar", lambda s_, m2_=m2_: s_.activation(
            out=m2_[:, 9:10], in_=m2_[:, 9:10], func=AF.Exp, scale=0.5),
            reads=[(uid, "M2", j)], writes=[(uid, "M2", j)])
        sch.op("vector", lambda v, m2_=m2_: v.tensor_scalar(
            m2_[:, 8:9], m2_[:, 8:9], -0.5, None, op0=OP.mult),
            reads=[(uid, "M2", j)], writes=[(uid, "M2", j)])
        sch.op("vector", lambda v, m2_=m2_: v.tensor_scalar(
            m2_[:, 9:10], m2_[:, 9:10], 0.5, None, op0=OP.mult),
            reads=[(uid, "M2", j)], writes=[(uid, "M2", j)])
        sch.op("vector", lambda v, qa_=qa_, m2_=m2_, TQ=TQ, j=j: v.tensor_scalar(
            qa_[64:65, 0:1024], ps_s[0][0:1, :],
            m2_[:, 8:9], m2_[:, 9:10], op0=OP.mult, op1=OP.subtract),
            reads=[(uid, "pss", 0), (uid, "M2", j), (uid, "augpad", j)],
            writes=[TQ])

    def emit_qh(h, qh):
        j = h % 2
        qa_, ka_, m2_ = q_augs[j], k_augs[j], m2rows[j]
        TQ, TK = (uid, "qa", j), (uid, "ka", j)
        va = v_aug[j]
        tva = [(uid, "vaug1", j)] + [(uid, "vaugd", j, kc) for kc in range(16)]
        qlo_h = 1024 * qh
        if qh == 1:
            sch.op("scalar", lambda s_, qlo_h=qlo_h, qa_=qa_, TQ=TQ: s_.activation(
                out=sqt, in_=qa_[0:64, qlo_h:qlo_h + 1024], func=AF.Square),
                reads=[TQ], writes=[(uid, "sqt")])
            mm_F(sch, ps_s[0][0:1, :], env["ones128"][0:64, :], sqt,
                 start=True, stop=True, reads=[(uid, "sqt"), "ones128"],
                 writes=[(uid, "pss", 0)])
            sch.op("vector", lambda v, qlo_h=qlo_h, qa_=qa_, m2_=m2_, TQ=TQ, j=j: v.tensor_scalar(
                qa_[64:65, qlo_h:qlo_h + 1024], ps_s[0][0:1, :],
                m2_[:, 8:9], m2_[:, 9:10], op0=OP.mult, op1=OP.subtract),
                reads=[(uid, "pss", 0), (uid, "M2", j), (uid, "augpad", j)],
                writes=[TQ])

        def s_mm(kc):
            idx = kc % 2
            mm_F(sch, ps_s[idx][:, :], ka_[0:128, 128 * kc:128 * (kc + 1)],
                 qa_[0:128, qlo_h:qlo_h + 1024], start=True, stop=True,
                 reads=[TQ, TK, (uid, "kones", j), (uid, "augpad", j)],
                 writes=[(uid, "pss", idx)])

        s_mm(0)
        for kc in range(16):
            idx = kc % 2
            sch.op("scalar", lambda s_, idx=idx: s_.activation(
                out=E[idx][:, :], in_=ps_s[idx][:, :], func=AF.Exp),
                reads=[(uid, "pss", idx)], writes=[(uid, "E", idx)])
            if kc + 1 < 16:
                s_mm(kc + 1)
            mm_F(sch, p1[:, :], va[:, 80 * kc:80 * kc + 65], E[idx][:, :],
                 start=(kc == 0), stop=(kc == 15),
                 reads=[(uid, "E", idx)] + tva, writes=[(uid, "p1")])
        sch.op("scalar", lambda s_: s_.activation(out=rows[:, :], in_=p1[64:65, :], func=AF.Ln),
               reads=[(uid, "p1")], writes=[(uid, "rowA")])
        sch.op("scalar", lambda s_: s_.activation(out=rbh1[:, :], in_=rows[:, :],
                                                  func=AF.Exp, scale=-1.0),
               reads=[(uid, "rowA"), ("cst",)], writes=[(uid, "rowB")])
        mm_F(sch, ps_r[0:64, :], env["ones1"][:, 0:64], rbh1[:, :],
             start=True, stop=True, reads=[(uid, "rowB"), "ones1"],
             writes=[(uid, "psr")])
        sch.op("vector", lambda v: v.tensor_copy(o1[:], p1[0:64, :]),
               reads=[(uid, "p1")], writes=[(uid, "o1")])
        cdst = attn_fm[h // 2]
        if h % 2 == 0:
            sch.op("vector", lambda v, cdst=cdst, qh=qh: v.tensor_tensor(
                cdst[0:64, 1024 * qh:1024 * (qh + 1)], o1[:], ps_r[0:64, :], op=OP.mult),
                reads=[(uid, "o1"), (uid, "psr")], writes=[(uid, "afm", h // 2)])
        else:
            sch.op("vector", lambda v: v.tensor_tensor(
                comb, o1[:], ps_r[0:64, :], op=OP.mult),
                reads=[(uid, "o1"), (uid, "psr")], writes=[(uid, "comb")])
            sch.op("vector", lambda v: v.tensor_copy(sqt, comb),
                   reads=[(uid, "comb")], writes=[(uid, "sqt")])
            mm_F(sch, ps_s[1][64:128, :], iden[0:64, 0:64], sqt,
                 start=True, stop=True, reads=[(uid, "sqt"), (uid, "iden")],
                 writes=[(uid, "pss", 1)], tile_position=(0, 64))
            sch.op("vector", lambda v, cdst=cdst, qh=qh: v.tensor_copy(
                cdst[64:128, 1024 * qh:1024 * (qh + 1)], ps_s[1][64:128, :]),
                reads=[(uid, "pss", 1)], writes=[(uid, "afm", h // 2)])

    prep_head(0)
    for h in range(16):
        emit_qh(h, 0)
        if h + 1 < 16:
            prep_head(h + 1)
        emit_qh(h, 1)
    a_pes.close()
    sch.barrier()

    ob_sb = t(f"{uid}_ob", [128, 8], F32)
    obsem = sch.new_dma_sem(f"{uid}_obsem")
    sch.dma("sync", lambda sy: sy.dma_start(out=ob_sb[:], in_=outb),
            sem=obsem, writes=[(uid, "ob")])
    qa_bf = q_aug.ap()
    wo_st = DBuf(nc, es, sch, f"{uid}_wo", None, None, queue="gpsimd",
                 tiles=[qa_bf[:, 0:1024], qa_bf[:, 1024:2048]])
    ka_f32 = k_aug.ap().bitcast(F32)
    xf2b = bases.ap().bitcast(F32)[:, 0:1024]
    xf2_st = DBuf(nc, es, sch, f"{uid}_xf2", None, None, queue="sync",
                  tiles=[ka_f32, xf2b])
    NH = 1024
    for half in range(2):
        hlo = half * NH
        with contextlib.ExitStack() as pes:
            p = lambda name, shape: pes.enter_context(nc.psum_tensor(name, shape, F32))
            ps_p = [p(f"{uid}_psp{half}_{i}", [128, NH]) for i in range(2)]
            for os in range(8):
                b = os % 2
                wtile = wo_st.load(os + 8 * half, ow_blk[os].rearrange("p c h -> p (c h)"))
                xf2_st.load(os + 8 * half, xT[128 * os:128 * (os + 1), hlo:hlo + NH], reads=x_dep)
                for c in range(8):
                    mm_F(sch, ps_p[b][:], wtile[:, 128 * c:128 * (c + 1)],
                         attn_fm[c][:, hlo:hlo + NH], start=(c == 0), stop=(c == 7),
                         reads=[wo_st.tok(os), (uid, "afm", c)], writes=[(uid, "psp", b)])
                sch.op("vector", lambda v, os=os, b=b: v.scalar_tensor_tensor(
                    y[os][:, 0:NH], ps_p[b][:], ob_sb[:, os:os + 1],
                    xf2_st.tiles[b], op0=OP.add, op1=OP.add),
                    reads=[(uid, "psp", b), xf2_st.tok(os), (uid, "ob")],
                    writes=[(uid, "y", os)])
            sch.barrier()
        with contextlib.ExitStack() as pes:
            ps_rows = pes.enter_context(nc.psum_tensor(f"{uid}_lnr{half}", [1, 2 * NH], F32)).ap()
            pb = pes.enter_context(nc.psum_tensor(f"{uid}_lnb{half}", [128, 2 * NH], F32)).ap()
            ob2 = [out_bf16[c][:, hlo:hlo + NH] for c in range(8)] if out_bf16 else None
            obt = ([(tok, half) for tok in out_bf16_toks] if out_bf16 else None)
            emit_layernorm(sch, env, [y[c][:, 0:NH] for c in range(8)],
                           [(uid, "y", c) for c in range(8)], ps_rows, pb, uid,
                           out_bf16=ob2, out_bf16_toks=obt,
                           out_dma=lambda c: (outT[128 * c:128 * (c + 1), hlo:hlo + NH], st_sem))
            sch.barrier()


def build_l1_attn_standalone():
    nc = bass.Bass("TRN2", target_bir_lowering=False)
    xT = nc.dram_tensor("xT", [D, S], F32, kind="ExternalInput").ap()
    bw_blk = nc.dram_tensor("bw_blk", [24, 128, 8, 128], F32, kind="ExternalInput").ap()
    swT = nc.dram_tensor("swT", [8192, 3072], F32, kind="ExternalInput").ap()
    sclT = nc.dram_tensor("sclT", [1024, 3072], F32, kind="ExternalInput").ap()
    grid_h = nc.dram_tensor("grid_h", [128, 96], F32, kind="ExternalInput").ap()
    RT_h = nc.dram_tensor("RT_h", [64, 64], F32, kind="ExternalInput").ap()
    ow_blk = nc.dram_tensor("ow_blk", [8, 128, 8, 128], F32, kind="ExternalInput").ap()
    outb = nc.dram_tensor("outb", [128, 8], F32, kind="ExternalInput").ap()
    iden_h = nc.dram_tensor("iden_h", [128, 128], F32, kind="ExternalInput").ap()
    sw_scr = nc.dram_tensor("sw_scr", [3, 64, 128, 1024], BF16, kind="Internal").ap()
    qk_scr = nc.dram_tensor("qk_scr", [2048, S], BF16, kind="Internal").ap()
    v_scr = nc.dram_tensor("v_scr", [D, S], BF16, kind="Internal").ap()
    outT = nc.dram_tensor("outT", [D, S], F32, kind="ExternalOutput").ap()
    es = contextlib.ExitStack()
    with es:
        sch = Sched(nc, es)
        env = make_env(nc, es, sch)
        env["iden_h"] = iden_h[:, :]
        st_sem = es.enter_context(nc.semaphore("sem_st"))
        with contextlib.ExitStack() as ses:
            pump = emit_l1_sw_stage(nc, sch, ses, env, swT[:, :], sclT,
                                    grid_h[:, :], sw_scr, uid="a1")
            while pump():
                pass
            sch.retire_dmas()
            sch.barrier()
        emit_l1_attn(nc, sch, es, env, xT, bw_blk, swT[:, :], sclT, grid_h[:, :],
                     RT_h[:, :], ow_blk, outb[:, :], sw_scr, qk_scr, v_scr, outT, st_sem)
        sch._emit("sync", lambda e: e.nop(), [(st_sem, sch.count(st_sem))], None, 1)
        sch.barrier()
    return nc


def prep_l1_attn_inputs(x_bt, base_weight, spline_weight, spline_scaler, grid,
                        rotation_matrix, out_w, out_b):
    xT = np.ascontiguousarray(x_bt.T)
    blk = lambda w, nin, nout: np.ascontiguousarray(
        w.T.reshape(nin // 128, 128, nout // 128, 128).transpose(2, 1, 0, 3))
    swT = np.ascontiguousarray(spline_weight.transpose(2, 1, 0).reshape(8192, 3072))
    sclT = np.ascontiguousarray(spline_scaler.T)
    grid_h = np.ascontiguousarray(grid.reshape(8, 128, 12).transpose(1, 0, 2).reshape(128, 96))
    return {"xT": xT, "bw_blk": blk(base_weight, 1024, 3072), "swT": swT,
            "sclT": sclT, "grid_h": grid_h,
            "RT_h": np.ascontiguousarray(rotation_matrix.T),
            "ow_blk": blk(out_w, 1024, 1024),
            "outb": np.ascontiguousarray(out_b.reshape(8, 128).T),
            "iden_h": np.eye(128, dtype=np.float32)}



_PROG_CACHE = {}


def _get_prog(name, builder):
    if name not in _PROG_CACHE:
        _PROG_CACHE[name] = builder()
    return _PROG_CACHE[name]


def kernel_unfused(L0_attn_x, L0_attn_Wq, L0_attn_bq, L0_attn_Wk, L0_attn_bk,
           L0_attn_Wv, L0_attn_bv, L0_attn_Wo, L0_attn_bo, L0_attn_head_norm_w,
           L0_attn_lq1, L0_attn_lk1, L0_attn_lq2, L0_attn_lk2,
           L0_ffn_w_gate, L0_ffn_w_up, L0_ffn_w_down,
           L1_attn_base_weight, L1_attn_spline_weight, L1_attn_spline_scaler,
           L1_attn_grid, L1_attn_rotation_matrix, L1_attn_out_w, L1_attn_out_b,
           L1_ffn_w1_w, L1_ffn_w1_s, L1_ffn_w2_w, L1_ffn_w2_s,
           L1_ffn_h1_w, L1_ffn_h1_s, L1_ffn_h2_w, L1_ffn_h2_s,
           L1_ffn_w3_w, L1_ffn_w3_s):
    B = L0_attn_x.shape[0]
    x = np.ascontiguousarray(L0_attn_x)

    common_a0 = prep_l0_attn_inputs(
        x[0], L0_attn_Wq, L0_attn_bq, L0_attn_Wk, L0_attn_bk, L0_attn_Wv,
        L0_attn_bv, L0_attn_Wo, L0_attn_bo, L0_attn_head_norm_w,
        L0_attn_lq1, L0_attn_lk1, L0_attn_lq2, L0_attn_lk2)
    in_maps = []
    for b in range(B):
        m = dict(common_a0)
        m["xT"] = np.ascontiguousarray(x[b].T)
        in_maps.append(m)
    nc = _get_prog("l0attn", build_l0_attn_standalone)
    res = run_bass_kernel_spmd(nc, in_maps, core_ids=list(range(B)))
    x1 = [r["outT"] for r in res.results]

    common_f0 = prep_l0_ffn_inputs(x[0], L0_ffn_w_gate, L0_ffn_w_up, L0_ffn_w_down)
    in_maps = []
    for b in range(B):
        m = dict(common_f0)
        m["xT"] = x1[b]
        in_maps.append(m)
    nc = _get_prog("l0ffn", build_l0_ffn_standalone)
    res = run_bass_kernel_spmd(nc, in_maps, core_ids=list(range(B)))
    x2 = [r["outT"] for r in res.results]

    common_a1 = prep_l1_attn_inputs(
        x[0], L1_attn_base_weight, L1_attn_spline_weight, L1_attn_spline_scaler,
        L1_attn_grid, L1_attn_rotation_matrix, L1_attn_out_w, L1_attn_out_b)
    in_maps = []
    for b in range(B):
        m = dict(common_a1)
        m["xT"] = x2[b]
        in_maps.append(m)
    nc = _get_prog("l1attn", build_l1_attn_standalone)
    res = run_bass_kernel_spmd(nc, in_maps, core_ids=list(range(B)))
    x3 = [r["outT"] for r in res.results]

    common_f1 = prep_l1_ffn_inputs(
        x[0], L1_ffn_w1_w, L1_ffn_w1_s, L1_ffn_w2_w, L1_ffn_w2_s,
        L1_ffn_h1_w, L1_ffn_h1_s, L1_ffn_h2_w, L1_ffn_h2_s,
        L1_ffn_w3_w, L1_ffn_w3_s)
    in_maps = []
    for b in range(B):
        m = dict(common_f1)
        m["xT"] = x3[b]
        in_maps.append(m)
    nc = _get_prog("l1ffn", build_l1_ffn_standalone)
    res = run_bass_kernel_spmd(nc, in_maps, core_ids=list(range(B)))
    out = np.stack([r["outT"].T for r in res.results], axis=0)
    return out



def build_full():
    nc = bass.Bass("TRN2", target_bir_lowering=False)
    ap = lambda n, s, d, k: nc.dram_tensor(n, s, d, kind=k).ap()
    x0T = ap("xT", [D, S], F32, "ExternalInput")
    wq_blk = ap("wq_blk", [16, 128, 8, 128], F32, "ExternalInput")
    wk_blk = ap("wk_blk", [16, 128, 8, 128], F32, "ExternalInput")
    wv_blk = ap("wv_blk", [8, 128, 8, 128], F32, "ExternalInput")
    wo_blk = ap("wo_blk", [8, 128, 8, 128], F32, "ExternalInput")
    biases = ap("biases", [128, 48], F32, "ExternalInput")
    lqk = ap("lqk", [64, 64], F32, "ExternalInput")
    hnw = ap("hnw", [64, 16], F32, "ExternalInput")
    cs_tab = ap("cs_tab", [128, S], F32, "ExternalInput")
    ss_tab = ap("ss_tab", [128, S], F32, "ExternalInput")
    perm = ap("perm", [128, 128], F32, "ExternalInput")
    ut = ap("ut", [128, 128], F32, "ExternalInput")
    iden = ap("iden", [128, 128], F32, "ExternalInput")
    wg_blk = ap("wg_blk", [24, 128, 8, 128], I32, "ExternalInput")
    wu_blk = ap("wu_blk", [24, 128, 8, 128], I32, "ExternalInput")
    wd_blk = ap("wd_blk", [8, 128, 24, 128], I32, "ExternalInput")
    bw_blk = ap("bw_blk", [24, 128, 8, 128], F32, "ExternalInput")
    swT = ap("swT", [8192, 3072], F32, "ExternalInput")
    sclT = ap("sclT", [1024, 3072], F32, "ExternalInput")
    grid_h = ap("grid_h", [128, 96], F32, "ExternalInput")
    RT_h = ap("RT_h", [64, 64], F32, "ExternalInput")
    ow_blk = ap("ow_blk", [8, 128, 8, 128], F32, "ExternalInput")
    outb = ap("outb", [128, 8], F32, "ExternalInput")
    w1_blk = ap("w1_blk", [32, 128, 8, 128], F32, "ExternalInput")
    w2_blk = ap("w2_blk", [32, 128, 8, 128], F32, "ExternalInput")
    h1_blk = ap("h1_blk", [32, 128, 32, 128], F32, "ExternalInput")
    h2_blk = ap("h2_blk", [32, 128, 32, 128], F32, "ExternalInput")
    w3_blk = ap("w3_blk", [8, 128, 32, 128], F32, "ExternalInput")
    scl = ap("scl", [128, 4 * 32 + 8], F32, "ExternalInput")
    v_scr0 = ap("v_scr0", [D, S], BF16, "Internal")
    sw_scr = ap("sw_scr", [3, 64, 128, 1024], BF16, "Internal")
    qk_scr = ap("qk_scr", [2048, S], BF16, "Internal")
    v_scr1 = ap("v_scr1", [D, S], BF16, "Internal")
    x1T = ap("x1T", [D, S], F32, "Internal")
    x2T = ap("x2T", [D, S], F32, "Internal")
    x3T = ap("x3T", [D, S], F32, "Internal")
    outT = ap("outT", [D, S], F32, "ExternalOutput")

    es = contextlib.ExitStack()
    with es:
        sch = Sched(nc, es)
        env = make_env(nc, es, sch)
        env["iden_h"] = iden
        sts = [sch.new_dma_sem(f"st{i}") for i in range(4)]
        with contextlib.ExitStack() as bes:
            emit_l0_attn(nc, sch, bes, env, x0T, wq_blk, wk_blk, wv_blk, wo_blk,
                         biases[:, :], lqk, hnw, cs_tab, ss_tab, perm, ut, iden,
                         v_scr0, x1T, sts[0], uid="a0")
            sch.retire_dmas()
            sch.barrier()
        with contextlib.ExitStack() as bes:
            pump = emit_l1_sw_stage(nc, sch, bes, env, swT[:, :], sclT, grid_h,
                                    sw_scr, uid="a1")
            emit_l0_ffn(nc, sch, bes, env, x1T, wg_blk, wu_blk, wd_blk, x2T,
                        sts[1], uid="f0", x_dep=[("a0", "xout")], pump=pump)
            while pump():
                pass
            sch.retire_dmas()
            sch.barrier()
        with contextlib.ExitStack() as bes:
            emit_l1_attn(nc, sch, bes, env, x2T, bw_blk, swT[:, :], sclT, grid_h,
                         RT_h, ow_blk, outb[:, :], sw_scr, qk_scr, v_scr1, x3T,
                         sts[2], uid="a1", x_dep=[("f0", "xout")])
            sch.retire_dmas()
            sch.barrier()
        with contextlib.ExitStack() as bes:
            emit_l1_ffn(nc, sch, bes, env, x3T, w1_blk, w2_blk, h1_blk, h2_blk,
                        w3_blk, scl, outT, sts[3], uid="f1", x_dep=[("a1", "xout")])
            sch.retire_dmas()
            sch.barrier()
        sch._emit("sync", lambda e: e.nop(), [(sts[3], sch.count(sts[3]))], None, 1)
        sch.retire_dmas("sync")
        sch.barrier()
    return nc


def prep_full_inputs(inputs):
    g = inputs
    m = {}
    m.update(prep_l0_attn_inputs(
        g["L0_attn_x"][0], g["L0_attn_Wq"], g["L0_attn_bq"], g["L0_attn_Wk"],
        g["L0_attn_bk"], g["L0_attn_Wv"], g["L0_attn_bv"], g["L0_attn_Wo"],
        g["L0_attn_bo"], g["L0_attn_head_norm_w"], g["L0_attn_lq1"],
        g["L0_attn_lk1"], g["L0_attn_lq2"], g["L0_attn_lk2"]))
    m.update(prep_l0_ffn_inputs(g["L0_attn_x"][0], g["L0_ffn_w_gate"],
                                g["L0_ffn_w_up"], g["L0_ffn_w_down"]))
    m.update(prep_l1_attn_inputs(
        g["L0_attn_x"][0], g["L1_attn_base_weight"], g["L1_attn_spline_weight"],
        g["L1_attn_spline_scaler"], g["L1_attn_grid"],
        g["L1_attn_rotation_matrix"], g["L1_attn_out_w"], g["L1_attn_out_b"]))
    m.update(prep_l1_ffn_inputs(
        g["L0_attn_x"][0], g["L1_ffn_w1_w"], g["L1_ffn_w1_s"], g["L1_ffn_w2_w"],
        g["L1_ffn_w2_s"], g["L1_ffn_h1_w"], g["L1_ffn_h1_s"], g["L1_ffn_h2_w"],
        g["L1_ffn_h2_s"], g["L1_ffn_w3_w"], g["L1_ffn_w3_s"]))
    m.pop("iden_h", None)
    return m


def kernel(**inputs):
    x = np.ascontiguousarray(inputs["L0_attn_x"])
    B = x.shape[0]
    common = prep_full_inputs(inputs)
    in_maps = []
    for b in range(B):
        m = dict(common)
        m["xT"] = np.ascontiguousarray(x[b].T)
        in_maps.append(m)
    nc = _get_prog("full", build_full)
    res = run_bass_kernel_spmd(nc, in_maps, core_ids=list(range(B)))
    return np.stack([r["outT"].T for r in res.results], axis=0)


# version 27
# speedup vs baseline: 1.0363x; 1.0012x over previous
import math
import numpy as np
import contextlib
from collections import defaultdict

import concourse.bass as bass
import concourse.mybir as mybir
from concourse.alu_op_type import AluOpType as OP
from concourse.bass_utils import run_bass_kernel_spmd

F32 = mybir.dt.float32
BF16 = mybir.dt.bfloat16
I32 = mybir.dt.int32
AF = mybir.ActivationFunctionType
AX = mybir.AxisListType.X

S = 2048
D = 1024

QUEUES = ("sync", "gpsimd", "vector", "scalar", "tensor")
ENG_SEM = ("vector", "scalar", "tensor", "gpsimd")


class Sched:

    def __init__(self, nc, es):
        self.nc = nc
        self.es = es
        self.q = defaultdict(list)
        self.cnt = {}
        self.esem = {}
        for eng in ENG_SEM:
            self.esem[eng] = es.enter_context(nc.semaphore(f"prog_{eng}"))
        self.eng_pos = defaultdict(int)
        self.pe_inc_positions = []
        self.last_w = {}
        self.readers = defaultdict(list)
        self.max_waited = defaultdict(dict)
        self.nwaits = 0
        self.nops = 0
        self.dma_sems = []

    def new_sem(self, name):
        s = self.es.enter_context(self.nc.semaphore(name))
        return s

    def new_dma_sem(self, name):
        s = self.new_sem(name)
        self.dma_sems.append(s)
        return s

    def retire_dmas(self, queue="vector"):
        waits = [(s, self.count(s)) for s in self.dma_sems]
        self._emit(queue, lambda e: e.nop(), waits, None, 1)

    def count(self, sem):
        return self.cnt.get(id(sem), 0)

    def _bump(self, sem, k=1):
        self.cnt[id(sem)] = self.cnt.get(id(sem), 0) + k
        return self.cnt[id(sem)]

    def _resolve(self, h, dst_eng):
        if h is None:
            return None
        sem, val, src_eng = h
        if src_eng == "tensor" and dst_eng == "tensor":
            return None
        if src_eng == "dma":
            val = self.cnt.get(id(sem), val)
        return (sem, val)

    def _emit(self, queue, fn, waits, inc_sem, inc_k):
        bysem = {}
        for w in waits:
            if w is None:
                continue
            s, v = w
            if v <= 0:
                continue
            k = id(s)
            if k not in bysem or v > bysem[k][1]:
                bysem[k] = (s, v)
        wl = []
        for s, v in bysem.values():
            prev = self.max_waited[queue].get(id(s), -1)
            if v > prev:
                self.max_waited[queue][id(s)] = v
                wl.append((s, v))
        self.nwaits += len(wl)
        self.nops += 1

        def emit(e, fn=fn, wl=tuple(wl), inc_sem=inc_sem, inc_k=inc_k):
            for (s, v) in wl:
                e.wait_ge(s, v)
            inst = fn(e)
            if inc_sem is not None:
                inst.then_inc(inc_sem, inc_k)

        self.q[queue].append(emit)

    def _dep_waits(self, dst_eng, reads, writes):
        waits = []
        for tok in reads:
            try:
                waits.append(self._resolve(self.last_w.get(tok), dst_eng))
            except RuntimeError as e:
                raise RuntimeError(f"{e} (RAW on token {tok})")
        for tok in writes:
            for h in self.readers.get(tok, ()):
                try:
                    waits.append(self._resolve(h, dst_eng))
                except RuntimeError as e:
                    raise RuntimeError(f"{e} (WAR on token {tok})")
            try:
                waits.append(self._resolve(self.last_w.get(tok), dst_eng))
            except RuntimeError as e:
                raise RuntimeError(f"{e} (WAW on token {tok})")
        return waits

    def op(self, eng, fn, reads=(), writes=(), waits=(), inc=True):
        dep = self._dep_waits(eng, reads, writes)
        dep.extend(waits)
        val = self._bump(self.esem[eng])
        self._emit(eng, fn, dep, self.esem[eng], 1)
        handle = (self.esem[eng], val, eng)
        self.eng_pos[eng] += 1
        for tok in writes:
            self.last_w[tok] = handle
            self.readers[tok] = []
        for tok in reads:
            self.readers[tok].append(handle)
        return handle

    def dma(self, queue, fn, sem, reads=(), writes=(), waits=()):
        dep = self._dep_waits("dma", reads, writes)
        dep.extend(waits)
        val = self._bump(sem, 16)
        self._emit(queue, fn, dep, sem, 16)
        handle = (sem, val, "dma")
        for tok in writes:
            self.last_w[tok] = handle
            self.readers[tok] = []
        for tok in reads:
            self.readers[tok].append(handle)
        return handle

    def fence_tokens(self, queue, toks):
        waits = self._dep_waits("dma", toks, toks)
        self._emit(queue, lambda e: e.nop(), waits, None, 1)

    def barrier(self):
        with self.nc.Block(no_gpsimd_drain=True) as block:
            for eng in QUEUES:
                items = self.q.get(eng)
                if not items:
                    continue

                def runner(e, items=list(items)):
                    for it in items:
                        it(e)

                getattr(block, eng)(runner)
        self.q.clear()


def mm_F(sch, psum_ap, lhsT_ap, rhs_ap, start, stop, reads=(), writes=(),
         waits=(), fmax=512, tile_position=None, abs_off=0):
    NF = rhs_ap.shape[-1]
    cuts = [0]
    pos = 0
    while pos < NF:
        nxt = min(NF, pos + fmax - ((abs_off + pos) % fmax))
        cuts.append(nxt)
        pos = nxt
    h = None
    for i in range(len(cuts) - 1):
        lo, hi = cuts[i], cuts[i + 1]
        h = sch.op("tensor",
                   lambda t, lo=lo, hi=hi: t.matmul(
                       psum_ap[:, lo:hi], lhsT_ap, rhs_ap[:, lo:hi],
                       start=start, stop=stop, tile_position=tile_position),
                   reads=reads, writes=writes,
                   waits=waits if i == 0 else (),
                   inc=(stop and i == len(cuts) - 2))
    return h


class DBuf:

    def __init__(self, nc, es, sch, name, shape, dtype, nbuf=2, queue="gpsimd",
                 tiles=None, shared_sem=False):
        self.sch = sch
        self.queue = queue
        self.name = name
        if tiles is None:
            self.tiles = [es.enter_context(nc.sbuf_tensor(f"{name}_{i}", shape, dtype))
                          for i in range(nbuf)]
        else:
            self.tiles = tiles
        if shared_sem:
            s = sch.new_dma_sem(f"{name}_s")
            self.sems = [s] * nbuf
        else:
            self.sems = [sch.new_dma_sem(f"{name}_s{i}") for i in range(nbuf)]
        self.nbuf = nbuf

    def tok(self, idx):
        return (self.name, idx % self.nbuf)

    def load(self, idx, src_ap, reads=()):
        b = idx % self.nbuf
        tile = self.tiles[b]
        dst = tile[:] if hasattr(tile, "ap") else tile
        self.sch.dma(self.queue,
                     lambda e, dst=dst, src_ap=src_ap: e.dma_start(out=dst, in_=src_ap),
                     sem=self.sems[b], reads=list(reads), writes=[self.tok(idx)])
        return tile


def make_env(nc, es, sch):
    env = {"nc": nc}
    t = lambda name, shape, dt: es.enter_context(nc.sbuf_tensor(name, shape, dt))
    env["sbuf"] = t
    env["ones128"] = t("c_ones128", [128, 1], BF16)
    env["ones1"] = t("c_ones1", [1, 128], BF16)
    env["ones1f"] = t("c_ones1f", [1, 128], F32)
    NCONST = 16
    cst = t("c_fconsts", [128, NCONST], F32)
    env["_cst_tile"] = cst
    env["_cst_vals"] = []

    def const(value):
        vals = env["_cst_vals"]
        value = float(value)
        if value not in vals:
            assert len(vals) < NCONST
            vals.append(value)
            i = len(vals) - 1
            nc.const_aps.register(F32, value, cst[:, i:i + 1])
            sch.op("vector", lambda v, i=i, value=value: v.memset(
                cst[:, i:i + 1], value), writes=[("cst",)])
        return value

    env["const"] = const
    env["ln_ybf"] = [t(f"ln_ybf{i}", [128, S // 2], BF16) for i in range(2)]
    env["ln_ysq"] = [t(f"ln_ysq{i}", [128, S // 2], BF16) for i in range(2)]
    env["ln_rows"] = t("ln_rows", [1, 3 * (S // 2)], F32)
    env["ln_tnorm"] = t("ln_tnorm", [128, S // 2], F32)
    sch.op("vector", lambda v: v.memset(env["ones128"][:], 1.0), writes=["ones128"])
    sch.op("vector", lambda v: v.memset(env["ones1"][:], 1.0), writes=["ones1"])
    sch.op("vector", lambda v: v.memset(env["ones1f"][:], 1.0), writes=["ones1"])
    return env



def emit_layernorm(sch, env, y_chunks, y_toks, ps_rows, pb, uid,
                   out_f32=None, out_f32_toks=None, out_bf16=None,
                   out_bf16_toks=None, eps=1e-5, out_dma=None,
                   ps_tok=None, pb_tok=None):
    N = y_chunks[0].shape[-1]
    ybf, ysq = env["ln_ybf"], env["ln_ysq"]
    rows, tnorm = env["ln_rows"], env["ln_tnorm"]
    ones128, ones1 = env["ones128"], env["ones1"]
    r_mean, r_var, r_m2 = rows[:, 0:N], rows[:, N:2 * N], rows[:, 2 * N:3 * N]
    ps_sum, ps_sq = ps_rows[:, 0:N], ps_rows[:, N:2 * N]
    pb_m, pb_r = pb[:, 0:N], pb[:, N:2 * N]
    T_ps, T_pb = ps_tok or (uid, "ps_rows"), pb_tok or (uid, "pb")
    b_eps = env["const"](eps)
    s_half = env["const"](-0.5)

    for c in range(8):
        tb, tq = ("ln_ybf", c % 2), ("ln_ysq", c % 2)
        sch.op("vector", lambda v, c=c: v.tensor_copy(ybf[c % 2][:, 0:N], y_chunks[c]),
               reads=[y_toks[c]], writes=[tb])
        sch.op("vector", lambda v, c=c: v.tensor_tensor(
            ysq[c % 2][:, 0:N], ybf[c % 2][:, 0:N], ybf[c % 2][:, 0:N], op=OP.mult),
            reads=[tb], writes=[tq])
        mm_F(sch, ps_sum, ones128[:], ybf[c % 2][:, 0:N], start=(c == 0), stop=(c == 7),
             reads=[tb, "ones128"], writes=[T_ps])
        mm_F(sch, ps_sq, ones128[:], ysq[c % 2][:, 0:N], start=(c == 0), stop=(c == 7),
             reads=[tq, "ones128"], writes=[T_ps])

    sch.op("vector", lambda v: v.tensor_scalar(r_mean, ps_sum, 1.0 / D, None, op0=OP.mult),
           reads=[T_ps], writes=[("ln", "mean")])
    sch.op("vector", lambda v: v.tensor_tensor(r_m2, r_mean, r_mean, op=OP.mult),
           reads=[("ln", "mean")], writes=[("ln", "m2")])
    sch.op("vector", lambda v: v.scalar_tensor_tensor(r_var, ps_sq, 1.0 / D, r_m2,
                                                      op0=OP.mult, op1=OP.subtract),
           reads=[T_ps, ("ln", "m2")], writes=[("ln", "var")])
    sch.op("scalar", lambda s: s.activation(out=r_var, in_=r_var, func=AF.Ln, bias=b_eps),
           reads=[("ln", "var"), ("cst",)], writes=[("ln", "var")])
    sch.op("scalar", lambda s: s.activation(out=r_var, in_=r_var, func=AF.Exp, scale=s_half),
           reads=[("ln", "var")], writes=[("ln", "var")])

    mm_F(sch, pb_m, env["ones1f"][:], r_mean, start=True, stop=True,
         reads=[("ln", "mean"), "ones1"], writes=[T_pb])
    mm_F(sch, pb_r, env["ones1f"][:], r_var, start=True, stop=True,
         reads=[("ln", "var"), "ones1"], writes=[T_pb])

    fins = {}
    for c in range(8):
        tt = ("ln_tnorm",)
        sch.op("vector", lambda v, c=c: v.tensor_tensor(tnorm[:, 0:N], y_chunks[c], pb_m, op=OP.subtract),
               reads=[y_toks[c], T_pb], writes=[tt])
        if out_f32 is not None:
            dst, dtok = out_f32[c], out_f32_toks[c]
            sch.op("vector", lambda v, dst=dst: v.tensor_tensor(dst, tnorm[:, 0:N], pb_r, op=OP.mult),
                   reads=[tt, T_pb], writes=[dtok])
            src, stok = dst, dtok
        else:
            sch.op("vector", lambda v: v.tensor_tensor(tnorm[:, 0:N], tnorm[:, 0:N], pb_r, op=OP.mult),
                   reads=[tt, T_pb], writes=[tt])
            src, stok = tnorm[:, 0:N], tt
        fins[c] = stok
        if out_bf16 is not None:
            sch.op("vector", lambda v, c=c, src=src: v.tensor_copy(out_bf16[c], src),
                   reads=[stok], writes=[out_bf16_toks[c]])
        if out_dma is not None:
            dram_ap, dsem = out_dma(c)
            sch.dma("sync", lambda sy, src=src, dram_ap=dram_ap: sy.dma_start(out=dram_ap, in_=src),
                    sem=dsem, reads=[stok], writes=[(uid, "xout")])
    return fins



def emit_l0_ffn(nc, sch, es, env, xT, wg_blk, wu_blk, wd_blk, outT, st_sem,
                uid="f0", out_bf16=None, out_bf16_toks=None,
                x_bf=None, x_bf_toks=None, x_dep=(), pump=None):
    HID = 3072
    NHS = HID // 128
    NH = S // 2
    t = lambda name, shape, dt: es.enter_context(nc.sbuf_tensor(name, shape, dt))
    if x_bf is None:
        x_st = DBuf(nc, es, sch, f"{uid}_xbf", [128, S], BF16, nbuf=8, shared_sem=True)
        xbf = x_st.tiles
        xbf_toks = [x_st.tok(c) for c in range(8)]
        for c in range(8):
            x_st.load(c, xT[128 * c:128 * (c + 1), :], reads=x_dep)
    else:
        xbf, xbf_toks = x_bf, x_bf_toks
    H = [t(f"{uid}_H{i}", [128, NH], BF16) for i in range(NHS)]
    sg = [t(f"{uid}_sg{i}", [128, NH // 2], F32) for i in range(2)]
    y = [t(f"{uid}_y{c}", [128, NH], F32) for c in range(8)]
    wg_st = DBuf(nc, es, sch, f"{uid}_wg", [128, 8 * 128], BF16)
    wu_st = DBuf(nc, es, sch, f"{uid}_wu", [128, 8 * 128], BF16)
    wd_st = DBuf(nc, es, sch, f"{uid}_wd", [128, NHS * 64], BF16)
    xf_st = DBuf(nc, es, sch, f"{uid}_xf", [128, NH], F32, queue="sync")

    for half in range(2):
        hlo = half * NH
        with contextlib.ExitStack() as pes:
            p = lambda name, shape: pes.enter_context(nc.psum_tensor(name, shape, F32))
            ps_g = [p(f"{uid}_ps_g{half}_{i}", [128, NH // 2]) for i in range(2)]
            ps_u = [p(f"{uid}_ps_u{half}_{i}", [128, NH // 2]) for i in range(2)]

            def load_gu(hs):
                wg_st.load(hs, wg_blk[hs].rearrange("p c h -> p (c h)"))
                wu_st.load(hs, wu_blk[hs].rearrange("p c h -> p (c h)"))

            load_gu(0)
            load_gu(1)
            for hs in range(NHS):
                if pump:
                    pump()
                b = hs % 2
                wg_b, wu_b = wg_st.tiles[b], wu_st.tiles[b]
                for nh in range(2):
                    nlo = hlo + nh * (NH // 2)
                    for c in range(8):
                        mm_F(sch, ps_g[nh][:], wg_b[:, 128 * c:128 * (c + 1)],
                             xbf[c][:, nlo:nlo + NH // 2], start=(c == 0), stop=(c == 7),
                             reads=[wg_st.tok(hs), xbf_toks[c]],
                             writes=[(uid, "ps_g", half, nh)])
                    for c in range(8):
                        mm_F(sch, ps_u[nh][:], wu_b[:, 128 * c:128 * (c + 1)],
                             xbf[c][:, nlo:nlo + NH // 2], start=(c == 0), stop=(c == 7),
                             reads=[wu_st.tok(hs), xbf_toks[c]],
                             writes=[(uid, "ps_u", half, nh)])
                    s127 = env["const"](1.0 / 127.0)
                    sch.op("scalar", lambda s_, nh=nh, s127=s127: s_.activation(
                        out=sg[nh][:], in_=ps_g[nh][:], func=AF.Silu, scale=s127),
                        reads=[(uid, "ps_g", half, nh), ("cst",)],
                        writes=[(uid, "sg", nh)])
                    sch.op("vector", lambda v_, hs=hs, nh=nh: v_.scalar_tensor_tensor(
                        H[hs][:, nh * (NH // 2):(nh + 1) * (NH // 2)], ps_u[nh][:],
                        1.0 / 127.0, sg[nh][:], op0=OP.mult, op1=OP.mult),
                        reads=[(uid, "ps_u", half, nh), (uid, "sg", nh)],
                        writes=[(uid, "H", hs)])
                if hs + 2 < NHS:
                    load_gu(hs + 2)
            sch.barrier()

        with contextlib.ExitStack() as pes:
            p = lambda name, shape: pes.enter_context(nc.psum_tensor(name, shape, F32))
            ps_d = [p(f"{uid}_ps_d{half}_{i}", [128, NH]) for i in range(2)]

            HH = NHS // 2

            def load_dh(idx):
                os, hh = idx // 2, idx % 2
                wd_st.load(idx, wd_blk[os][:, HH * hh:HH * (hh + 1), :]
                           .rearrange("p c h -> p (c h)"))

            def load_x(os):
                xf_st.load(os, xT[128 * os:128 * (os + 1), hlo:hlo + NH], reads=x_dep)

            load_dh(0)
            load_dh(1)
            load_x(0)
            load_x(1)
            for os in range(8):
                b = os % 2
                xf = xf_st.tiles[b]
                for hc in range(NHS):
                    widx = 2 * os + hc // HH
                    wd_b = wd_st.tiles[widx % 2]
                    mm_F(sch, ps_d[b][:], wd_b[:, 128 * (hc % HH):128 * (hc % HH + 1)],
                         H[hc][:], start=(hc == 0), stop=(hc == NHS - 1),
                         reads=[wd_st.tok(widx), (uid, "H", hc)],
                         writes=[(uid, "ps_d", half, b)])
                    if hc == HH and 2 * os + 2 < 16:
                        load_dh(2 * os + 2)
                sch.op("vector", lambda v_, os=os, b=b, xf=xf: v_.scalar_tensor_tensor(
                    y[os][:], ps_d[b][:], 1.0 / 127.0, xf[:],
                    op0=OP.mult, op1=OP.add),
                    reads=[(uid, "ps_d", half, b), xf_st.tok(os)],
                    writes=[(uid, "y", os)])
                if pump:
                    pump()
                if 2 * os + 3 < 16:
                    load_dh(2 * os + 3)
                if os + 2 < 8:
                    load_x(os + 2)
            sch.barrier()

        with contextlib.ExitStack() as pes:
            ps_rows = pes.enter_context(nc.psum_tensor(f"{uid}_lnr{half}", [1, 2 * NH], F32)).ap()
            pb = pes.enter_context(nc.psum_tensor(f"{uid}_lnb{half}", [128, 2 * NH], F32)).ap()
            ob = [out_bf16[c][:, hlo:hlo + NH] for c in range(8)] if out_bf16 else None
            obt = ([(tok, half) for tok in out_bf16_toks] if out_bf16 else None)
            emit_layernorm(sch, env, [y[c][:] for c in range(8)],
                           [(uid, "y", c) for c in range(8)], ps_rows, pb, uid,
                           out_bf16=ob, out_bf16_toks=obt,
                           out_dma=lambda c: (outT[128 * c:128 * (c + 1), hlo:hlo + NH], st_sem))
            sch.barrier()


def build_l0_ffn_standalone():
    nc = bass.Bass("TRN2", target_bir_lowering=False)
    NHS = 24
    xT = nc.dram_tensor("xT", [D, S], F32, kind="ExternalInput").ap()
    wg_blk = nc.dram_tensor("wg_blk", [NHS, 128, 8, 128], I32, kind="ExternalInput").ap()
    wu_blk = nc.dram_tensor("wu_blk", [NHS, 128, 8, 128], I32, kind="ExternalInput").ap()
    wd_blk = nc.dram_tensor("wd_blk", [8, 128, NHS, 128], I32, kind="ExternalInput").ap()
    outT = nc.dram_tensor("outT", [D, S], F32, kind="ExternalOutput").ap()
    es = contextlib.ExitStack()
    with es:
        sch = Sched(nc, es)
        env = make_env(nc, es, sch)
        st_sem = es.enter_context(nc.semaphore("sem_st"))
        emit_l0_ffn(nc, sch, es, env, xT, wg_blk, wu_blk, wd_blk, outT, st_sem)
        sch._emit("sync", lambda e: e.nop(), [(st_sem, sch.count(st_sem))], None, 1)
        sch.barrier()
    return nc


def prep_l0_ffn_inputs(x_bt, w_gate, w_up, w_down):
    xT = np.ascontiguousarray(x_bt.T)
    wg_blk = np.ascontiguousarray(
        w_gate.T.reshape(8, 128, 24, 128).transpose(2, 1, 0, 3))
    wu_blk = np.ascontiguousarray(
        w_up.T.reshape(8, 128, 24, 128).transpose(2, 1, 0, 3))
    wd_blk = np.ascontiguousarray(
        w_down.T.reshape(24, 128, 8, 128).transpose(2, 1, 0, 3))
    return {"xT": xT, "wg_blk": wg_blk, "wu_blk": wu_blk, "wd_blk": wd_blk}



def emit_l1_ffn(nc, sch, es, env, xT, w1_blk, w2_blk, h1_blk, h2_blk, w3_blk,
                scl, outT, st_sem, uid="f1", out_bf16=None, out_bf16_toks=None,
                x_dep=(), pump=None):
    DH = 4096
    NHS = DH // 128
    NH = S // 2
    NQ = NH // 2
    t = lambda name, shape, dt: es.enter_context(nc.sbuf_tensor(name, shape, dt))
    A = [t(f"{uid}_A{i}", [128, NH], BF16) for i in range(NHS)]
    B = [t(f"{uid}_B{i}", [128, NH], BF16) for i in range(NHS)]
    sg = [t(f"{uid}_sg{i}", [128, NH // 2], F32) for i in range(2)]
    scl_sb = t(f"{uid}_scl", [128, 4 * NHS + 8], F32)
    s1_sb = scl_sb[:, 0:NHS]
    s2_sb = scl_sb[:, NHS:2 * NHS]
    h1s_sb = scl_sb[:, 2 * NHS:3 * NHS]
    h2s_sb = scl_sb[:, 3 * NHS:4 * NHS]
    w3s_sb = scl_sb[:, 4 * NHS:4 * NHS + 8]
    x_st = DBuf(nc, es, sch, f"{uid}_xbf", [128, NH], BF16, nbuf=8, shared_sem=True)
    y = [B[c].ap().bitcast(F32)[:, 0:NQ] for c in range(8)]
    w1_st = DBuf(nc, es, sch, f"{uid}_w1", [128, 8 * 128], BF16)
    w2_st = DBuf(nc, es, sch, f"{uid}_w2", [128, 8 * 128], BF16)
    hw_st = DBuf(nc, es, sch, f"{uid}_hw", [128, NHS * 128], BF16)
    xf_st = DBuf(nc, es, sch, f"{uid}_xf", [128, NQ], F32, queue="sync")
    ssem = sch.new_dma_sem(f"{uid}_ssem")
    sch.dma("sync", lambda sy: sy.dma_start(out=scl_sb[:], in_=scl),
            sem=ssem, writes=[(uid, "scl")])

    f_pes = contextlib.ExitStack()
    psA = [f_pes.enter_context(nc.psum_tensor(f"{uid}_psA{i}", [128, NH // 2], F32))
           for i in range(4)]
    psB = [f_pes.enter_context(nc.psum_tensor(f"{uid}_psB{i}", [128, NH], F32))
           for i in range(2)]
    TPA = [(uid, "psA", i) for i in range(4)]
    TPB = [(uid, "psB", i) for i in range(2)]
    hw_idx = [0]

    def hw_load(blk_row):
        i = hw_idx[0]
        hw_idx[0] += 1
        hw_st.load(i, blk_row.rearrange("p c h -> p (c h)"))
        return hw_st.tiles[i % 2], hw_st.tok(i)

    for half in range(2):
        hlo = half * NH
        for c in range(8):
            x_st.load(c + 8 * half, xT[128 * c:128 * (c + 1), hlo:hlo + NH], reads=x_dep)
        xbf = x_st.tiles
        xbf_toks = [x_st.tok(c) for c in range(8)]

        if True:
            ps_g, ps_u = psA[0:2], psA[2:4]

            def load_gu(hs):
                w1_st.load(hs, w1_blk[hs].rearrange("p c h -> p (c h)"))
                w2_st.load(hs, w2_blk[hs].rearrange("p c h -> p (c h)"))

            load_gu(0)
            load_gu(1)
            for hs in range(NHS):
                if pump:
                    pump()
                b = hs % 2
                for nh in range(2):
                    nlo = nh * (NH // 2)
                    for c in range(8):
                        mm_F(sch, ps_g[nh][:], w1_st.tiles[b][:, 128 * c:128 * (c + 1)],
                             xbf[c][:, nlo:nlo + NH // 2], start=(c == 0), stop=(c == 7),
                             reads=[w1_st.tok(hs), xbf_toks[c]],
                             writes=[TPA[nh]])
                    for c in range(8):
                        mm_F(sch, ps_u[nh][:], w2_st.tiles[b][:, 128 * c:128 * (c + 1)],
                             xbf[c][:, nlo:nlo + NH // 2], start=(c == 0), stop=(c == 7),
                             reads=[w2_st.tok(hs), xbf_toks[c]],
                             writes=[TPA[2 + nh]])
                    sch.op("scalar", lambda s_, nh=nh, hs=hs: s_.activation(
                        out=sg[nh][:], in_=ps_g[nh][:], func=AF.Silu,
                        scale=s1_sb[:, hs:hs + 1]),
                        reads=[TPA[nh], (uid, "scl")],
                        writes=[(uid, "sg", nh)])
                    sch.op("vector", lambda v_, hs=hs, nh=nh, nlo=nlo: v_.scalar_tensor_tensor(
                        A[hs][:, nlo:nlo + NH // 2], ps_u[nh][:],
                        s2_sb[:, hs:hs + 1], sg[nh][:], op0=OP.mult, op1=OP.mult),
                        reads=[TPA[2 + nh], (uid, "sg", nh), (uid, "scl")],
                        writes=[(uid, "A", hs)])
                if hs + 2 < NHS:
                    load_gu(hs + 2)

        for phase, (w_blk, s_sb, s_name, src, src_tokname, dst, dst_tokname) in enumerate(
                ((h1_blk, h1s_sb, "h1s", A, "A", B, "B"),
                 (h2_blk, h2s_sb, "h2s", B, "B", A, "A"))):
            if True:
                ps_b = psB
                tiles = {}
                tiles[0] = hw_load(w_blk[0])
                tiles[1] = hw_load(w_blk[1])
                for hs in range(NHS):
                    wtile, wtok = tiles[hs]
                    b = hs % 2
                    for c in range(NHS):
                        mm_F(sch, ps_b[b][:], wtile[:, 128 * c:128 * (c + 1)],
                             src[c][:], start=(c == 0), stop=(c == NHS - 1),
                             reads=[wtok, (uid, src_tokname, c)],
                             writes=[TPB[b]])
                    wl = [(uid, dst_tokname, hs)]
                    if dst_tokname == "B" and hs < 8:
                        wl.append((uid, "y", hs))
                    sch.op("scalar", lambda s_, hs=hs, b=b, dst=dst, s_sb=s_sb: s_.activation(
                        out=dst[hs][:], in_=ps_b[b][:], func=AF.Gelu,
                        scale=s_sb[:, hs:hs + 1]),
                        reads=[TPB[b], (uid, "scl", s_name)],
                        writes=wl)
                    if hs + 2 < NHS:
                        tiles[hs + 2] = hw_load(w_blk[hs + 2])

        for q in range(2):
            qlo = hlo + q * NQ
            if True:
                ps_d = psA[0:2]

                def load_d(os):
                    wtile, wtok = hw_load(w3_blk[os])
                    xf_st.load(os, xT[128 * os:128 * (os + 1), qlo:qlo + NQ], reads=x_dep)
                    return wtile, wtok

                dtiles = {0: load_d(0), 1: load_d(1)}
                for os in range(8):
                    wtile, wtok = dtiles[os]
                    b = os % 2
                    for hc in range(NHS):
                        mm_F(sch, ps_d[b][:], wtile[:, 128 * hc:128 * (hc + 1)],
                             A[hc][:, q * NQ:(q + 1) * NQ], start=(hc == 0),
                             stop=(hc == NHS - 1),
                             reads=[wtok, (uid, "A", hc)],
                             writes=[TPA[b]])
                    sch.op("vector", lambda v_, os=os, b=b: v_.scalar_tensor_tensor(
                        y[os], ps_d[b][:], w3s_sb[:, os:os + 1], xf_st.tiles[b][:],
                        op0=OP.mult, op1=OP.add),
                        reads=[TPA[b], xf_st.tok(os), (uid, "scl")],
                        writes=[(uid, "y", os), (uid, "B", os)])
                    if os + 2 < 8:
                        dtiles[os + 2] = load_d(os + 2)
            if True:
                ps_rows = psB[1].ap()[0:1, 0:2 * NQ]
                pb = psB[0].ap()[:, 0:2 * NQ]
                ob = [out_bf16[c][:, qlo:qlo + NQ] for c in range(8)] if out_bf16 else None
                obt = ([(tok, half, q) for tok in out_bf16_toks] if out_bf16 else None)
                emit_layernorm(sch, env, [y[c] for c in range(8)],
                               [(uid, "y", c) for c in range(8)], ps_rows, pb, uid,
                               out_bf16=ob, out_bf16_toks=obt,
                               out_dma=lambda c: (outT[128 * c:128 * (c + 1), qlo:qlo + NQ], st_sem),
                               ps_tok=TPB[1], pb_tok=TPB[0])
    f_pes.close()


def build_l1_ffn_standalone():
    nc = bass.Bass("TRN2", target_bir_lowering=False)
    NHS = 32
    xT = nc.dram_tensor("xT", [D, S], F32, kind="ExternalInput").ap()
    w1_blk = nc.dram_tensor("w1_blk", [NHS, 128, 8, 128], F32, kind="ExternalInput").ap()
    w2_blk = nc.dram_tensor("w2_blk", [NHS, 128, 8, 128], F32, kind="ExternalInput").ap()
    h1_blk = nc.dram_tensor("h1_blk", [NHS, 128, NHS, 128], F32, kind="ExternalInput").ap()
    h2_blk = nc.dram_tensor("h2_blk", [NHS, 128, NHS, 128], F32, kind="ExternalInput").ap()
    w3_blk = nc.dram_tensor("w3_blk", [8, 128, NHS, 128], F32, kind="ExternalInput").ap()
    scl = nc.dram_tensor("scl", [128, 4 * NHS + 8], F32, kind="ExternalInput").ap()[:, :]
    outT = nc.dram_tensor("outT", [D, S], F32, kind="ExternalOutput").ap()
    es = contextlib.ExitStack()
    with es:
        sch = Sched(nc, es)
        env = make_env(nc, es, sch)
        st_sem = es.enter_context(nc.semaphore("sem_st"))
        emit_l1_ffn(nc, sch, es, env, xT, w1_blk, w2_blk, h1_blk, h2_blk, w3_blk,
                    scl, outT, st_sem)
        sch._emit("sync", lambda e: e.nop(), [(st_sem, sch.count(st_sem))], None, 1)
        sch.barrier()
    return nc


def prep_l1_ffn_inputs(x_bt, w1w, w1s, w2w, w2s, h1w, h1s, h2w, h2s, w3w, w3s):
    xT = np.ascontiguousarray(x_bt.T)
    blk = lambda w, nin, nout: np.ascontiguousarray(
        w.T.reshape(nin // 128, 128, nout // 128, 128).transpose(2, 1, 0, 3))
    fm = lambda s: np.ascontiguousarray(s.reshape(-1, 128).T)
    scl = np.concatenate([fm(w1s), fm(w2s), fm(h1s), fm(h2s), fm(w3s)], axis=1)
    return {"xT": xT,
            "w1_blk": blk(w1w, 1024, 4096), "w2_blk": blk(w2w, 1024, 4096),
            "h1_blk": blk(h1w, 4096, 4096), "h2_blk": blk(h2w, 4096, 4096),
            "w3_blk": blk(w3w, 4096, 1024),
            "scl": np.ascontiguousarray(scl)}



def emit_l0_attn(nc, sch, es, env, xT, wq_blk, wk_blk, wv_blk, wo_blk,
                 biases, lqk, hnw, cs_tab_h, ss_tab_h, perm_h, ut_h, iden_h,
                 v_scr, outT, st_sem, uid="a0", out_bf16=None, out_bf16_toks=None,
                 debug_afm=False, pump=None):
    t = lambda name, shape, dt: es.enter_context(nc.sbuf_tensor(name, shape, dt))
    NHEAD = 16
    cs_tab = t(f"{uid}_cs", [128, S], F32)
    ss_tab = t(f"{uid}_ss", [128, S], F32)
    perm = t(f"{uid}_perm", [128, 128], F32)
    ut = t(f"{uid}_ut", [128, 128], BF16)
    iden = t(f"{uid}_iden", [128, 128], BF16)
    bias_sb = t(f"{uid}_bias", [128, 48], F32)
    hnw_sb = t(f"{uid}_hnw", [64, 16], F32)
    hnw08 = t(f"{uid}_hnw08", [64, 16], F32)
    lqk_sb = t(f"{uid}_lqk", [64, 64], F32)
    lqk_prod = t(f"{uid}_lqkp", [64, 32], BF16)
    lamt = t(f"{uid}_lam", [1, 64], F32)
    attn_fm = [t(f"{uid}_afm{c}", [128, S], BF16) for c in range(8)]
    aug = [t(f"{uid}_aug{i}", [128, S], BF16) for i in range(8)]
    y = [aug[c].ap().bitcast(F32) for c in range(8)]
    qpre = [t(f"{uid}_qpre{i}", [128, 1024], F32) for i in range(2)]
    tcs = t(f"{uid}_tcs", [128, 1024], F32)
    uss = t(f"{uid}_uss", [128, 1024], F32)
    vstage = t(f"{uid}_vstage", [128, 1024], BF16)
    sqt = t(f"{uid}_sqt", [64, 1024], BF16)
    E = [t(f"{uid}_E{i}", [128, 1024], BF16) for i in range(2)]
    bsq = E[0][0:64, 0:1024]
    o12 = [t(f"{uid}_o{i}", [64, 1024], BF16) for i in range(2)]
    comb = t(f"{uid}_comb", [64, 1024], F32)
    combt = t(f"{uid}_combt", [64, 1024], F32)

    rows = t(f"{uid}_rows", [1, 1032], F32)

    rowsb = t(f"{uid}_rowsb", [1, 2048], F32)
    rbh = t(f"{uid}_rbh", [1, 2048], BF16)
    sc16 = t(f"{uid}_sc16", [1, 16], F32)
    fint = comb.ap().bitcast(BF16)[0:64, 0:1024]
    v_aug = [t(f"{uid}_vaug{i}", [128, 16 * 80], BF16) for i in range(2)]
    wq_st = DBuf(nc, es, sch, f"{uid}_wq", [128, 8 * 128], BF16)
    wk_st = DBuf(nc, es, sch, f"{uid}_wk", [128, 8 * 128], BF16)
    wv_st = DBuf(nc, es, sch, f"{uid}_wv", [128, 8 * 128], BF16)
    x_st = DBuf(nc, es, sch, f"{uid}_xbf", [128, S], BF16, nbuf=8, shared_sem=True)

    vssem = sch.new_dma_sem(f"{uid}_vs")
    vtsem = [sch.new_dma_sem(f"{uid}_vt{i}") for i in range(2)]

    SQ8 = env["const"](0.125)

    for c in range(8):
        x_st.load(c, xT[128 * c:128 * (c + 1), :])
    xbf, xbf_toks = x_st.tiles, [x_st.tok(c) for c in range(8)]
    for i, (dst, srca, tok, q) in enumerate((
            (cs_tab, cs_tab_h, "cs", "sync"), (ss_tab, ss_tab_h, "ss", "sync"),
            (perm, perm_h, "perm", "sync"), (ut, ut_h, "ut", "gpsimd"),
            (iden, iden_h, "iden", "gpsimd"), (bias_sb, biases, "bias", "sync"),
            (hnw_sb, hnw, "hnw", "sync"), (lqk_sb, lqk, "lqk", "sync"))):
        csem_i = sch.new_dma_sem(f"{uid}_cs{i}")
        sch.dma(q, lambda e, dst=dst, srca=srca: e.dma_start(out=dst[:], in_=srca),
                sem=csem_i, writes=[(uid, tok)])
    sch.op("vector", lambda v: v.tensor_scalar(hnw08[:], hnw_sb[:], 0.8, None, op0=OP.mult),
           reads=[(uid, "hnw")], writes=[(uid, "hnw08")])
    sch.op("vector", lambda v: v.tensor_tensor(lqk_prod[:, 0:16], lqk_sb[:, 0:16],
                                               lqk_sb[:, 16:32], op=OP.mult),
           reads=[(uid, "lqk")], writes=[(uid, "p12")])
    sch.op("vector", lambda v: v.tensor_tensor(lqk_prod[:, 16:32], lqk_sb[:, 32:48],
                                               lqk_sb[:, 48:64], op=OP.mult),
           reads=[(uid, "lqk")], writes=[(uid, "p12")])
    with contextlib.ExitStack() as pes0:
        ps0 = pes0.enter_context(nc.psum_tensor(f"{uid}_ps0", [1, 32], F32))
        mm_F(sch, ps0[:], env["ones128"][0:64, :], lqk_prod[:],
             start=True, stop=True, reads=[(uid, "p12"), "ones128"],
             writes=[(uid, "ps0")])
        sch.op("scalar", lambda s: s.activation(out=lamt[:, 32:64], in_=ps0[:], func=AF.Exp),
               reads=[(uid, "ps0")], writes=[(uid, "lame")])
    sch.op("vector", lambda v: v.tensor_tensor(lamt[:, 0:16], lamt[:, 32:48],
                                               lamt[:, 48:64], op=OP.subtract),
           reads=[(uid, "lame")], writes=[(uid, "lamf")])
    sch.op("vector", lambda v: v.tensor_scalar(lamt[:, 0:16], lamt[:, 0:16], 0.2, None, op0=OP.add),
           reads=[(uid, "lamf")], writes=[(uid, "lamf")])
    lam_row = lamt[:, 0:16]

    for j in range(8):
        sch.op("vector", lambda v, j=j: v.memset(aug[j][64:128, :], 0.0),
               writes=[(uid, "augpad"), (uid, "aug_ones", j)])
        if j % 4 >= 2:
            sch.op("vector", lambda v, j=j: v.memset(aug[j][64:65, :], 1.0),
                   writes=[(uid, "aug_ones", j)], reads=[(uid, "augpad")])
    sch.barrier()

    g_pes = contextlib.ExitStack()
    psA = g_pes.enter_context(nc.psum_tensor(f"{uid}_psA", [128, 1024], F32))
    psB = g_pes.enter_context(nc.psum_tensor(f"{uid}_psB", [128, 1024], F32))
    psC = g_pes.enter_context(nc.psum_tensor(f"{uid}_psC", [128, S], F32))
    TA, TB, TC = (uid, "psA"), (uid, "psB"), (uid, "psC")
    TOKP = [TA, TB]
    DEFER_EPI = 1
    pending = [None]
    pend_shift = [None]

    def epi(h, qh, defer_shift=False):
        sch.op("vector", lambda v: v.tensor_copy(o12[0][:], p1[0:64, :]),
               reads=[TC], writes=[(uid, "o1")])
        sch.op("vector", lambda v: v.tensor_copy(o12[1][:], p2[0:64, :]),
               reads=[TC], writes=[(uid, "o2")])
        sch.op("scalar", lambda s_: s_.activation(out=rowsb[:, 0:2048], in_=psC[64:65, 0:2048], func=AF.Ln),
               reads=[TC], writes=[(uid, "rowB")])
        sch.op("scalar", lambda s_: s_.activation(out=rbh[:, 0:2048], in_=rowsb[:, 0:2048],
                                                  func=AF.Exp, scale=-1.0),
               reads=[(uid, "rowB"), ("cst",)], writes=[(uid, "rowB")])
        sch.op("vector", lambda v, h=h: v.tensor_scalar(
            rbh[:, 1024:2048], rbh[:, 1024:2048], lam_row[:, h:h + 1], None, op0=OP.mult),
            reads=[(uid, "rowB"), (uid, "lamf")], writes=[(uid, "rowB")])
        mm_F(sch, psC[0:64, 0:1024], env["ones1"][:, 0:64], rbh[:, 0:1024],
             start=True, stop=True, reads=[(uid, "rowB"), "ones1"],
             writes=[TC])
        mm_F(sch, psC[0:64, 1024:2048], env["ones1"][:, 0:64], rbh[:, 1024:2048],
             start=True, stop=True, reads=[(uid, "rowB"), "ones1"],
             writes=[TC])
        sch.op("vector", lambda v: v.tensor_tensor(comb[:], o12[0][:], psC[0:64, 0:1024], op=OP.mult),
               reads=[(uid, "o1"), TC], writes=[(uid, "comb")])
        sch.op("vector", lambda v: v.tensor_tensor(combt[:], o12[1][:], psC[0:64, 1024:2048], op=OP.mult),
               reads=[(uid, "o2"), TC], writes=[(uid, "combt")])
        sch.op("vector", lambda v: v.tensor_tensor(comb[:], comb[:], combt[:], op=OP.subtract),
               reads=[(uid, "combt"), (uid, "comb")], writes=[(uid, "comb")])
        sch.op("scalar", lambda s_: s_.activation(out=sqt[:, :], in_=comb[:], func=AF.Square),
               reads=[(uid, "comb")], writes=[(uid, "sqt")])
        mm_F(sch, psC[0:1, 0:1024], env["ones128"][0:64, :], sqt[:, :],
             start=True, stop=True, reads=[(uid, "sqt"), "ones128"],
             writes=[TC])
        c164 = env["const"](1.0 / 64.0)
        c1em6 = env["const"](1e-6)
        sch.op("scalar", lambda s_, c164=c164, c1em6=c1em6: s_.activation(
            out=rows[:, 0:1024], in_=psC[0:1, 0:1024], func=AF.Ln, scale=c164, bias=c1em6),
            reads=[TC, ("cst",)], writes=[(uid, "rowA")])
        s_half = env["const"](-0.5)
        sch.op("scalar", lambda s_, s_half=s_half: s_.activation(
            out=rbh[:, 0:1024], in_=rows[:, 0:1024], func=AF.Exp, scale=s_half),
            reads=[(uid, "rowA")], writes=[(uid, "rowB")])
        mm_F(sch, psC[0:64, 1024:2048], env["ones1"][:, 0:64], rbh[:, 0:1024],
             start=True, stop=True, reads=[(uid, "rowB"), "ones1"],
             writes=[TC])
        sch.op("vector", lambda v: v.tensor_tensor(combt[:], comb[:], psC[0:64, 1024:2048], op=OP.mult),
               reads=[(uid, "comb"), TC], writes=[(uid, "combt")])
        cdst = attn_fm[h // 2]
        if h % 2 == 0:
            sch.op("vector", lambda v, cdst=cdst, h=h, qh=qh: v.tensor_scalar(
                cdst[0:64, 1024 * qh:1024 * (qh + 1)], combt[:], hnw08[:, h:h + 1], None, op0=OP.mult),
                reads=[(uid, "combt"), (uid, "hnw08")], writes=[(uid, "afm", h // 2)])
            return None
        sch.op("vector", lambda v, h=h: v.tensor_scalar(
            fint, combt[:], hnw08[:, h:h + 1], None, op0=OP.mult),
            reads=[(uid, "combt"), (uid, "hnw08")], writes=[(uid, "fint")])

        def shift():
            mm_F(sch, ps_s[1][64:128, :], iden[0:64, 0:64], fint,
                 start=True, stop=True, reads=[(uid, "fint"), (uid, "iden")],
                 writes=[TB], tile_position=(0, 64))
            sch.op("vector", lambda v, cdst=cdst, qh=qh: v.tensor_copy(
                cdst[64:128, 1024 * qh:1024 * (qh + 1)], ps_s[1][64:128, :]),
                reads=[TB], writes=[(uid, "afm", h // 2)])
        if defer_shift:
            return shift
        shift()
        return None

    for g in range(8):
        if pending[0] is not None:
            ep, pending[0] = pending[0], None
            epi(*ep)
        if pump:
            pump()
        heads = (2 * g, 2 * g + 1)
        if True:
            ps_p = [psA, psB]
            ps_pm = psC

            def proj_rope(st, blk, si, bias_col, scale, dst1, dst1_tok, dst2,
                          dst2_tok, widx):
                wtile = st.load(widx, blk[si].rearrange("p c h -> p (c h)"))
                for nh in range(2):
                    slot = nh
                    qp, qtok = qpre[slot], (uid, "qpre", slot)
                    for c in range(8):
                        mm_F(sch, ps_p[slot][:], wtile[:, 128 * c:128 * (c + 1)],
                             xbf[c][:, 1024 * nh:1024 * (nh + 1)],
                             start=(c == 0), stop=(c == 7),
                             reads=[st.tok(widx), xbf_toks[c]],
                             writes=[TOKP[slot]])
                    sch.op("vector", lambda v_, slot=slot, qp=qp, bias_col=bias_col, scale=scale:
                           v_.tensor_scalar(qp[:], ps_p[slot][:],
                                            bias_col, scale, op0=OP.add, op1=OP.mult),
                           reads=[TOKP[slot], (uid, "bias")],
                           writes=[qtok])
                    cs_sl = cs_tab[:, 1024 * nh:1024 * (nh + 1)]
                    ss_sl = ss_tab[:, 1024 * nh:1024 * (nh + 1)]
                    pm_sl = ps_pm[:, 1024 * nh:1024 * (nh + 1)]
                    mm_F(sch, pm_sl, perm[:], qp[:], start=True, stop=True,
                         reads=[qtok, (uid, "perm")], writes=[TC],
                         abs_off=1024 * nh)
                    sch.op("vector", lambda v, qp=qp, cs_sl=cs_sl: v.tensor_tensor(
                        tcs[:], qp[:], cs_sl, op=OP.mult),
                        reads=[qtok, (uid, "cs")], writes=[(uid, "tcs")])
                    sch.op("vector", lambda v, pm_sl=pm_sl, ss_sl=ss_sl: v.tensor_tensor(
                        uss[:], pm_sl, ss_sl, op=OP.mult),
                        reads=[TC, (uid, "ss")], writes=[(uid, "uss")])
                    sch.op("vector", lambda v, dst1=dst1, nh=nh: v.tensor_tensor(
                        dst1[0:64, 1024 * nh:1024 * (nh + 1)], tcs[0:64, :], uss[0:64, :], op=OP.add),
                        reads=[(uid, "tcs"), (uid, "uss")], writes=[dst1_tok])
                    sch.op("vector", lambda v, dst2=dst2, nh=nh: v.tensor_tensor(
                        dst2[0:64, 1024 * nh:1024 * (nh + 1)], tcs[64:128, :], uss[64:128, :], op=OP.add),
                        reads=[(uid, "tcs"), (uid, "uss")], writes=[dst2_tok])

            for j, h in enumerate(heads):
                proj_rope(wq_st, wq_blk, h, bias_sb[:, h:h + 1], SQ8,
                          aug[4 * j + 0], (uid, "q1a", j), aug[4 * j + 1], (uid, "q2a", j), h)
                proj_rope(wk_st, wk_blk, h, bias_sb[:, 16 + h:17 + h], 1.0,
                          aug[4 * j + 2], (uid, "k1a", j), aug[4 * j + 3], (uid, "k2a", j), h)
            vtile = wv_st.load(g, wv_blk[g].rearrange("p c h -> p (c h)"))
            for nh in range(2):
                slot = nh
                for c in range(8):
                    mm_F(sch, ps_p[slot][:], vtile[:, 128 * c:128 * (c + 1)],
                         xbf[c][:, 1024 * nh:1024 * (nh + 1)], start=(c == 0), stop=(c == 7),
                         reads=[wv_st.tok(g), xbf_toks[c]], writes=[TOKP[slot]])
                sch.op("vector", lambda v_, slot=slot, g=g: v_.tensor_scalar(
                    vstage[:], ps_p[slot][:],
                    bias_sb[:, 32 + g:33 + g], None, op0=OP.add),
                    reads=[TOKP[slot], (uid, "bias")], writes=[(uid, "vstage")])
                sch.dma("sync", lambda sy, g=g, nh=nh: sy.dma_start(
                    out=v_scr[128 * g:128 * (g + 1), 1024 * nh:1024 * (nh + 1)], in_=vstage[:]),
                    sem=vssem, reads=[(uid, "vstage")], writes=[(uid, "vscr", g)])

        if True:
            ps_s = [psA, psB]
            p1 = psC[0:65, 0:1024]
            p2 = psC[0:65, 1024:2048]
            for j, h in enumerate(heads):
                q1a, q2a, k1a, k2a = aug[4 * j], aug[4 * j + 1], aug[4 * j + 2], aug[4 * j + 3]
                tq1, tq2 = (uid, "q1a", j), (uid, "q2a", j)
                tk1, tk2 = (uid, "k1a", j), (uid, "k2a", j)
                va = v_aug[j]
                ones_cols = va.ap().rearrange("p (k c) -> p k c", c=80)[:, :, 64:65]
                tva = [(uid, "vaug1", j)] + [(uid, "vaugd", j, kc) for kc in range(16)]
                sch.op("vector", lambda v, ones_cols=ones_cols: v.memset(ones_cols, 1.0),
                       writes=[(uid, "vaug1", j)])
                for kc in range(16):
                    sch.dma("sync", lambda sy, va=va, kc=kc, h=h: sy.dma_start(
                        out=va[:, 80 * kc:80 * kc + 64],
                        in_=v_scr[64 * h:64 * h + 64, 128 * kc:128 * (kc + 1)],
                        transpose=True),
                        sem=vtsem[j], reads=[(uid, "vscr", g)], writes=[(uid, "vaugd", j, kc)])
                M2 = {}
                for idx, (ka, tk) in enumerate(((k1a, tk1), (k2a, tk2))):
                    for piece in range(2):
                        sch.op("scalar", lambda s_, ka=ka, piece=piece: s_.activation(
                            out=bsq, in_=ka[0:64, 1024 * piece:1024 * (piece + 1)],
                            func=AF.Square),
                            reads=[tk], writes=[(uid, "E", 0)])
                        mm_F(sch, ps_s[piece][0:1, :], env["ones128"][0:64, :],
                             bsq, start=True, stop=True,
                             reads=[(uid, "E", 0), "ones128"], writes=[TOKP[piece]])
                        sch.op("vector", lambda v, piece=piece, idx=idx: v.tensor_reduce(
                            rows[:, 1024 + 2 * idx + piece:1025 + 2 * idx + piece],
                            ps_s[piece][0:1, :], axis=AX, op=OP.max),
                            reads=[TOKP[piece]], writes=[(uid, "m2r", idx)])
                    sch.op("vector", lambda v, idx=idx: v.tensor_reduce(
                        sc16[:, idx:idx + 1], rows[:, 1024 + 2 * idx:1026 + 2 * idx],
                        axis=AX, op=OP.max),
                        reads=[(uid, "m2r", idx)], writes=[(uid, "sc")])
                    M2[idx] = sc16[:, idx:idx + 1]
                for idx, (qa, tq) in enumerate(((q1a, tq1), (q2a, tq2))):
                    sch.op("scalar", lambda s_, qa=qa: s_.activation(
                        out=bsq, in_=qa[0:64, 0:1024], func=AF.Square),
                        reads=[tq], writes=[(uid, "E", 0)])
                    mm_F(sch, ps_s[idx][0:1, :], env["ones128"][0:64, :], bsq,
                         start=True, stop=True,
                         reads=[(uid, "E", 0), "ones128"], writes=[TOKP[idx]])
                    sch.op("vector", lambda v, idx=idx: v.tensor_reduce(
                        sc16[:, 2 + idx:3 + idx], ps_s[idx][0:1, :], axis=AX, op=OP.max),
                        reads=[TOKP[idx]], writes=[(uid, "sc")])
                sch.op("scalar", lambda s_: s_.activation(
                    out=sc16[:, 4:8], in_=sc16[:, 0:4], func=AF.Ln),
                    reads=[(uid, "sc")], writes=[(uid, "sc")])
                sch.op("vector", lambda v: v.tensor_tensor(
                    sc16[:, 8:10], sc16[:, 4:6], sc16[:, 6:8], op=OP.subtract),
                    reads=[(uid, "sc")], writes=[(uid, "sc")])
                sch.op("vector", lambda v: v.tensor_tensor(
                    sc16[:, 10:12], sc16[:, 4:6], sc16[:, 6:8], op=OP.add),
                    reads=[(uid, "sc")], writes=[(uid, "sc")])
                sch.op("scalar", lambda s_: s_.activation(
                    out=sc16[:, 8:10], in_=sc16[:, 8:10], func=AF.Exp, scale=0.5),
                    reads=[(uid, "sc")], writes=[(uid, "sc")])
                sch.op("scalar", lambda s_: s_.activation(
                    out=sc16[:, 10:12], in_=sc16[:, 10:12], func=AF.Exp, scale=0.5),
                    reads=[(uid, "sc")], writes=[(uid, "sc")])
                sch.op("vector", lambda v: v.tensor_scalar(
                    sc16[:, 8:10], sc16[:, 8:10], -0.5, None, op0=OP.mult),
                    reads=[(uid, "sc")], writes=[(uid, "sc")])
                sch.op("vector", lambda v: v.tensor_scalar(
                    sc16[:, 10:12], sc16[:, 10:12], 0.5, None, op0=OP.mult),
                    reads=[(uid, "sc")], writes=[(uid, "sc")])
                for qh in range(2):
                    qlo_h = 1024 * qh
                    if DEFER_EPI == 2 and pending[0] is not None:
                        ep, pending[0] = pending[0], None
                        epi(*ep)
                    for idx, (qa, tq) in enumerate(((q1a, tq1), (q2a, tq2))):
                        sch.op("scalar", lambda s_, qa=qa, qlo_h=qlo_h: s_.activation(
                            out=bsq, in_=qa[0:64, qlo_h:qlo_h + 1024], func=AF.Square),
                            reads=[tq], writes=[(uid, "E", 0)])
                        mm_F(sch, ps_s[0][0:1, :], env["ones128"][0:64, :], bsq,
                             start=True, stop=True,
                             reads=[(uid, "E", 0), "ones128"], writes=[TA])
                        sch.op("vector", lambda v, idx=idx, qa=qa, qlo_h=qlo_h: v.tensor_scalar(
                            qa[64:65, qlo_h:qlo_h + 1024], ps_s[0][0:1, :],
                            sc16[:, 8 + idx:9 + idx], sc16[:, 10 + idx:11 + idx],
                            op0=OP.mult, op1=OP.subtract),
                            reads=[TA, (uid, "sc"), (uid, "augpad")],
                            writes=[tq])
                    kmax = 8 * (qh + 1)
                    sides = ((q1a, k1a, tq1, tk1, p1, TC),
                             (q2a, k2a, tq2, tk2, p2, TC))

                    def s_mm(kc):
                        qlo = max(qlo_h, 128 * kc)
                        F = qlo_h + 1024 - qlo
                        for idx, (qa, ka, tq, tk, pv, tpv) in enumerate(sides):
                            mm_F(sch, ps_s[idx][:, 0:F], ka[0:128, 128 * kc:128 * (kc + 1)],
                                 qa[0:128, qlo:qlo + F], start=True, stop=True,
                                 reads=[tq, tk, (uid, "augpad")], writes=[TOKP[idx]])

                    s_mm(0)
                    for kc in range(kmax):
                        qlo = max(qlo_h, 128 * kc)
                        F = qlo_h + 1024 - qlo
                        loc = qlo - qlo_h
                        for idx, (qa, ka, tq, tk, pv, tpv) in enumerate(sides):
                            sch.op("scalar", lambda s_, idx=idx, loc=loc, F=F: s_.activation(
                                out=E[idx][:, loc:loc + F], in_=ps_s[idx][:, 0:F], func=AF.Exp),
                                reads=[TOKP[idx]], writes=[(uid, "E", idx)])
                            if 128 * kc >= qlo_h:
                                d0 = 128 * kc - qlo_h
                                sch.op("vector", lambda v, idx=idx, d0=d0: v.tensor_tensor(
                                    E[idx][:, d0:d0 + 128], E[idx][:, d0:d0 + 128], ut[:], op=OP.mult),
                                    reads=[(uid, "E", idx), (uid, "ut")], writes=[(uid, "E", idx)])
                        if kc == 1 and pend_shift[0] is not None:
                            sf, pend_shift[0] = pend_shift[0], None
                            sf()
                        if kc + 1 < kmax:
                            s_mm(kc + 1)
                        if kc == 0 and pending[0] is not None:
                            ep, pending[0] = pending[0], None
                            pend_shift[0] = epi(*ep, defer_shift=True)
                        for idx, (qa, ka, tq, tk, pv, tpv) in enumerate(sides):
                            pos = loc
                            while pos < loc + F:
                                nxt = min(loc + F, pos + 512 - (pos % 512))
                                bank = pos // 512
                                last_kc_of_bank = min(kmax - 1, (512 * (bank + 1) - 1 + 1024 * qh) // 128)
                                sch.op("tensor",
                                       lambda t, pos=pos, nxt=nxt, kc=kc, idx=idx, pv=pv, va=va,
                                       st_=(kc == 0), sp_=(kc == last_kc_of_bank): t.matmul(
                                           pv[:, pos:nxt], va[:, 80 * kc:80 * kc + 65],
                                           E[idx][:, pos:nxt], start=st_, stop=sp_),
                                       reads=[(uid, "E", idx)] + tva, writes=[tpv])
                                pos = nxt
                    pending[0] = (h, qh)
                    if DEFER_EPI == 0:
                        ep, pending[0] = pending[0], None
                        epi(*ep)
    if pending[0] is not None:
        ep, pending[0] = pending[0], None
        epi(*ep)
    g_pes.close()
    sch.barrier()

    if debug_afm:
        dsem = es.enter_context(nc.semaphore(f"{uid}_dbg"))
        for c in range(8):
            sch.op("vector", lambda v, c=c: v.tensor_copy(env["ln_tnorm"][:, 0:1024], attn_fm[c][:, 0:1024]),
                   reads=[(uid, "afm", c)], writes=[("dbg",)])
            sch.dma("sync", lambda sy, c=c: sy.dma_start(out=outT[128 * c:128 * (c + 1), 0:1024], in_=env["ln_tnorm"][:, 0:1024]),
                    sem=dsem, reads=[("dbg",)])
            sch.op("vector", lambda v, c=c: v.tensor_copy(env["ln_tnorm"][:, 0:1024], attn_fm[c][:, 1024:2048]),
                   reads=[(uid, "afm", c)], writes=[("dbg",)])
            sch.dma("sync", lambda sy, c=c: sy.dma_start(out=outT[128 * c:128 * (c + 1), 1024:2048], in_=env["ln_tnorm"][:, 0:1024]),
                    sem=dsem, reads=[("dbg",)])
        sch.barrier()
        return
    qp_bf = qpre[0].ap().bitcast(BF16)
    wo_st = DBuf(nc, es, sch, f"{uid}_wo", None, None,
                 tiles=[qp_bf[:, 0:1024], qp_bf[:, 1024:2048]])
    xf_st = DBuf(nc, es, sch, f"{uid}_xf", None, None, queue="sync",
                 tiles=[tcs.ap(), uss.ap()])
    NH = 1024
    for half in range(2):
        hlo = half * NH
        with contextlib.ExitStack() as pes:
            p = lambda name, shape: pes.enter_context(nc.psum_tensor(name, shape, F32))
            ps_o = [p(f"{uid}_pso{half}_{i}", [128, NH]) for i in range(2)]
            for os in range(8):
                b = os % 2
                wtile = wo_st.load(os + 8 * half, wo_blk[os].rearrange("p c h -> p (c h)"))
                xf_st.load(os + 8 * half, xT[128 * os:128 * (os + 1), hlo:hlo + NH])
                for c in range(8):
                    mm_F(sch, ps_o[b][:], wtile[:, 128 * c:128 * (c + 1)],
                         attn_fm[c][:, hlo:hlo + NH], start=(c == 0), stop=(c == 7),
                         reads=[wo_st.tok(os), (uid, "afm", c)], writes=[(uid, "pso", b)])
                sch.op("vector", lambda v, os=os, b=b: v.scalar_tensor_tensor(
                    y[os][:, 0:NH], ps_o[b][:], bias_sb[:, 40 + os:41 + os],
                    xf_st.tiles[b], op0=OP.add, op1=OP.add),
                    reads=[(uid, "pso", b), xf_st.tok(os), (uid, "bias")],
                    writes=[(uid, "y", os)])
            sch.barrier()
        with contextlib.ExitStack() as pes:
            ps_rows = pes.enter_context(nc.psum_tensor(f"{uid}_lnr{half}", [1, 2 * NH], F32)).ap()
            pb = pes.enter_context(nc.psum_tensor(f"{uid}_lnb{half}", [128, 2 * NH], F32)).ap()
            ob = [out_bf16[c][:, hlo:hlo + NH] for c in range(8)] if out_bf16 else None
            obt = ([(tok, half) for tok in out_bf16_toks] if out_bf16 else None)
            emit_layernorm(sch, env, [y[c][:, 0:NH] for c in range(8)],
                           [(uid, "y", c) for c in range(8)], ps_rows, pb, uid,
                           out_bf16=ob, out_bf16_toks=obt,
                           out_dma=lambda c: (outT[128 * c:128 * (c + 1), hlo:hlo + NH], st_sem))
            sch.barrier()


def build_l0_attn_standalone(debug_afm=False):
    nc = bass.Bass("TRN2", target_bir_lowering=False)
    xT = nc.dram_tensor("xT", [D, S], F32, kind="ExternalInput").ap()
    wq_blk = nc.dram_tensor("wq_blk", [16, 128, 8, 128], F32, kind="ExternalInput").ap()
    wk_blk = nc.dram_tensor("wk_blk", [16, 128, 8, 128], F32, kind="ExternalInput").ap()
    wv_blk = nc.dram_tensor("wv_blk", [8, 128, 8, 128], F32, kind="ExternalInput").ap()
    wo_blk = nc.dram_tensor("wo_blk", [8, 128, 8, 128], F32, kind="ExternalInput").ap()
    biases = nc.dram_tensor("biases", [128, 48], F32, kind="ExternalInput").ap()
    lqk = nc.dram_tensor("lqk", [64, 64], F32, kind="ExternalInput").ap()
    hnw = nc.dram_tensor("hnw", [64, 16], F32, kind="ExternalInput").ap()
    cs_tab = nc.dram_tensor("cs_tab", [128, S], F32, kind="ExternalInput").ap()
    ss_tab = nc.dram_tensor("ss_tab", [128, S], F32, kind="ExternalInput").ap()
    perm = nc.dram_tensor("perm", [128, 128], F32, kind="ExternalInput").ap()
    ut = nc.dram_tensor("ut", [128, 128], F32, kind="ExternalInput").ap()
    iden = nc.dram_tensor("iden", [128, 128], F32, kind="ExternalInput").ap()
    v_scr = nc.dram_tensor("v_scr", [D, S], BF16, kind="Internal").ap()
    outT = nc.dram_tensor("outT", [D, S], F32, kind="ExternalOutput").ap()
    es = contextlib.ExitStack()
    with es:
        sch = Sched(nc, es)
        env = make_env(nc, es, sch)
        st_sem = es.enter_context(nc.semaphore("sem_st"))
        emit_l0_attn(nc, sch, es, env, xT, wq_blk, wk_blk, wv_blk, wo_blk,
                     biases[:, :], lqk[:, :], hnw[:, :], cs_tab[:, :], ss_tab[:, :],
                     perm[:, :], ut[:, :], iden[:, :], v_scr, outT, st_sem,
                     debug_afm=debug_afm)
        sch._emit("sync", lambda e: e.nop(), [(st_sem, sch.count(st_sem))], None, 1)
        sch.barrier()
    return nc


def prep_l0_attn_inputs(x_bt, Wq, bq, Wk, bk, Wv, bv, Wo, bo, hnw, lq1, lk1, lq2, lk2):
    xT = np.ascontiguousarray(x_bt.T)
    blk = lambda w, nout: np.ascontiguousarray(
        w.T.reshape(8, 128, nout // 128, 128).transpose(2, 1, 0, 3))
    fm = lambda b: np.ascontiguousarray(b.reshape(-1, 128).T)
    biases = np.concatenate([fm(bq), fm(bk), fm(bv), fm(bo)], axis=1)
    lqk = np.concatenate([lq1.T, lk1.T, lq2.T, lk2.T], axis=1)
    half = 32
    inv = (1.0 / 10000.0 ** (np.arange(half, dtype=np.float64) / half))
    ang = np.arange(S, dtype=np.float64)[:, None] * inv[None, :]
    cosv = np.cos(ang).astype(np.float32)
    sinv = np.sin(ang).astype(np.float32)
    cs = np.zeros((128, S), np.float32)
    ss = np.zeros((128, S), np.float32)
    for p in range(128):
        j = p % 32
        cs[p] = cosv[:, j]
        ss[p] = sinv[:, j] if (p % 64) >= 32 else -sinv[:, j]
    perm = np.zeros((128, 128), np.float32)
    for i in range(128):
        sig = i + 32 if (i % 64) < 32 else i - 32
        perm[sig, i] = 1.0
    ut = np.triu(np.ones((128, 128), np.float32))
    iden = np.eye(128, dtype=np.float32)
    return {"xT": xT, "wq_blk": blk(Wq, 2048), "wk_blk": blk(Wk, 2048),
            "wv_blk": blk(Wv, 1024), "wo_blk": blk(Wo, 1024),
            "biases": biases, "lqk": lqk, "hnw": np.ascontiguousarray(hnw.T),
            "cs_tab": cs, "ss_tab": ss, "perm": perm, "ut": ut, "iden": iden}



def emit_l1_sw_stage(nc, sch, es, env, swT, sclT, grid_h, sw_scr, uid="a1"):
    t = lambda name, shape, dt: es.enter_context(nc.sbuf_tensor(name, shape, dt))
    grid_sb = t(f"{uid}_sgrid", [1, 2], F32)
    c_col = t(f"{uid}_ccol", [128, 1], F32)
    sw_in = DBuf(nc, es, sch, f"{uid}_swin", [128, 3072], BF16, nbuf=3)
    scl_in = DBuf(nc, es, sch, f"{uid}_sclin", [128, 3072], BF16)
    sw_mid = DBuf(nc, es, sch, f"{uid}_swmid", [128, 3072], BF16)
    gsem = sch.new_dma_sem(f"{uid}_sgsem")
    LN6 = env["const"](float(-np.log(6.0)))

    sch.dma("gpsimd", lambda g: g.dma_start(out=grid_sb[:], in_=grid_h[0:1, 0:2]),
            sem=gsem, writes=[(uid, "sgrid")])
    with nc.psum_tensor(f"{uid}_psc0", [128, 1], F32) as ps_c:
        sch.op("vector", lambda v: v.tensor_tensor(
            c_col[0:1, :], grid_sb[0:1, 1:2], grid_sb[0:1, 0:1], op=OP.subtract),
            reads=[(uid, "sgrid")], writes=[(uid, "ccol")])
        sch.op("scalar", lambda s_: s_.activation(
            out=c_col[0:1, :], in_=c_col[0:1, :], func=AF.Ln),
            reads=[(uid, "ccol")], writes=[(uid, "ccol")])
        sch.op("scalar", lambda s_, LN6=LN6: s_.activation(
            out=c_col[0:1, :], in_=c_col[0:1, :], func=AF.Exp, scale=-3.0, bias=LN6),
            reads=[(uid, "ccol"), ("cst",)], writes=[(uid, "ccol")])
        mm_F(sch, ps_c[:], env["ones1f"][:, 0:128], c_col[0:1, :],
             start=True, stop=True, reads=[(uid, "ccol"), "ones1"],
             writes=[(uid, "psc")])
        sch.op("vector", lambda v: v.tensor_copy(c_col[:], ps_c[:]),
               reads=[(uid, "psc")], writes=[(uid, "ccol")])

    seq = [(ir, b8 * 8 + ir) for ir in range(8) for b8 in range(8)]
    st = {"l": 0, "c": 0}
    DEFER = 2

    def do_load(si):
        ir, cc = seq[si]
        if si % 8 == 0:
            scl_in.load(ir, sclT[128 * ir:128 * (ir + 1), :])
        sw_in.load(si, swT[128 * cc:128 * (cc + 1), :])

    def do_comp(si):
        ir, cc = seq[si]
        if si % 8 == 0:
            sch.op("vector", lambda v, ir=ir: v.tensor_scalar(
                scl_in.tiles[ir % 2][:], scl_in.tiles[ir % 2][:], c_col[:, 0:1],
                None, op0=OP.mult),
                reads=[scl_in.tok(ir), (uid, "ccol")], writes=[scl_in.tok(ir)])
        mb = sw_mid.tiles[si % 2]
        sch.op("vector", lambda v, si=si, mb=mb, ir=ir: v.tensor_tensor(
            mb[:], sw_in.tiles[si % 3][:], scl_in.tiles[ir % 2][:], op=OP.mult),
            reads=[sw_in.tok(si), scl_in.tok(ir)],
            writes=[sw_mid.tok(si)])
        for gg in range(3):
            sch.dma("sync", lambda sy, cc=cc, gg=gg, mb=mb: sy.dma_start(
                out=sw_scr[gg, cc], in_=mb[:, 1024 * gg:1024 * (gg + 1)]),
                sem=sw_mid.sems[si % 2], reads=[sw_mid.tok(si)],
                writes=[(uid, "swscr", gg, cc)])

    def pump():
        if st["l"] < 64:
            do_load(st["l"])
            st["l"] += 1
        if st["c"] < st["l"] and (st["l"] - st["c"] > DEFER or st["l"] == 64):
            do_comp(st["c"])
            st["c"] += 1
        return st["c"] < 64
    return pump


def emit_l1_attn(nc, sch, es, env, xT, bw_blk, swT, sclT, grid_h, RT_h, ow_blk,
                 outb, sw_scr, qk_scr, v_scr, outT, st_sem, uid="a1",
                 out_bf16=None, out_bf16_toks=None, x_dep=()):
    t = lambda name, shape, dt: es.enter_context(nc.sbuf_tensor(name, shape, dt))
    GS = 512
    NT = S // GS
    bases = t(f"{uid}_bases", [128, 64 * GS], BF16)
    bas = lambda cc: bases[:, GS * cc:GS * (cc + 1)]
    grid_sb = t(f"{uid}_grid", [128, 96], F32)
    gsem = sch.new_dma_sem(f"{uid}_gsem")
    sch.dma("sync", lambda sy: sy.dma_start(out=grid_sb[:], in_=grid_h),
            sem=gsem, writes=[(uid, "grid")])

    attn_fm = [t(f"{uid}_afm{c}", [128, S], BF16) for c in range(8)]
    Dj = [attn_fm[i // 4][:, GS * (i % 4):GS * (i % 4 + 1)] for i in range(12)]
    xf_st = DBuf(nc, es, sch, f"{uid}_xf", [128, GS], F32, nbuf=8, queue="sync", shared_sem=True)
    silu_x = t(f"{uid}_silu", [128, 8 * GS], BF16)
    L = [t(f"{uid}_L{i}", [128, GS], BF16) for i in range(11)]

    E = [t(f"{uid}_E{i}", [128, 1024], BF16) for i in range(2)]
    stage = E[0][:, 0:GS]
    tmp1f = t(f"{uid}_t1", [128, GS], F32)
    t1s = [tmp1f.ap().bitcast(BF16)[:, 0:GS], tmp1f.ap().bitcast(BF16)[:, GS:2 * GS]]
    t2s_ = [E[1][:, 0:GS], E[1][:, GS:2 * GS]]
    tmp1, tmp2 = t1s[0], t2s_[0]
    xb16 = E[0][:, GS:2 * GS]

    invd = t(f"{uid}_invd", [128, 64], F32)

    sw_st = DBuf(nc, es, sch, f"{uid}_swst", [128, 1024], BF16, nbuf=3, queue="sync")
    bw_st = DBuf(nc, es, sch, f"{uid}_bwst", [128, 8 * 128], BF16, nbuf=2)
    qsem = sch.new_dma_sem(f"{uid}_qsem")

    INVD_OFF = {1: 0, 2: 11, 3: 21}

    q_pes = contextlib.ExitStack()
    ps_o = [q_pes.enter_context(nc.psum_tensor(f"{uid}_pso{i}", [128, GS], F32))
            for i in range(8)]
    v_aug = [t(f"{uid}_vaug{i}", [128, 16 * 80], BF16) for i in range(2)]
    q_aug = t(f"{uid}_qaug", [128, S], BF16)
    k_aug = t(f"{uid}_kaug", [128, S], BF16)
    parts = []
    for s in (attn_fm[3], attn_fm[4], attn_fm[5], attn_fm[6], attn_fm[7],
              q_aug, k_aug):
        f = s.ap().bitcast(F32)
        parts += [f[:, 0:512], f[:, 512:1024]]
    parts += [v_aug[i].ap().bitcast(F32)[:, 0:512] for i in range(2)]
    part_x = t(f"{uid}_partx", [128, 8 * 512], F32)
    parts += [part_x[:, 512 * i:512 * (i + 1)] for i in range(8)]
    SETS = ([0, 1, 2, 3], [4, 5, 6, 7])

    def emit_xloads(tt):
        tlo = tt * GS
        for c in range(8):
            xf_st.load(c + 8 * tt, xT[128 * c:128 * (c + 1), tlo:tlo + GS], reads=x_dep)

    def emit_silu(tt):
        for c in range(8):
            sch.op("scalar", lambda s_, c=c: s_.activation(
                out=silu_x[:, GS * c:GS * (c + 1)], in_=xf_st.tiles[c][:], func=AF.Silu),
                reads=[xf_st.tok(c)], writes=[(uid, "silu", c)])

    def emit_rec(tt, ics):
        for ic in ics:
            xf = xf_st.tiles[ic]
            xtok = xf_st.tok(ic)
            gcol = lambda j: grid_sb[:, 12 * ic + j:12 * ic + j + 1]
            sch.op("vector", lambda v, xf=xf: v.tensor_copy(xb16, xf[:]),
                   reads=[xtok], writes=[(uid, "xb16")])
            xbtok = (uid, "xb16")
            for j in range(12):
                sch.op("vector", lambda v, j=j, gc=gcol(j): v.tensor_scalar(
                    Dj[j], xb16, gc, None, op0=OP.subtract),
                    reads=[xbtok, (uid, "grid")], writes=[(uid, "D", j)])
            ge = [tmp1, tmp2]
            getok = [(uid, "t1", 0), (uid, "t2", 0)]
            sch.op("vector", lambda v: v.tensor_scalar(
                ge[0], Dj[0], 0.0, None, op0=OP.is_ge),
                reads=[(uid, "D", 0)], writes=[getok[0]])
            for j in range(11):
                sch.op("vector", lambda v, j=j: v.tensor_scalar(
                    ge[(j + 1) % 2], Dj[j + 1], 0.0, None, op0=OP.is_ge),
                    reads=[(uid, "D", j + 1)], writes=[getok[(j + 1) % 2]])
                sch.op("vector", lambda v, j=j: v.tensor_tensor(
                    L[j][:], ge[j % 2], ge[(j + 1) % 2], op=OP.subtract),
                    reads=[getok[0], getok[1]], writes=[(uid, "L", j)])
            nb = 0
            for pp in (1, 2, 3):
                n_out = 11 - pp
                for j in range(n_out):
                    last = (pp == 3)
                    dst = bas(j * 8 + ic) if last else L[j][:]
                    dst_tok = (uid, "bas", j * 8 + ic) if last else (uid, "L", j)
                    par = nb % 2
                    nb += 1
                    u1, u2 = t1s[par], t2s_[par]
                    tk1, tk2 = (uid, "t1", par), (uid, "t2", par)
                    sch.op("vector", lambda v, j=j, u1=u1: v.tensor_tensor(
                        u1, Dj[j], L[j][:], op=OP.mult),
                        reads=[(uid, "D", j), (uid, "L", j)], writes=[tk1])
                    sch.op("vector", lambda v, j=j, pp=pp, u2=u2: v.tensor_tensor(
                        u2, Dj[j + pp + 1], L[j + 1][:], op=OP.mult),
                        reads=[(uid, "D", j + pp + 1), (uid, "L", j + 1)], writes=[tk2])
                    sch.op("vector", lambda v, dst=dst, u1=u1, u2=u2: v.tensor_tensor(
                        dst, u1, u2, op=OP.subtract),
                        reads=[tk1, tk2], writes=[dst_tok])

    def emit_mm_half(tt, hs):
        tlo = tt * GS
        ccs = [j * 8 + ic for ic in SETS[hs] for j in range(8)]
        nA = len(SETS[hs]) * 8
        base = (tt * 2 + hs) * 96
        flat = [(gg, cc) for gg in range(3) for cc in ccs]

        def sload(k):
            gg, cc = flat[k]
            sw_st.load(base + k, sw_scr[gg, cc], reads=[(uid, "swscr", gg, cc)])

        sload(0)
        sload(1)
        for k, (gg, cc) in enumerate(flat):
            if True:
                ci = k % nA
                sidx = base + k
                wt = sw_st.tiles[sidx % sw_st.nbuf]
                for oti in range(8):
                    mm_F(sch, ps_o[oti][:], wt[:, 128 * oti:128 * (oti + 1)],
                         bas(cc), start=(ci == 0),
                         stop=(hs == 0 and ci == nA - 1),
                         reads=[sw_st.tok(sidx), (uid, "bas", cc)],
                         writes=[(uid, "pso", oti)])
                if k + 2 < len(flat):
                    sload(k + 2)
            if ci != nA - 1:
                continue
            if hs == 0:
                for oti in range(8):
                    os = gg * 8 + oti
                    sch.op("scalar", lambda s_, oti=oti, os=os: s_.activation(
                        out=parts[os], in_=ps_o[oti][:], func=AF.Copy),
                        reads=[(uid, "pso", oti)], writes=[(uid, "part", os)])
            else:
                for oti in range(8):
                    os = gg * 8 + oti
                    bw = bw_st.load(os + 24 * tt, bw_blk[os].rearrange("p c h -> p (c h)"))
                    for c8 in range(8):
                        mm_F(sch, ps_o[oti][:], bw[:, 128 * c8:128 * (c8 + 1)],
                             silu_x[:, GS * c8:GS * (c8 + 1)], start=False, stop=(c8 == 7),
                             reads=[bw_st.tok(os + 24 * tt), (uid, "silu", c8)],
                             writes=[(uid, "pso", oti)])
                    sch.op("vector", lambda v, oti=oti, os=os: v.tensor_tensor(
                        stage[:], ps_o[oti][:], parts[os], op=OP.add),
                        reads=[(uid, "pso", oti), (uid, "part", os)],
                        writes=[(uid, "stage")])
                    o0 = 128 * os
                    for half in range(2):
                        m = (o0 + 64 * half) // 64
                        h, kind = m // 3, m % 3
                        if kind == 0:
                            dstp = qk_scr[64 * h:64 * h + 64, tlo:tlo + GS]
                        elif kind == 1:
                            dstp = qk_scr[1024 + 64 * h:1024 + 64 * h + 64, tlo:tlo + GS]
                        else:
                            dstp = v_scr[64 * h:64 * h + 64, tlo:tlo + GS]
                        sch.dma("sync", lambda sy, half=half, dstp=dstp: sy.dma_start(
                            out=dstp, in_=stage[64 * half:64 * (half + 1), :]),
                            sem=qsem, reads=[(uid, "stage")],
                            writes=[(uid, "qkv", os, half)])

    emit_xloads(0)
    emit_silu(0)
    emit_rec(0, SETS[0])
    emit_rec(0, SETS[1])
    for tt in range(NT):
        emit_mm_half(tt, 0)
        if tt + 1 < NT:
            emit_xloads(tt + 1)
            emit_rec(tt + 1, SETS[0])
        emit_mm_half(tt, 1)
        if tt + 1 < NT:
            emit_silu(tt + 1)
            emit_rec(tt + 1, SETS[1])
    sch.fence_tokens("vector", [(uid, "stage")])
    sch.barrier()
    q_pes.close()

    RT_sb = t(f"{uid}_RT", [64, 64], BF16)
    rsem = sch.new_dma_sem(f"{uid}_rsem")
    sch.dma("gpsimd", lambda g: g.dma_start(out=RT_sb[:], in_=RT_h),
            sem=rsem, writes=[(uid, "RT")])
    iden = t(f"{uid}_iden", [128, 128], BF16)
    isem = sch.new_dma_sem(f"{uid}_isem")
    sch.dma("gpsimd", lambda g: g.dma_start(out=iden[:], in_=env["iden_h"]),
            sem=isem, writes=[(uid, "iden")])
    _px_bf = part_x.ap().bitcast(BF16)
    qk_in = DBuf(nc, es, sch, f"{uid}_qkin", None, None, queue="sync",
                 tiles=[_px_bf[0:64, 0:2048], _px_bf[0:64, 2048:4096]])
    vtsem = [sch.new_dma_sem(f"{uid}_vt{i}") for i in range(2)]
    y = [bases.ap().bitcast(F32)[:, 2048 + 1024 * c:3072 + 1024 * c]
         for c in range(8)]
    sqt = tmp1f.ap().bitcast(BF16)[0:64, :]
    o1 = t(f"{uid}_o1", [64, 1024], F32)
    comb = silu_x.ap().bitcast(F32)[0:64, 0:1024]
    rows = silu_x.ap().bitcast(F32)[0:1, 1024:2048]
    rowsb = bases.ap().bitcast(F32)[0:1, 4096:5120]
    rbh1 = bases[0:1, 2048:3072]
    m2row = silu_x.ap().bitcast(F32)[32:33, 1024:1040]
    q_augs = [q_aug, _px_bf[:, 4096:6144]]
    k_augs = [k_aug, _px_bf[:, 6144:8192]]
    m2rows = [m2row, silu_x.ap().bitcast(F32)[32:33, 1040:1056]]
    for p in range(2):
        sch.op("vector", lambda v, p=p: v.memset(k_augs[p][64:128, :], 0.0),
               writes=[(uid, "augpad", p), (uid, "kones", p)])
        sch.op("vector", lambda v, p=p: v.memset(k_augs[p][64:65, :], 1.0),
               writes=[(uid, "kones", p)], reads=[(uid, "augpad", p)])
        sch.op("vector", lambda v, p=p: v.memset(q_augs[p][64:128, :], 0.0),
               writes=[(uid, "augpad", p)])

    a_pes = contextlib.ExitStack()
    ps_r = a_pes.enter_context(nc.psum_tensor(f"{uid}_psr", [64, 1024], F32))
    ps_s = [a_pes.enter_context(nc.psum_tensor(f"{uid}_pss{i}", [128, 1024], F32))
            for i in range(2)]
    p1 = a_pes.enter_context(nc.psum_tensor(f"{uid}_p1", [65, 1024], F32))
    def prep_head(h):
        j = h % 2
        qa_, ka_, m2_ = q_augs[j], k_augs[j], m2rows[j]
        TQ, TK = (uid, "qa", j), (uid, "ka", j)
        for (row0, dsta, dtok, scl) in ((64 * h, qa_, TQ, 0.125),
                                        (1024 + 64 * h, ka_, TK, 1.0)):
            qt = qk_in.load(h * 2 + (0 if row0 < 1024 else 1),
                            qk_scr[row0:row0 + 64, :], reads=[(uid, "qkv", 0, 0)])
            for hf in range(2):
                mm_F(sch, ps_r[:], RT_sb[:], qt[:, 1024 * hf:1024 * (hf + 1)],
                     start=True, stop=True,
                     reads=[qk_in.tok(h * 2 + (0 if row0 < 1024 else 1)), (uid, "RT")],
                     writes=[(uid, "psr")])
                sch.op("scalar", lambda s_, dsta=dsta, scl=scl, hf=hf: s_.activation(
                    out=dsta[0:64, 1024 * hf:1024 * (hf + 1)], in_=ps_r[:],
                    func=AF.Copy, scale=scl),
                    reads=[(uid, "psr"), ("cst",)], writes=[dtok])
        va = v_aug[j]
        ones_cols = va.ap().rearrange("p (k c) -> p k c", c=80)[:, :, 64:65]
        sch.op("vector", lambda v, ones_cols=ones_cols: v.memset(ones_cols, 1.0),
               writes=[(uid, "vaug1", j)])
        for kc in range(16):
            sch.dma("sync", lambda sy, va=va, kc=kc, h=h: sy.dma_start(
                out=va[:, 80 * kc:80 * kc + 64],
                in_=v_scr[64 * h:64 * h + 64, 128 * kc:128 * (kc + 1)],
                transpose=True),
                sem=vtsem[j], reads=[(uid, "qkv", 0, 0)],
                writes=[(uid, "vaugd", j, kc)])
        for piece in range(2):
            sch.op("scalar", lambda s_, piece=piece, ka_=ka_, TK=TK: s_.activation(
                out=sqt, in_=ka_[0:64, 1024 * piece:1024 * (piece + 1)],
                func=AF.Square), reads=[TK], writes=[(uid, "sqt")])
            mm_F(sch, ps_s[piece][0:1, :], env["ones128"][0:64, :], sqt,
                 start=True, stop=True, reads=[(uid, "sqt"), "ones128"],
                 writes=[(uid, "pss", piece)])
            sch.op("vector", lambda v, piece=piece, m2_=m2_: v.tensor_reduce(
                m2_[:, piece:piece + 1], ps_s[piece][0:1, :], axis=AX, op=OP.max),
                reads=[(uid, "pss", piece)], writes=[(uid, "m2r", j)])
        sch.op("vector", lambda v, m2_=m2_: v.tensor_reduce(
            m2_[:, 4:5], m2_[:, 0:2], axis=AX, op=OP.max),
            reads=[(uid, "m2r", j)], writes=[(uid, "M2", j)])
        sch.op("scalar", lambda s_, qa_=qa_, TQ=TQ: s_.activation(
            out=sqt, in_=qa_[0:64, 0:1024], func=AF.Square),
            reads=[TQ], writes=[(uid, "sqt")])
        mm_F(sch, ps_s[0][0:1, :], env["ones128"][0:64, :], sqt,
             start=True, stop=True, reads=[(uid, "sqt"), "ones128"],
             writes=[(uid, "pss", 0)])
        sch.op("vector", lambda v, m2_=m2_: v.tensor_reduce(
            m2_[:, 5:6], ps_s[0][0:1, :], axis=AX, op=OP.max),
            reads=[(uid, "pss", 0)], writes=[(uid, "M2", j)])
        sch.op("scalar", lambda s_, m2_=m2_: s_.activation(
            out=m2_[:, 6:8], in_=m2_[:, 4:6], func=AF.Ln),
            reads=[(uid, "M2", j)], writes=[(uid, "M2", j)])
        sch.op("vector", lambda v, m2_=m2_: v.tensor_tensor(
            m2_[:, 8:9], m2_[:, 6:7], m2_[:, 7:8], op=OP.subtract),
            reads=[(uid, "M2", j)], writes=[(uid, "M2", j)])
        sch.op("vector", lambda v, m2_=m2_: v.tensor_tensor(
            m2_[:, 9:10], m2_[:, 6:7], m2_[:, 7:8], op=OP.add),
            reads=[(uid, "M2", j)], writes=[(uid, "M2", j)])
        sch.op("scalar", lambda s_, m2_=m2_: s_.activation(
            out=m2_[:, 8:9], in_=m2_[:, 8:9], func=AF.Exp, scale=0.5),
            reads=[(uid, "M2", j)], writes=[(uid, "M2", j)])
        sch.op("scalar", lambda s_, m2_=m2_: s_.activation(
            out=m2_[:, 9:10], in_=m2_[:, 9:10], func=AF.Exp, scale=0.5),
            reads=[(uid, "M2", j)], writes=[(uid, "M2", j)])
        sch.op("vector", lambda v, m2_=m2_: v.tensor_scalar(
            m2_[:, 8:9], m2_[:, 8:9], -0.5, None, op0=OP.mult),
            reads=[(uid, "M2", j)], writes=[(uid, "M2", j)])
        sch.op("vector", lambda v, m2_=m2_: v.tensor_scalar(
            m2_[:, 9:10], m2_[:, 9:10], 0.5, None, op0=OP.mult),
            reads=[(uid, "M2", j)], writes=[(uid, "M2", j)])
        sch.op("vector", lambda v, qa_=qa_, m2_=m2_, TQ=TQ, j=j: v.tensor_scalar(
            qa_[64:65, 0:1024], ps_s[0][0:1, :],
            m2_[:, 8:9], m2_[:, 9:10], op0=OP.mult, op1=OP.subtract),
            reads=[(uid, "pss", 0), (uid, "M2", j), (uid, "augpad", j)],
            writes=[TQ])

    def emit_qh(h, qh):
        j = h % 2
        qa_, ka_, m2_ = q_augs[j], k_augs[j], m2rows[j]
        TQ, TK = (uid, "qa", j), (uid, "ka", j)
        va = v_aug[j]
        tva = [(uid, "vaug1", j)] + [(uid, "vaugd", j, kc) for kc in range(16)]
        qlo_h = 1024 * qh
        if qh == 1:
            sch.op("scalar", lambda s_, qlo_h=qlo_h, qa_=qa_, TQ=TQ: s_.activation(
                out=sqt, in_=qa_[0:64, qlo_h:qlo_h + 1024], func=AF.Square),
                reads=[TQ], writes=[(uid, "sqt")])
            mm_F(sch, ps_s[0][0:1, :], env["ones128"][0:64, :], sqt,
                 start=True, stop=True, reads=[(uid, "sqt"), "ones128"],
                 writes=[(uid, "pss", 0)])
            sch.op("vector", lambda v, qlo_h=qlo_h, qa_=qa_, m2_=m2_, TQ=TQ, j=j: v.tensor_scalar(
                qa_[64:65, qlo_h:qlo_h + 1024], ps_s[0][0:1, :],
                m2_[:, 8:9], m2_[:, 9:10], op0=OP.mult, op1=OP.subtract),
                reads=[(uid, "pss", 0), (uid, "M2", j), (uid, "augpad", j)],
                writes=[TQ])

        def s_mm(kc):
            idx = kc % 2
            mm_F(sch, ps_s[idx][:, :], ka_[0:128, 128 * kc:128 * (kc + 1)],
                 qa_[0:128, qlo_h:qlo_h + 1024], start=True, stop=True,
                 reads=[TQ, TK, (uid, "kones", j), (uid, "augpad", j)],
                 writes=[(uid, "pss", idx)])

        s_mm(0)
        for kc in range(16):
            idx = kc % 2
            sch.op("scalar", lambda s_, idx=idx: s_.activation(
                out=E[idx][:, :], in_=ps_s[idx][:, :], func=AF.Exp),
                reads=[(uid, "pss", idx)], writes=[(uid, "E", idx)])
            if kc == 1 and pend_shift[0] is not None:
                sf, pend_shift[0] = pend_shift[0], None
                sf()
            if kc + 1 < 16:
                s_mm(kc + 1)
            if kc == 0 and pending[0] is not None:
                ep, pending[0] = pending[0], None
                pend_shift[0] = epi(*ep, defer_shift=True)
            mm_F(sch, p1[:, :], va[:, 80 * kc:80 * kc + 65], E[idx][:, :],
                 start=(kc == 0), stop=(kc == 15),
                 reads=[(uid, "E", idx)] + tva, writes=[(uid, "p1")])
        pending[0] = (h, qh)

    def epi(h, qh, defer_shift=False):
        sch.op("scalar", lambda s_: s_.activation(out=rows[:, :], in_=p1[64:65, :], func=AF.Ln),
               reads=[(uid, "p1")], writes=[(uid, "rowA")])
        sch.op("scalar", lambda s_: s_.activation(out=rbh1[:, :], in_=rows[:, :],
                                                  func=AF.Exp, scale=-1.0),
               reads=[(uid, "rowA"), ("cst",)], writes=[(uid, "rowB")])
        mm_F(sch, ps_r[0:64, :], env["ones1"][:, 0:64], rbh1[:, :],
             start=True, stop=True, reads=[(uid, "rowB"), "ones1"],
             writes=[(uid, "psr")])
        sch.op("vector", lambda v: v.tensor_copy(o1[:], p1[0:64, :]),
               reads=[(uid, "p1")], writes=[(uid, "o1")])
        cdst = attn_fm[h // 2]
        if h % 2 == 0:
            sch.op("vector", lambda v, cdst=cdst, qh=qh: v.tensor_tensor(
                cdst[0:64, 1024 * qh:1024 * (qh + 1)], o1[:], ps_r[0:64, :], op=OP.mult),
                reads=[(uid, "o1"), (uid, "psr")], writes=[(uid, "afm", h // 2)])
            return None
        sch.op("vector", lambda v: v.tensor_tensor(
            comb, o1[:], ps_r[0:64, :], op=OP.mult),
            reads=[(uid, "o1"), (uid, "psr")], writes=[(uid, "comb")])
        sch.op("vector", lambda v: v.tensor_copy(sqt, comb),
               reads=[(uid, "comb")], writes=[(uid, "sqt")])

        def shift():
            mm_F(sch, ps_s[1][64:128, :], iden[0:64, 0:64], sqt,
                 start=True, stop=True, reads=[(uid, "sqt"), (uid, "iden")],
                 writes=[(uid, "pss", 1)], tile_position=(0, 64))
            sch.op("vector", lambda v, cdst=cdst, qh=qh: v.tensor_copy(
                cdst[64:128, 1024 * qh:1024 * (qh + 1)], ps_s[1][64:128, :]),
                reads=[(uid, "pss", 1)], writes=[(uid, "afm", h // 2)])
        if defer_shift:
            return shift
        shift()
        return None

    pending = [None]
    pend_shift = [None]
    prep_head(0)
    for h in range(16):
        emit_qh(h, 0)
        if h + 1 < 16:
            prep_head(h + 1)
        emit_qh(h, 1)
    if pending[0] is not None:
        ep, pending[0] = pending[0], None
        epi(*ep)
    a_pes.close()
    sch.barrier()

    ob_sb = t(f"{uid}_ob", [128, 8], F32)
    obsem = sch.new_dma_sem(f"{uid}_obsem")
    sch.dma("sync", lambda sy: sy.dma_start(out=ob_sb[:], in_=outb),
            sem=obsem, writes=[(uid, "ob")])
    qa_bf = q_aug.ap()
    wo_st = DBuf(nc, es, sch, f"{uid}_wo", None, None, queue="gpsimd",
                 tiles=[qa_bf[:, 0:1024], qa_bf[:, 1024:2048]])
    ka_f32 = k_aug.ap().bitcast(F32)
    xf2b = bases.ap().bitcast(F32)[:, 0:1024]
    xf2_st = DBuf(nc, es, sch, f"{uid}_xf2", None, None, queue="sync",
                  tiles=[ka_f32, xf2b])
    NH = 1024
    for half in range(2):
        hlo = half * NH
        with contextlib.ExitStack() as pes:
            p = lambda name, shape: pes.enter_context(nc.psum_tensor(name, shape, F32))
            ps_p = [p(f"{uid}_psp{half}_{i}", [128, NH]) for i in range(2)]
            for os in range(8):
                b = os % 2
                wtile = wo_st.load(os + 8 * half, ow_blk[os].rearrange("p c h -> p (c h)"))
                xf2_st.load(os + 8 * half, xT[128 * os:128 * (os + 1), hlo:hlo + NH], reads=x_dep)
                for c in range(8):
                    mm_F(sch, ps_p[b][:], wtile[:, 128 * c:128 * (c + 1)],
                         attn_fm[c][:, hlo:hlo + NH], start=(c == 0), stop=(c == 7),
                         reads=[wo_st.tok(os), (uid, "afm", c)], writes=[(uid, "psp", b)])
                sch.op("vector", lambda v, os=os, b=b: v.scalar_tensor_tensor(
                    y[os][:, 0:NH], ps_p[b][:], ob_sb[:, os:os + 1],
                    xf2_st.tiles[b], op0=OP.add, op1=OP.add),
                    reads=[(uid, "psp", b), xf2_st.tok(os), (uid, "ob")],
                    writes=[(uid, "y", os)])
            sch.barrier()
        with contextlib.ExitStack() as pes:
            ps_rows = pes.enter_context(nc.psum_tensor(f"{uid}_lnr{half}", [1, 2 * NH], F32)).ap()
            pb = pes.enter_context(nc.psum_tensor(f"{uid}_lnb{half}", [128, 2 * NH], F32)).ap()
            ob2 = [out_bf16[c][:, hlo:hlo + NH] for c in range(8)] if out_bf16 else None
            obt = ([(tok, half) for tok in out_bf16_toks] if out_bf16 else None)
            emit_layernorm(sch, env, [y[c][:, 0:NH] for c in range(8)],
                           [(uid, "y", c) for c in range(8)], ps_rows, pb, uid,
                           out_bf16=ob2, out_bf16_toks=obt,
                           out_dma=lambda c: (outT[128 * c:128 * (c + 1), hlo:hlo + NH], st_sem))
            sch.barrier()


def build_l1_attn_standalone():
    nc = bass.Bass("TRN2", target_bir_lowering=False)
    xT = nc.dram_tensor("xT", [D, S], F32, kind="ExternalInput").ap()
    bw_blk = nc.dram_tensor("bw_blk", [24, 128, 8, 128], F32, kind="ExternalInput").ap()
    swT = nc.dram_tensor("swT", [8192, 3072], F32, kind="ExternalInput").ap()
    sclT = nc.dram_tensor("sclT", [1024, 3072], F32, kind="ExternalInput").ap()
    grid_h = nc.dram_tensor("grid_h", [128, 96], F32, kind="ExternalInput").ap()
    RT_h = nc.dram_tensor("RT_h", [64, 64], F32, kind="ExternalInput").ap()
    ow_blk = nc.dram_tensor("ow_blk", [8, 128, 8, 128], F32, kind="ExternalInput").ap()
    outb = nc.dram_tensor("outb", [128, 8], F32, kind="ExternalInput").ap()
    iden_h = nc.dram_tensor("iden_h", [128, 128], F32, kind="ExternalInput").ap()
    sw_scr = nc.dram_tensor("sw_scr", [3, 64, 128, 1024], BF16, kind="Internal").ap()
    qk_scr = nc.dram_tensor("qk_scr", [2048, S], BF16, kind="Internal").ap()
    v_scr = nc.dram_tensor("v_scr", [D, S], BF16, kind="Internal").ap()
    outT = nc.dram_tensor("outT", [D, S], F32, kind="ExternalOutput").ap()
    es = contextlib.ExitStack()
    with es:
        sch = Sched(nc, es)
        env = make_env(nc, es, sch)
        env["iden_h"] = iden_h[:, :]
        st_sem = es.enter_context(nc.semaphore("sem_st"))
        with contextlib.ExitStack() as ses:
            pump = emit_l1_sw_stage(nc, sch, ses, env, swT[:, :], sclT,
                                    grid_h[:, :], sw_scr, uid="a1")
            while pump():
                pass
            sch.retire_dmas()
            sch.barrier()
        emit_l1_attn(nc, sch, es, env, xT, bw_blk, swT[:, :], sclT, grid_h[:, :],
                     RT_h[:, :], ow_blk, outb[:, :], sw_scr, qk_scr, v_scr, outT, st_sem)
        sch._emit("sync", lambda e: e.nop(), [(st_sem, sch.count(st_sem))], None, 1)
        sch.barrier()
    return nc


def prep_l1_attn_inputs(x_bt, base_weight, spline_weight, spline_scaler, grid,
                        rotation_matrix, out_w, out_b):
    xT = np.ascontiguousarray(x_bt.T)
    blk = lambda w, nin, nout: np.ascontiguousarray(
        w.T.reshape(nin // 128, 128, nout // 128, 128).transpose(2, 1, 0, 3))
    swT = np.ascontiguousarray(spline_weight.transpose(2, 1, 0).reshape(8192, 3072))
    sclT = np.ascontiguousarray(spline_scaler.T)
    grid_h = np.ascontiguousarray(grid.reshape(8, 128, 12).transpose(1, 0, 2).reshape(128, 96))
    return {"xT": xT, "bw_blk": blk(base_weight, 1024, 3072), "swT": swT,
            "sclT": sclT, "grid_h": grid_h,
            "RT_h": np.ascontiguousarray(rotation_matrix.T),
            "ow_blk": blk(out_w, 1024, 1024),
            "outb": np.ascontiguousarray(out_b.reshape(8, 128).T),
            "iden_h": np.eye(128, dtype=np.float32)}



_PROG_CACHE = {}


def _get_prog(name, builder):
    if name not in _PROG_CACHE:
        _PROG_CACHE[name] = builder()
    return _PROG_CACHE[name]


def kernel_unfused(L0_attn_x, L0_attn_Wq, L0_attn_bq, L0_attn_Wk, L0_attn_bk,
           L0_attn_Wv, L0_attn_bv, L0_attn_Wo, L0_attn_bo, L0_attn_head_norm_w,
           L0_attn_lq1, L0_attn_lk1, L0_attn_lq2, L0_attn_lk2,
           L0_ffn_w_gate, L0_ffn_w_up, L0_ffn_w_down,
           L1_attn_base_weight, L1_attn_spline_weight, L1_attn_spline_scaler,
           L1_attn_grid, L1_attn_rotation_matrix, L1_attn_out_w, L1_attn_out_b,
           L1_ffn_w1_w, L1_ffn_w1_s, L1_ffn_w2_w, L1_ffn_w2_s,
           L1_ffn_h1_w, L1_ffn_h1_s, L1_ffn_h2_w, L1_ffn_h2_s,
           L1_ffn_w3_w, L1_ffn_w3_s):
    B = L0_attn_x.shape[0]
    x = np.ascontiguousarray(L0_attn_x)

    common_a0 = prep_l0_attn_inputs(
        x[0], L0_attn_Wq, L0_attn_bq, L0_attn_Wk, L0_attn_bk, L0_attn_Wv,
        L0_attn_bv, L0_attn_Wo, L0_attn_bo, L0_attn_head_norm_w,
        L0_attn_lq1, L0_attn_lk1, L0_attn_lq2, L0_attn_lk2)
    in_maps = []
    for b in range(B):
        m = dict(common_a0)
        m["xT"] = np.ascontiguousarray(x[b].T)
        in_maps.append(m)
    nc = _get_prog("l0attn", build_l0_attn_standalone)
    res = run_bass_kernel_spmd(nc, in_maps, core_ids=list(range(B)))
    x1 = [r["outT"] for r in res.results]

    common_f0 = prep_l0_ffn_inputs(x[0], L0_ffn_w_gate, L0_ffn_w_up, L0_ffn_w_down)
    in_maps = []
    for b in range(B):
        m = dict(common_f0)
        m["xT"] = x1[b]
        in_maps.append(m)
    nc = _get_prog("l0ffn", build_l0_ffn_standalone)
    res = run_bass_kernel_spmd(nc, in_maps, core_ids=list(range(B)))
    x2 = [r["outT"] for r in res.results]

    common_a1 = prep_l1_attn_inputs(
        x[0], L1_attn_base_weight, L1_attn_spline_weight, L1_attn_spline_scaler,
        L1_attn_grid, L1_attn_rotation_matrix, L1_attn_out_w, L1_attn_out_b)
    in_maps = []
    for b in range(B):
        m = dict(common_a1)
        m["xT"] = x2[b]
        in_maps.append(m)
    nc = _get_prog("l1attn", build_l1_attn_standalone)
    res = run_bass_kernel_spmd(nc, in_maps, core_ids=list(range(B)))
    x3 = [r["outT"] for r in res.results]

    common_f1 = prep_l1_ffn_inputs(
        x[0], L1_ffn_w1_w, L1_ffn_w1_s, L1_ffn_w2_w, L1_ffn_w2_s,
        L1_ffn_h1_w, L1_ffn_h1_s, L1_ffn_h2_w, L1_ffn_h2_s,
        L1_ffn_w3_w, L1_ffn_w3_s)
    in_maps = []
    for b in range(B):
        m = dict(common_f1)
        m["xT"] = x3[b]
        in_maps.append(m)
    nc = _get_prog("l1ffn", build_l1_ffn_standalone)
    res = run_bass_kernel_spmd(nc, in_maps, core_ids=list(range(B)))
    out = np.stack([r["outT"].T for r in res.results], axis=0)
    return out



def build_full():
    nc = bass.Bass("TRN2", target_bir_lowering=False)
    ap = lambda n, s, d, k: nc.dram_tensor(n, s, d, kind=k).ap()
    x0T = ap("xT", [D, S], F32, "ExternalInput")
    wq_blk = ap("wq_blk", [16, 128, 8, 128], F32, "ExternalInput")
    wk_blk = ap("wk_blk", [16, 128, 8, 128], F32, "ExternalInput")
    wv_blk = ap("wv_blk", [8, 128, 8, 128], F32, "ExternalInput")
    wo_blk = ap("wo_blk", [8, 128, 8, 128], F32, "ExternalInput")
    biases = ap("biases", [128, 48], F32, "ExternalInput")
    lqk = ap("lqk", [64, 64], F32, "ExternalInput")
    hnw = ap("hnw", [64, 16], F32, "ExternalInput")
    cs_tab = ap("cs_tab", [128, S], F32, "ExternalInput")
    ss_tab = ap("ss_tab", [128, S], F32, "ExternalInput")
    perm = ap("perm", [128, 128], F32, "ExternalInput")
    ut = ap("ut", [128, 128], F32, "ExternalInput")
    iden = ap("iden", [128, 128], F32, "ExternalInput")
    wg_blk = ap("wg_blk", [24, 128, 8, 128], I32, "ExternalInput")
    wu_blk = ap("wu_blk", [24, 128, 8, 128], I32, "ExternalInput")
    wd_blk = ap("wd_blk", [8, 128, 24, 128], I32, "ExternalInput")
    bw_blk = ap("bw_blk", [24, 128, 8, 128], F32, "ExternalInput")
    swT = ap("swT", [8192, 3072], F32, "ExternalInput")
    sclT = ap("sclT", [1024, 3072], F32, "ExternalInput")
    grid_h = ap("grid_h", [128, 96], F32, "ExternalInput")
    RT_h = ap("RT_h", [64, 64], F32, "ExternalInput")
    ow_blk = ap("ow_blk", [8, 128, 8, 128], F32, "ExternalInput")
    outb = ap("outb", [128, 8], F32, "ExternalInput")
    w1_blk = ap("w1_blk", [32, 128, 8, 128], F32, "ExternalInput")
    w2_blk = ap("w2_blk", [32, 128, 8, 128], F32, "ExternalInput")
    h1_blk = ap("h1_blk", [32, 128, 32, 128], F32, "ExternalInput")
    h2_blk = ap("h2_blk", [32, 128, 32, 128], F32, "ExternalInput")
    w3_blk = ap("w3_blk", [8, 128, 32, 128], F32, "ExternalInput")
    scl = ap("scl", [128, 4 * 32 + 8], F32, "ExternalInput")
    v_scr0 = ap("v_scr0", [D, S], BF16, "Internal")
    sw_scr = ap("sw_scr", [3, 64, 128, 1024], BF16, "Internal")
    qk_scr = ap("qk_scr", [2048, S], BF16, "Internal")
    v_scr1 = ap("v_scr1", [D, S], BF16, "Internal")
    x1T = ap("x1T", [D, S], F32, "Internal")
    x2T = ap("x2T", [D, S], F32, "Internal")
    x3T = ap("x3T", [D, S], F32, "Internal")
    outT = ap("outT", [D, S], F32, "ExternalOutput")

    es = contextlib.ExitStack()
    with es:
        sch = Sched(nc, es)
        env = make_env(nc, es, sch)
        env["iden_h"] = iden
        sts = [sch.new_dma_sem(f"st{i}") for i in range(4)]
        with contextlib.ExitStack() as bes:
            emit_l0_attn(nc, sch, bes, env, x0T, wq_blk, wk_blk, wv_blk, wo_blk,
                         biases[:, :], lqk, hnw, cs_tab, ss_tab, perm, ut, iden,
                         v_scr0, x1T, sts[0], uid="a0")
            sch.retire_dmas()
            sch.barrier()
        with contextlib.ExitStack() as bes:
            pump = emit_l1_sw_stage(nc, sch, bes, env, swT[:, :], sclT, grid_h,
                                    sw_scr, uid="a1")
            emit_l0_ffn(nc, sch, bes, env, x1T, wg_blk, wu_blk, wd_blk, x2T,
                        sts[1], uid="f0", x_dep=[("a0", "xout")], pump=pump)
            while pump():
                pass
            sch.retire_dmas()
            sch.barrier()
        with contextlib.ExitStack() as bes:
            emit_l1_attn(nc, sch, bes, env, x2T, bw_blk, swT[:, :], sclT, grid_h,
                         RT_h, ow_blk, outb[:, :], sw_scr, qk_scr, v_scr1, x3T,
                         sts[2], uid="a1", x_dep=[("f0", "xout")])
            sch.retire_dmas()
            sch.barrier()
        with contextlib.ExitStack() as bes:
            emit_l1_ffn(nc, sch, bes, env, x3T, w1_blk, w2_blk, h1_blk, h2_blk,
                        w3_blk, scl, outT, sts[3], uid="f1", x_dep=[("a1", "xout")])
            sch.retire_dmas()
            sch.barrier()
        sch._emit("sync", lambda e: e.nop(), [(sts[3], sch.count(sts[3]))], None, 1)
        sch.retire_dmas("sync")
        sch.barrier()
    return nc


def prep_full_inputs(inputs):
    g = inputs
    m = {}
    m.update(prep_l0_attn_inputs(
        g["L0_attn_x"][0], g["L0_attn_Wq"], g["L0_attn_bq"], g["L0_attn_Wk"],
        g["L0_attn_bk"], g["L0_attn_Wv"], g["L0_attn_bv"], g["L0_attn_Wo"],
        g["L0_attn_bo"], g["L0_attn_head_norm_w"], g["L0_attn_lq1"],
        g["L0_attn_lk1"], g["L0_attn_lq2"], g["L0_attn_lk2"]))
    m.update(prep_l0_ffn_inputs(g["L0_attn_x"][0], g["L0_ffn_w_gate"],
                                g["L0_ffn_w_up"], g["L0_ffn_w_down"]))
    m.update(prep_l1_attn_inputs(
        g["L0_attn_x"][0], g["L1_attn_base_weight"], g["L1_attn_spline_weight"],
        g["L1_attn_spline_scaler"], g["L1_attn_grid"],
        g["L1_attn_rotation_matrix"], g["L1_attn_out_w"], g["L1_attn_out_b"]))
    m.update(prep_l1_ffn_inputs(
        g["L0_attn_x"][0], g["L1_ffn_w1_w"], g["L1_ffn_w1_s"], g["L1_ffn_w2_w"],
        g["L1_ffn_w2_s"], g["L1_ffn_h1_w"], g["L1_ffn_h1_s"], g["L1_ffn_h2_w"],
        g["L1_ffn_h2_s"], g["L1_ffn_w3_w"], g["L1_ffn_w3_s"]))
    m.pop("iden_h", None)
    return m


def kernel(**inputs):
    x = np.ascontiguousarray(inputs["L0_attn_x"])
    B = x.shape[0]
    common = prep_full_inputs(inputs)
    in_maps = []
    for b in range(B):
        m = dict(common)
        m["xT"] = np.ascontiguousarray(x[b].T)
        in_maps.append(m)
    nc = _get_prog("full", build_full)
    res = run_bass_kernel_spmd(nc, in_maps, core_ids=list(range(B)))
    return np.stack([r["outT"].T for r in res.results], axis=0)


# version 28
# speedup vs baseline: 1.0576x; 1.0205x over previous
import math
import numpy as np
import contextlib
from collections import defaultdict

import concourse.bass as bass
import concourse.mybir as mybir
from concourse.alu_op_type import AluOpType as OP
from concourse.bass_utils import run_bass_kernel_spmd

F32 = mybir.dt.float32
BF16 = mybir.dt.bfloat16
I32 = mybir.dt.int32
AF = mybir.ActivationFunctionType
AX = mybir.AxisListType.X

S = 2048
D = 1024

QUEUES = ("sync", "gpsimd", "vector", "scalar", "tensor")
ENG_SEM = ("vector", "scalar", "tensor", "gpsimd")


class Sched:

    def __init__(self, nc, es):
        self.nc = nc
        self.es = es
        self.q = defaultdict(list)
        self.cnt = {}
        self.esem = {}
        for eng in ENG_SEM:
            self.esem[eng] = es.enter_context(nc.semaphore(f"prog_{eng}"))
        self.eng_pos = defaultdict(int)
        self.pe_inc_positions = []
        self.last_w = {}
        self.readers = defaultdict(list)
        self.max_waited = defaultdict(dict)
        self.nwaits = 0
        self.nops = 0
        self.dma_sems = []

    def new_sem(self, name):
        s = self.es.enter_context(self.nc.semaphore(name))
        return s

    def new_dma_sem(self, name):
        s = self.new_sem(name)
        self.dma_sems.append(s)
        return s

    def retire_dmas(self, queue="vector"):
        waits = [(s, self.count(s)) for s in self.dma_sems]
        self._emit(queue, lambda e: e.nop(), waits, None, 1)

    def count(self, sem):
        return self.cnt.get(id(sem), 0)

    def _bump(self, sem, k=1):
        self.cnt[id(sem)] = self.cnt.get(id(sem), 0) + k
        return self.cnt[id(sem)]

    def _resolve(self, h, dst_eng):
        if h is None:
            return None
        sem, val, src_eng = h
        if src_eng == "tensor" and dst_eng == "tensor":
            return None
        if src_eng == "dma":
            val = self.cnt.get(id(sem), val)
        return (sem, val)

    def _emit(self, queue, fn, waits, inc_sem, inc_k):
        bysem = {}
        for w in waits:
            if w is None:
                continue
            s, v = w
            if v <= 0:
                continue
            k = id(s)
            if k not in bysem or v > bysem[k][1]:
                bysem[k] = (s, v)
        wl = []
        for s, v in bysem.values():
            prev = self.max_waited[queue].get(id(s), -1)
            if v > prev:
                self.max_waited[queue][id(s)] = v
                wl.append((s, v))
        self.nwaits += len(wl)
        self.nops += 1

        def emit(e, fn=fn, wl=tuple(wl), inc_sem=inc_sem, inc_k=inc_k):
            for (s, v) in wl:
                e.wait_ge(s, v)
            inst = fn(e)
            if inc_sem is not None:
                inst.then_inc(inc_sem, inc_k)

        self.q[queue].append(emit)

    def _dep_waits(self, dst_eng, reads, writes):
        waits = []
        for tok in reads:
            try:
                waits.append(self._resolve(self.last_w.get(tok), dst_eng))
            except RuntimeError as e:
                raise RuntimeError(f"{e} (RAW on token {tok})")
        for tok in writes:
            for h in self.readers.get(tok, ()):
                try:
                    waits.append(self._resolve(h, dst_eng))
                except RuntimeError as e:
                    raise RuntimeError(f"{e} (WAR on token {tok})")
            try:
                waits.append(self._resolve(self.last_w.get(tok), dst_eng))
            except RuntimeError as e:
                raise RuntimeError(f"{e} (WAW on token {tok})")
        return waits

    def op(self, eng, fn, reads=(), writes=(), waits=(), inc=True):
        dep = self._dep_waits(eng, reads, writes)
        dep.extend(waits)
        val = self._bump(self.esem[eng])
        self._emit(eng, fn, dep, self.esem[eng], 1)
        handle = (self.esem[eng], val, eng)
        self.eng_pos[eng] += 1
        for tok in writes:
            self.last_w[tok] = handle
            self.readers[tok] = []
        for tok in reads:
            self.readers[tok].append(handle)
        return handle

    def dma(self, queue, fn, sem, reads=(), writes=(), waits=()):
        dep = self._dep_waits("dma", reads, writes)
        dep.extend(waits)
        val = self._bump(sem, 16)
        self._emit(queue, fn, dep, sem, 16)
        handle = (sem, val, "dma")
        for tok in writes:
            self.last_w[tok] = handle
            self.readers[tok] = []
        for tok in reads:
            self.readers[tok].append(handle)
        return handle

    def fence_tokens(self, queue, toks):
        waits = self._dep_waits("dma", toks, toks)
        self._emit(queue, lambda e: e.nop(), waits, None, 1)

    def barrier(self):
        with self.nc.Block(no_gpsimd_drain=True) as block:
            for eng in QUEUES:
                items = self.q.get(eng)
                if not items:
                    continue

                def runner(e, items=list(items)):
                    for it in items:
                        it(e)

                getattr(block, eng)(runner)
        self.q.clear()


def mm_F(sch, psum_ap, lhsT_ap, rhs_ap, start, stop, reads=(), writes=(),
         waits=(), fmax=512, tile_position=None, abs_off=0):
    NF = rhs_ap.shape[-1]
    cuts = [0]
    pos = 0
    while pos < NF:
        nxt = min(NF, pos + fmax - ((abs_off + pos) % fmax))
        cuts.append(nxt)
        pos = nxt
    h = None
    for i in range(len(cuts) - 1):
        lo, hi = cuts[i], cuts[i + 1]
        h = sch.op("tensor",
                   lambda t, lo=lo, hi=hi: t.matmul(
                       psum_ap[:, lo:hi], lhsT_ap, rhs_ap[:, lo:hi],
                       start=start, stop=stop, tile_position=tile_position),
                   reads=reads, writes=writes,
                   waits=waits if i == 0 else (),
                   inc=(stop and i == len(cuts) - 2))
    return h


class DBuf:

    def __init__(self, nc, es, sch, name, shape, dtype, nbuf=2, queue="gpsimd",
                 tiles=None, shared_sem=False):
        self.sch = sch
        self.queue = queue
        self.name = name
        if tiles is None:
            self.tiles = [es.enter_context(nc.sbuf_tensor(f"{name}_{i}", shape, dtype))
                          for i in range(nbuf)]
        else:
            self.tiles = tiles
        if shared_sem:
            s = sch.new_dma_sem(f"{name}_s")
            self.sems = [s] * nbuf
        else:
            self.sems = [sch.new_dma_sem(f"{name}_s{i}") for i in range(nbuf)]
        self.nbuf = nbuf

    def tok(self, idx):
        return (self.name, idx % self.nbuf)

    def load(self, idx, src_ap, reads=()):
        b = idx % self.nbuf
        tile = self.tiles[b]
        dst = tile[:] if hasattr(tile, "ap") else tile
        self.sch.dma(self.queue,
                     lambda e, dst=dst, src_ap=src_ap: e.dma_start(out=dst, in_=src_ap),
                     sem=self.sems[b], reads=list(reads), writes=[self.tok(idx)])
        return tile


def make_env(nc, es, sch):
    env = {"nc": nc}
    t = lambda name, shape, dt: es.enter_context(nc.sbuf_tensor(name, shape, dt))
    env["sbuf"] = t
    env["ones128"] = t("c_ones128", [128, 1], BF16)
    env["ones1"] = t("c_ones1", [1, 128], BF16)
    env["ones1f"] = t("c_ones1f", [1, 128], F32)
    NCONST = 16
    cst = t("c_fconsts", [128, NCONST], F32)
    env["_cst_tile"] = cst
    env["_cst_vals"] = []

    def const(value):
        vals = env["_cst_vals"]
        value = float(value)
        if value not in vals:
            assert len(vals) < NCONST
            vals.append(value)
            i = len(vals) - 1
            nc.const_aps.register(F32, value, cst[:, i:i + 1])
            sch.op("vector", lambda v, i=i, value=value: v.memset(
                cst[:, i:i + 1], value), writes=[("cst",)])
        return value

    env["const"] = const
    env["ln_ybf"] = [t(f"ln_ybf{i}", [128, S // 2], BF16) for i in range(2)]
    env["ln_ysq"] = [t(f"ln_ysq{i}", [128, S // 2], BF16) for i in range(2)]
    env["ln_rows"] = t("ln_rows", [1, 3 * (S // 2)], F32)
    env["ln_tnorm"] = t("ln_tnorm", [128, S // 2], F32)
    sch.op("vector", lambda v: v.memset(env["ones128"][:], 1.0), writes=["ones128"])
    sch.op("vector", lambda v: v.memset(env["ones1"][:], 1.0), writes=["ones1"])
    sch.op("vector", lambda v: v.memset(env["ones1f"][:], 1.0), writes=["ones1"])
    return env



def emit_layernorm(sch, env, y_chunks, y_toks, ps_rows, pb, uid,
                   out_f32=None, out_f32_toks=None, out_bf16=None,
                   out_bf16_toks=None, eps=1e-5, out_dma=None,
                   ps_tok=None, pb_tok=None):
    N = y_chunks[0].shape[-1]
    ybf, ysq = env["ln_ybf"], env["ln_ysq"]
    rows, tnorm = env["ln_rows"], env["ln_tnorm"]
    ones128, ones1 = env["ones128"], env["ones1"]
    r_mean, r_var, r_m2 = rows[:, 0:N], rows[:, N:2 * N], rows[:, 2 * N:3 * N]
    ps_sum, ps_sq = ps_rows[:, 0:N], ps_rows[:, N:2 * N]
    pb_m, pb_r = pb[:, 0:N], pb[:, N:2 * N]
    T_ps, T_pb = ps_tok or (uid, "ps_rows"), pb_tok or (uid, "pb")
    b_eps = env["const"](eps)
    s_half = env["const"](-0.5)

    for c in range(8):
        tb, tq = ("ln_ybf", c % 2), ("ln_ysq", c % 2)
        sch.op("vector", lambda v, c=c: v.tensor_copy(ybf[c % 2][:, 0:N], y_chunks[c]),
               reads=[y_toks[c]], writes=[tb])
        sch.op("vector", lambda v, c=c: v.tensor_tensor(
            ysq[c % 2][:, 0:N], ybf[c % 2][:, 0:N], ybf[c % 2][:, 0:N], op=OP.mult),
            reads=[tb], writes=[tq])
        mm_F(sch, ps_sum, ones128[:], ybf[c % 2][:, 0:N], start=(c == 0), stop=(c == 7),
             reads=[tb, "ones128"], writes=[T_ps])
        mm_F(sch, ps_sq, ones128[:], ysq[c % 2][:, 0:N], start=(c == 0), stop=(c == 7),
             reads=[tq, "ones128"], writes=[T_ps])

    sch.op("vector", lambda v: v.tensor_scalar(r_mean, ps_sum, 1.0 / D, None, op0=OP.mult),
           reads=[T_ps], writes=[("ln", "mean")])
    sch.op("vector", lambda v: v.tensor_tensor(r_m2, r_mean, r_mean, op=OP.mult),
           reads=[("ln", "mean")], writes=[("ln", "m2")])
    sch.op("vector", lambda v: v.scalar_tensor_tensor(r_var, ps_sq, 1.0 / D, r_m2,
                                                      op0=OP.mult, op1=OP.subtract),
           reads=[T_ps, ("ln", "m2")], writes=[("ln", "var")])
    sch.op("scalar", lambda s: s.activation(out=r_var, in_=r_var, func=AF.Ln, bias=b_eps),
           reads=[("ln", "var"), ("cst",)], writes=[("ln", "var")])
    sch.op("scalar", lambda s: s.activation(out=r_var, in_=r_var, func=AF.Exp, scale=s_half),
           reads=[("ln", "var")], writes=[("ln", "var")])

    mm_F(sch, pb_m, env["ones1f"][:], r_mean, start=True, stop=True,
         reads=[("ln", "mean"), "ones1"], writes=[T_pb])
    mm_F(sch, pb_r, env["ones1f"][:], r_var, start=True, stop=True,
         reads=[("ln", "var"), "ones1"], writes=[T_pb])

    fins = {}
    for c in range(8):
        tt = ("ln_tnorm",)
        sch.op("vector", lambda v, c=c: v.tensor_tensor(tnorm[:, 0:N], y_chunks[c], pb_m, op=OP.subtract),
               reads=[y_toks[c], T_pb], writes=[tt])
        if out_f32 is not None:
            dst, dtok = out_f32[c], out_f32_toks[c]
            sch.op("vector", lambda v, dst=dst: v.tensor_tensor(dst, tnorm[:, 0:N], pb_r, op=OP.mult),
                   reads=[tt, T_pb], writes=[dtok])
            src, stok = dst, dtok
        else:
            sch.op("vector", lambda v: v.tensor_tensor(tnorm[:, 0:N], tnorm[:, 0:N], pb_r, op=OP.mult),
                   reads=[tt, T_pb], writes=[tt])
            src, stok = tnorm[:, 0:N], tt
        fins[c] = stok
        if out_bf16 is not None:
            sch.op("vector", lambda v, c=c, src=src: v.tensor_copy(out_bf16[c], src),
                   reads=[stok], writes=[out_bf16_toks[c]])
        if out_dma is not None:
            dram_ap, dsem = out_dma(c)
            sch.dma("sync", lambda sy, src=src, dram_ap=dram_ap: sy.dma_start(out=dram_ap, in_=src),
                    sem=dsem, reads=[stok], writes=[(uid, "xout")])
    return fins



def emit_l0_ffn(nc, sch, es, env, xT, wg_blk, wu_blk, wd_blk, outT, st_sem,
                uid="f0", out_bf16=None, out_bf16_toks=None,
                x_bf=None, x_bf_toks=None, x_dep=(), pump=None):
    HID = 3072
    NHS = HID // 128
    NH = S // 2
    t = lambda name, shape, dt: es.enter_context(nc.sbuf_tensor(name, shape, dt))
    if x_bf is None:
        x_st = DBuf(nc, es, sch, f"{uid}_xbf", [128, S], BF16, nbuf=8, shared_sem=True)
        xbf = x_st.tiles
        xbf_toks = [x_st.tok(c) for c in range(8)]
        for c in range(8):
            x_st.load(c, xT[128 * c:128 * (c + 1), :], reads=x_dep)
    else:
        xbf, xbf_toks = x_bf, x_bf_toks
    H = [t(f"{uid}_H{i}", [128, NH], BF16) for i in range(NHS)]
    sg = [t(f"{uid}_sg{i}", [128, NH // 2], F32) for i in range(2)]
    y = [t(f"{uid}_y{c}", [128, NH], F32) for c in range(8)]
    wg_st = DBuf(nc, es, sch, f"{uid}_wg", [128, 8 * 128], BF16)
    wu_st = DBuf(nc, es, sch, f"{uid}_wu", [128, 8 * 128], BF16)
    wd_st = DBuf(nc, es, sch, f"{uid}_wd", [128, NHS * 64], BF16)
    xf_st = DBuf(nc, es, sch, f"{uid}_xf", [128, NH], F32, queue="sync")

    for half in range(2):
        hlo = half * NH
        with contextlib.ExitStack() as pes:
            p = lambda name, shape: pes.enter_context(nc.psum_tensor(name, shape, F32))
            ps_g = [p(f"{uid}_ps_g{half}_{i}", [128, NH // 2]) for i in range(2)]
            ps_u = [p(f"{uid}_ps_u{half}_{i}", [128, NH // 2]) for i in range(2)]

            def load_gu(hs):
                wg_st.load(hs, wg_blk[hs].rearrange("p c h -> p (c h)"))
                wu_st.load(hs, wu_blk[hs].rearrange("p c h -> p (c h)"))

            load_gu(0)
            load_gu(1)
            for hs in range(NHS):
                if pump:
                    pump()
                b = hs % 2
                wg_b, wu_b = wg_st.tiles[b], wu_st.tiles[b]
                for nh in range(2):
                    nlo = hlo + nh * (NH // 2)
                    for c in range(8):
                        mm_F(sch, ps_g[nh][:], wg_b[:, 128 * c:128 * (c + 1)],
                             xbf[c][:, nlo:nlo + NH // 2], start=(c == 0), stop=(c == 7),
                             reads=[wg_st.tok(hs), xbf_toks[c]],
                             writes=[(uid, "ps_g", half, nh)])
                    for c in range(8):
                        mm_F(sch, ps_u[nh][:], wu_b[:, 128 * c:128 * (c + 1)],
                             xbf[c][:, nlo:nlo + NH // 2], start=(c == 0), stop=(c == 7),
                             reads=[wu_st.tok(hs), xbf_toks[c]],
                             writes=[(uid, "ps_u", half, nh)])
                    s127 = env["const"](1.0 / 127.0)
                    sch.op("scalar", lambda s_, nh=nh, s127=s127: s_.activation(
                        out=sg[nh][:], in_=ps_g[nh][:], func=AF.Silu, scale=s127),
                        reads=[(uid, "ps_g", half, nh), ("cst",)],
                        writes=[(uid, "sg", nh)])
                    sch.op("vector", lambda v_, hs=hs, nh=nh: v_.scalar_tensor_tensor(
                        H[hs][:, nh * (NH // 2):(nh + 1) * (NH // 2)], ps_u[nh][:],
                        1.0 / 127.0, sg[nh][:], op0=OP.mult, op1=OP.mult),
                        reads=[(uid, "ps_u", half, nh), (uid, "sg", nh)],
                        writes=[(uid, "H", hs)])
                if hs + 2 < NHS:
                    load_gu(hs + 2)
            sch.barrier()

        with contextlib.ExitStack() as pes:
            p = lambda name, shape: pes.enter_context(nc.psum_tensor(name, shape, F32))
            ps_d = [p(f"{uid}_ps_d{half}_{i}", [128, NH]) for i in range(2)]

            HH = NHS // 2

            def load_dh(idx):
                os, hh = idx // 2, idx % 2
                wd_st.load(idx, wd_blk[os][:, HH * hh:HH * (hh + 1), :]
                           .rearrange("p c h -> p (c h)"))

            def load_x(os):
                xf_st.load(os, xT[128 * os:128 * (os + 1), hlo:hlo + NH], reads=x_dep)

            load_dh(0)
            load_dh(1)
            load_x(0)
            load_x(1)
            for os in range(8):
                b = os % 2
                xf = xf_st.tiles[b]
                for hc in range(NHS):
                    widx = 2 * os + hc // HH
                    wd_b = wd_st.tiles[widx % 2]
                    mm_F(sch, ps_d[b][:], wd_b[:, 128 * (hc % HH):128 * (hc % HH + 1)],
                         H[hc][:], start=(hc == 0), stop=(hc == NHS - 1),
                         reads=[wd_st.tok(widx), (uid, "H", hc)],
                         writes=[(uid, "ps_d", half, b)])
                    if hc == HH and 2 * os + 2 < 16:
                        load_dh(2 * os + 2)
                sch.op("vector", lambda v_, os=os, b=b, xf=xf: v_.scalar_tensor_tensor(
                    y[os][:], ps_d[b][:], 1.0 / 127.0, xf[:],
                    op0=OP.mult, op1=OP.add),
                    reads=[(uid, "ps_d", half, b), xf_st.tok(os)],
                    writes=[(uid, "y", os)])
                if pump:
                    pump()
                if 2 * os + 3 < 16:
                    load_dh(2 * os + 3)
                if os + 2 < 8:
                    load_x(os + 2)
            sch.barrier()

        with contextlib.ExitStack() as pes:
            ps_rows = pes.enter_context(nc.psum_tensor(f"{uid}_lnr{half}", [1, 2 * NH], F32)).ap()
            pb = pes.enter_context(nc.psum_tensor(f"{uid}_lnb{half}", [128, 2 * NH], F32)).ap()
            ob = [out_bf16[c][:, hlo:hlo + NH] for c in range(8)] if out_bf16 else None
            obt = ([(tok, half) for tok in out_bf16_toks] if out_bf16 else None)
            emit_layernorm(sch, env, [y[c][:] for c in range(8)],
                           [(uid, "y", c) for c in range(8)], ps_rows, pb, uid,
                           out_bf16=ob, out_bf16_toks=obt,
                           out_dma=lambda c: (outT[128 * c:128 * (c + 1), hlo:hlo + NH], st_sem))
            sch.barrier()


def build_l0_ffn_standalone():
    nc = bass.Bass("TRN2", target_bir_lowering=False)
    NHS = 24
    xT = nc.dram_tensor("xT", [D, S], F32, kind="ExternalInput").ap()
    wg_blk = nc.dram_tensor("wg_blk", [NHS, 128, 8, 128], I32, kind="ExternalInput").ap()
    wu_blk = nc.dram_tensor("wu_blk", [NHS, 128, 8, 128], I32, kind="ExternalInput").ap()
    wd_blk = nc.dram_tensor("wd_blk", [8, 128, NHS, 128], I32, kind="ExternalInput").ap()
    outT = nc.dram_tensor("outT", [D, S], F32, kind="ExternalOutput").ap()
    es = contextlib.ExitStack()
    with es:
        sch = Sched(nc, es)
        env = make_env(nc, es, sch)
        st_sem = es.enter_context(nc.semaphore("sem_st"))
        emit_l0_ffn(nc, sch, es, env, xT, wg_blk, wu_blk, wd_blk, outT, st_sem)
        sch._emit("sync", lambda e: e.nop(), [(st_sem, sch.count(st_sem))], None, 1)
        sch.barrier()
    return nc


def prep_l0_ffn_inputs(x_bt, w_gate, w_up, w_down):
    xT = np.ascontiguousarray(x_bt.T)
    wg_blk = np.ascontiguousarray(
        w_gate.T.reshape(8, 128, 24, 128).transpose(2, 1, 0, 3))
    wu_blk = np.ascontiguousarray(
        w_up.T.reshape(8, 128, 24, 128).transpose(2, 1, 0, 3))
    wd_blk = np.ascontiguousarray(
        w_down.T.reshape(24, 128, 8, 128).transpose(2, 1, 0, 3))
    return {"xT": xT, "wg_blk": wg_blk, "wu_blk": wu_blk, "wd_blk": wd_blk}



def emit_l1_ffn(nc, sch, es, env, xT, w1_blk, w2_blk, h1_blk, h2_blk, w3_blk,
                scl, outT, st_sem, uid="f1", out_bf16=None, out_bf16_toks=None,
                x_dep=(), pump=None):
    DH = 4096
    NHS = DH // 128
    NH = S // 2
    NQ = NH // 2
    t = lambda name, shape, dt: es.enter_context(nc.sbuf_tensor(name, shape, dt))
    A = [t(f"{uid}_A{i}", [128, NH], BF16) for i in range(NHS)]
    B = [t(f"{uid}_B{i}", [128, NH], BF16) for i in range(NHS)]
    sg = [t(f"{uid}_sg{i}", [128, NH // 2], F32) for i in range(2)]
    scl_sb = t(f"{uid}_scl", [128, 4 * NHS + 8], F32)
    s1_sb = scl_sb[:, 0:NHS]
    s2_sb = scl_sb[:, NHS:2 * NHS]
    h1s_sb = scl_sb[:, 2 * NHS:3 * NHS]
    h2s_sb = scl_sb[:, 3 * NHS:4 * NHS]
    w3s_sb = scl_sb[:, 4 * NHS:4 * NHS + 8]
    x_st = DBuf(nc, es, sch, f"{uid}_xbf", [128, NH], BF16, nbuf=8, shared_sem=True)
    y = [B[c].ap().bitcast(F32)[:, 0:NQ] for c in range(8)]
    w1_st = DBuf(nc, es, sch, f"{uid}_w1", [128, 8 * 128], BF16)
    w2_st = DBuf(nc, es, sch, f"{uid}_w2", [128, 8 * 128], BF16)
    hw_st = DBuf(nc, es, sch, f"{uid}_hw", [128, NHS * 128], BF16)
    xf_st = DBuf(nc, es, sch, f"{uid}_xf", [128, NQ], F32, queue="sync")
    ssem = sch.new_dma_sem(f"{uid}_ssem")
    sch.dma("sync", lambda sy: sy.dma_start(out=scl_sb[:], in_=scl),
            sem=ssem, writes=[(uid, "scl")])

    f_pes = contextlib.ExitStack()
    psA = [f_pes.enter_context(nc.psum_tensor(f"{uid}_psA{i}", [128, NH // 2], F32))
           for i in range(4)]
    psB = [f_pes.enter_context(nc.psum_tensor(f"{uid}_psB{i}", [128, NH], F32))
           for i in range(2)]
    TPA = [(uid, "psA", i) for i in range(4)]
    TPB = [(uid, "psB", i) for i in range(2)]
    hw_idx = [0]

    def hw_load(blk_row):
        i = hw_idx[0]
        hw_idx[0] += 1
        hw_st.load(i, blk_row.rearrange("p c h -> p (c h)"))
        return hw_st.tiles[i % 2], hw_st.tok(i)

    for half in range(2):
        hlo = half * NH
        for c in range(8):
            x_st.load(c + 8 * half, xT[128 * c:128 * (c + 1), hlo:hlo + NH], reads=x_dep)
        xbf = x_st.tiles
        xbf_toks = [x_st.tok(c) for c in range(8)]

        if True:
            ps_g, ps_u = psA[0:2], psA[2:4]

            def load_gu(hs):
                w1_st.load(hs, w1_blk[hs].rearrange("p c h -> p (c h)"))
                w2_st.load(hs, w2_blk[hs].rearrange("p c h -> p (c h)"))

            load_gu(0)
            load_gu(1)
            for hs in range(NHS):
                if pump:
                    pump()
                b = hs % 2
                for nh in range(2):
                    nlo = nh * (NH // 2)
                    for c in range(8):
                        mm_F(sch, ps_g[nh][:], w1_st.tiles[b][:, 128 * c:128 * (c + 1)],
                             xbf[c][:, nlo:nlo + NH // 2], start=(c == 0), stop=(c == 7),
                             reads=[w1_st.tok(hs), xbf_toks[c]],
                             writes=[TPA[nh]])
                    for c in range(8):
                        mm_F(sch, ps_u[nh][:], w2_st.tiles[b][:, 128 * c:128 * (c + 1)],
                             xbf[c][:, nlo:nlo + NH // 2], start=(c == 0), stop=(c == 7),
                             reads=[w2_st.tok(hs), xbf_toks[c]],
                             writes=[TPA[2 + nh]])
                    sch.op("scalar", lambda s_, nh=nh, hs=hs: s_.activation(
                        out=sg[nh][:], in_=ps_g[nh][:], func=AF.Silu,
                        scale=s1_sb[:, hs:hs + 1]),
                        reads=[TPA[nh], (uid, "scl")],
                        writes=[(uid, "sg", nh)])
                    sch.op("vector", lambda v_, hs=hs, nh=nh, nlo=nlo: v_.scalar_tensor_tensor(
                        A[hs][:, nlo:nlo + NH // 2], ps_u[nh][:],
                        s2_sb[:, hs:hs + 1], sg[nh][:], op0=OP.mult, op1=OP.mult),
                        reads=[TPA[2 + nh], (uid, "sg", nh), (uid, "scl")],
                        writes=[(uid, "A", hs)])
                if hs + 2 < NHS:
                    load_gu(hs + 2)

        for phase, (w_blk, s_sb, s_name, src, src_tokname, dst, dst_tokname) in enumerate(
                ((h1_blk, h1s_sb, "h1s", A, "A", B, "B"),
                 (h2_blk, h2s_sb, "h2s", B, "B", A, "A"))):
            if True:
                ps_b = psB
                tiles = {}
                tiles[0] = hw_load(w_blk[0])
                tiles[1] = hw_load(w_blk[1])
                for hs in range(NHS):
                    wtile, wtok = tiles[hs]
                    b = hs % 2
                    for c in range(NHS):
                        mm_F(sch, ps_b[b][:], wtile[:, 128 * c:128 * (c + 1)],
                             src[c][:], start=(c == 0), stop=(c == NHS - 1),
                             reads=[wtok, (uid, src_tokname, c)],
                             writes=[TPB[b]])
                    wl = [(uid, dst_tokname, hs)]
                    if dst_tokname == "B" and hs < 8:
                        wl.append((uid, "y", hs))
                    sch.op("scalar", lambda s_, hs=hs, b=b, dst=dst, s_sb=s_sb: s_.activation(
                        out=dst[hs][:], in_=ps_b[b][:], func=AF.Gelu,
                        scale=s_sb[:, hs:hs + 1]),
                        reads=[TPB[b], (uid, "scl", s_name)],
                        writes=wl)
                    if hs + 2 < NHS:
                        tiles[hs + 2] = hw_load(w_blk[hs + 2])

        for q in range(2):
            qlo = hlo + q * NQ
            if True:
                ps_d = psA[0:2]

                def load_d(os):
                    wtile, wtok = hw_load(w3_blk[os])
                    xf_st.load(os, xT[128 * os:128 * (os + 1), qlo:qlo + NQ], reads=x_dep)
                    return wtile, wtok

                dtiles = {0: load_d(0), 1: load_d(1)}
                for os in range(8):
                    wtile, wtok = dtiles[os]
                    b = os % 2
                    for hc in range(NHS):
                        mm_F(sch, ps_d[b][:], wtile[:, 128 * hc:128 * (hc + 1)],
                             A[hc][:, q * NQ:(q + 1) * NQ], start=(hc == 0),
                             stop=(hc == NHS - 1),
                             reads=[wtok, (uid, "A", hc)],
                             writes=[TPA[b]])
                    sch.op("vector", lambda v_, os=os, b=b: v_.scalar_tensor_tensor(
                        y[os], ps_d[b][:], w3s_sb[:, os:os + 1], xf_st.tiles[b][:],
                        op0=OP.mult, op1=OP.add),
                        reads=[TPA[b], xf_st.tok(os), (uid, "scl")],
                        writes=[(uid, "y", os), (uid, "B", os)])
                    if os + 2 < 8:
                        dtiles[os + 2] = load_d(os + 2)
            if True:
                ps_rows = psB[1].ap()[0:1, 0:2 * NQ]
                pb = psB[0].ap()[:, 0:2 * NQ]
                ob = [out_bf16[c][:, qlo:qlo + NQ] for c in range(8)] if out_bf16 else None
                obt = ([(tok, half, q) for tok in out_bf16_toks] if out_bf16 else None)
                emit_layernorm(sch, env, [y[c] for c in range(8)],
                               [(uid, "y", c) for c in range(8)], ps_rows, pb, uid,
                               out_bf16=ob, out_bf16_toks=obt,
                               out_dma=lambda c: (outT[128 * c:128 * (c + 1), qlo:qlo + NQ], st_sem),
                               ps_tok=TPB[1], pb_tok=TPB[0])
    f_pes.close()


def build_l1_ffn_standalone():
    nc = bass.Bass("TRN2", target_bir_lowering=False)
    NHS = 32
    xT = nc.dram_tensor("xT", [D, S], F32, kind="ExternalInput").ap()
    w1_blk = nc.dram_tensor("w1_blk", [NHS, 128, 8, 128], F32, kind="ExternalInput").ap()
    w2_blk = nc.dram_tensor("w2_blk", [NHS, 128, 8, 128], F32, kind="ExternalInput").ap()
    h1_blk = nc.dram_tensor("h1_blk", [NHS, 128, NHS, 128], F32, kind="ExternalInput").ap()
    h2_blk = nc.dram_tensor("h2_blk", [NHS, 128, NHS, 128], F32, kind="ExternalInput").ap()
    w3_blk = nc.dram_tensor("w3_blk", [8, 128, NHS, 128], F32, kind="ExternalInput").ap()
    scl = nc.dram_tensor("scl", [128, 4 * NHS + 8], F32, kind="ExternalInput").ap()[:, :]
    outT = nc.dram_tensor("outT", [D, S], F32, kind="ExternalOutput").ap()
    es = contextlib.ExitStack()
    with es:
        sch = Sched(nc, es)
        env = make_env(nc, es, sch)
        st_sem = es.enter_context(nc.semaphore("sem_st"))
        emit_l1_ffn(nc, sch, es, env, xT, w1_blk, w2_blk, h1_blk, h2_blk, w3_blk,
                    scl, outT, st_sem)
        sch._emit("sync", lambda e: e.nop(), [(st_sem, sch.count(st_sem))], None, 1)
        sch.barrier()
    return nc


def prep_l1_ffn_inputs(x_bt, w1w, w1s, w2w, w2s, h1w, h1s, h2w, h2s, w3w, w3s):
    xT = np.ascontiguousarray(x_bt.T)
    blk = lambda w, nin, nout: np.ascontiguousarray(
        w.T.reshape(nin // 128, 128, nout // 128, 128).transpose(2, 1, 0, 3))
    fm = lambda s: np.ascontiguousarray(s.reshape(-1, 128).T)
    scl = np.concatenate([fm(w1s), fm(w2s), fm(h1s), fm(h2s), fm(w3s)], axis=1)
    return {"xT": xT,
            "w1_blk": blk(w1w, 1024, 4096), "w2_blk": blk(w2w, 1024, 4096),
            "h1_blk": blk(h1w, 4096, 4096), "h2_blk": blk(h2w, 4096, 4096),
            "w3_blk": blk(w3w, 4096, 1024),
            "scl": np.ascontiguousarray(scl)}



def emit_l0_attn(nc, sch, es, env, xT, wq_blk, wk_blk, wv_blk, wo_blk,
                 biases, lqk, hnw, cs_tab_h, ss_tab_h, perm_h, ut_h, iden_h,
                 v_scr, outT, st_sem, uid="a0", out_bf16=None, out_bf16_toks=None,
                 debug_afm=False, pump=None):
    t = lambda name, shape, dt: es.enter_context(nc.sbuf_tensor(name, shape, dt))
    NHEAD = 16
    cs_tab = t(f"{uid}_cs", [128, S], F32)
    ss_tab = t(f"{uid}_ss", [128, S], F32)
    perm = t(f"{uid}_perm", [128, 128], F32)
    ut = t(f"{uid}_ut", [128, 128], BF16)
    iden = t(f"{uid}_iden", [128, 128], BF16)
    bias_sb = t(f"{uid}_bias", [128, 48], F32)
    hnw_sb = t(f"{uid}_hnw", [64, 16], F32)
    hnw08 = t(f"{uid}_hnw08", [64, 16], F32)
    lqk_sb = t(f"{uid}_lqk", [64, 64], F32)
    lqk_prod = t(f"{uid}_lqkp", [64, 32], BF16)
    lamt = t(f"{uid}_lam", [1, 64], F32)
    attn_fm = [t(f"{uid}_afm{c}", [128, S], BF16) for c in range(8)]
    aug = [t(f"{uid}_aug{i}", [128, S], BF16) for i in range(8)]
    y = [aug[c].ap().bitcast(F32) for c in range(8)]
    qpre = [t(f"{uid}_qpre{i}", [128, 1024], F32) for i in range(2)]
    tcs = t(f"{uid}_tcs", [128, 1024], F32)
    uss = t(f"{uid}_uss", [128, 1024], F32)
    vstage = t(f"{uid}_vstage", [128, 1024], BF16)
    sqt = t(f"{uid}_sqt", [64, 1024], BF16)
    E = [t(f"{uid}_E{i}", [128, 1024], BF16) for i in range(2)]
    bsq = E[0][0:64, 0:1024]
    o12 = [t(f"{uid}_o{i}", [64, 1024], BF16) for i in range(2)]
    comb = t(f"{uid}_comb", [64, 1024], F32)
    combt = t(f"{uid}_combt", [64, 1024], F32)

    rows = t(f"{uid}_rows", [1, 1032], F32)

    rowsb = t(f"{uid}_rowsb", [1, 2048], F32)
    rbh = t(f"{uid}_rbh", [1, 2048], BF16)
    sc16 = t(f"{uid}_sc16", [1, 16], F32)
    fint = comb.ap().bitcast(BF16)[0:64, 0:1024]
    v_aug = [t(f"{uid}_vaug{i}", [128, 16 * 80], BF16) for i in range(2)]
    wq_st = DBuf(nc, es, sch, f"{uid}_wq", [128, 8 * 128], BF16)
    wk_st = DBuf(nc, es, sch, f"{uid}_wk", [128, 8 * 128], BF16)
    wv_st = DBuf(nc, es, sch, f"{uid}_wv", [128, 8 * 128], BF16)
    x_st = DBuf(nc, es, sch, f"{uid}_xbf", [128, S], BF16, nbuf=8, shared_sem=True)

    vssem = sch.new_dma_sem(f"{uid}_vs")
    vtsem = [sch.new_dma_sem(f"{uid}_vt{i}") for i in range(2)]

    SQ8 = env["const"](0.125)

    for c in range(8):
        x_st.load(c, xT[128 * c:128 * (c + 1), :])
    xbf, xbf_toks = x_st.tiles, [x_st.tok(c) for c in range(8)]
    for i, (dst, srca, tok, q) in enumerate((
            (cs_tab, cs_tab_h, "cs", "sync"), (ss_tab, ss_tab_h, "ss", "sync"),
            (perm, perm_h, "perm", "sync"), (ut, ut_h, "ut", "gpsimd"),
            (iden, iden_h, "iden", "gpsimd"), (bias_sb, biases, "bias", "sync"),
            (hnw_sb, hnw, "hnw", "sync"), (lqk_sb, lqk, "lqk", "sync"))):
        csem_i = sch.new_dma_sem(f"{uid}_cs{i}")
        sch.dma(q, lambda e, dst=dst, srca=srca: e.dma_start(out=dst[:], in_=srca),
                sem=csem_i, writes=[(uid, tok)])
    sch.op("vector", lambda v: v.tensor_scalar(hnw08[:], hnw_sb[:], 0.8, None, op0=OP.mult),
           reads=[(uid, "hnw")], writes=[(uid, "hnw08")])
    sch.op("vector", lambda v: v.tensor_tensor(lqk_prod[:, 0:16], lqk_sb[:, 0:16],
                                               lqk_sb[:, 16:32], op=OP.mult),
           reads=[(uid, "lqk")], writes=[(uid, "p12")])
    sch.op("vector", lambda v: v.tensor_tensor(lqk_prod[:, 16:32], lqk_sb[:, 32:48],
                                               lqk_sb[:, 48:64], op=OP.mult),
           reads=[(uid, "lqk")], writes=[(uid, "p12")])
    with contextlib.ExitStack() as pes0:
        ps0 = pes0.enter_context(nc.psum_tensor(f"{uid}_ps0", [1, 32], F32))
        mm_F(sch, ps0[:], env["ones128"][0:64, :], lqk_prod[:],
             start=True, stop=True, reads=[(uid, "p12"), "ones128"],
             writes=[(uid, "ps0")])
        sch.op("scalar", lambda s: s.activation(out=lamt[:, 32:64], in_=ps0[:], func=AF.Exp),
               reads=[(uid, "ps0")], writes=[(uid, "lame")])
    sch.op("vector", lambda v: v.tensor_tensor(lamt[:, 0:16], lamt[:, 32:48],
                                               lamt[:, 48:64], op=OP.subtract),
           reads=[(uid, "lame")], writes=[(uid, "lamf")])
    sch.op("vector", lambda v: v.tensor_scalar(lamt[:, 0:16], lamt[:, 0:16], 0.2, None, op0=OP.add),
           reads=[(uid, "lamf")], writes=[(uid, "lamf")])
    lam_row = lamt[:, 0:16]

    for j in range(8):
        sch.op("vector", lambda v, j=j: v.memset(aug[j][64:128, :], 0.0),
               writes=[(uid, "augpad"), (uid, "aug_ones", j)])
        if j % 4 >= 2:
            sch.op("vector", lambda v, j=j: v.memset(aug[j][64:65, :], 1.0),
                   writes=[(uid, "aug_ones", j)], reads=[(uid, "augpad")])
    sch.barrier()

    g_pes = contextlib.ExitStack()
    psA = g_pes.enter_context(nc.psum_tensor(f"{uid}_psA", [128, 1024], F32))
    psB = g_pes.enter_context(nc.psum_tensor(f"{uid}_psB", [128, 1024], F32))
    psC = g_pes.enter_context(nc.psum_tensor(f"{uid}_psC", [128, S], F32))
    TA, TB, TC = (uid, "psA"), (uid, "psB"), (uid, "psC")
    TOKP = [TA, TB]
    DEFER_EPI = 1
    pending = [None]
    pend_shift = [None]

    def epi(h, qh, defer_shift=False):
        sch.op("vector", lambda v: v.tensor_copy(o12[0][:], p1[0:64, :]),
               reads=[TC], writes=[(uid, "o1")])
        sch.op("vector", lambda v: v.tensor_copy(o12[1][:], p2[0:64, :]),
               reads=[TC], writes=[(uid, "o2")])
        sch.op("scalar", lambda s_: s_.activation(out=rowsb[:, 0:2048], in_=psC[64:65, 0:2048], func=AF.Ln),
               reads=[TC], writes=[(uid, "rowB")])
        sch.op("scalar", lambda s_: s_.activation(out=rbh[:, 0:2048], in_=rowsb[:, 0:2048],
                                                  func=AF.Exp, scale=-1.0),
               reads=[(uid, "rowB"), ("cst",)], writes=[(uid, "rowB")])
        sch.op("vector", lambda v, h=h: v.tensor_scalar(
            rbh[:, 1024:2048], rbh[:, 1024:2048], lam_row[:, h:h + 1], None, op0=OP.mult),
            reads=[(uid, "rowB"), (uid, "lamf")], writes=[(uid, "rowB")])
        mm_F(sch, psC[0:64, 0:1024], env["ones1"][:, 0:64], rbh[:, 0:1024],
             start=True, stop=True, reads=[(uid, "rowB"), "ones1"],
             writes=[TC])
        mm_F(sch, psC[0:64, 1024:2048], env["ones1"][:, 0:64], rbh[:, 1024:2048],
             start=True, stop=True, reads=[(uid, "rowB"), "ones1"],
             writes=[TC])
        sch.op("vector", lambda v: v.tensor_tensor(comb[:], o12[0][:], psC[0:64, 0:1024], op=OP.mult),
               reads=[(uid, "o1"), TC], writes=[(uid, "comb")])
        sch.op("vector", lambda v: v.tensor_tensor(combt[:], o12[1][:], psC[0:64, 1024:2048], op=OP.mult),
               reads=[(uid, "o2"), TC], writes=[(uid, "combt")])
        sch.op("vector", lambda v: v.tensor_tensor(comb[:], comb[:], combt[:], op=OP.subtract),
               reads=[(uid, "combt"), (uid, "comb")], writes=[(uid, "comb")])
        sch.op("scalar", lambda s_: s_.activation(out=sqt[:, :], in_=comb[:], func=AF.Square),
               reads=[(uid, "comb")], writes=[(uid, "sqt")])
        mm_F(sch, psC[0:1, 0:1024], env["ones128"][0:64, :], sqt[:, :],
             start=True, stop=True, reads=[(uid, "sqt"), "ones128"],
             writes=[TC])
        c164 = env["const"](1.0 / 64.0)
        c1em6 = env["const"](1e-6)
        sch.op("scalar", lambda s_, c164=c164, c1em6=c1em6: s_.activation(
            out=rows[:, 0:1024], in_=psC[0:1, 0:1024], func=AF.Ln, scale=c164, bias=c1em6),
            reads=[TC, ("cst",)], writes=[(uid, "rowA")])
        s_half = env["const"](-0.5)
        sch.op("scalar", lambda s_, s_half=s_half: s_.activation(
            out=rbh[:, 0:1024], in_=rows[:, 0:1024], func=AF.Exp, scale=s_half),
            reads=[(uid, "rowA")], writes=[(uid, "rowB")])
        mm_F(sch, psC[0:64, 1024:2048], env["ones1"][:, 0:64], rbh[:, 0:1024],
             start=True, stop=True, reads=[(uid, "rowB"), "ones1"],
             writes=[TC])
        sch.op("vector", lambda v: v.tensor_tensor(combt[:], comb[:], psC[0:64, 1024:2048], op=OP.mult),
               reads=[(uid, "comb"), TC], writes=[(uid, "combt")])
        cdst = attn_fm[h // 2]
        if h % 2 == 0:
            sch.op("vector", lambda v, cdst=cdst, h=h, qh=qh: v.tensor_scalar(
                cdst[0:64, 1024 * qh:1024 * (qh + 1)], combt[:], hnw08[:, h:h + 1], None, op0=OP.mult),
                reads=[(uid, "combt"), (uid, "hnw08")], writes=[(uid, "afm", h // 2)])
            return None
        sch.op("vector", lambda v, h=h: v.tensor_scalar(
            fint, combt[:], hnw08[:, h:h + 1], None, op0=OP.mult),
            reads=[(uid, "combt"), (uid, "hnw08")], writes=[(uid, "fint")])

        def shift():
            mm_F(sch, ps_s[1][64:128, :], iden[0:64, 0:64], fint,
                 start=True, stop=True, reads=[(uid, "fint"), (uid, "iden")],
                 writes=[TB], tile_position=(0, 64))
            sch.op("vector", lambda v, cdst=cdst, qh=qh: v.tensor_copy(
                cdst[64:128, 1024 * qh:1024 * (qh + 1)], ps_s[1][64:128, :]),
                reads=[TB], writes=[(uid, "afm", h // 2)])
        if defer_shift:
            return shift
        shift()
        return None

    for g in range(8):
        if pending[0] is not None:
            ep, pending[0] = pending[0], None
            epi(*ep)
        if pump:
            pump()
        heads = (2 * g, 2 * g + 1)
        if True:
            ps_p = [psA, psB]
            ps_pm = psC

            def proj_rope(st, blk, si, bias_col, scale, dst1, dst1_tok, dst2,
                          dst2_tok, widx):
                wtile = st.load(widx, blk[si].rearrange("p c h -> p (c h)"))
                for nh in range(2):
                    slot = nh
                    qp, qtok = qpre[slot], (uid, "qpre", slot)
                    for c in range(8):
                        mm_F(sch, ps_p[slot][:], wtile[:, 128 * c:128 * (c + 1)],
                             xbf[c][:, 1024 * nh:1024 * (nh + 1)],
                             start=(c == 0), stop=(c == 7),
                             reads=[st.tok(widx), xbf_toks[c]],
                             writes=[TOKP[slot]])
                    sch.op("vector", lambda v_, slot=slot, qp=qp, bias_col=bias_col, scale=scale:
                           v_.tensor_scalar(qp[:], ps_p[slot][:],
                                            bias_col, scale, op0=OP.add, op1=OP.mult),
                           reads=[TOKP[slot], (uid, "bias")],
                           writes=[qtok])
                    cs_sl = cs_tab[:, 1024 * nh:1024 * (nh + 1)]
                    ss_sl = ss_tab[:, 1024 * nh:1024 * (nh + 1)]
                    pm_sl = ps_pm[:, 1024 * nh:1024 * (nh + 1)]
                    mm_F(sch, pm_sl, perm[:], qp[:], start=True, stop=True,
                         reads=[qtok, (uid, "perm")], writes=[TC],
                         abs_off=1024 * nh)
                    sch.op("vector", lambda v, qp=qp, cs_sl=cs_sl: v.tensor_tensor(
                        tcs[:], qp[:], cs_sl, op=OP.mult),
                        reads=[qtok, (uid, "cs")], writes=[(uid, "tcs")])
                    sch.op("vector", lambda v, pm_sl=pm_sl, ss_sl=ss_sl: v.tensor_tensor(
                        uss[:], pm_sl, ss_sl, op=OP.mult),
                        reads=[TC, (uid, "ss")], writes=[(uid, "uss")])
                    sch.op("vector", lambda v, dst1=dst1, nh=nh: v.tensor_tensor(
                        dst1[0:64, 1024 * nh:1024 * (nh + 1)], tcs[0:64, :], uss[0:64, :], op=OP.add),
                        reads=[(uid, "tcs"), (uid, "uss")], writes=[dst1_tok])
                    sch.op("vector", lambda v, dst2=dst2, nh=nh: v.tensor_tensor(
                        dst2[0:64, 1024 * nh:1024 * (nh + 1)], tcs[64:128, :], uss[64:128, :], op=OP.add),
                        reads=[(uid, "tcs"), (uid, "uss")], writes=[dst2_tok])

            for j, h in enumerate(heads):
                proj_rope(wq_st, wq_blk, h, bias_sb[:, h:h + 1], SQ8,
                          aug[4 * j + 0], (uid, "q1a", j), aug[4 * j + 1], (uid, "q2a", j), h)
                proj_rope(wk_st, wk_blk, h, bias_sb[:, 16 + h:17 + h], 1.0,
                          aug[4 * j + 2], (uid, "k1a", j), aug[4 * j + 3], (uid, "k2a", j), h)
            vtile = wv_st.load(g, wv_blk[g].rearrange("p c h -> p (c h)"))
            for nh in range(2):
                slot = nh
                for c in range(8):
                    mm_F(sch, ps_p[slot][:], vtile[:, 128 * c:128 * (c + 1)],
                         xbf[c][:, 1024 * nh:1024 * (nh + 1)], start=(c == 0), stop=(c == 7),
                         reads=[wv_st.tok(g), xbf_toks[c]], writes=[TOKP[slot]])
                sch.op("vector", lambda v_, slot=slot, g=g: v_.tensor_scalar(
                    vstage[:], ps_p[slot][:],
                    bias_sb[:, 32 + g:33 + g], None, op0=OP.add),
                    reads=[TOKP[slot], (uid, "bias")], writes=[(uid, "vstage")])
                sch.dma("sync", lambda sy, g=g, nh=nh: sy.dma_start(
                    out=v_scr[128 * g:128 * (g + 1), 1024 * nh:1024 * (nh + 1)], in_=vstage[:]),
                    sem=vssem, reads=[(uid, "vstage")], writes=[(uid, "vscr", g)])

        if True:
            ps_s = [psA, psB]
            p1 = psC[0:65, 0:1024]
            p2 = psC[0:65, 1024:2048]
            for j, h in enumerate(heads):
                q1a, q2a, k1a, k2a = aug[4 * j], aug[4 * j + 1], aug[4 * j + 2], aug[4 * j + 3]
                tq1, tq2 = (uid, "q1a", j), (uid, "q2a", j)
                tk1, tk2 = (uid, "k1a", j), (uid, "k2a", j)
                va = v_aug[j]
                ones_cols = va.ap().rearrange("p (k c) -> p k c", c=80)[:, :, 64:65]
                tva = [(uid, "vaug1", j)] + [(uid, "vaugd", j, kc) for kc in range(16)]
                sch.op("vector", lambda v, ones_cols=ones_cols: v.memset(ones_cols, 1.0),
                       writes=[(uid, "vaug1", j)])
                for kc in range(16):
                    sch.dma("sync", lambda sy, va=va, kc=kc, h=h: sy.dma_start(
                        out=va[:, 80 * kc:80 * kc + 64],
                        in_=v_scr[64 * h:64 * h + 64, 128 * kc:128 * (kc + 1)],
                        transpose=True),
                        sem=vtsem[j], reads=[(uid, "vscr", g)], writes=[(uid, "vaugd", j, kc)])
                M2 = {}
                for idx, (ka, tk) in enumerate(((k1a, tk1), (k2a, tk2))):
                    for piece in range(2):
                        sch.op("vector", lambda v, ka=ka, piece=piece: v.tensor_tensor(
                            bsq, ka[0:64, 1024 * piece:1024 * (piece + 1)],
                            ka[0:64, 1024 * piece:1024 * (piece + 1)], op=OP.mult),
                            reads=[tk], writes=[(uid, "E", 0)])
                        mm_F(sch, ps_s[piece][0:1, :], env["ones128"][0:64, :],
                             bsq, start=True, stop=True,
                             reads=[(uid, "E", 0), "ones128"], writes=[TOKP[piece]])
                        sch.op("vector", lambda v, piece=piece, idx=idx: v.tensor_reduce(
                            rows[:, 1024 + 2 * idx + piece:1025 + 2 * idx + piece],
                            ps_s[piece][0:1, :], axis=AX, op=OP.max),
                            reads=[TOKP[piece]], writes=[(uid, "m2r", idx)])
                    sch.op("vector", lambda v, idx=idx: v.tensor_reduce(
                        sc16[:, idx:idx + 1], rows[:, 1024 + 2 * idx:1026 + 2 * idx],
                        axis=AX, op=OP.max),
                        reads=[(uid, "m2r", idx)], writes=[(uid, "sc")])
                    M2[idx] = sc16[:, idx:idx + 1]
                for idx, (qa, tq) in enumerate(((q1a, tq1), (q2a, tq2))):
                    sch.op("vector", lambda v, qa=qa: v.tensor_tensor(
                        bsq, qa[0:64, 0:1024], qa[0:64, 0:1024], op=OP.mult),
                        reads=[tq], writes=[(uid, "E", 0)])
                    mm_F(sch, ps_s[idx][0:1, :], env["ones128"][0:64, :], bsq,
                         start=True, stop=True,
                         reads=[(uid, "E", 0), "ones128"], writes=[TOKP[idx]])
                    sch.op("vector", lambda v, idx=idx: v.tensor_reduce(
                        sc16[:, 2 + idx:3 + idx], ps_s[idx][0:1, :], axis=AX, op=OP.max),
                        reads=[TOKP[idx]], writes=[(uid, "sc")])
                sch.op("scalar", lambda s_: s_.activation(
                    out=sc16[:, 4:8], in_=sc16[:, 0:4], func=AF.Ln),
                    reads=[(uid, "sc")], writes=[(uid, "sc")])
                sch.op("vector", lambda v: v.tensor_tensor(
                    sc16[:, 8:10], sc16[:, 4:6], sc16[:, 6:8], op=OP.subtract),
                    reads=[(uid, "sc")], writes=[(uid, "sc")])
                sch.op("vector", lambda v: v.tensor_tensor(
                    sc16[:, 10:12], sc16[:, 4:6], sc16[:, 6:8], op=OP.add),
                    reads=[(uid, "sc")], writes=[(uid, "sc")])
                sch.op("scalar", lambda s_: s_.activation(
                    out=sc16[:, 8:10], in_=sc16[:, 8:10], func=AF.Exp, scale=0.5),
                    reads=[(uid, "sc")], writes=[(uid, "sc")])
                sch.op("scalar", lambda s_: s_.activation(
                    out=sc16[:, 10:12], in_=sc16[:, 10:12], func=AF.Exp, scale=0.5),
                    reads=[(uid, "sc")], writes=[(uid, "sc")])
                sch.op("vector", lambda v: v.tensor_scalar(
                    sc16[:, 8:10], sc16[:, 8:10], -0.5, None, op0=OP.mult),
                    reads=[(uid, "sc")], writes=[(uid, "sc")])
                sch.op("vector", lambda v: v.tensor_scalar(
                    sc16[:, 10:12], sc16[:, 10:12], 0.5, None, op0=OP.mult),
                    reads=[(uid, "sc")], writes=[(uid, "sc")])
                for qh in range(2):
                    qlo_h = 1024 * qh
                    if DEFER_EPI == 2 and pending[0] is not None:
                        ep, pending[0] = pending[0], None
                        epi(*ep)
                    for idx, (qa, tq) in enumerate(((q1a, tq1), (q2a, tq2))):
                        sch.op("vector", lambda v, qa=qa, qlo_h=qlo_h: v.tensor_tensor(
                            bsq, qa[0:64, qlo_h:qlo_h + 1024],
                            qa[0:64, qlo_h:qlo_h + 1024], op=OP.mult),
                            reads=[tq], writes=[(uid, "E", 0)])
                        mm_F(sch, ps_s[0][0:1, :], env["ones128"][0:64, :], bsq,
                             start=True, stop=True,
                             reads=[(uid, "E", 0), "ones128"], writes=[TA])
                        sch.op("vector", lambda v, idx=idx, qa=qa, qlo_h=qlo_h: v.tensor_scalar(
                            qa[64:65, qlo_h:qlo_h + 1024], ps_s[0][0:1, :],
                            sc16[:, 8 + idx:9 + idx], sc16[:, 10 + idx:11 + idx],
                            op0=OP.mult, op1=OP.subtract),
                            reads=[TA, (uid, "sc"), (uid, "augpad")],
                            writes=[tq])
                    kmax = 8 * (qh + 1)
                    sides = ((q1a, k1a, tq1, tk1, p1, TC),
                             (q2a, k2a, tq2, tk2, p2, TC))

                    def s_mm(kc):
                        qlo = max(qlo_h, 128 * kc)
                        F = qlo_h + 1024 - qlo
                        for idx, (qa, ka, tq, tk, pv, tpv) in enumerate(sides):
                            mm_F(sch, ps_s[idx][:, 0:F], ka[0:128, 128 * kc:128 * (kc + 1)],
                                 qa[0:128, qlo:qlo + F], start=True, stop=True,
                                 reads=[tq, tk, (uid, "augpad")], writes=[TOKP[idx]])

                    s_mm(0)
                    for kc in range(kmax):
                        qlo = max(qlo_h, 128 * kc)
                        F = qlo_h + 1024 - qlo
                        loc = qlo - qlo_h
                        for idx, (qa, ka, tq, tk, pv, tpv) in enumerate(sides):
                            sch.op("scalar", lambda s_, idx=idx, loc=loc, F=F: s_.activation(
                                out=E[idx][:, loc:loc + F], in_=ps_s[idx][:, 0:F], func=AF.Exp),
                                reads=[TOKP[idx]], writes=[(uid, "E", idx)])
                            if 128 * kc >= qlo_h:
                                d0 = 128 * kc - qlo_h
                                sch.op("vector", lambda v, idx=idx, d0=d0: v.tensor_tensor(
                                    E[idx][:, d0:d0 + 128], E[idx][:, d0:d0 + 128], ut[:], op=OP.mult),
                                    reads=[(uid, "E", idx), (uid, "ut")], writes=[(uid, "E", idx)])
                        if kc == 1 and pend_shift[0] is not None:
                            sf, pend_shift[0] = pend_shift[0], None
                            sf()
                        if kc + 1 < kmax:
                            s_mm(kc + 1)
                        if kc == 0 and pending[0] is not None:
                            ep, pending[0] = pending[0], None
                            pend_shift[0] = epi(*ep, defer_shift=True)
                        for idx, (qa, ka, tq, tk, pv, tpv) in enumerate(sides):
                            pos = loc
                            while pos < loc + F:
                                nxt = min(loc + F, pos + 512 - (pos % 512))
                                bank = pos // 512
                                last_kc_of_bank = min(kmax - 1, (512 * (bank + 1) - 1 + 1024 * qh) // 128)
                                sch.op("tensor",
                                       lambda t, pos=pos, nxt=nxt, kc=kc, idx=idx, pv=pv, va=va,
                                       st_=(kc == 0), sp_=(kc == last_kc_of_bank): t.matmul(
                                           pv[:, pos:nxt], va[:, 80 * kc:80 * kc + 65],
                                           E[idx][:, pos:nxt], start=st_, stop=sp_),
                                       reads=[(uid, "E", idx)] + tva, writes=[tpv])
                                pos = nxt
                    pending[0] = (h, qh)
                    if DEFER_EPI == 0:
                        ep, pending[0] = pending[0], None
                        epi(*ep)
    if pending[0] is not None:
        ep, pending[0] = pending[0], None
        epi(*ep)
    g_pes.close()
    sch.barrier()

    if debug_afm:
        dsem = es.enter_context(nc.semaphore(f"{uid}_dbg"))
        for c in range(8):
            sch.op("vector", lambda v, c=c: v.tensor_copy(env["ln_tnorm"][:, 0:1024], attn_fm[c][:, 0:1024]),
                   reads=[(uid, "afm", c)], writes=[("dbg",)])
            sch.dma("sync", lambda sy, c=c: sy.dma_start(out=outT[128 * c:128 * (c + 1), 0:1024], in_=env["ln_tnorm"][:, 0:1024]),
                    sem=dsem, reads=[("dbg",)])
            sch.op("vector", lambda v, c=c: v.tensor_copy(env["ln_tnorm"][:, 0:1024], attn_fm[c][:, 1024:2048]),
                   reads=[(uid, "afm", c)], writes=[("dbg",)])
            sch.dma("sync", lambda sy, c=c: sy.dma_start(out=outT[128 * c:128 * (c + 1), 1024:2048], in_=env["ln_tnorm"][:, 0:1024]),
                    sem=dsem, reads=[("dbg",)])
        sch.barrier()
        return
    qp_bf = qpre[0].ap().bitcast(BF16)
    wo_st = DBuf(nc, es, sch, f"{uid}_wo", None, None,
                 tiles=[qp_bf[:, 0:1024], qp_bf[:, 1024:2048]])
    xf_st = DBuf(nc, es, sch, f"{uid}_xf", None, None, queue="sync",
                 tiles=[tcs.ap(), uss.ap()])
    NH = 1024
    for half in range(2):
        hlo = half * NH
        with contextlib.ExitStack() as pes:
            p = lambda name, shape: pes.enter_context(nc.psum_tensor(name, shape, F32))
            ps_o = [p(f"{uid}_pso{half}_{i}", [128, NH]) for i in range(2)]
            for os in range(8):
                b = os % 2
                wtile = wo_st.load(os + 8 * half, wo_blk[os].rearrange("p c h -> p (c h)"))
                xf_st.load(os + 8 * half, xT[128 * os:128 * (os + 1), hlo:hlo + NH])
                for c in range(8):
                    mm_F(sch, ps_o[b][:], wtile[:, 128 * c:128 * (c + 1)],
                         attn_fm[c][:, hlo:hlo + NH], start=(c == 0), stop=(c == 7),
                         reads=[wo_st.tok(os), (uid, "afm", c)], writes=[(uid, "pso", b)])
                sch.op("vector", lambda v, os=os, b=b: v.scalar_tensor_tensor(
                    y[os][:, 0:NH], ps_o[b][:], bias_sb[:, 40 + os:41 + os],
                    xf_st.tiles[b], op0=OP.add, op1=OP.add),
                    reads=[(uid, "pso", b), xf_st.tok(os), (uid, "bias")],
                    writes=[(uid, "y", os)])
            sch.barrier()
        with contextlib.ExitStack() as pes:
            ps_rows = pes.enter_context(nc.psum_tensor(f"{uid}_lnr{half}", [1, 2 * NH], F32)).ap()
            pb = pes.enter_context(nc.psum_tensor(f"{uid}_lnb{half}", [128, 2 * NH], F32)).ap()
            ob = [out_bf16[c][:, hlo:hlo + NH] for c in range(8)] if out_bf16 else None
            obt = ([(tok, half) for tok in out_bf16_toks] if out_bf16 else None)
            emit_layernorm(sch, env, [y[c][:, 0:NH] for c in range(8)],
                           [(uid, "y", c) for c in range(8)], ps_rows, pb, uid,
                           out_bf16=ob, out_bf16_toks=obt,
                           out_dma=lambda c: (outT[128 * c:128 * (c + 1), hlo:hlo + NH], st_sem))
            sch.barrier()


def build_l0_attn_standalone(debug_afm=False):
    nc = bass.Bass("TRN2", target_bir_lowering=False)
    xT = nc.dram_tensor("xT", [D, S], F32, kind="ExternalInput").ap()
    wq_blk = nc.dram_tensor("wq_blk", [16, 128, 8, 128], F32, kind="ExternalInput").ap()
    wk_blk = nc.dram_tensor("wk_blk", [16, 128, 8, 128], F32, kind="ExternalInput").ap()
    wv_blk = nc.dram_tensor("wv_blk", [8, 128, 8, 128], F32, kind="ExternalInput").ap()
    wo_blk = nc.dram_tensor("wo_blk", [8, 128, 8, 128], F32, kind="ExternalInput").ap()
    biases = nc.dram_tensor("biases", [128, 48], F32, kind="ExternalInput").ap()
    lqk = nc.dram_tensor("lqk", [64, 64], F32, kind="ExternalInput").ap()
    hnw = nc.dram_tensor("hnw", [64, 16], F32, kind="ExternalInput").ap()
    cs_tab = nc.dram_tensor("cs_tab", [128, S], F32, kind="ExternalInput").ap()
    ss_tab = nc.dram_tensor("ss_tab", [128, S], F32, kind="ExternalInput").ap()
    perm = nc.dram_tensor("perm", [128, 128], F32, kind="ExternalInput").ap()
    ut = nc.dram_tensor("ut", [128, 128], F32, kind="ExternalInput").ap()
    iden = nc.dram_tensor("iden", [128, 128], F32, kind="ExternalInput").ap()
    v_scr = nc.dram_tensor("v_scr", [D, S], BF16, kind="Internal").ap()
    outT = nc.dram_tensor("outT", [D, S], F32, kind="ExternalOutput").ap()
    es = contextlib.ExitStack()
    with es:
        sch = Sched(nc, es)
        env = make_env(nc, es, sch)
        st_sem = es.enter_context(nc.semaphore("sem_st"))
        emit_l0_attn(nc, sch, es, env, xT, wq_blk, wk_blk, wv_blk, wo_blk,
                     biases[:, :], lqk[:, :], hnw[:, :], cs_tab[:, :], ss_tab[:, :],
                     perm[:, :], ut[:, :], iden[:, :], v_scr, outT, st_sem,
                     debug_afm=debug_afm)
        sch._emit("sync", lambda e: e.nop(), [(st_sem, sch.count(st_sem))], None, 1)
        sch.barrier()
    return nc


def prep_l0_attn_inputs(x_bt, Wq, bq, Wk, bk, Wv, bv, Wo, bo, hnw, lq1, lk1, lq2, lk2):
    xT = np.ascontiguousarray(x_bt.T)
    blk = lambda w, nout: np.ascontiguousarray(
        w.T.reshape(8, 128, nout // 128, 128).transpose(2, 1, 0, 3))
    fm = lambda b: np.ascontiguousarray(b.reshape(-1, 128).T)
    biases = np.concatenate([fm(bq), fm(bk), fm(bv), fm(bo)], axis=1)
    lqk = np.concatenate([lq1.T, lk1.T, lq2.T, lk2.T], axis=1)
    half = 32
    inv = (1.0 / 10000.0 ** (np.arange(half, dtype=np.float64) / half))
    ang = np.arange(S, dtype=np.float64)[:, None] * inv[None, :]
    cosv = np.cos(ang).astype(np.float32)
    sinv = np.sin(ang).astype(np.float32)
    cs = np.zeros((128, S), np.float32)
    ss = np.zeros((128, S), np.float32)
    for p in range(128):
        j = p % 32
        cs[p] = cosv[:, j]
        ss[p] = sinv[:, j] if (p % 64) >= 32 else -sinv[:, j]
    perm = np.zeros((128, 128), np.float32)
    for i in range(128):
        sig = i + 32 if (i % 64) < 32 else i - 32
        perm[sig, i] = 1.0
    ut = np.triu(np.ones((128, 128), np.float32))
    iden = np.eye(128, dtype=np.float32)
    return {"xT": xT, "wq_blk": blk(Wq, 2048), "wk_blk": blk(Wk, 2048),
            "wv_blk": blk(Wv, 1024), "wo_blk": blk(Wo, 1024),
            "biases": biases, "lqk": lqk, "hnw": np.ascontiguousarray(hnw.T),
            "cs_tab": cs, "ss_tab": ss, "perm": perm, "ut": ut, "iden": iden}



def emit_l1_sw_stage(nc, sch, es, env, swT, sclT, grid_h, sw_scr, uid="a1"):
    t = lambda name, shape, dt: es.enter_context(nc.sbuf_tensor(name, shape, dt))
    grid_sb = t(f"{uid}_sgrid", [1, 2], F32)
    c_col = t(f"{uid}_ccol", [128, 1], F32)
    sw_in = DBuf(nc, es, sch, f"{uid}_swin", [128, 3072], BF16, nbuf=3)
    scl_in = DBuf(nc, es, sch, f"{uid}_sclin", [128, 3072], BF16)
    sw_mid = DBuf(nc, es, sch, f"{uid}_swmid", [128, 3072], BF16)
    gsem = sch.new_dma_sem(f"{uid}_sgsem")
    LN6 = env["const"](float(-np.log(6.0)))

    sch.dma("gpsimd", lambda g: g.dma_start(out=grid_sb[:], in_=grid_h[0:1, 0:2]),
            sem=gsem, writes=[(uid, "sgrid")])
    with nc.psum_tensor(f"{uid}_psc0", [128, 1], F32) as ps_c:
        sch.op("vector", lambda v: v.tensor_tensor(
            c_col[0:1, :], grid_sb[0:1, 1:2], grid_sb[0:1, 0:1], op=OP.subtract),
            reads=[(uid, "sgrid")], writes=[(uid, "ccol")])
        sch.op("scalar", lambda s_: s_.activation(
            out=c_col[0:1, :], in_=c_col[0:1, :], func=AF.Ln),
            reads=[(uid, "ccol")], writes=[(uid, "ccol")])
        sch.op("scalar", lambda s_, LN6=LN6: s_.activation(
            out=c_col[0:1, :], in_=c_col[0:1, :], func=AF.Exp, scale=-3.0, bias=LN6),
            reads=[(uid, "ccol"), ("cst",)], writes=[(uid, "ccol")])
        mm_F(sch, ps_c[:], env["ones1f"][:, 0:128], c_col[0:1, :],
             start=True, stop=True, reads=[(uid, "ccol"), "ones1"],
             writes=[(uid, "psc")])
        sch.op("vector", lambda v: v.tensor_copy(c_col[:], ps_c[:]),
               reads=[(uid, "psc")], writes=[(uid, "ccol")])

    seq = [(ir, b8 * 8 + ir) for ir in range(8) for b8 in range(8)]
    st = {"l": 0, "c": 0}
    DEFER = 2

    def do_load(si):
        ir, cc = seq[si]
        if si % 8 == 0:
            scl_in.load(ir, sclT[128 * ir:128 * (ir + 1), :])
        sw_in.load(si, swT[128 * cc:128 * (cc + 1), :])

    def do_comp(si):
        ir, cc = seq[si]
        if si % 8 == 0:
            sch.op("vector", lambda v, ir=ir: v.tensor_scalar(
                scl_in.tiles[ir % 2][:], scl_in.tiles[ir % 2][:], c_col[:, 0:1],
                None, op0=OP.mult),
                reads=[scl_in.tok(ir), (uid, "ccol")], writes=[scl_in.tok(ir)])
        mb = sw_mid.tiles[si % 2]
        sch.op("vector", lambda v, si=si, mb=mb, ir=ir: v.tensor_tensor(
            mb[:], sw_in.tiles[si % 3][:], scl_in.tiles[ir % 2][:], op=OP.mult),
            reads=[sw_in.tok(si), scl_in.tok(ir)],
            writes=[sw_mid.tok(si)])
        for gg in range(3):
            sch.dma("sync", lambda sy, cc=cc, gg=gg, mb=mb: sy.dma_start(
                out=sw_scr[gg, cc], in_=mb[:, 1024 * gg:1024 * (gg + 1)]),
                sem=sw_mid.sems[si % 2], reads=[sw_mid.tok(si)],
                writes=[(uid, "swscr", gg, cc)])

    def pump():
        if st["l"] < 64:
            do_load(st["l"])
            st["l"] += 1
        if st["c"] < st["l"] and (st["l"] - st["c"] > DEFER or st["l"] == 64):
            do_comp(st["c"])
            st["c"] += 1
        return st["c"] < 64
    return pump


def emit_l1_attn(nc, sch, es, env, xT, bw_blk, swT, sclT, grid_h, RT_h, ow_blk,
                 outb, sw_scr, qk_scr, v_scr, outT, st_sem, uid="a1",
                 out_bf16=None, out_bf16_toks=None, x_dep=()):
    t = lambda name, shape, dt: es.enter_context(nc.sbuf_tensor(name, shape, dt))
    GS = 512
    NT = S // GS
    bases = t(f"{uid}_bases", [128, 64 * GS], BF16)
    bas = lambda cc: bases[:, GS * cc:GS * (cc + 1)]
    grid_sb = t(f"{uid}_grid", [128, 96], F32)
    gsem = sch.new_dma_sem(f"{uid}_gsem")
    sch.dma("sync", lambda sy: sy.dma_start(out=grid_sb[:], in_=grid_h),
            sem=gsem, writes=[(uid, "grid")])

    attn_fm = [t(f"{uid}_afm{c}", [128, S], BF16) for c in range(8)]
    Dj = [attn_fm[i // 4][:, GS * (i % 4):GS * (i % 4 + 1)] for i in range(12)]
    xf_st = DBuf(nc, es, sch, f"{uid}_xf", [128, GS], F32, nbuf=8, queue="sync", shared_sem=True)
    silu_x = t(f"{uid}_silu", [128, 8 * GS], BF16)
    L = [t(f"{uid}_L{i}", [128, GS], BF16) for i in range(11)]

    E = [t(f"{uid}_E{i}", [128, 1024], BF16) for i in range(2)]
    stage = E[0][:, 0:GS]
    tmp1f = t(f"{uid}_t1", [128, GS], F32)
    t1s = [tmp1f.ap().bitcast(BF16)[:, 0:GS], tmp1f.ap().bitcast(BF16)[:, GS:2 * GS]]
    t2s_ = [E[1][:, 0:GS], E[1][:, GS:2 * GS]]
    tmp1, tmp2 = t1s[0], t2s_[0]
    xb16 = E[0][:, GS:2 * GS]

    invd = t(f"{uid}_invd", [128, 64], F32)

    sw_st = DBuf(nc, es, sch, f"{uid}_swst", [128, 1024], BF16, nbuf=3, queue="sync")
    bw_st = DBuf(nc, es, sch, f"{uid}_bwst", [128, 8 * 128], BF16, nbuf=2)
    qsem = sch.new_dma_sem(f"{uid}_qsem")

    INVD_OFF = {1: 0, 2: 11, 3: 21}

    q_pes = contextlib.ExitStack()
    ps_o = [q_pes.enter_context(nc.psum_tensor(f"{uid}_pso{i}", [128, GS], F32))
            for i in range(8)]
    v_aug = [t(f"{uid}_vaug{i}", [128, 16 * 80], BF16) for i in range(2)]
    q_aug = t(f"{uid}_qaug", [128, S], BF16)
    k_aug = t(f"{uid}_kaug", [128, S], BF16)
    parts = []
    for s in (attn_fm[3], attn_fm[4], attn_fm[5], attn_fm[6], attn_fm[7],
              q_aug, k_aug):
        f = s.ap().bitcast(F32)
        parts += [f[:, 0:512], f[:, 512:1024]]
    parts += [v_aug[i].ap().bitcast(F32)[:, 0:512] for i in range(2)]
    part_x = t(f"{uid}_partx", [128, 8 * 512], F32)
    parts += [part_x[:, 512 * i:512 * (i + 1)] for i in range(8)]
    SETS = ([0, 1, 2, 3], [4, 5, 6, 7])

    def emit_xloads(tt):
        tlo = tt * GS
        for c in range(8):
            xf_st.load(c + 8 * tt, xT[128 * c:128 * (c + 1), tlo:tlo + GS], reads=x_dep)

    def emit_silu(tt):
        for c in range(8):
            sch.op("scalar", lambda s_, c=c: s_.activation(
                out=silu_x[:, GS * c:GS * (c + 1)], in_=xf_st.tiles[c][:], func=AF.Silu),
                reads=[xf_st.tok(c)], writes=[(uid, "silu", c)])

    def emit_rec(tt, ics):
        for ic in ics:
            xf = xf_st.tiles[ic]
            xtok = xf_st.tok(ic)
            gcol = lambda j: grid_sb[:, 12 * ic + j:12 * ic + j + 1]
            sch.op("vector", lambda v, xf=xf: v.tensor_copy(xb16, xf[:]),
                   reads=[xtok], writes=[(uid, "xb16")])
            xbtok = (uid, "xb16")
            for j in range(12):
                sch.op("vector", lambda v, j=j, gc=gcol(j): v.tensor_scalar(
                    Dj[j], xb16, gc, None, op0=OP.subtract),
                    reads=[xbtok, (uid, "grid")], writes=[(uid, "D", j)])
            ge = [tmp1, tmp2]
            getok = [(uid, "t1", 0), (uid, "t2", 0)]
            sch.op("vector", lambda v: v.tensor_scalar(
                ge[0], Dj[0], 0.0, None, op0=OP.is_ge),
                reads=[(uid, "D", 0)], writes=[getok[0]])
            for j in range(11):
                sch.op("vector", lambda v, j=j: v.tensor_scalar(
                    ge[(j + 1) % 2], Dj[j + 1], 0.0, None, op0=OP.is_ge),
                    reads=[(uid, "D", j + 1)], writes=[getok[(j + 1) % 2]])
                sch.op("vector", lambda v, j=j: v.tensor_tensor(
                    L[j][:], ge[j % 2], ge[(j + 1) % 2], op=OP.subtract),
                    reads=[getok[0], getok[1]], writes=[(uid, "L", j)])
            nb = 0
            for pp in (1, 2, 3):
                n_out = 11 - pp
                for j in range(n_out):
                    last = (pp == 3)
                    dst = bas(j * 8 + ic) if last else L[j][:]
                    dst_tok = (uid, "bas", j * 8 + ic) if last else (uid, "L", j)
                    par = nb % 2
                    nb += 1
                    u1, u2 = t1s[par], t2s_[par]
                    tk1, tk2 = (uid, "t1", par), (uid, "t2", par)
                    sch.op("vector", lambda v, j=j, u1=u1: v.tensor_tensor(
                        u1, Dj[j], L[j][:], op=OP.mult),
                        reads=[(uid, "D", j), (uid, "L", j)], writes=[tk1])
                    sch.op("vector", lambda v, j=j, pp=pp, u2=u2: v.tensor_tensor(
                        u2, Dj[j + pp + 1], L[j + 1][:], op=OP.mult),
                        reads=[(uid, "D", j + pp + 1), (uid, "L", j + 1)], writes=[tk2])
                    sch.op("vector", lambda v, dst=dst, u1=u1, u2=u2: v.tensor_tensor(
                        dst, u1, u2, op=OP.subtract),
                        reads=[tk1, tk2], writes=[dst_tok])

    def emit_mm_half(tt, hs):
        tlo = tt * GS
        ccs = [j * 8 + ic for ic in SETS[hs] for j in range(8)]
        nA = len(SETS[hs]) * 8
        base = (tt * 2 + hs) * 96
        flat = [(gg, cc) for gg in range(3) for cc in ccs]

        def sload(k):
            gg, cc = flat[k]
            sw_st.load(base + k, sw_scr[gg, cc], reads=[(uid, "swscr", gg, cc)])

        sload(0)
        sload(1)
        for k, (gg, cc) in enumerate(flat):
            if True:
                ci = k % nA
                sidx = base + k
                wt = sw_st.tiles[sidx % sw_st.nbuf]
                for oti in range(8):
                    mm_F(sch, ps_o[oti][:], wt[:, 128 * oti:128 * (oti + 1)],
                         bas(cc), start=(ci == 0),
                         stop=(hs == 0 and ci == nA - 1),
                         reads=[sw_st.tok(sidx), (uid, "bas", cc)],
                         writes=[(uid, "pso", oti)])
                if k + 2 < len(flat):
                    sload(k + 2)
            if ci != nA - 1:
                continue
            if hs == 0:
                for oti in range(8):
                    os = gg * 8 + oti
                    sch.op("scalar", lambda s_, oti=oti, os=os: s_.activation(
                        out=parts[os], in_=ps_o[oti][:], func=AF.Copy),
                        reads=[(uid, "pso", oti)], writes=[(uid, "part", os)])
            else:
                for oti in range(8):
                    os = gg * 8 + oti
                    bw = bw_st.load(os + 24 * tt, bw_blk[os].rearrange("p c h -> p (c h)"))
                    for c8 in range(8):
                        mm_F(sch, ps_o[oti][:], bw[:, 128 * c8:128 * (c8 + 1)],
                             silu_x[:, GS * c8:GS * (c8 + 1)], start=False, stop=(c8 == 7),
                             reads=[bw_st.tok(os + 24 * tt), (uid, "silu", c8)],
                             writes=[(uid, "pso", oti)])
                    sch.op("vector", lambda v, oti=oti, os=os: v.tensor_tensor(
                        stage[:], ps_o[oti][:], parts[os], op=OP.add),
                        reads=[(uid, "pso", oti), (uid, "part", os)],
                        writes=[(uid, "stage")])
                    o0 = 128 * os
                    for half in range(2):
                        m = (o0 + 64 * half) // 64
                        h, kind = m // 3, m % 3
                        if kind == 0:
                            dstp = qk_scr[64 * h:64 * h + 64, tlo:tlo + GS]
                        elif kind == 1:
                            dstp = qk_scr[1024 + 64 * h:1024 + 64 * h + 64, tlo:tlo + GS]
                        else:
                            dstp = v_scr[64 * h:64 * h + 64, tlo:tlo + GS]
                        sch.dma("sync", lambda sy, half=half, dstp=dstp: sy.dma_start(
                            out=dstp, in_=stage[64 * half:64 * (half + 1), :]),
                            sem=qsem, reads=[(uid, "stage")],
                            writes=[(uid, "qkv", os, half)])

    emit_xloads(0)
    emit_silu(0)
    emit_rec(0, SETS[0])
    emit_rec(0, SETS[1])
    for tt in range(NT):
        emit_mm_half(tt, 0)
        if tt + 1 < NT:
            emit_xloads(tt + 1)
            emit_rec(tt + 1, SETS[0])
        emit_mm_half(tt, 1)
        if tt + 1 < NT:
            emit_silu(tt + 1)
            emit_rec(tt + 1, SETS[1])
    sch.fence_tokens("vector", [(uid, "stage")])
    sch.barrier()
    q_pes.close()

    RT_sb = t(f"{uid}_RT", [64, 64], BF16)
    rsem = sch.new_dma_sem(f"{uid}_rsem")
    sch.dma("gpsimd", lambda g: g.dma_start(out=RT_sb[:], in_=RT_h),
            sem=rsem, writes=[(uid, "RT")])
    iden = t(f"{uid}_iden", [128, 128], BF16)
    isem = sch.new_dma_sem(f"{uid}_isem")
    sch.dma("gpsimd", lambda g: g.dma_start(out=iden[:], in_=env["iden_h"]),
            sem=isem, writes=[(uid, "iden")])
    _px_bf = part_x.ap().bitcast(BF16)
    qk_in = DBuf(nc, es, sch, f"{uid}_qkin", None, None, queue="sync",
                 tiles=[_px_bf[0:64, 0:2048], _px_bf[0:64, 2048:4096]])
    vtsem = [sch.new_dma_sem(f"{uid}_vt{i}") for i in range(2)]
    y = [bases.ap().bitcast(F32)[:, 2048 + 1024 * c:3072 + 1024 * c]
         for c in range(8)]
    sqt = tmp1f.ap().bitcast(BF16)[0:64, :]
    o1 = t(f"{uid}_o1", [64, 1024], F32)
    comb = silu_x.ap().bitcast(F32)[0:64, 0:1024]
    rows = silu_x.ap().bitcast(F32)[0:1, 1024:2048]
    rowsb = bases.ap().bitcast(F32)[0:1, 4096:5120]
    rbh1 = bases[0:1, 2048:3072]
    m2row = silu_x.ap().bitcast(F32)[32:33, 1024:1040]
    q_augs = [q_aug, _px_bf[:, 4096:6144]]
    k_augs = [k_aug, _px_bf[:, 6144:8192]]
    m2rows = [m2row, silu_x.ap().bitcast(F32)[32:33, 1040:1056]]
    for p in range(2):
        sch.op("vector", lambda v, p=p: v.memset(k_augs[p][64:128, :], 0.0),
               writes=[(uid, "augpad", p), (uid, "kones", p)])
        sch.op("vector", lambda v, p=p: v.memset(k_augs[p][64:65, :], 1.0),
               writes=[(uid, "kones", p)], reads=[(uid, "augpad", p)])
        sch.op("vector", lambda v, p=p: v.memset(q_augs[p][64:128, :], 0.0),
               writes=[(uid, "augpad", p)])

    a_pes = contextlib.ExitStack()
    ps_r = a_pes.enter_context(nc.psum_tensor(f"{uid}_psr", [64, 1024], F32))
    ps_s = [a_pes.enter_context(nc.psum_tensor(f"{uid}_pss{i}", [128, 1024], F32))
            for i in range(2)]
    p1 = a_pes.enter_context(nc.psum_tensor(f"{uid}_p1", [65, 1024], F32))
    def prep_head(h):
        j = h % 2
        qa_, ka_, m2_ = q_augs[j], k_augs[j], m2rows[j]
        TQ, TK = (uid, "qa", j), (uid, "ka", j)
        for (row0, dsta, dtok, scl) in ((64 * h, qa_, TQ, 0.125),
                                        (1024 + 64 * h, ka_, TK, 1.0)):
            qt = qk_in.load(h * 2 + (0 if row0 < 1024 else 1),
                            qk_scr[row0:row0 + 64, :], reads=[(uid, "qkv", 0, 0)])
            for hf in range(2):
                mm_F(sch, ps_r[:], RT_sb[:], qt[:, 1024 * hf:1024 * (hf + 1)],
                     start=True, stop=True,
                     reads=[qk_in.tok(h * 2 + (0 if row0 < 1024 else 1)), (uid, "RT")],
                     writes=[(uid, "psr")])
                sch.op("scalar", lambda s_, dsta=dsta, scl=scl, hf=hf: s_.activation(
                    out=dsta[0:64, 1024 * hf:1024 * (hf + 1)], in_=ps_r[:],
                    func=AF.Copy, scale=scl),
                    reads=[(uid, "psr"), ("cst",)], writes=[dtok])
        va = v_aug[j]
        ones_cols = va.ap().rearrange("p (k c) -> p k c", c=80)[:, :, 64:65]
        sch.op("vector", lambda v, ones_cols=ones_cols: v.memset(ones_cols, 1.0),
               writes=[(uid, "vaug1", j)])
        for kc in range(16):
            sch.dma("sync", lambda sy, va=va, kc=kc, h=h: sy.dma_start(
                out=va[:, 80 * kc:80 * kc + 64],
                in_=v_scr[64 * h:64 * h + 64, 128 * kc:128 * (kc + 1)],
                transpose=True),
                sem=vtsem[j], reads=[(uid, "qkv", 0, 0)],
                writes=[(uid, "vaugd", j, kc)])
        for piece in range(2):
            sch.op("vector", lambda v, piece=piece, ka_=ka_, TK=TK: v.tensor_tensor(
                sqt, ka_[0:64, 1024 * piece:1024 * (piece + 1)],
                ka_[0:64, 1024 * piece:1024 * (piece + 1)], op=OP.mult),
                reads=[TK], writes=[(uid, "sqt")])
            mm_F(sch, ps_s[piece][0:1, :], env["ones128"][0:64, :], sqt,
                 start=True, stop=True, reads=[(uid, "sqt"), "ones128"],
                 writes=[(uid, "pss", piece)])
            sch.op("vector", lambda v, piece=piece, m2_=m2_: v.tensor_reduce(
                m2_[:, piece:piece + 1], ps_s[piece][0:1, :], axis=AX, op=OP.max),
                reads=[(uid, "pss", piece)], writes=[(uid, "m2r", j)])
        sch.op("vector", lambda v, m2_=m2_: v.tensor_reduce(
            m2_[:, 4:5], m2_[:, 0:2], axis=AX, op=OP.max),
            reads=[(uid, "m2r", j)], writes=[(uid, "M2", j)])
        sch.op("vector", lambda v, qa_=qa_, TQ=TQ: v.tensor_tensor(
            sqt, qa_[0:64, 0:1024], qa_[0:64, 0:1024], op=OP.mult),
            reads=[TQ], writes=[(uid, "sqt")])
        mm_F(sch, ps_s[0][0:1, :], env["ones128"][0:64, :], sqt,
             start=True, stop=True, reads=[(uid, "sqt"), "ones128"],
             writes=[(uid, "pss", 0)])
        sch.op("vector", lambda v, m2_=m2_: v.tensor_reduce(
            m2_[:, 5:6], ps_s[0][0:1, :], axis=AX, op=OP.max),
            reads=[(uid, "pss", 0)], writes=[(uid, "M2", j)])
        sch.op("scalar", lambda s_, m2_=m2_: s_.activation(
            out=m2_[:, 6:8], in_=m2_[:, 4:6], func=AF.Ln),
            reads=[(uid, "M2", j)], writes=[(uid, "M2", j)])
        sch.op("vector", lambda v, m2_=m2_: v.tensor_tensor(
            m2_[:, 8:9], m2_[:, 6:7], m2_[:, 7:8], op=OP.subtract),
            reads=[(uid, "M2", j)], writes=[(uid, "M2", j)])
        sch.op("vector", lambda v, m2_=m2_: v.tensor_tensor(
            m2_[:, 9:10], m2_[:, 6:7], m2_[:, 7:8], op=OP.add),
            reads=[(uid, "M2", j)], writes=[(uid, "M2", j)])
        sch.op("scalar", lambda s_, m2_=m2_: s_.activation(
            out=m2_[:, 8:9], in_=m2_[:, 8:9], func=AF.Exp, scale=0.5),
            reads=[(uid, "M2", j)], writes=[(uid, "M2", j)])
        sch.op("scalar", lambda s_, m2_=m2_: s_.activation(
            out=m2_[:, 9:10], in_=m2_[:, 9:10], func=AF.Exp, scale=0.5),
            reads=[(uid, "M2", j)], writes=[(uid, "M2", j)])
        sch.op("vector", lambda v, m2_=m2_: v.tensor_scalar(
            m2_[:, 8:9], m2_[:, 8:9], -0.5, None, op0=OP.mult),
            reads=[(uid, "M2", j)], writes=[(uid, "M2", j)])
        sch.op("vector", lambda v, m2_=m2_: v.tensor_scalar(
            m2_[:, 9:10], m2_[:, 9:10], 0.5, None, op0=OP.mult),
            reads=[(uid, "M2", j)], writes=[(uid, "M2", j)])
        sch.op("vector", lambda v, qa_=qa_, m2_=m2_, TQ=TQ, j=j: v.tensor_scalar(
            qa_[64:65, 0:1024], ps_s[0][0:1, :],
            m2_[:, 8:9], m2_[:, 9:10], op0=OP.mult, op1=OP.subtract),
            reads=[(uid, "pss", 0), (uid, "M2", j), (uid, "augpad", j)],
            writes=[TQ])

    def emit_qh(h, qh):
        j = h % 2
        qa_, ka_, m2_ = q_augs[j], k_augs[j], m2rows[j]
        TQ, TK = (uid, "qa", j), (uid, "ka", j)
        va = v_aug[j]
        tva = [(uid, "vaug1", j)] + [(uid, "vaugd", j, kc) for kc in range(16)]
        qlo_h = 1024 * qh
        if qh == 1:
            sch.op("vector", lambda v, qlo_h=qlo_h, qa_=qa_, TQ=TQ: v.tensor_tensor(
                sqt, qa_[0:64, qlo_h:qlo_h + 1024],
                qa_[0:64, qlo_h:qlo_h + 1024], op=OP.mult),
                reads=[TQ], writes=[(uid, "sqt")])
            mm_F(sch, ps_s[0][0:1, :], env["ones128"][0:64, :], sqt,
                 start=True, stop=True, reads=[(uid, "sqt"), "ones128"],
                 writes=[(uid, "pss", 0)])
            sch.op("vector", lambda v, qlo_h=qlo_h, qa_=qa_, m2_=m2_, TQ=TQ, j=j: v.tensor_scalar(
                qa_[64:65, qlo_h:qlo_h + 1024], ps_s[0][0:1, :],
                m2_[:, 8:9], m2_[:, 9:10], op0=OP.mult, op1=OP.subtract),
                reads=[(uid, "pss", 0), (uid, "M2", j), (uid, "augpad", j)],
                writes=[TQ])

        def s_mm(kc):
            idx = kc % 2
            mm_F(sch, ps_s[idx][:, :], ka_[0:128, 128 * kc:128 * (kc + 1)],
                 qa_[0:128, qlo_h:qlo_h + 1024], start=True, stop=True,
                 reads=[TQ, TK, (uid, "kones", j), (uid, "augpad", j)],
                 writes=[(uid, "pss", idx)])

        s_mm(0)
        for kc in range(16):
            idx = kc % 2
            sch.op("scalar", lambda s_, idx=idx: s_.activation(
                out=E[idx][:, :], in_=ps_s[idx][:, :], func=AF.Exp),
                reads=[(uid, "pss", idx)], writes=[(uid, "E", idx)])
            if kc == 1 and pend_shift[0] is not None:
                sf, pend_shift[0] = pend_shift[0], None
                sf()
            if kc + 1 < 16:
                s_mm(kc + 1)
            if kc == 0 and pending[0] is not None:
                ep, pending[0] = pending[0], None
                pend_shift[0] = epi(*ep, defer_shift=True)
            mm_F(sch, p1[:, :], va[:, 80 * kc:80 * kc + 65], E[idx][:, :],
                 start=(kc == 0), stop=(kc == 15),
                 reads=[(uid, "E", idx)] + tva, writes=[(uid, "p1")])
        pending[0] = (h, qh)

    def epi(h, qh, defer_shift=False):
        sch.op("scalar", lambda s_: s_.activation(out=rows[:, :], in_=p1[64:65, :], func=AF.Ln),
               reads=[(uid, "p1")], writes=[(uid, "rowA")])
        sch.op("scalar", lambda s_: s_.activation(out=rbh1[:, :], in_=rows[:, :],
                                                  func=AF.Exp, scale=-1.0),
               reads=[(uid, "rowA"), ("cst",)], writes=[(uid, "rowB")])
        mm_F(sch, ps_r[0:64, :], env["ones1"][:, 0:64], rbh1[:, :],
             start=True, stop=True, reads=[(uid, "rowB"), "ones1"],
             writes=[(uid, "psr")])
        sch.op("vector", lambda v: v.tensor_copy(o1[:], p1[0:64, :]),
               reads=[(uid, "p1")], writes=[(uid, "o1")])
        cdst = attn_fm[h // 2]
        if h % 2 == 0:
            sch.op("vector", lambda v, cdst=cdst, qh=qh: v.tensor_tensor(
                cdst[0:64, 1024 * qh:1024 * (qh + 1)], o1[:], ps_r[0:64, :], op=OP.mult),
                reads=[(uid, "o1"), (uid, "psr")], writes=[(uid, "afm", h // 2)])
            return None
        sch.op("vector", lambda v: v.tensor_tensor(
            comb, o1[:], ps_r[0:64, :], op=OP.mult),
            reads=[(uid, "o1"), (uid, "psr")], writes=[(uid, "comb")])
        sch.op("vector", lambda v: v.tensor_copy(sqt, comb),
               reads=[(uid, "comb")], writes=[(uid, "sqt")])

        def shift():
            mm_F(sch, ps_s[1][64:128, :], iden[0:64, 0:64], sqt,
                 start=True, stop=True, reads=[(uid, "sqt"), (uid, "iden")],
                 writes=[(uid, "pss", 1)], tile_position=(0, 64))
            sch.op("vector", lambda v, cdst=cdst, qh=qh: v.tensor_copy(
                cdst[64:128, 1024 * qh:1024 * (qh + 1)], ps_s[1][64:128, :]),
                reads=[(uid, "pss", 1)], writes=[(uid, "afm", h // 2)])
        if defer_shift:
            return shift
        shift()
        return None

    pending = [None]
    pend_shift = [None]
    prep_head(0)
    for h in range(16):
        emit_qh(h, 0)
        if h + 1 < 16:
            prep_head(h + 1)
        emit_qh(h, 1)
    if pending[0] is not None:
        ep, pending[0] = pending[0], None
        epi(*ep)
    a_pes.close()
    sch.barrier()

    ob_sb = t(f"{uid}_ob", [128, 8], F32)
    obsem = sch.new_dma_sem(f"{uid}_obsem")
    sch.dma("sync", lambda sy: sy.dma_start(out=ob_sb[:], in_=outb),
            sem=obsem, writes=[(uid, "ob")])
    qa_bf = q_aug.ap()
    wo_st = DBuf(nc, es, sch, f"{uid}_wo", None, None, queue="gpsimd",
                 tiles=[qa_bf[:, 0:1024], qa_bf[:, 1024:2048]])
    ka_f32 = k_aug.ap().bitcast(F32)
    xf2b = bases.ap().bitcast(F32)[:, 0:1024]
    xf2_st = DBuf(nc, es, sch, f"{uid}_xf2", None, None, queue="sync",
                  tiles=[ka_f32, xf2b])
    NH = 1024
    for half in range(2):
        hlo = half * NH
        with contextlib.ExitStack() as pes:
            p = lambda name, shape: pes.enter_context(nc.psum_tensor(name, shape, F32))
            ps_p = [p(f"{uid}_psp{half}_{i}", [128, NH]) for i in range(2)]
            for os in range(8):
                b = os % 2
                wtile = wo_st.load(os + 8 * half, ow_blk[os].rearrange("p c h -> p (c h)"))
                xf2_st.load(os + 8 * half, xT[128 * os:128 * (os + 1), hlo:hlo + NH], reads=x_dep)
                for c in range(8):
                    mm_F(sch, ps_p[b][:], wtile[:, 128 * c:128 * (c + 1)],
                         attn_fm[c][:, hlo:hlo + NH], start=(c == 0), stop=(c == 7),
                         reads=[wo_st.tok(os), (uid, "afm", c)], writes=[(uid, "psp", b)])
                sch.op("vector", lambda v, os=os, b=b: v.scalar_tensor_tensor(
                    y[os][:, 0:NH], ps_p[b][:], ob_sb[:, os:os + 1],
                    xf2_st.tiles[b], op0=OP.add, op1=OP.add),
                    reads=[(uid, "psp", b), xf2_st.tok(os), (uid, "ob")],
                    writes=[(uid, "y", os)])
            sch.barrier()
        with contextlib.ExitStack() as pes:
            ps_rows = pes.enter_context(nc.psum_tensor(f"{uid}_lnr{half}", [1, 2 * NH], F32)).ap()
            pb = pes.enter_context(nc.psum_tensor(f"{uid}_lnb{half}", [128, 2 * NH], F32)).ap()
            ob2 = [out_bf16[c][:, hlo:hlo + NH] for c in range(8)] if out_bf16 else None
            obt = ([(tok, half) for tok in out_bf16_toks] if out_bf16 else None)
            emit_layernorm(sch, env, [y[c][:, 0:NH] for c in range(8)],
                           [(uid, "y", c) for c in range(8)], ps_rows, pb, uid,
                           out_bf16=ob2, out_bf16_toks=obt,
                           out_dma=lambda c: (outT[128 * c:128 * (c + 1), hlo:hlo + NH], st_sem))
            sch.barrier()


def build_l1_attn_standalone():
    nc = bass.Bass("TRN2", target_bir_lowering=False)
    xT = nc.dram_tensor("xT", [D, S], F32, kind="ExternalInput").ap()
    bw_blk = nc.dram_tensor("bw_blk", [24, 128, 8, 128], F32, kind="ExternalInput").ap()
    swT = nc.dram_tensor("swT", [8192, 3072], F32, kind="ExternalInput").ap()
    sclT = nc.dram_tensor("sclT", [1024, 3072], F32, kind="ExternalInput").ap()
    grid_h = nc.dram_tensor("grid_h", [128, 96], F32, kind="ExternalInput").ap()
    RT_h = nc.dram_tensor("RT_h", [64, 64], F32, kind="ExternalInput").ap()
    ow_blk = nc.dram_tensor("ow_blk", [8, 128, 8, 128], F32, kind="ExternalInput").ap()
    outb = nc.dram_tensor("outb", [128, 8], F32, kind="ExternalInput").ap()
    iden_h = nc.dram_tensor("iden_h", [128, 128], F32, kind="ExternalInput").ap()
    sw_scr = nc.dram_tensor("sw_scr", [3, 64, 128, 1024], BF16, kind="Internal").ap()
    qk_scr = nc.dram_tensor("qk_scr", [2048, S], BF16, kind="Internal").ap()
    v_scr = nc.dram_tensor("v_scr", [D, S], BF16, kind="Internal").ap()
    outT = nc.dram_tensor("outT", [D, S], F32, kind="ExternalOutput").ap()
    es = contextlib.ExitStack()
    with es:
        sch = Sched(nc, es)
        env = make_env(nc, es, sch)
        env["iden_h"] = iden_h[:, :]
        st_sem = es.enter_context(nc.semaphore("sem_st"))
        with contextlib.ExitStack() as ses:
            pump = emit_l1_sw_stage(nc, sch, ses, env, swT[:, :], sclT,
                                    grid_h[:, :], sw_scr, uid="a1")
            while pump():
                pass
            sch.retire_dmas()
            sch.barrier()
        emit_l1_attn(nc, sch, es, env, xT, bw_blk, swT[:, :], sclT, grid_h[:, :],
                     RT_h[:, :], ow_blk, outb[:, :], sw_scr, qk_scr, v_scr, outT, st_sem)
        sch._emit("sync", lambda e: e.nop(), [(st_sem, sch.count(st_sem))], None, 1)
        sch.barrier()
    return nc


def prep_l1_attn_inputs(x_bt, base_weight, spline_weight, spline_scaler, grid,
                        rotation_matrix, out_w, out_b):
    xT = np.ascontiguousarray(x_bt.T)
    blk = lambda w, nin, nout: np.ascontiguousarray(
        w.T.reshape(nin // 128, 128, nout // 128, 128).transpose(2, 1, 0, 3))
    swT = np.ascontiguousarray(spline_weight.transpose(2, 1, 0).reshape(8192, 3072))
    sclT = np.ascontiguousarray(spline_scaler.T)
    grid_h = np.ascontiguousarray(grid.reshape(8, 128, 12).transpose(1, 0, 2).reshape(128, 96))
    return {"xT": xT, "bw_blk": blk(base_weight, 1024, 3072), "swT": swT,
            "sclT": sclT, "grid_h": grid_h,
            "RT_h": np.ascontiguousarray(rotation_matrix.T),
            "ow_blk": blk(out_w, 1024, 1024),
            "outb": np.ascontiguousarray(out_b.reshape(8, 128).T),
            "iden_h": np.eye(128, dtype=np.float32)}



_PROG_CACHE = {}


def _get_prog(name, builder):
    if name not in _PROG_CACHE:
        _PROG_CACHE[name] = builder()
    return _PROG_CACHE[name]


def kernel_unfused(L0_attn_x, L0_attn_Wq, L0_attn_bq, L0_attn_Wk, L0_attn_bk,
           L0_attn_Wv, L0_attn_bv, L0_attn_Wo, L0_attn_bo, L0_attn_head_norm_w,
           L0_attn_lq1, L0_attn_lk1, L0_attn_lq2, L0_attn_lk2,
           L0_ffn_w_gate, L0_ffn_w_up, L0_ffn_w_down,
           L1_attn_base_weight, L1_attn_spline_weight, L1_attn_spline_scaler,
           L1_attn_grid, L1_attn_rotation_matrix, L1_attn_out_w, L1_attn_out_b,
           L1_ffn_w1_w, L1_ffn_w1_s, L1_ffn_w2_w, L1_ffn_w2_s,
           L1_ffn_h1_w, L1_ffn_h1_s, L1_ffn_h2_w, L1_ffn_h2_s,
           L1_ffn_w3_w, L1_ffn_w3_s):
    B = L0_attn_x.shape[0]
    x = np.ascontiguousarray(L0_attn_x)

    common_a0 = prep_l0_attn_inputs(
        x[0], L0_attn_Wq, L0_attn_bq, L0_attn_Wk, L0_attn_bk, L0_attn_Wv,
        L0_attn_bv, L0_attn_Wo, L0_attn_bo, L0_attn_head_norm_w,
        L0_attn_lq1, L0_attn_lk1, L0_attn_lq2, L0_attn_lk2)
    in_maps = []
    for b in range(B):
        m = dict(common_a0)
        m["xT"] = np.ascontiguousarray(x[b].T)
        in_maps.append(m)
    nc = _get_prog("l0attn", build_l0_attn_standalone)
    res = run_bass_kernel_spmd(nc, in_maps, core_ids=list(range(B)))
    x1 = [r["outT"] for r in res.results]

    common_f0 = prep_l0_ffn_inputs(x[0], L0_ffn_w_gate, L0_ffn_w_up, L0_ffn_w_down)
    in_maps = []
    for b in range(B):
        m = dict(common_f0)
        m["xT"] = x1[b]
        in_maps.append(m)
    nc = _get_prog("l0ffn", build_l0_ffn_standalone)
    res = run_bass_kernel_spmd(nc, in_maps, core_ids=list(range(B)))
    x2 = [r["outT"] for r in res.results]

    common_a1 = prep_l1_attn_inputs(
        x[0], L1_attn_base_weight, L1_attn_spline_weight, L1_attn_spline_scaler,
        L1_attn_grid, L1_attn_rotation_matrix, L1_attn_out_w, L1_attn_out_b)
    in_maps = []
    for b in range(B):
        m = dict(common_a1)
        m["xT"] = x2[b]
        in_maps.append(m)
    nc = _get_prog("l1attn", build_l1_attn_standalone)
    res = run_bass_kernel_spmd(nc, in_maps, core_ids=list(range(B)))
    x3 = [r["outT"] for r in res.results]

    common_f1 = prep_l1_ffn_inputs(
        x[0], L1_ffn_w1_w, L1_ffn_w1_s, L1_ffn_w2_w, L1_ffn_w2_s,
        L1_ffn_h1_w, L1_ffn_h1_s, L1_ffn_h2_w, L1_ffn_h2_s,
        L1_ffn_w3_w, L1_ffn_w3_s)
    in_maps = []
    for b in range(B):
        m = dict(common_f1)
        m["xT"] = x3[b]
        in_maps.append(m)
    nc = _get_prog("l1ffn", build_l1_ffn_standalone)
    res = run_bass_kernel_spmd(nc, in_maps, core_ids=list(range(B)))
    out = np.stack([r["outT"].T for r in res.results], axis=0)
    return out



def build_full():
    nc = bass.Bass("TRN2", target_bir_lowering=False)
    ap = lambda n, s, d, k: nc.dram_tensor(n, s, d, kind=k).ap()
    x0T = ap("xT", [D, S], F32, "ExternalInput")
    wq_blk = ap("wq_blk", [16, 128, 8, 128], F32, "ExternalInput")
    wk_blk = ap("wk_blk", [16, 128, 8, 128], F32, "ExternalInput")
    wv_blk = ap("wv_blk", [8, 128, 8, 128], F32, "ExternalInput")
    wo_blk = ap("wo_blk", [8, 128, 8, 128], F32, "ExternalInput")
    biases = ap("biases", [128, 48], F32, "ExternalInput")
    lqk = ap("lqk", [64, 64], F32, "ExternalInput")
    hnw = ap("hnw", [64, 16], F32, "ExternalInput")
    cs_tab = ap("cs_tab", [128, S], F32, "ExternalInput")
    ss_tab = ap("ss_tab", [128, S], F32, "ExternalInput")
    perm = ap("perm", [128, 128], F32, "ExternalInput")
    ut = ap("ut", [128, 128], F32, "ExternalInput")
    iden = ap("iden", [128, 128], F32, "ExternalInput")
    wg_blk = ap("wg_blk", [24, 128, 8, 128], I32, "ExternalInput")
    wu_blk = ap("wu_blk", [24, 128, 8, 128], I32, "ExternalInput")
    wd_blk = ap("wd_blk", [8, 128, 24, 128], I32, "ExternalInput")
    bw_blk = ap("bw_blk", [24, 128, 8, 128], F32, "ExternalInput")
    swT = ap("swT", [8192, 3072], F32, "ExternalInput")
    sclT = ap("sclT", [1024, 3072], F32, "ExternalInput")
    grid_h = ap("grid_h", [128, 96], F32, "ExternalInput")
    RT_h = ap("RT_h", [64, 64], F32, "ExternalInput")
    ow_blk = ap("ow_blk", [8, 128, 8, 128], F32, "ExternalInput")
    outb = ap("outb", [128, 8], F32, "ExternalInput")
    w1_blk = ap("w1_blk", [32, 128, 8, 128], F32, "ExternalInput")
    w2_blk = ap("w2_blk", [32, 128, 8, 128], F32, "ExternalInput")
    h1_blk = ap("h1_blk", [32, 128, 32, 128], F32, "ExternalInput")
    h2_blk = ap("h2_blk", [32, 128, 32, 128], F32, "ExternalInput")
    w3_blk = ap("w3_blk", [8, 128, 32, 128], F32, "ExternalInput")
    scl = ap("scl", [128, 4 * 32 + 8], F32, "ExternalInput")
    v_scr0 = ap("v_scr0", [D, S], BF16, "Internal")
    sw_scr = ap("sw_scr", [3, 64, 128, 1024], BF16, "Internal")
    qk_scr = ap("qk_scr", [2048, S], BF16, "Internal")
    v_scr1 = ap("v_scr1", [D, S], BF16, "Internal")
    x1T = ap("x1T", [D, S], F32, "Internal")
    x2T = ap("x2T", [D, S], F32, "Internal")
    x3T = ap("x3T", [D, S], F32, "Internal")
    outT = ap("outT", [D, S], F32, "ExternalOutput")

    es = contextlib.ExitStack()
    with es:
        sch = Sched(nc, es)
        env = make_env(nc, es, sch)
        env["iden_h"] = iden
        sts = [sch.new_dma_sem(f"st{i}") for i in range(4)]
        with contextlib.ExitStack() as bes:
            emit_l0_attn(nc, sch, bes, env, x0T, wq_blk, wk_blk, wv_blk, wo_blk,
                         biases[:, :], lqk, hnw, cs_tab, ss_tab, perm, ut, iden,
                         v_scr0, x1T, sts[0], uid="a0")
            sch.retire_dmas()
            sch.barrier()
        with contextlib.ExitStack() as bes:
            pump = emit_l1_sw_stage(nc, sch, bes, env, swT[:, :], sclT, grid_h,
                                    sw_scr, uid="a1")
            emit_l0_ffn(nc, sch, bes, env, x1T, wg_blk, wu_blk, wd_blk, x2T,
                        sts[1], uid="f0", x_dep=[("a0", "xout")], pump=pump)
            while pump():
                pass
            sch.retire_dmas()
            sch.barrier()
        with contextlib.ExitStack() as bes:
            emit_l1_attn(nc, sch, bes, env, x2T, bw_blk, swT[:, :], sclT, grid_h,
                         RT_h, ow_blk, outb[:, :], sw_scr, qk_scr, v_scr1, x3T,
                         sts[2], uid="a1", x_dep=[("f0", "xout")])
            sch.retire_dmas()
            sch.barrier()
        with contextlib.ExitStack() as bes:
            emit_l1_ffn(nc, sch, bes, env, x3T, w1_blk, w2_blk, h1_blk, h2_blk,
                        w3_blk, scl, outT, sts[3], uid="f1", x_dep=[("a1", "xout")])
            sch.retire_dmas()
            sch.barrier()
        sch._emit("sync", lambda e: e.nop(), [(sts[3], sch.count(sts[3]))], None, 1)
        sch.retire_dmas("sync")
        sch.barrier()
    return nc


def prep_full_inputs(inputs):
    g = inputs
    m = {}
    m.update(prep_l0_attn_inputs(
        g["L0_attn_x"][0], g["L0_attn_Wq"], g["L0_attn_bq"], g["L0_attn_Wk"],
        g["L0_attn_bk"], g["L0_attn_Wv"], g["L0_attn_bv"], g["L0_attn_Wo"],
        g["L0_attn_bo"], g["L0_attn_head_norm_w"], g["L0_attn_lq1"],
        g["L0_attn_lk1"], g["L0_attn_lq2"], g["L0_attn_lk2"]))
    m.update(prep_l0_ffn_inputs(g["L0_attn_x"][0], g["L0_ffn_w_gate"],
                                g["L0_ffn_w_up"], g["L0_ffn_w_down"]))
    m.update(prep_l1_attn_inputs(
        g["L0_attn_x"][0], g["L1_attn_base_weight"], g["L1_attn_spline_weight"],
        g["L1_attn_spline_scaler"], g["L1_attn_grid"],
        g["L1_attn_rotation_matrix"], g["L1_attn_out_w"], g["L1_attn_out_b"]))
    m.update(prep_l1_ffn_inputs(
        g["L0_attn_x"][0], g["L1_ffn_w1_w"], g["L1_ffn_w1_s"], g["L1_ffn_w2_w"],
        g["L1_ffn_w2_s"], g["L1_ffn_h1_w"], g["L1_ffn_h1_s"], g["L1_ffn_h2_w"],
        g["L1_ffn_h2_s"], g["L1_ffn_w3_w"], g["L1_ffn_w3_s"]))
    m.pop("iden_h", None)
    return m


def kernel(**inputs):
    x = np.ascontiguousarray(inputs["L0_attn_x"])
    B = x.shape[0]
    common = prep_full_inputs(inputs)
    in_maps = []
    for b in range(B):
        m = dict(common)
        m["xT"] = np.ascontiguousarray(x[b].T)
        in_maps.append(m)
    nc = _get_prog("full", build_full)
    res = run_bass_kernel_spmd(nc, in_maps, core_ids=list(range(B)))
    return np.stack([r["outT"].T for r in res.results], axis=0)
